# version 09
# baseline (speedup 1.0000x reference)
import math
from contextlib import ExitStack
import numpy as np
import ml_dtypes
import concourse.bass as bass
import concourse.mybir as mybir
from concourse.bass_utils import run_bass_kernel_spmd

F32 = mybir.dt.float32
BF16 = mybir.dt.bfloat16
AF = mybir.ActivationFunctionType
ALU = mybir.AluOpType
AX = mybir.AxisListType
BF = ml_dtypes.bfloat16

PD = 128


class TBuf:
    __slots__ = ("w", "r")
    def __init__(self):
        self.w = {}
        self.r = {}

def _merge(dst, src):
    for k, v in src.items():
        if dst.get(k, -1) < v:
            dst[k] = v

class Ctx:
    SEM_OF = {"tensor": "T", "vector": "V", "scalar": "S", "gpsimd": "G", "sync": "Y"}
    SECTIONS = ("tensor", "vector", "scalar", "gpsimd", "sync")
    NDMA = 64

    SEM_LIM = 30000

    def __init__(self, nc):
        self.nc = nc
        self.ops = {s: [] for s in self.SECTIONS}
        self.cnt = {}
        self.cur = {}
        for s in ("T", "V", "S", "G", "Y"):
            self._new_sem(s)
        for i in range(self.NDMA):
            self._new_sem(f"D{i}")
        self.regions = {}
        self.floor = {}
        self.chmap = {}
        self.chprev = {}
        self.chfree = {"hw": list(range(self.NDMA - 12)),
                       "sw": list(range(self.NDMA - 12, self.NDMA))}

    def _new_sem(self, base):
        n = 0
        name = base
        while name in self.cnt:
            n += 1
            name = f"{base}.{n}"
        self.cnt[name] = 0
        self.cur[base] = name
        return name

    def _sem_for(self, base, inc):
        name = self.cur[base]
        if self.cnt[name] + inc > self.SEM_LIM:
            name = self._new_sem(base)
        return name

    def tb(self, name, key=None):
        reg = self.regions.setdefault(name, {})
        if key not in reg:
            t = TBuf()
            base = reg.get(None)
            if base is not None and key is not None:
                _merge(t.w, base.w)
                _merge(t.r, base.r)
            reg[key] = t
        return reg[key]

    def fence(self, name):
        reg = self.regions.get(name)
        if not reg:
            return
        base = TBuf()
        for t in reg.values():
            _merge(base.w, t.w)
            _merge(base.r, t.r)
        self.regions[name] = {None: base}

    def phase(self, sb):
        self.global_barrier()
        sb.close()
        return SB(self.nc)

    def global_barrier(self):
        self.floor = dict(self.cnt)
        self.regions = {}
        self.chmap = {}
        self.chprev = {}
        self.chfree = {"hw": list(range(self.NDMA - 12)),
                       "sw": list(range(self.NDMA - 12, self.NDMA))}

    def _chan(self, ch, section):
        kind = "sw" if section == "gpsimd" else "hw"
        if ch not in self.chmap:
            if not self.chfree[kind]:
                raise RuntimeError("out of DMA channel sems")
            idx = self.chfree[kind].pop(0)
            self.chmap[ch] = (f"D{idx}", kind)
        sem, k0 = self.chmap[ch]
        assert k0 == kind, f"channel {ch} used from both SWDGE and HWDGE"
        return sem

    def emit(self, section, fns, reads=(), writes=(), dma=False, ch=None):
        if callable(fns):
            fns = [fns]
        if dma:
            assert ch is not None, "dma emit needs a channel"
            semname = self._sem_for(self._chan(ch, section), 16)
        else:
            semname = self._sem_for(self.SEM_OF[section], 1)
        deps = dict(self.floor)
        for b in reads:
            _merge(deps, b.w)
        for b in writes:
            _merge(deps, b.w)
            _merge(deps, b.r)
        if dma:
            prev = self.chprev.get(ch)
            if prev is not None:
                ptok, pwrites = prev
                _merge(deps, ptok)
                for pb in pwrites:
                    _merge(deps, pb.r)
        if section == "tensor" and not dma:
            deps.pop("T", None)
        inc = 16 if dma else 1
        self.cnt[semname] += inc
        tok = {semname: self.cnt[semname]}
        self.ops[section].append((fns, deps, semname, inc))
        for b in writes:
            b.w = dict(tok)
            b.r = {}
        for b in reads:
            _merge(b.r, tok)
        if dma:
            self.chprev[ch] = (dict(tok), list(writes))
        return tok

    def finalize(self, final_waits=None):
        nc = self.nc
        if final_waits is None:
            final_waits = dict(self.cnt)
        stack = ExitStack()
        sems = {s: stack.enter_context(nc.semaphore(f"sem_{s}")) for s in self.cnt}
        blk = stack.enter_context(nc.Block())

        def body(section):
            def run(eng):
                waited = {}
                for fns, deps, semname, inc in self.ops[section]:
                    for ds in sorted(deps):
                        dv = deps[ds]
                        if dv > 0 and waited.get(ds, -1) < dv:
                            eng.wait_ge(sems[ds], dv)
                            waited[ds] = dv
                    inst = None
                    for fn in fns:
                        inst = fn(eng)
                    inst.then_inc(sems[semname], inc)
                if section == "sync":
                    for ds in sorted(final_waits):
                        dv = final_waits[ds]
                        if dv > 0 and waited.get(ds, -1) < dv:
                            eng.wait_ge(sems[ds], dv)
                            waited[ds] = dv
            return run

        blk.tensor(body("tensor"))
        blk.vector(body("vector"))
        blk.scalar(body("scalar"))
        blk.gpsimd(body("gpsimd"))
        blk.sync(body("sync"))
        stack.close()


class SB:
    _uid = [0]
    def __init__(self, nc):
        self.nc = nc
        self.stack = ExitStack()

    def alloc(self, shape, dtype, name=None):
        SB._uid[0] += 1
        name = name or f"sb_{SB._uid[0]}"
        return self.stack.enter_context(self.nc.sbuf_tensor(name, shape, dtype))

    def close(self):
        self.stack.close()


def cdiv(a, b):
    return (a + b - 1) // b


BIG_CONSTS = {"ABSD", "xq_c", "xq_s", "xk_c", "xk_s", "ropecos", "ropesin",
              "lmask", "cmask"}


def cst_lazy(c, sb, cst, name):
    d, shape, dt = cst["_lazy"][name]
    t = sb.alloc(list(shape), dt)
    c.emit("sync", lambda e, t=t, d=d: e.dma_start(t[:, :], d),
           reads=[], writes=[c.tb("cst")], dma=True, ch="CZ" + name)
    return t



def emit_prenorm(c, sb, cst, P, src, dst, T, TSn, mode="ln", gamma_col=None,
                 pname="pn", src_bf=False, beta_col=None, shift_half=None,
                 dst_f32=None):
    E = src.shape[0]
    EC = E // PD
    nslab = T // TSn
    xa = [sb.alloc([PD, EC * TSn], F32) for _ in range(2)]
    sqa = [sb.alloc([PD, TSn], F32) for _ in range(2)]
    rows = [sb.alloc([1, 3 * TSn], F32) for _ in range(2)]
    xnb = [sb.alloc([PD, EC * TSn], BF16) for _ in range(2)]
    tnorm = [sb.alloc([PD, TSn], F32) for _ in range(2)]
    src_r = src.rearrange("(ci p) t -> p ci t", p=PD)
    dst_r = dst.rearrange("(ci p) t -> p ci t", p=PD)
    bcst = c.tb("cst")
    eps = 1e-5
    tail = None
    for i in range(nslab):
        j = i % 2
        t0 = i * TSn
        psA = P[4 + j]
        psB = P[6 + j]
        bx = c.tb(pname + "x", j)
        bsq = c.tb(pname + "sq", j)
        brow = c.tb(pname + "row", j)
        bxn = c.tb(pname + "xn", j)
        btn = c.tb(pname + "tn", j)
        bpA = c.tb(f"psum{4 + j}")
        bpB = c.tb(f"psum{6 + j}")
        lq = "gpsimd" if src_bf else "sync"
        if shift_half is None:
            c.emit(lq, lambda e, j=j, t0=t0: e.dma_start(
                xa[j][:, :].rearrange("p (ci t) -> p ci t", ci=EC),
                src_r[:, :, t0:t0 + TSn]),
                reads=[c.tb(src.name)], writes=[bx], dma=True,
                ch=pname + f"L{j}")
        else:
            EH = EC // 2
            Tb_ = shift_half
            c.emit(lq, lambda e, j=j, t0=t0: e.dma_start(
                xa[j][:, 0:EH * TSn].rearrange("p (ci t) -> p ci t", ci=EH),
                src_r[:, 0:EH, t0:t0 + TSn]),
                reads=[c.tb(src.name)], writes=[bx], dma=True,
                ch=pname + f"L{j}")
            if t0 % Tb_ == 0:
                c.emit(lq, lambda e, j=j, t0=t0: e.dma_start(
                    xa[j][:, EH * TSn:].rearrange(
                        "p (ci t) -> p ci t", ci=EH)[:, :, 1:TSn],
                    src_r[:, EH:, t0:t0 + TSn - 1]),
                    reads=[c.tb(src.name)], writes=[bx], dma=True,
                    ch=pname + f"L{j}")
                c.emit("vector", lambda e, j=j: e.memset(
                    xa[j][:, EH * TSn:EH * TSn + 1]
                    .rearrange("p (ci t) -> p ci t", ci=1) if False else
                    xa[j][:, :].rearrange("p (ci t) -> p ci t", ci=EC)
                    [:, EH:, 0:1], 0.0),
                    reads=[], writes=[bx])
            else:
                c.emit(lq, lambda e, j=j, t0=t0: e.dma_start(
                    xa[j][:, EH * TSn:].rearrange("p (ci t) -> p ci t", ci=EH),
                    src_r[:, EH:, t0 - 1:t0 + TSn - 1]),
                    reads=[c.tb(src.name)], writes=[bx], dma=True,
                    ch=pname + f"L{j}")
        if mode == "ln":
            for ci in range(EC):
                c.emit("tensor", lambda e, j=j, ci=ci, psA=psA: e.matmul(
                    psA[:1, 0:TSn], cst["ones_col_f"][:, 0:1],
                    xa[j][:, ci * TSn:(ci + 1) * TSn],
                    start=(ci == 0), stop=(ci == EC - 1)),
                    reads=[bx, bcst], writes=[bpA])
        for ci in range(EC):
            c.emit("scalar", lambda e, j=j, ci=ci: e.square(
                sqa[j][:, :], xa[j][:, ci * TSn:(ci + 1) * TSn]),
                reads=[bx], writes=[bsq])
            c.emit("tensor", lambda e, j=j, ci=ci, psB=psB: e.matmul(
                psB[:1, 0:TSn], cst["ones_col_f"][:, 0:1], sqa[j][:, :],
                start=(ci == 0), stop=(ci == EC - 1)),
                reads=[bsq, bcst], writes=[bpB])
        A = rows[j][:1, 0:TSn]
        Brow = rows[j][:1, TSn:2 * TSn]
        TMP = rows[j][:1, 2 * TSn:3 * TSn]
        if mode == "ln":
            c.emit("vector", lambda e, Brow=Brow, psA=psA: e.tensor_scalar(
                Brow, psA[:1, 0:TSn], -1.0 / E, None, ALU.mult),
                reads=[bpA], writes=[brow])
            c.emit("vector", lambda e, Brow=Brow, TMP=TMP: e.tensor_tensor(
                TMP, Brow, Brow, ALU.mult),
                reads=[brow], writes=[brow])
            c.emit("vector", lambda e, TMP=TMP, psB=psB: e.scalar_tensor_tensor(
                TMP, psB[:1, 0:TSn], 1.0 / E, TMP, ALU.mult, ALU.subtract),
                reads=[bpB, brow], writes=[brow])
            c.emit("scalar", lambda e, TMP=TMP: e.activation(
                TMP, TMP, AF.Ln, bias=np.float32(eps)),
                reads=[brow], writes=[brow])
            c.emit("scalar", lambda e, A=A, TMP=TMP: e.activation(
                A, TMP, AF.Exp, scale=-0.5),
                reads=[brow], writes=[brow])
        else:
            c.emit("vector", lambda e, TMP=TMP, psB=psB: e.tensor_scalar(
                TMP, psB[:1, 0:TSn], 1e-24, None, ALU.max),
                reads=[bpB], writes=[brow])
            c.emit("scalar", lambda e, TMP=TMP: e.activation(
                TMP, TMP, AF.Ln, scale=np.float32(1.0 / E)),
                reads=[brow], writes=[brow])
            c.emit("scalar", lambda e, A=A, TMP=TMP: e.activation(
                A, TMP, AF.Exp, scale=-0.5),
                reads=[brow], writes=[brow])
        if tail is not None:
            tail()

        def mk_tail(i=i, j=j, t0=t0, psA=psA, psB=psB, bx=bx, brow=brow,
                    bpA=bpA, bpB=bpB, bxn=bxn, btn=btn, A=A, Brow=Brow):
            def run():
                c.emit("tensor", lambda e: e.matmul(
                    psA[:, 0:TSn], cst["ones_row_f"][:1, :], A,
                    start=True, stop=True),
                    reads=[brow, bcst], writes=[bpA])
                if mode == "ln":
                    c.emit("tensor", lambda e: e.matmul(
                        psB[:, 0:TSn], cst["ones_row_f"][:1, :], Brow,
                        start=True, stop=True),
                        reads=[brow, bcst], writes=[bpB])
                for ci in range(EC):
                    if mode == "ln":
                        c.emit("vector", lambda e, ci=ci: e.tensor_tensor(
                            tnorm[j][:, :], xa[j][:, ci * TSn:(ci + 1) * TSn],
                            psB[:, 0:TSn], ALU.add),
                            reads=[bx, bpB], writes=[btn])
                        if gamma_col is None:
                            c.emit("vector", lambda e, ci=ci:
                                   e.tensor_tensor(
                                xnb[j][:, ci * TSn:(ci + 1) * TSn],
                                tnorm[j][:, :], psA[:, 0:TSn], ALU.mult),
                                reads=[btn, bpA], writes=[bxn])
                        else:
                            c.emit("vector", lambda e, ci=ci:
                                   e.tensor_tensor(
                                tnorm[j][:, :], tnorm[j][:, :], psA[:, 0:TSn],
                                ALU.mult),
                                reads=[btn, bpA], writes=[btn])
                            c.emit("vector", lambda e, ci=ci: e.tensor_scalar(
                                xnb[j][:, ci * TSn:(ci + 1) * TSn],
                                tnorm[j][:, :],
                                gamma_col[:, ci:ci + 1], beta_col[:, ci:ci + 1],
                                ALU.mult, ALU.add),
                                reads=[btn, c.tb("cst")], writes=[bxn])
                    else:
                        c.emit("vector", lambda e, ci=ci:
                               e.scalar_tensor_tensor(
                            xnb[j][:, ci * TSn:(ci + 1) * TSn],
                            xa[j][:, ci * TSn:(ci + 1) * TSn],
                            gamma_col[:, ci:ci + 1], psA[:, 0:TSn],
                            ALU.mult, ALU.mult),
                            reads=[bx, bpA, bcst], writes=[bxn])
                c.emit("scalar", lambda e: e.dma_start(
                    dst_r[:, :, t0:t0 + TSn],
                    xnb[j][:, :].rearrange("p (ci t) -> p ci t", ci=EC)),
                    reads=[bxn], writes=[c.tb(dst.name, ("sl", i))], dma=True,
                    ch=pname + f"S{j}")
                if dst_f32 is not None:
                    c.emit("gpsimd", lambda e: e.dma_start(
                        dst_f32.rearrange("(ci p) t -> p ci t", p=PD)
                        [:, :, t0:t0 + TSn],
                        xnb[j][:, :].rearrange("p (ci t) -> p ci t", ci=EC)),
                        reads=[bxn], writes=[c.tb(dst_f32.name, ("sl", i))],
                        dma=True, ch=pname + f"F{j}")
            return run
        tail = mk_tail()
    tail()



def emit_proj(c, sb, cst, P, xn, W, out, *, T, TSn, bias_col=None, act=None,
              out_dt=BF16, res=None, res_out=None, mask_sb=None, mask_tb=None,
              banks=(0, 1, 2, 3), pname="pj", shift_half=None, res2=None,
              also_bf=None):
    din, dout = W.shape
    ncin = cdiv(din, PD)
    nco = cdiv(dout, PD)
    wsb = sb.alloc([PD, ncin * dout], BF16)
    bws = [c.tb(pname + "w", ci) for ci in range(ncin)]
    for ci in range(ncin):
        r0, r1 = ci * PD, min((ci + 1) * PD, din)
        c.emit("gpsimd", lambda e, ci=ci, r0=r0, r1=r1: e.dma_start(
            wsb[0:r1 - r0, ci * dout:(ci + 1) * dout], W[r0:r1, :]),
            reads=[c.tb(W.name)], writes=[bws[ci]], dma=True,
            ch=pname + f"W{ci % 3}")
    xh = [sb.alloc([PD, ncin * TSn], BF16) for _ in range(2)]
    ot = [sb.alloc([PD, TSn], out_dt) for _ in range(4)]
    rt = [sb.alloc([PD, TSn], F32) for _ in range(2)]
    t1 = [sb.alloc([PD, TSn], F32) for _ in range(2)]
    nslab = T // TSn
    nbk = len(banks)
    k = 0
    for s in range(nslab):
        t0 = s * TSn
        js = s % 2
        bxh = c.tb(pname + "xh", js)
        for ci in range(ncin):
            r0, r1 = ci * PD, min((ci + 1) * PD, din)
            shifted = shift_half is not None and ci >= ncin // 2
            if not shifted:
                c.emit("sync", lambda e, ci=ci, r0=r0, r1=r1, js=js, t0=t0:
                       e.dma_start(xh[js][0:r1 - r0, ci * TSn:(ci + 1) * TSn],
                                   xn[r0:r1, t0:t0 + TSn]),
                    reads=[c.tb(xn.name)], writes=[bxh], dma=True,
                    ch=pname + f"X{js}")
            elif t0 % shift_half == 0:
                c.emit("sync", lambda e, ci=ci, r0=r0, r1=r1, js=js, t0=t0:
                       e.dma_start(
                    xh[js][0:r1 - r0, ci * TSn + 1:(ci + 1) * TSn],
                    xn[r0:r1, t0:t0 + TSn - 1]),
                    reads=[c.tb(xn.name)], writes=[bxh], dma=True,
                    ch=pname + f"X{js}")
                c.emit("vector", lambda e, ci=ci, r0=r0, r1=r1, js=js: e.memset(
                    xh[js][0:r1 - r0, ci * TSn:ci * TSn + 1], 0.0),
                    reads=[], writes=[bxh])
            else:
                c.emit("sync", lambda e, ci=ci, r0=r0, r1=r1, js=js, t0=t0:
                       e.dma_start(xh[js][0:r1 - r0, ci * TSn:(ci + 1) * TSn],
                                   xn[r0:r1, t0 - 1:t0 + TSn - 1]),
                    reads=[c.tb(xn.name)], writes=[bxh], dma=True,
                    ch=pname + f"X{js}")
        for oc in range(nco):
            o0, o1 = oc * PD, min((oc + 1) * PD, dout)
            mc = o1 - o0
            bk = banks[k % nbk]
            jr = k % 2
            ko = k % 4
            k += 1
            ps = P[bk]
            bps = c.tb(f"psum{bk}")
            for ci in range(ncin):
                c.emit("tensor",
                       lambda e, ci=ci, js=js, first=(ci == 0),
                       last=(ci == ncin - 1), mc=mc, o0=o0, ps=ps: e.matmul(
                    ps[0:mc, 0:TSn],
                    wsb[0:min(PD, din - ci * PD), ci * dout + o0:ci * dout + o0 + mc],
                    xh[js][0:min(PD, din - ci * PD), ci * TSn:(ci + 1) * TSn],
                    start=first, stop=last),
                    reads=[bws[ci], bxh], writes=[bps])
            fn = act if act is not None else AF.Identity
            bias_ap = bias_col[0:mc, oc:oc + 1] if bias_col is not None else 0.0
            if res is None and mask_sb is None:
                bo = c.tb(pname + "ot", ko)
                c.emit("scalar", lambda e, ps=ps, mc=mc, fn=fn, bias_ap=bias_ap,
                       ko=ko: e.activation(
                    ot[ko][0:mc, :], ps[0:mc, 0:TSn], fn, bias=bias_ap),
                    reads=[bps, c.tb("cst")], writes=[bo])
                c.emit("scalar", lambda e, ko=ko, mc=mc, o0=o0, o1=o1, t0=t0:
                       e.dma_start(out[o0:o1, t0:t0 + TSn], ot[ko][0:mc, :]),
                    reads=[bo], writes=[c.tb(out.name, ("t", s, oc))], dma=True,
                    ch=pname + f"S{ko}")
            else:
                bt1 = c.tb(pname + "t1", jr)
                c.emit("scalar", lambda e, ps=ps, mc=mc, fn=fn, bias_ap=bias_ap,
                       jr=jr: e.activation(
                    t1[jr][0:mc, :], ps[0:mc, 0:TSn], fn, bias=bias_ap),
                    reads=[bps, c.tb("cst")], writes=[bt1])
                if mask_sb is not None:
                    c.emit("vector", lambda e, jr=jr, mc=mc, t0=t0: e.tensor_tensor(
                        t1[jr][0:mc, :], t1[jr][0:mc, :], mask_sb[0:mc, t0:t0 + TSn],
                        ALU.mult),
                        reads=[bt1, mask_tb], writes=[bt1])
                if res is not None:
                    brt = c.tb(pname + "rt", jr)
                    c.emit("sync", lambda e, jr=jr, o0=o0, o1=o1, t0=t0: e.dma_start(
                        rt[jr][0:o1 - o0, :], res[o0:o1, t0:t0 + TSn]),
                        reads=[c.tb(res.name)], writes=[brt], dma=True,
                        ch=pname + f"R{jr}")
                    bo2 = c.tb(pname + "o2", ko)
                    c.emit("vector", lambda e, jr=jr, mc=mc, ko=ko: e.tensor_tensor(
                        ot[ko][0:mc, :], t1[jr][0:mc, :], rt[jr][0:mc, :], ALU.add),
                        reads=[bt1, brt], writes=[bo2])
                    dst = res_out if res_out is not None else out
                    c.emit("scalar", lambda e, ko=ko, mc=mc, o0=o0, o1=o1, t0=t0:
                           e.dma_start(dst[o0:o1, t0:t0 + TSn], ot[ko][0:mc, :]),
                        reads=[bo2], writes=[c.tb(dst.name, ("t", s, oc))],
                        dma=True, ch=pname + f"S{ko}")
                    if also_bf is not None:
                        c.emit("gpsimd", lambda e, ko=ko, mc=mc, o0=o0, o1=o1,
                               t0=t0: e.dma_start(
                            also_bf[o0:o1, t0:t0 + TSn], ot[ko][0:mc, :]),
                            reads=[bo2],
                            writes=[c.tb(also_bf.name, ("t", s, oc))],
                            dma=True, ch=pname + f"B{ko % 2}")
                else:
                    c.emit("scalar", lambda e, jr=jr, mc=mc, o0=o0, o1=o1, t0=t0:
                           e.dma_start(out[o0:o1, t0:t0 + TSn], t1[jr][0:mc, :]),
                        reads=[bt1], writes=[c.tb(out.name, ("t", s, oc))],
                        dma=True, ch=pname + f"St{jr}")


def emit_vtm_proj(c, sb, cst, P, xn, W, out, *, T, TSn, bias_row=None,
                  pname="vt", banks=(0, 1), shift_half=None):
    din, dout = W.shape
    ncin = din // PD
    ktn = T // PD
    nno = cdiv(dout, TSn)
    wsb = sb.alloc([PD, ncin * dout], BF16)
    bws = [c.tb(pname + "w", ci) for ci in range(ncin)]
    for ci in range(ncin):
        c.emit("gpsimd", lambda e, ci=ci: e.dma_start(
            wsb[:, ci * dout:(ci + 1) * dout], W[ci * PD:(ci + 1) * PD, :]),
            reads=[c.tb(W.name)], writes=[bws[ci]], dma=True,
            ch=pname + f"W{ci % 3}")
    brow = None
    if bias_row is not None:
        bsb = sb.alloc([1, dout], BF16)
        brow = c.tb(pname + "b")
        c.emit("gpsimd", lambda e: e.dma_start(bsb[:1, :], bias_row[:1, :]),
               reads=[c.tb(bias_row.name)], writes=[brow], dma=True,
               ch=pname + "W")
    xk = [sb.alloc([PD, ncin * PD], BF16) for _ in range(2)]
    ot = [sb.alloc([PD, dout], BF16) for _ in range(2)]
    xn_r = xn.rearrange("(ci p) t -> p ci t", p=PD)
    for kt in range(ktn):
        j = kt % 2
        bxk = c.tb(pname + "xk", j)
        bot = c.tb(pname + "ot", j)
        nl = ncin // 2 if shift_half is not None else ncin
        c.emit("sync", lambda e, kt=kt, j=j, nl=nl: e.dma_start(
            xk[j][:, 0:nl * PD].rearrange("p (ci q) -> p ci q", ci=nl),
            xn_r[:, 0:nl, kt * PD:(kt + 1) * PD]),
            reads=[c.tb(xn.name)], writes=[bxk], dma=True, ch=pname + f"X{j}")
        if shift_half is not None:
            t0 = kt * PD
            if t0 % shift_half == 0:
                c.emit("sync", lambda e, j=j, nl=nl, t0=t0: e.dma_start(
                    xk[j][:, nl * PD:].rearrange(
                        "p (ci q) -> p ci q", ci=nl)[:, :, 1:PD],
                    xn_r[:, nl:, t0:t0 + PD - 1]),
                    reads=[c.tb(xn.name)], writes=[bxk], dma=True,
                    ch=pname + f"X{j}")
                c.emit("vector", lambda e, j=j, nl=nl: e.memset(
                    xk[j][:, :].rearrange("p (ci q) -> p ci q", ci=ncin)
                    [:, nl:, 0:1], 0.0),
                    reads=[], writes=[bxk])
            else:
                c.emit("sync", lambda e, j=j, nl=nl, t0=t0: e.dma_start(
                    xk[j][:, nl * PD:].rearrange("p (ci q) -> p ci q", ci=nl),
                    xn_r[:, nl:, t0 - 1:t0 + PD - 1]),
                    reads=[c.tb(xn.name)], writes=[bxk], dma=True,
                    ch=pname + f"X{j}")
        for no in range(nno):
            n0, n1 = no * TSn, min((no + 1) * TSn, dout)
            bk = banks[(kt * nno + no) % len(banks)]
            ps = P[bk]
            bps = c.tb(f"psum{bk}")
            for ci in range(ncin):
                c.emit("tensor",
                       lambda e, ci=ci, j=j, n0=n0, n1=n1, ps=ps,
                       first=(ci == 0): e.matmul(
                    ps[:, 0:n1 - n0], xk[j][:, ci * PD:(ci + 1) * PD],
                    wsb[:, ci * dout + n0:ci * dout + n1],
                    start=first, stop=(bias_row is None and ci == ncin - 1)),
                    reads=[bws[ci], bxk], writes=[bps])
            if bias_row is not None:
                c.emit("tensor", lambda e, n0=n0, n1=n1, ps=ps: e.matmul(
                    ps[:, 0:n1 - n0], cst["ones_row_bf"][:1, 0:PD],
                    bsb[:1, n0:n1], start=False, stop=True),
                    reads=[brow, c.tb("cst")], writes=[bps])
            c.emit("vector", lambda e, j=j, ps=ps, n0=n0, n1=n1: e.tensor_copy(
                ot[j][:, n0:n1], ps[:, 0:n1 - n0]),
                reads=[bps], writes=[bot])
        c.emit("scalar", lambda e, kt=kt, j=j: e.dma_start(
            out[kt * PD:(kt + 1) * PD, 0:dout], ot[j][:, :]),
            reads=[bot], writes=[c.tb(out.name, ("kt", kt))], dma=True,
            ch=pname + f"S{j}")



def emit_attn(c, sb, cst, P, *, qf, kf, vtm, out, Tb, nb, Hq, hdq, hdv,
              TQ, scale, Cshift, causal, decay=False, pname="at", local=False):
    HK = Hq * hdq
    KC = cdiv(HK, PD)
    Vw = Hq * (hdv + 1)
    ktn = Tb // PD
    nqb = Tb // TQ
    ksb = sb.alloc([PD, KC * Tb], BF16)
    vsb = sb.alloc([PD, ktn * Vw], BF16)
    qsb = [sb.alloc([PD, KC * TQ], BF16) for _ in range(2)]
    NPB = 4
    pbuf = [sb.alloc([PD, TQ], BF16) for _ in range(NPB)]
    sdec = [sb.alloc([PD, TQ], F32) for _ in range(2)]
    rrow = [sb.alloc([hdv + 1, TQ], F32) for _ in range(2)]
    rr2 = [sb.alloc([hdv + 1, TQ], F32) for _ in range(2)]
    rb = [sb.alloc([PD, TQ], F32) for _ in range(2)]
    obuf = [sb.alloc([PD, TQ], BF16) for _ in range(2)]
    absd_t = cst_lazy(c, sb, cst, "ABSD") if decay else None
    cmask_t = cst_lazy(c, sb, cst, "cmask") if causal else None
    lmask_t = cst_lazy(c, sb, cst, "lmask") if local else None
    pdbg = cst.get("_pdbg")
    SNM = [0, 1, 4, 7] if decay else [0, 1, 4]
    NSB = len(SNM)
    POB, PON = [P[2], P[5]], [2, 5]
    PBB, PBN = [P[3], P[6]], [3, 6]
    decT = None
    if decay:
        Wd = absd_t.shape[1]
        decT = sb.alloc([PD, Wd], F32)
        bdecT = c.tb(pname + "decT")
        for u0 in range(0, Wd, 512):
            u1 = min(u0 + 512, Wd)
            c.emit("scalar", lambda e, u0=u0, u1=u1: e.activation(
                decT[:, u0:u1], absd_t[:, u0:u1], AF.Exp,
                scale=cst["negabsdd"][:, 0:1]),
                reads=[c.tb("cst")], writes=[bdecT])
    for b in range(nb):
        tb0 = b * Tb
        bk_ = c.tb(pname + "k", b)
        bv_ = c.tb(pname + "v", b)
        for ci in range(KC):
            r0, r1 = ci * PD, min((ci + 1) * PD, HK)
            c.emit("sync", lambda e, ci=ci, r0=r0, r1=r1, tb0=tb0, ksb=ksb:
                   e.dma_start(
                ksb[0:r1 - r0, ci * Tb:(ci + 1) * Tb], kf[r0:r1, tb0:tb0 + Tb]),
                reads=[c.tb(kf.name)], writes=[bk_], dma=True, ch=pname + "K")
        for kt in range(ktn):
            c.emit("sync", lambda e, kt=kt, tb0=tb0, vsb=vsb: e.dma_start(
                vsb[:, kt * Vw:(kt + 1) * Vw],
                vtm[tb0 + kt * PD:tb0 + (kt + 1) * PD, 0:Vw]),
                reads=[c.tb(vtm.name)], writes=[bv_], dma=True, ch=pname + "V")
        pend = None
        for qb in range(nqb):
            q0 = tb0 + qb * TQ
            jq = qb % 2
            bq_ = c.tb(pname + "q", (b, jq))
            for ci in range(KC):
                r0, r1 = ci * PD, min((ci + 1) * PD, HK)
                c.emit("sync", lambda e, ci=ci, r0=r0, r1=r1, q0=q0, jq=jq:
                       e.dma_start(qsb[jq][0:r1 - r0, ci * TQ:(ci + 1) * TQ],
                                   qf[r0:r1, q0:q0 + TQ]),
                    reads=[c.tb(qf.name)], writes=[bq_], dma=True,
                    ch=pname + f"Q{jq}")
            for h in range(Hq):
                kchunk, kbase = divmod(h * hdq, PD)
                nu = TQ // PD
                if local:
                    kt_beg = max(0, (qb - 1) * nu)
                    kt_lim = min(ktn, (qb + 1) * nu)
                else:
                    kt_beg = 0
                    kt_lim = ktn if not causal else min(ktn, ((qb + 1) * TQ) // PD)
                ndiag = nu if causal else 0
                kts = list(range(kt_beg, kt_lim))
                n = len(kts)
                jh = h % 2
                po = POB[jh]
                bo_ps = c.tb(pname + "pso", jh)

                def mk_S(i, jq=jq, bq_=bq_, bk_=bk_, kchunk=kchunk,
                         kbase=kbase, kts=kts):
                    kt = kts[i]
                    js = P[SNM[i % NSB]]
                    bjs = c.tb(f"psum{SNM[i % NSB]}")
                    c.emit("tensor", lambda e, kt=kt, jq=jq, js=js,
                           kchunk=kchunk, kbase=kbase: e.matmul(
                        js[:, 0:TQ],
                        ksb[kbase:kbase + hdq,
                            kchunk * Tb + kt * PD:kchunk * Tb + (kt + 1) * PD],
                        qsb[jq][kbase:kbase + hdq,
                                kchunk * TQ:(kchunk + 1) * TQ],
                        start=True, stop=True, tile_position=(kbase, 0)),
                        reads=[bk_, bq_], writes=[bjs])

                for i in range(n):
                    kt = kts[i]
                    if i == 0:
                        mk_S(0)
                        if n > 1:
                            mk_S(1)
                    jp = i % NPB
                    js = P[SNM[i % NSB]]
                    bjs = c.tb(f"psum{SNM[i % NSB]}")
                    bpb = c.tb(pname + "pb", jp)
                    is_diag = causal and kt >= kt_lim - ndiag
                    if decay:
                        jd = i % 2
                        bsd = c.tb(pname + "sd", jd)
                        s0 = (Tb - PD) - (kt * PD - qb * TQ)
                        c.emit("vector", lambda e, js=js, jd=jd, s0=s0:
                               e.scalar_tensor_tensor(
                            sdec[jd][:, :], js[:, 0:TQ], scale,
                            decT[:, s0:s0 + TQ], ALU.mult, ALU.mult),
                            reads=[bjs, bdecT], writes=[bsd])
                        c.emit("scalar", lambda e, jd=jd, jp=jp: e.activation(
                            pbuf[jp][:, :], sdec[jd][:, :], AF.Exp,
                            bias=np.float32(-Cshift)),
                            reads=[bsd], writes=[bpb])
                    else:
                        c.emit("scalar", lambda e, js=js, jp=jp: e.activation(
                            pbuf[jp][:, :], js[:, 0:TQ], AF.Exp,
                            bias=np.float32(-Cshift), scale=scale),
                            reads=[bjs], writes=[bpb])
                    if local:
                        u = nu * qb - kt
                        mi = u + nu - 1
                        c.emit("vector", lambda e, jp=jp, mi=mi: e.tensor_tensor(
                            pbuf[jp][:, :], pbuf[jp][:, :],
                            lmask_t[:, mi * TQ:(mi + 1) * TQ], ALU.mult),
                            reads=[bpb, c.tb("cst")], writes=[bpb])
                    elif is_diag:
                        rel = kt - (qb * TQ) // PD
                        c.emit("vector", lambda e, jp=jp, rel=rel: e.tensor_tensor(
                            pbuf[jp][:, :], pbuf[jp][:, :],
                            cmask_t[:, rel * TQ:(rel + 1) * TQ], ALU.mult),
                            reads=[bpb, c.tb("cst")], writes=[bpb])
                    if i + 2 < n:
                        mk_S(i + 2)
                    c.emit("tensor", lambda e, kt=kt, jp=jp, h=h, po=po,
                           first=(i == 0), last=(i == n - 1):
                           e.matmul(
                        po[0:hdv + 1, 0:TQ],
                        vsb[:, kt * Vw + h * (hdv + 1):kt * Vw + (h + 1) * (hdv + 1)],
                        pbuf[jp][:, :], start=first, stop=last),
                        reads=[bv_, bpb], writes=[bo_ps])
                    if pdbg is not None and b == 0 and qb <= 1 and h == 0:
                        c.emit("scalar", lambda e, kt=kt, jp=jp, qb=qb:
                               e.dma_start(
                            pdbg[kt * PD:(kt + 1) * PD,
                                 qb * TQ:qb * TQ + TQ], pbuf[jp][:, :]),
                            reads=[bpb], writes=[c.tb("s_pdbg", (qb, kt))],
                            dma=True, ch=pname + "PD")
                    if i == 1 and pend is not None:
                        pend()
                        pend = None
                if pend is not None:
                    pend()
                    pend = None

                def mk_tail(jh=jh, po=po, bo_ps=bo_ps, h=h, q0=q0, b=b, qb=qb):
                    def tail():
                        brr = c.tb(pname + "rr", jh)
                        c.emit("vector", lambda e: e.tensor_scalar(
                            rrow[jh][0:hdv + 1, :], po[0:hdv + 1, 0:TQ], 1e-30,
                            None, ALU.max),
                            reads=[bo_ps], writes=[brr])
                        if pdbg is not None and b == 0 and qb == 0 and h == 0:
                            c.emit("gpsimd", lambda e: e.dma_start(
                                pdbg[1024:1089, 0:TQ], rrow[jh][0:hdv + 1, :]),
                                reads=[brr], writes=[c.tb("s_pdbg", "rr")],
                                dma=True, ch=pname + "PR")
                        brr2 = c.tb(pname + "rr2", jh)
                        c.emit("scalar", lambda e: e.activation(
                            rr2[jh][0:hdv + 1, :], rrow[jh][0:hdv + 1, :],
                            AF.Ln),
                            reads=[brr], writes=[brr2])
                        c.emit("scalar", lambda e: e.activation(
                            rr2[jh][0:hdv + 1, :], rr2[jh][0:hdv + 1, :],
                            AF.Exp, scale=-1.0),
                            reads=[brr2], writes=[brr2])
                        bps3 = c.tb(f"psum{PBN[jh]}")
                        c.emit("tensor", lambda e: e.matmul(
                            PBB[jh][0:hdv, 0:TQ], cst["sel64"][0:hdv + 1, 0:hdv],
                            rr2[jh][0:hdv + 1, :], start=True, stop=True),
                            reads=[brr2, c.tb("cst")], writes=[bps3])
                        brb = c.tb(pname + "rb", jh)
                        c.emit("scalar", lambda e: e.copy(
                            rb[jh][0:hdv, :], PBB[jh][0:hdv, 0:TQ]),
                            reads=[bps3], writes=[brb])
                        bob = c.tb(pname + "ob", jh)
                        c.emit("vector", lambda e: e.tensor_tensor(
                            obuf[jh][0:hdv, :], po[0:hdv, 0:TQ], rb[jh][0:hdv, :],
                            ALU.mult),
                            reads=[bo_ps, brb], writes=[bob])
                        c.emit("scalar", lambda e: e.dma_start(
                            out[h * hdv:(h + 1) * hdv, q0:q0 + TQ],
                            obuf[jh][0:hdv, :]),
                            reads=[bob], writes=[c.tb(out.name, ("o", b, qb, h))],
                            dma=True, ch=pname + f"S{jh}")
                    return tail
                pend = mk_tail()
        if pend is not None:
            pend()
            pend = None



def emit_negabs_scalar(c, sb, cst, P, src_dram, name):
    t = sb.alloc([1, 1], F32, name="nab_t")
    col = sb.alloc([PD, 1], F32, name="nab_c")
    bt = c.tb("nab", name)
    c.emit("sync", lambda e: e.dma_start(t[:1, :1], src_dram[:1, :1]),
           reads=[c.tb(src_dram.name)], writes=[bt], dma=True, ch="NA")
    c.emit("scalar", lambda e: e.activation(t[:1, :1], t[:1, :1], AF.Abs),
           reads=[bt], writes=[bt])
    c.emit("vector", lambda e: e.tensor_scalar(t[:1, :1], t[:1, :1], -1.0, None,
                                               ALU.mult),
           reads=[bt], writes=[bt])
    bp = c.tb("psum7")
    c.emit("tensor", lambda e: e.matmul(P[7][:, 0:1], cst["ones_row_f"][:1, :],
                                        t[:1, :1], start=True, stop=True),
           reads=[bt, c.tb("cst")], writes=[bp])
    c.emit("scalar", lambda e: e.copy(col[:, 0:1], P[7][:, 0:1]),
           reads=[bp], writes=[c.tb("cst")])
    cst[name] = col



def sl_l0_attn_a(c, P, cst, cfg, w, scr, res_in, res_out):
    nc = c.nc
    E, T, Tb, nb, TSn = cfg["E"], cfg["T"], cfg["Tb"], cfg["nb"], cfg["TS"]
    H = cfg["l0a_H"]
    hd = E // H
    sbk = SB(nc)
    sb = SB(nc)
    emit_negabs_scalar(c, sbk, cst, P, w["l0a_dd"], "negabsdd")
    xn = scr["xn"]
    emit_prenorm(c, sb, cst, P, res_in, xn, T, TSn, mode="ln", pname="aaln")
    c.fence(xn.name)
    sb = c.phase(sb)
    emit_proj(c, sb, cst, P, xn, w["l0a_wq"], scr["qf"], T=T, TSn=TSn,
              bias_col=cst["l0a_bq"], pname="aaq")
    sb = c.phase(sb)
    emit_proj(c, sb, cst, P, xn, w["l0a_wk"], scr["kf"], T=T, TSn=TSn,
              bias_col=cst["l0a_bk"], pname="aak")
    sb = c.phase(sb)
    emit_vtm_proj(c, sb, cst, P, xn, w["l0a_wve"], scr["vtm"], T=T, TSn=TSn,
                  bias_row=w["l0a_bve"], pname="aav")
    sb = c.phase(sb)
    TQ = min(512, Tb)
    emit_attn(c, sb, cst, P, qf=scr["qf"], kf=scr["kf"], vtm=scr["vtm"],
              out=scr["ao"], Tb=Tb, nb=nb, Hq=H, hdq=hd, hdv=hd, TQ=TQ,
              scale=1.0 / math.sqrt(hd), Cshift=cfg["C"], causal=False,
              decay=True, pname="aat")
    sb = c.phase(sb)
    emit_proj(c, sb, cst, P, scr["ao"], w["l0a_wo"], res_out, T=T, TSn=TSn,
              bias_col=cst["l0a_bo"], res=res_in, res_out=res_out,
              out_dt=F32, pname="aao")
    c.fence(res_out.name)
    c.global_barrier()
    sb.close()
    sbk.close()


def emit_rope_evens(c, sb, cst, src, dst, *, T, rows, coff, pname):
    RC = rows // PD
    rc_t = cst_lazy(c, sb, cst, "ropecos")
    rs_t = cst_lazy(c, sb, cst, "ropesin")
    TSn = min(512, T)
    a = [sb.alloc([PD, TSn], F32) for _ in range(2)]
    b_ = [sb.alloc([PD, TSn], F32) for _ in range(2)]
    t1 = [sb.alloc([PD, TSn], F32) for _ in range(2)]
    t2 = [sb.alloc([PD, TSn], F32) for _ in range(2)]
    ob = [sb.alloc([PD, TSn], BF16) for _ in range(2)]
    k = 0
    for s in range(T // TSn):
        t0 = s * TSn
        for ci in range(RC):
            j = k % 2
            k += 1
            ba = c.tb(pname + "a", j)
            bb = c.tb(pname + "b", j)
            bt1 = c.tb(pname + "t1", j)
            bt2 = c.tb(pname + "t2", j)
            bo = c.tb(pname + "o", j)
            r0 = coff * rows + ci * PD
            r1 = (coff + 1) * rows + ci * PD
            c.emit("sync", lambda e, j=j, r0=r0, t0=t0: e.dma_start(
                a[j][:, :], src[r0:r0 + PD, t0:t0 + TSn]),
                reads=[c.tb(src.name)], writes=[ba], dma=True,
                ch=pname + f"A{j}")
            c.emit("sync", lambda e, j=j, r1=r1, t0=t0: e.dma_start(
                b_[j][:, :], src[r1:r1 + PD, t0:t0 + TSn]),
                reads=[c.tb(src.name)], writes=[bb], dma=True,
                ch=pname + f"B{j}")
            c.emit("vector", lambda e, j=j, t0=t0: e.tensor_tensor(
                t1[j][:, :], a[j][:, :], rc_t[:, t0:t0 + TSn], ALU.mult),
                reads=[ba, c.tb("cst")], writes=[bt1])
            c.emit("vector", lambda e, j=j, t0=t0: e.tensor_tensor(
                t2[j][:, :], b_[j][:, :], rs_t[:, t0:t0 + TSn], ALU.mult),
                reads=[bb, c.tb("cst")], writes=[bt2])
            c.emit("vector", lambda e, j=j: e.tensor_tensor(
                ob[j][:, :], t1[j][:, :], t2[j][:, :], ALU.subtract),
                reads=[bt1, bt2], writes=[bo])
            c.emit("scalar", lambda e, j=j, ci=ci, t0=t0: e.dma_start(
                dst[ci * PD:(ci + 1) * PD, t0:t0 + TSn], ob[j][:, :]),
                reads=[bo], writes=[c.tb(dst.name, (s, ci))], dma=True,
                ch=pname + f"S{j}")


def sl_l1_attn_a(c, P, cst, cfg, w, scr, res_in, res_out):
    nc = c.nc
    E, T, Tb, nb, TSn = cfg["E"], cfg["T"], cfg["Tb"], cfg["nb"], cfg["TS"]
    H = E // 64
    hd = 64
    sb = SB(nc)
    xn = scr["xn"]
    emit_prenorm(c, sb, cst, P, res_in, xn, T, TSn, mode="ln", pname="baln")
    c.fence(xn.name)
    sb = c.phase(sb)
    emit_proj(c, sb, cst, P, xn, w["l1a_wq"], scr["qf"], T=T, TSn=TSn,
              pname="baq")
    sb = c.phase(sb)
    emit_proj(c, sb, cst, P, xn, w["l1a_wk"], scr["kf"], T=T, TSn=TSn,
              pname="bak")
    sb = c.phase(sb)
    emit_vtm_proj(c, sb, cst, P, xn, w["l1a_wve"], scr["vtm"], T=T, TSn=TSn,
                  bias_row=w["l1a_bve"], pname="bav")
    sb = c.phase(sb)
    TQ = min(512, Tb)
    emit_attn(c, sb, cst, P, qf=scr["qf"], kf=scr["kf"], vtm=scr["vtm"],
              out=scr["ao"], Tb=Tb, nb=nb, Hq=H, hdq=hd, hdv=hd, TQ=TQ,
              scale=1.0 / math.sqrt(hd), Cshift=cfg["C"], causal=True,
              decay=False, pname="bat")
    sb = c.phase(sb)
    emit_proj(c, sb, cst, P, scr["ao"], w["l1a_wo"], res_out, T=T, TSn=TSn,
              res=res_in, res_out=res_out, out_dt=F32, pname="bao")
    c.fence(res_out.name)
    c.global_barrier()
    sb.close()


def sl_l0_attn_b(c, P, cst, cfg, w, scr, res_in, res_out):
    nc = c.nc
    E, T, Tb, nb, TSn = cfg["E"], cfg["T"], cfg["Tb"], cfg["nb"], cfg["TS"]
    H = E // 64
    hdq, hdv = 32, 64
    sb = SB(nc)
    xn = scr["xn"]
    emit_prenorm(c, sb, cst, P, res_in, xn, T, TSn, mode="ln", pname="cbln")
    c.fence(xn.name)
    sb = c.phase(sb)
    emit_proj(c, sb, cst, P, xn, w["l0b_weo"], scr["qkro"], T=T, TSn=TSn,
              bias_col=cst["l0b_beo"], out_dt=F32, pname="cbp")
    sb = c.phase(sb)
    emit_rope_evens(c, sb, cst, scr["qkro"], scr["qf"], T=T, rows=E // 2,
                    coff=0, pname="cbrq")
    emit_rope_evens(c, sb, cst, scr["qkro"], scr["kf"], T=T, rows=E // 2,
                    coff=2, pname="cbrk")
    sb = c.phase(sb)
    emit_vtm_proj(c, sb, cst, P, xn, w["l0b_wve"], scr["vtm"], T=T, TSn=TSn,
                  bias_row=w["l0b_bve"], pname="cbv")
    sb = c.phase(sb)
    TQ = min(512, Tb)
    emit_attn(c, sb, cst, P, qf=scr["qf"], kf=scr["kf"], vtm=scr["vtm"],
              out=scr["ao"], Tb=Tb, nb=nb, Hq=H, hdq=hdq, hdv=hdv, TQ=TQ,
              scale=1.0 / math.sqrt(hdv), Cshift=cfg["C"], causal=True,
              decay=False, pname="cbt")
    sb = c.phase(sb)
    emit_proj(c, sb, cst, P, scr["ao"], w["l0b_wo"], res_out, T=T, TSn=TSn,
              bias_col=cst["l0b_bo"], res=res_in, res_out=res_out,
              out_dt=F32, pname="cbo")
    c.fence(res_out.name)
    c.global_barrier()
    sb.close()


def emit_routing(c, sb, cst, P, srow, mask_sb, cfg, pname="rt"):
    T, Tb, nb = cfg["T"], cfg["Tb"], cfg["nb"]
    CC = Tb // PD
    NC = nb * CC
    KH = float(cfg["nheavy"])
    logk = float(np.log(cfg["keff"]))
    assert NC <= 32 and Tb % 512 == 0 or True
    sr = sb.alloc([1, T], F32)
    s_tm = sb.alloc([PD, NC], F32)
    b_tm = sb.alloc([PD, NC], F32)
    t1 = sb.alloc([PD, NC], F32)
    e_tm = sb.alloc([PD, NC], F32)
    fold = sb.alloc([PD, 2], F32)
    foldB = sb.alloc([32, 32], F32)
    vt32 = sb.alloc([32, 32], F32)
    rows = sb.alloc([1, 16], F32)
    mcol = sb.alloc([PD, 1], F32)
    wrow = sb.alloc([1, T], F32)
    wb = sb.alloc([PD, Tb], F32)
    gtmp = sb.alloc([PD, Tb], F32)
    g_tm = sb.alloc([PD, NC], F32)
    pos_tm = sb.alloc([PD, NC], F32)
    m1 = sb.alloc([PD, NC], F32)
    cp = sb.alloc([1, 2 * NC], F32)
    mrow = sb.alloc([1, T], F32)
    b_ = {k: c.tb(pname + k) for k in
          ("sr", "stm", "btm", "t1", "etm", "fold", "foldB", "vt", "rows",
           "mcol", "wrow", "wb", "g", "pos", "m1", "cp", "mrow", "msb")}
    bcst = c.tb("cst")
    M = rows[:1, 0:1]
    NEGM = rows[:1, 1:2]
    LNS = rows[:1, 2:2 + nb]
    AROW = rows[:1, 8:8 + nb]
    TMPR = rows[:1, 12:12 + nb]
    c.emit("sync", lambda e: e.dma_start(sr[:1, :], srow[:1, :]),
           reads=[c.tb(srow.name)], writes=[b_["sr"]], dma=True, ch=pname + "L")
    c.emit("vector", lambda e: e.memset(foldB[:, :], 0.0),
           reads=[], writes=[b_["foldB"]])
    bps3 = c.tb("psum3")
    for col in range(NC):
        c.emit("tensor", lambda e, col=col: e.transpose(
            P[3][:, col:col + 1], sr[:1, col * PD:(col + 1) * PD],
            cst["ident_f"][0:1, 0:1]),
            reads=[b_["sr"], bcst], writes=[bps3])
    c.emit("vector", lambda e: e.tensor_copy(s_tm[:, :], P[3][:, 0:NC]),
           reads=[bps3], writes=[b_["stm"]])
    c.emit("vector", lambda e: e.tensor_scalar(
        b_tm[:, :], s_tm[:, :], -1.0, None, ALU.mult),
        reads=[b_["stm"]], writes=[b_["btm"]])
    apb = sb.alloc([PD, nb], F32)
    c.emit("vector", lambda e: e.memset(apb[:, :], 0.0),
           reads=[], writes=[b_["mcol"]])
    c.emit("vector", lambda e: e.memset(rows[:1, :], 0.0),
           reads=[], writes=[b_["rows"]])
    for it in range(50):
        c.emit("vector", lambda e: e.tensor_tensor(
            t1[:, :], s_tm[:, :], b_tm[:, :], ALU.add),
            reads=[b_["stm"], b_["btm"]], writes=[b_["t1"]])
        c.emit("vector", lambda e: e.tensor_tensor(
            e_tm[:, :].rearrange("p (b cc) -> p b cc", b=nb),
            t1[:, :].rearrange("p (b cc) -> p b cc", b=nb),
            apb[:, :].unsqueeze(2).broadcast_to([PD, nb, CC]), ALU.add),
            reads=[b_["t1"], b_["mcol"]], writes=[b_["etm"]])
        c.emit("scalar", lambda e: e.activation(
            e_tm[:, :], e_tm[:, :], AF.Exp),
            reads=[b_["etm"]], writes=[b_["etm"]])
        bps4 = c.tb("psum4")
        c.emit("tensor", lambda e: e.matmul(
            P[4][:1, 0:NC], cst["ones_col_f"][:, 0:1], e_tm[:, :],
            start=True, stop=True),
            reads=[b_["etm"], bcst], writes=[bps4])
        c.emit("vector", lambda e: e.tensor_reduce(
            LNS, P[4][:1, 0:NC].rearrange("p (b cc) -> p b cc", b=nb),
            AX.X, ALU.add),
            reads=[bps4], writes=[b_["rows"]])
        c.emit("scalar", lambda e: e.activation(LNS, LNS, AF.Ln),
            reads=[b_["rows"]], writes=[b_["rows"]])
        c.emit("vector", lambda e: e.tensor_tensor(
            TMPR, LNS, AROW, ALU.subtract),
            reads=[b_["rows"]], writes=[b_["rows"]])
        c.emit("vector", lambda e: e.tensor_scalar(
            AROW, TMPR, -1.0, float(logk), ALU.mult, ALU.add),
            reads=[b_["rows"]], writes=[b_["rows"]])
        bps2b = c.tb("psum2")
        c.emit("tensor", lambda e: e.matmul(
            P[2][:, 1:1 + nb], cst["ones_row_f"][:1, :], AROW, start=True,
            stop=True),
            reads=[b_["rows"], bcst], writes=[bps2b])
        c.emit("scalar", lambda e: e.copy(apb[:, :], P[2][:, 1:1 + nb]),
            reads=[bps2b], writes=[b_["mcol"]])
        c.emit("vector", lambda e: e.tensor_tensor(
            t1[:, :].rearrange("p (b cc) -> p b cc", b=nb),
            s_tm[:, :].rearrange("p (b cc) -> p b cc", b=nb),
            apb[:, :].unsqueeze(2).broadcast_to([PD, nb, CC]),
            ALU.add),
            reads=[b_["stm"], b_["mcol"]], writes=[b_["t1"]])
        if it < 49:
            c.emit("vector", lambda e: e.tensor_scalar(
                b_tm[:, :], t1[:, :], -1.0, 0.0, ALU.mult, ALU.min),
                reads=[b_["t1"]], writes=[b_["btm"]])
    c.emit("vector", lambda e: e.tensor_scalar(
        pos_tm[:, :], t1[:, :], 0.0, None, ALU.is_ge),
        reads=[b_["t1"]], writes=[b_["pos"]])
    bps5 = c.tb("psum5")
    c.emit("tensor", lambda e: e.matmul(
        P[5][:, 0:NC], cst["tri_lt"][:, :], pos_tm[:, :], start=True,
        stop=False),
        reads=[b_["pos"], bcst], writes=[bps5])
    bps4b = c.tb("psum4")
    c.emit("tensor", lambda e: e.matmul(
        P[4][:1, 32:32 + NC], cst["ones_col_f"][:, 0:1], pos_tm[:, :],
        start=True, stop=True),
        reads=[b_["pos"], bcst], writes=[bps4b])
    c.emit("vector", lambda e: e.tensor_copy(cp[:1, 0:NC],
                                             P[4][:1, 32:32 + NC]),
        reads=[bps4b], writes=[b_["cp"]])
    sh = 1
    while sh < CC:
        c.emit("vector", lambda e, sh=sh: e.tensor_tensor(
            cp[:1, 0:NC].rearrange("p (b cc) -> p b cc", b=nb)[:, :, sh:CC],
            cp[:1, 0:NC].rearrange("p (b cc) -> p b cc", b=nb)[:, :, sh:CC],
            cp[:1, 0:NC].rearrange("p (b cc) -> p b cc", b=nb)[:, :, 0:CC - sh],
            ALU.add),
            reads=[b_["cp"]], writes=[b_["cp"]])
        sh *= 2
    c.emit("vector", lambda e: e.tensor_tensor(
        cp[:1, NC:2 * NC], cp[:1, 0:NC], P[4][:1, 32:32 + NC], ALU.subtract),
        reads=[b_["cp"], bps4b], writes=[b_["cp"]])
    c.emit("tensor", lambda e: e.matmul(
        P[5][:, 0:NC], cst["ones_row_f"][:1, :], cp[:1, NC:2 * NC],
        start=False, stop=True),
        reads=[b_["cp"], bcst], writes=[bps5])
    c.emit("vector", lambda e: e.tensor_scalar(
        m1[:, :], P[5][:, 0:NC], KH, None, ALU.is_lt),
        reads=[bps5], writes=[b_["m1"]])
    c.emit("vector", lambda e: e.tensor_tensor(
        m1[:, :], m1[:, :], pos_tm[:, :], ALU.mult),
        reads=[b_["m1"], b_["pos"]], writes=[b_["m1"]])
    for b in range(nb):
        c.emit("vector", lambda e, b=b: e.tensor_tensor(
            wrow[:1, b * Tb:(b + 1) * Tb], sr[:1, b * Tb:(b + 1) * Tb],
            AROW[:1, b:b + 1].broadcast_to([1, Tb]), ALU.add),
            reads=[b_["sr"], b_["rows"]], writes=[b_["wrow"]])
    for b in range(nb):
        bps6 = c.tb("psum6")
        for q in range(cdiv(Tb, 512)):
            wq = min(512, Tb - q * 512)
            c.emit("tensor", lambda e, b=b, q=q, wq=wq: e.matmul(
                P[6][:, 0:wq], cst["ones_row_f"][:1, :],
                wrow[:1, b * Tb + q * 512:b * Tb + q * 512 + wq],
                start=True, stop=True),
                reads=[b_["wrow"], bcst], writes=[bps6])
            c.emit("vector", lambda e, q=q, wq=wq: e.tensor_copy(
                wb[:, q * 512:q * 512 + wq], P[6][:, 0:wq]),
                reads=[bps6], writes=[b_["wb"]])
        for cc in range(CC):
            col = b * CC + cc
            c.emit("vector", lambda e, col=col: e.tensor_scalar(
                gtmp[:, 0:Tb], wb[:, 0:Tb], t1[:, col:col + 1], 0.0,
                ALU.is_gt, ALU.add, accum_out=g_tm[:, col:col + 1]),
                reads=[b_["wb"], b_["t1"]], writes=[b_["g"]])
    c.emit("vector", lambda e: e.tensor_scalar(
        g_tm[:, :], g_tm[:, :], KH, None, ALU.is_lt),
        reads=[b_["g"]], writes=[b_["g"]])
    c.emit("vector", lambda e: e.tensor_scalar(
        pos_tm[:, :], pos_tm[:, :], -1.0, 1.0, ALU.mult, ALU.add),
        reads=[b_["pos"]], writes=[b_["pos"]])
    c.emit("vector", lambda e: e.tensor_tensor(
        g_tm[:, :], g_tm[:, :], pos_tm[:, :], ALU.mult),
        reads=[b_["g"], b_["pos"]], writes=[b_["g"]])
    c.emit("vector", lambda e: e.tensor_tensor(
        m1[:, :], m1[:, :], g_tm[:, :], ALU.add),
        reads=[b_["m1"], b_["g"]], writes=[b_["m1"]])
    bps2c = c.tb("psum2")
    for col in range(NC):
        c.emit("tensor", lambda e, col=col: e.transpose(
            P[2][0:1, 0:PD], m1[:, col:col + 1], cst["ident_f"][:, :]),
            reads=[b_["m1"], bcst], writes=[bps2c])
        c.emit("vector", lambda e, col=col: e.tensor_copy(
            mrow[:1, col * PD:(col + 1) * PD], P[2][0:1, 0:PD]),
            reads=[bps2c], writes=[b_["mrow"]])
    bmsb = b_["msb"]
    for q in range(cdiv(T, 512)):
        wq = min(512, T - q * 512)
        bps7 = c.tb("psum7")
        c.emit("tensor", lambda e, q=q, wq=wq: e.matmul(
            P[7][:, 0:wq], cst["ones_row_f"][:1, :],
            mrow[:1, q * 512:q * 512 + wq], start=True, stop=True),
            reads=[b_["mrow"], bcst], writes=[bps7])
        c.emit("vector", lambda e, q=q, wq=wq: e.tensor_copy(
            mask_sb[:, q * 512:q * 512 + wq], P[7][:, 0:wq]),
            reads=[bps7], writes=[bmsb])
    return bmsb


def sl_l0_ffn(c, P, cst, cfg, w, scr, res_in, res_out, tag):
    nc = c.nc
    E, T, TSn = cfg["E"], cfg["T"], cfg["TS"]
    dl = E // 2
    dh = cfg["dh"]
    sbk = SB(nc)
    mask_sb = sbk.alloc([PD, T], F32)
    sb = SB(nc)
    pn = "f" + tag
    xn = scr["xn"]
    emit_prenorm(c, sb, cst, P, res_in, xn, T, TSn, mode="ln", pname=pn + "ln")
    c.fence(xn.name)
    sb = c.phase(sb)
    emit_proj(c, sb, cst, P, xn, w["l0f_rt"], scr["srow"], T=T, TSn=TSn,
              out_dt=F32, pname=pn + "s")
    c.fence(scr["srow"].name)
    emit_routing(c, sb, cst, P, scr["srow"], mask_sb, cfg, pname=pn + "r")
    mask_keep = dict(self_w=None)
    mk = c.tb("l0f_mask")
    _merge(mk.w, c.tb(pn + "rmsb").w)
    sb = c.phase(sb)
    emit_prenorm(c, sb, cst, P, xn, scr["xr"], T, TSn, mode="rms",
                 gamma_col=cst["l0f_lgam"], pname=pn + "rl", src_bf=True)
    sb = c.phase(sb)
    emit_proj(c, sb, cst, P, scr["xr"], w["l0f_lw1"], scr["h1"], T=T, TSn=TSn,
              bias_col=cst["l0f_lb1"], act=AF.Gelu, pname=pn + "l1")
    sb = c.phase(sb)
    emit_proj(c, sb, cst, P, scr["h1"], w["l0f_lw2"], scr["resM"], T=T,
              TSn=TSn, bias_col=cst["l0f_lb2"], res=res_in,
              res_out=scr["resM"], out_dt=F32, pname=pn + "l2")
    sb = c.phase(sb)
    emit_prenorm(c, sb, cst, P, xn, scr["xr2"], T, TSn, mode="rms",
                 gamma_col=cst["l0f_hgam"], pname=pn + "rh", src_bf=True)
    sb = c.phase(sb)
    emit_proj(c, sb, cst, P, scr["xr2"], w["l0f_hw1"], scr["h2"], T=T,
              TSn=TSn, bias_col=cst["l0f_hb1"], act=AF.Gelu, pname=pn + "h1")
    sb = c.phase(sb)
    emit_proj(c, sb, cst, P, scr["h2"], w["l0f_hw2"], res_out, T=T, TSn=TSn,
              bias_col=cst["l0f_hb2"], res=scr["resM"], res_out=res_out,
              mask_sb=mask_sb, mask_tb=c.tb("l0f_mask"), out_dt=F32,
              pname=pn + "h2")
    c.fence(res_out.name)
    c.global_barrier()
    sb.close()
    sbk.close()


def emit_qknorm_rope(c, sb, cst, P, src, dst, *, T, Tb, E2, sc_col, xc, xs,
                     pname, roff=0):
    EC = E2 // PD
    H = E2 // 64
    TSn = min(512, Tb)
    nslab = T // TSn
    xc_t = cst_lazy(c, sb, cst, xc)
    xs_t = cst_lazy(c, sb, cst, xs)
    xa = [sb.alloc([PD, TSn], F32) for _ in range(2)]
    sq = [sb.alloc([PD, TSn], F32) for _ in range(2)]
    rr_ = [sb.alloc([8, TSn], F32) for _ in range(2)]
    qn = [sb.alloc([PD, TSn], BF16) for _ in range(2)]
    t1 = [sb.alloc([PD, TSn], F32) for _ in range(2)]
    t2 = [sb.alloc([PD, TSn], F32) for _ in range(2)]
    ob = [sb.alloc([PD, TSn], BF16) for _ in range(2)]
    bcst = c.tb("cst")
    for i in range(nslab):
        t0 = i * TSn
        tb_off = t0 % Tb
        for ci in range(EC):
            j = (i * EC + ci) % 2
            bx = c.tb(pname + "x", j)
            bsq = c.tb(pname + "sq", j)
            brr = c.tb(pname + "rr", j)
            bqn = c.tb(pname + "qn", j)
            bt1 = c.tb(pname + "t1", j)
            bt2 = c.tb(pname + "t2", j)
            bo = c.tb(pname + "o", j)
            c.emit("sync", lambda e, j=j, ci=ci, t0=t0: e.dma_start(
                xa[j][:, :], src[roff + ci * PD:roff + (ci + 1) * PD,
                                 t0:t0 + TSn]),
                reads=[c.tb(src.name)], writes=[bx], dma=True,
                ch=pname + f"L{j}")
            c.emit("scalar", lambda e, j=j: e.square(sq[j][:, :], xa[j][:, :]),
                reads=[bx], writes=[bsq])
            bps = c.tb("psum" + str(4 + j))
            c.emit("tensor", lambda e, j=j, ci=ci: e.matmul(
                P[4 + j][0:2, 0:TSn], cst["hsel2"][:, 0:2], sq[j][:, :],
                start=True, stop=True),
                reads=[bsq, bcst], writes=[bps])
            c.emit("vector", lambda e, j=j: e.tensor_scalar(
                rr_[j][0:2, :], P[4 + j][0:2, 0:TSn], 1e-24, None, ALU.max),
                reads=[bps], writes=[brr])
            c.emit("scalar", lambda e, j=j: e.activation(
                rr_[j][0:2, :], rr_[j][0:2, :], AF.Ln),
                reads=[brr], writes=[brr])
            c.emit("scalar", lambda e, j=j: e.activation(
                rr_[j][0:2, :], rr_[j][0:2, :], AF.Exp, scale=-0.5),
                reads=[brr], writes=[brr])
            c.emit("tensor", lambda e, j=j: e.matmul(
                P[4 + j][:, 0:TSn], cst["hexp2"][0:2, :], rr_[j][0:2, :],
                start=True, stop=True),
                reads=[brr, bcst], writes=[bps])
            c.emit("vector", lambda e, j=j: e.scalar_tensor_tensor(
                qn[j][:, :], xa[j][:, :], cst[sc_col][:, 0:1],
                P[4 + j][:, 0:TSn], ALU.mult, ALU.mult),
                reads=[bx, bps, bcst], writes=[bqn])
            bps2 = c.tb("psum" + str(6 + j))
            c.emit("tensor", lambda e, j=j: e.matmul(
                P[6 + j][:, 0:TSn], cst["R128"][:, :], qn[j][:, :],
                start=True, stop=True),
                reads=[bqn, bcst], writes=[bps2])
            c.emit("vector", lambda e, j=j, tb_off=tb_off: e.tensor_tensor(
                t1[j][:, :], qn[j][:, :], xc_t[:, tb_off:tb_off + TSn],
                ALU.mult),
                reads=[bqn, bcst], writes=[bt1])
            c.emit("vector", lambda e, j=j, tb_off=tb_off: e.tensor_tensor(
                t2[j][:, :], P[6 + j][:, 0:TSn],
                xs_t[:, tb_off:tb_off + TSn], ALU.mult),
                reads=[bps2, bcst], writes=[bt2])
            c.emit("vector", lambda e, j=j: e.tensor_tensor(
                ob[j][:, :], t1[j][:, :], t2[j][:, :], ALU.add),
                reads=[bt1, bt2], writes=[bo])
            c.emit("scalar", lambda e, j=j, ci=ci, t0=t0: e.dma_start(
                dst[ci * PD:(ci + 1) * PD, t0:t0 + TSn], ob[j][:, :]),
                reads=[bo], writes=[c.tb(dst.name, (i, ci))], dma=True,
                ch=pname + f"S{j}")


def emit_geglu(c, sb, cst, P, src, dst, *, T, TSn, HID, pname):
    NCH = cdiv(HID, PD)
    at = [sb.alloc([PD, TSn], BF16) for _ in range(2)]
    gt = [sb.alloc([PD, TSn], BF16) for _ in range(2)]
    gg = [sb.alloc([PD, TSn], F32) for _ in range(2)]
    ot = [sb.alloc([PD, TSn], BF16) for _ in range(2)]
    k = 0
    for s in range(T // TSn):
        t0 = s * TSn
        for ci in range(NCH):
            r0, r1 = ci * PD, min((ci + 1) * PD, HID)
            j = k % 2
            k += 1
            ba = c.tb(pname + "a", j)
            bg = c.tb(pname + "g", j)
            bgg = c.tb(pname + "gg", j)
            bo = c.tb(pname + "o", j)
            c.emit("sync", lambda e, j=j, r0=r0, r1=r1, t0=t0: e.dma_start(
                at[j][0:r1 - r0, :], src[r0:r1, t0:t0 + TSn]),
                reads=[c.tb(src.name)], writes=[ba], dma=True,
                ch=pname + f"A{j}")
            c.emit("sync", lambda e, j=j, r0=r0, r1=r1, t0=t0: e.dma_start(
                gt[j][0:r1 - r0, :], src[HID + r0:HID + r1, t0:t0 + TSn]),
                reads=[c.tb(src.name)], writes=[bg], dma=True,
                ch=pname + f"G{j}")
            c.emit("scalar", lambda e, j=j, mc=r1 - r0: e.activation(
                gg[j][0:mc, :], gt[j][0:mc, :], AF.Gelu),
                reads=[bg], writes=[bgg])
            c.emit("vector", lambda e, j=j, mc=r1 - r0: e.tensor_tensor(
                ot[j][0:mc, :], at[j][0:mc, :], gg[j][0:mc, :], ALU.mult),
                reads=[ba, bgg], writes=[bo])
            c.emit("scalar", lambda e, j=j, r0=r0, r1=r1, t0=t0: e.dma_start(
                dst[r0:r1, t0:t0 + TSn], ot[j][0:r1 - r0, :]),
                reads=[bo], writes=[c.tb(dst.name, (s, ci))], dma=True,
                ch=pname + f"S{j}")


def emit_add_pass(c, sb, cst, P, a, b, out, *, T, TSn, pname):
    E = a.shape[0]
    EC = E // PD
    at = [sb.alloc([PD, EC * TSn], F32) for _ in range(2)]
    bt = [sb.alloc([PD, EC * TSn], F32) for _ in range(2)]
    ot = [sb.alloc([PD, EC * TSn], F32) for _ in range(2)]
    a_r = a.rearrange("(ci p) t -> p ci t", p=PD)
    b_r = b.rearrange("(ci p) t -> p ci t", p=PD)
    o_r = out.rearrange("(ci p) t -> p ci t", p=PD)
    for i in range(T // TSn):
        j = i % 2
        t0 = i * TSn
        ba = c.tb(pname + "a", j)
        bb = c.tb(pname + "b", j)
        bo = c.tb(pname + "o", j)
        c.emit("sync", lambda e, j=j, t0=t0: e.dma_start(
            at[j][:, :].rearrange("p (ci t) -> p ci t", ci=EC),
            a_r[:, :, t0:t0 + TSn]),
            reads=[c.tb(a.name)], writes=[ba], dma=True, ch=pname + f"A{j}")
        c.emit("sync", lambda e, j=j, t0=t0: e.dma_start(
            bt[j][:, :].rearrange("p (ci t) -> p ci t", ci=EC),
            b_r[:, :, t0:t0 + TSn]),
            reads=[c.tb(b.name)], writes=[bb], dma=True, ch=pname + f"B{j}")
        for ci in range(EC):
            c.emit("vector", lambda e, j=j, ci=ci: e.tensor_tensor(
                ot[j][:, ci * TSn:(ci + 1) * TSn],
                at[j][:, ci * TSn:(ci + 1) * TSn],
                bt[j][:, ci * TSn:(ci + 1) * TSn], ALU.add),
                reads=[ba, bb], writes=[bo])
        c.emit("scalar", lambda e, j=j, t0=t0: e.dma_start(
            o_r[:, :, t0:t0 + TSn],
            ot[j][:, :].rearrange("p (ci t) -> p ci t", ci=EC)),
            reads=[bo], writes=[c.tb(out.name, i)], dma=True,
            ch=pname + f"S{j}")


def sl_l1_ffn(c, P, cst, cfg, w, scr, res_in, res_out, tag):
    nc = c.nc
    E, T, Tb, TSn, nb = cfg["E"], cfg["T"], cfg["Tb"], cfg["TS"], cfg["nb"]
    INNER = E // 2
    H = INNER // 64
    HID = cfg["hid"]
    WIN = cfg["window"]
    sb = SB(nc)
    pn = "g" + tag
    xn = scr["xn"]
    emit_prenorm(c, sb, cst, P, res_in, xn, T, TSn, mode="ln",
                 pname=pn + "ln", dst_f32=scr["resM"])
    sb = c.phase(sb)
    xcur_bf, xcur_f = xn, scr["resM"]
    for L in range(cfg.get("l1f_layers", 2)):
        pl = pn + str(L)
        emit_proj(c, sb, cst, P, xcur_bf, w[f"l1f_wqk{L}"], scr["qkro"],
                  T=T, TSn=TSn, out_dt=F32, pname=pl + "qk", shift_half=Tb)
        sb = c.phase(sb)
        emit_vtm_proj(c, sb, cst, P, xcur_bf, w[f"l1f_wv{L}"], scr["vtm"],
                      T=T, TSn=TSn, bias_row=w["l1f_bve"], pname=pl + "v",
                      shift_half=Tb)
        sb = c.phase(sb)
        emit_qknorm_rope(c, sb, cst, P, scr["qkro"], scr["qf"], T=T, Tb=Tb,
                         E2=INNER, sc_col=f"l1f_qs{L}", xc="xq_c", xs="xq_s",
                         pname=pl + "rq", roff=0)
        emit_qknorm_rope(c, sb, cst, P, scr["qkro"], scr["kf"], T=T, Tb=Tb,
                         E2=INNER, sc_col=f"l1f_ks{L}", xc="xk_c", xs="xk_s",
                         pname=pl + "rk", roff=INNER)
        sb = c.phase(sb)
        if L == 0 and "pdbg" in scr:
            cst["_pdbg"] = scr["pdbg"]
        emit_attn(c, sb, cst, P, qf=scr["qf"], kf=scr["kf"], vtm=scr["vtm"],
                  out=scr["ao"], Tb=Tb, nb=nb, Hq=H, hdq=64,
                  hdv=64, TQ=WIN, scale=8.0, Cshift=cfg["C"], causal=False,
                  local=True, pname=pl + "t")
        cst.pop("_pdbg", None)
        sb = c.phase(sb)
        xmid_f = scr["resN"]
        emit_proj(c, sb, cst, P, scr["ao"], w[f"l1f_wo{L}"],
                  xmid_f, T=T, TSn=TSn, res=xcur_f, res_out=xmid_f,
                  out_dt=F32, pname=pl + "o")
        sb = c.phase(sb)
        emit_prenorm(c, sb, cst, P, xmid_f, scr["xg"], T, TSn, mode="ln",
                     gamma_col=cst[f"l1f_lg{L}"], beta_col=cst[f"l1f_lb{L}"],
                     pname=pl + "mn", shift_half=Tb)
        sb = c.phase(sb)
        emit_proj(c, sb, cst, P, scr["xg"], w[f"l1f_w1{L}"], scr["h3"],
                  T=T, TSn=TSn, bias_col=cst[f"l1f_b1{L}"], pname=pl + "w1")
        sb = c.phase(sb)
        emit_geglu(c, sb, cst, P, scr["h3"], scr["h2"], T=T, TSn=TSn,
                   HID=HID, pname=pl + "gg")
        sb = c.phase(sb)
        if L == 0:
            nxt_bf, nxt_f = scr["xr2"], scr["resO"]
        else:
            nxt_bf, nxt_f = None, scr["resP"]
        emit_proj(c, sb, cst, P, scr["h2"], w[f"l1f_w2{L}"], nxt_f,
                  T=T, TSn=TSn, bias_col=cst[f"l1f_b2{L}"], res=xmid_f,
                  res_out=nxt_f, out_dt=F32, pname=pl + "w2",
                  also_bf=nxt_bf)
        sb = c.phase(sb)
        xcur_bf, xcur_f = nxt_bf, nxt_f
    emit_add_pass(c, sb, cst, P, res_in, scr["resP"], res_out, T=T, TSn=TSn,
                  pname=pn + "fin")
    c.fence(res_out.name)
    c.global_barrier()
    sb.close()



def emit_fir_conv(c, sb, cst, P, x32, out, *, T, Tb, TSn, taps, wdiag, pname,
                  act=None, out_dt=BF16, extra_out=None, D=None):
    k = taps
    DD = D if D is not None else x32.shape[0]
    EC = DD // PD
    nslab = T // TSn
    wsb = sb.alloc([PD, EC * k * PD], BF16)
    bw = c.tb(pname + "w")
    for ci in range(EC):
        c.emit("gpsimd", lambda e, ci=ci: e.dma_start(
            wsb[:, ci * k * PD:(ci + 1) * k * PD],
            wdiag[ci * PD:(ci + 1) * PD, :]),
            reads=[c.tb(wdiag.name)], writes=[bw], dma=True, ch=pname + "W")
    xt = [sb.alloc([PD, TSn + 31], BF16) for _ in range(2)]
    ot = [sb.alloc([PD, TSn], out_dt) for _ in range(2)]
    km1 = k - 1
    for s in range(nslab):
        t0 = s * TSn
        for ci in range(EC):
            j = (s * EC + ci) % 2
            bx = c.tb(pname + "x", j)
            bo = c.tb(pname + "o", j)
            if km1 == 0:
                c.emit("sync", lambda e, j=j, ci=ci, t0=t0: e.dma_start(
                    xt[j][:, 0:TSn], x32[ci * PD:(ci + 1) * PD, t0:t0 + TSn]),
                    reads=[c.tb(x32.name)], writes=[bx], dma=True,
                    ch=pname + f"L{j}")
            elif t0 % Tb == 0:
                c.emit("vector", lambda e, j=j: e.memset(xt[j][:, 0:km1], 0.0),
                       reads=[], writes=[bx])
                c.emit("sync", lambda e, j=j, ci=ci, t0=t0: e.dma_start(
                    xt[j][:, km1:km1 + TSn], x32[ci * PD:(ci + 1) * PD,
                                                 t0:t0 + TSn]),
                    reads=[c.tb(x32.name)], writes=[bx], dma=True,
                    ch=pname + f"L{j}")
            else:
                c.emit("sync", lambda e, j=j, ci=ci, t0=t0: e.dma_start(
                    xt[j][:, 0:km1 + TSn], x32[ci * PD:(ci + 1) * PD,
                                               t0 - km1:t0 + TSn]),
                    reads=[c.tb(x32.name)], writes=[bx], dma=True,
                    ch=pname + f"L{j}")
            bk = (s * EC + ci) % 2
            ps = P[bk]
            bps = c.tb(f"psum{bk}")
            mms = [
                (lambda e, j=j, ci=ci, kk=kk, first=(kk == 0),
                 last=(kk == k - 1), ps=ps: e.matmul(
                    ps[:, 0:TSn], wsb[:, (ci * k + kk) * PD:(ci * k + kk + 1) * PD],
                    xt[j][:, kk:kk + TSn], start=first, stop=last))
                for kk in range(k)]
            c.emit("tensor", mms, reads=[bw, bx], writes=[bps])
            fn = act if act is not None else AF.Identity
            c.emit("scalar", lambda e, j=j, ps=ps, fn=fn: e.activation(
                ot[j][:, :], ps[:, 0:TSn], fn),
                reads=[bps], writes=[bo])
            c.emit("scalar", lambda e, j=j, ci=ci, t0=t0: e.dma_start(
                out[ci * PD:(ci + 1) * PD, t0:t0 + TSn], ot[j][:, :]),
                reads=[bo], writes=[c.tb(out.name, (s, ci))], dma=True,
                ch=pname + f"S{j}")
            if extra_out is not None:
                c.emit("gpsimd", lambda e, j=j, ci=ci, t0=t0: e.dma_start(
                    extra_out[ci * PD:(ci + 1) * PD, t0:t0 + TSn], ot[j][:, :]),
                    reads=[bo], writes=[c.tb(extra_out.name, (s, ci))],
                    dma=True, ch=pname + f"E{j}")


def emit_headnorm(c, sb, cst, P, src, dsts, *, T, TSn, D, hd2, pname,
                  mode="l2", beta_sb=None, beta_tb=None, gamma_col=None):
    EC = D // PD
    H_ = D // hd2
    nslab = T // TSn
    xa = [sb.alloc([PD, EC * TSn], BF16) for _ in range(2)]
    sq = sb.alloc([PD, TSn], F32)
    rr_ = [sb.alloc([max(H_, 1), TSn], F32) for _ in range(2)]
    nb_ = [sb.alloc([PD, TSn], BF16) for _ in range(2)]
    bb_ = [sb.alloc([PD, TSn], BF16) for _ in range(2)]
    ob = [[sb.alloc([PD, TSn], BF16) for _ in range(len(dsts))]
          for _ in range(2)]
    bcst = c.tb("cst")
    for s in range(nslab):
        t0 = s * TSn
        js = s % 2
        bx = c.tb(pname + "x", js)
        psH = P[4 + js]
        bpsH = c.tb(f"psum{4 + js}")
        bsq = c.tb(pname + "sq")
        brr = c.tb(pname + "rr", js)
        for ci in range(EC):
            c.emit("sync", lambda e, js=js, ci=ci, t0=t0: e.dma_start(
                xa[js][:, ci * TSn:(ci + 1) * TSn],
                src[ci * PD:(ci + 1) * PD, t0:t0 + TSn]),
                reads=[c.tb(src.name)], writes=[bx], dma=True,
                ch=pname + f"L{js}")
        if mode != "none":
            for ci in range(EC):
                c.emit("scalar", lambda e, js=js, ci=ci: e.square(
                    sq[:, :], xa[js][:, ci * TSn:(ci + 1) * TSn]),
                    reads=[bx], writes=[bsq])
                c.emit("tensor", lambda e, ci=ci, psH=psH: e.matmul(
                    psH[0:H_, 0:TSn], cst["hselD"][:, ci * H_:(ci + 1) * H_],
                    sq[:, :], start=(ci == 0), stop=(ci == EC - 1)),
                    reads=[bsq, bcst], writes=[bpsH])
            if mode == "l2":
                c.emit("scalar", lambda e, js=js, psH=psH: e.activation(
                    rr_[js][0:H_, :], psH[0:H_, 0:TSn], AF.Ln,
                    bias=np.float32(1e-6)),
                    reads=[bpsH], writes=[brr])
            else:
                c.emit("scalar", lambda e, js=js, psH=psH: e.activation(
                    rr_[js][0:H_, :], psH[0:H_, 0:TSn], AF.Ln,
                    bias=np.float32(1e-5), scale=np.float32(1.0 / hd2)),
                    reads=[bpsH], writes=[brr])
            c.emit("scalar", lambda e, js=js: e.activation(
                rr_[js][0:H_, :], rr_[js][0:H_, :], AF.Exp, scale=-0.5),
                reads=[brr], writes=[brr])
        for ci in range(EC):
            jc = ci % 2
            bn = c.tb(pname + "bn", jc)
            bbb = c.tb(pname + "bb", jc)
            if mode != "none":
                psB = P[jc]
                bpsB = c.tb(f"psum{jc}")
                c.emit("tensor", lambda e, js=js, ci=ci, psB=psB: e.matmul(
                    psB[:, 0:TSn], cst["hexpD"][0:H_, ci * PD:(ci + 1) * PD],
                    rr_[js][0:H_, :], start=True, stop=True),
                    reads=[brr, bcst], writes=[bpsB])
                c.emit("scalar", lambda e, jc=jc, psB=psB: e.copy(
                    nb_[jc][:, :], psB[:, 0:TSn]),
                    reads=[bpsB], writes=[bn])
            if any(mb for _, mb in dsts):
                psB2 = P[2 + jc]
                bps2 = c.tb(f"psum{2 + jc}")
                c.emit("tensor", lambda e, ci=ci, t0=t0, psB2=psB2: e.matmul(
                    psB2[:, 0:TSn], cst["hexpD"][0:H_, ci * PD:(ci + 1) * PD],
                    beta_sb[0:H_, t0:t0 + TSn], start=True, stop=True),
                    reads=[beta_tb, bcst], writes=[bps2])
                c.emit("scalar", lambda e, jc=jc, psB2=psB2: e.copy(
                    bb_[jc][:, :], psB2[:, 0:TSn]),
                    reads=[bps2], writes=[bbb])
            for di, (dst, mul_beta) in enumerate(dsts):
                bo = c.tb(pname + "o", (jc, di))
                obt = ob[jc][di]
                if mode == "none":
                    c.emit("vector", lambda e, js=js, ci=ci, jc=jc, obt=obt:
                           e.tensor_tensor(
                        obt[:, :], xa[js][:, ci * TSn:(ci + 1) * TSn],
                        bb_[jc][:, :], ALU.mult),
                        reads=[bx, bbb], writes=[bo])
                elif gamma_col is not None:
                    c.emit("vector", lambda e, js=js, ci=ci, jc=jc, obt=obt:
                           e.scalar_tensor_tensor(
                        obt[:, :], xa[js][:, ci * TSn:(ci + 1) * TSn],
                        gamma_col[:, ci:ci + 1], nb_[jc][:, :], ALU.mult,
                        ALU.mult),
                        reads=[bx, bn, bcst], writes=[bo])
                else:
                    c.emit("vector", lambda e, js=js, ci=ci, jc=jc, obt=obt:
                           e.tensor_tensor(
                        obt[:, :], xa[js][:, ci * TSn:(ci + 1) * TSn],
                        nb_[jc][:, :], ALU.mult),
                        reads=[bx, bn], writes=[bo])
                if mul_beta and mode != "none":
                    c.emit("vector", lambda e, jc=jc, obt=obt: e.tensor_tensor(
                        obt[:, :], obt[:, :], bb_[jc][:, :], ALU.mult),
                        reads=[bo, bbb], writes=[bo])
                c.emit("scalar", lambda e, ci=ci, t0=t0, dst=dst, obt=obt:
                       e.dma_start(
                    dst[ci * PD:(ci + 1) * PD, t0:t0 + TSn], obt[:, :]),
                    reads=[bo], writes=[c.tb(dst.name, (s, ci))], dma=True,
                    ch=pname + f"S{jc}{di}")


def emit_transpose_pass(c, sb, cst, P, src, dst, *, R, T, pname):
    RC = R // PD
    TC = T // PD
    tin = [sb.alloc([PD, PD], BF16) for _ in range(2)]
    tout = [sb.alloc([PD, PD], BF16) for _ in range(2)]
    k = 0
    for tc in range(TC):
        for rc in range(RC):
            j = k % 2
            bk = 2 + k % 2
            k += 1
            bx = c.tb(pname + "x", j)
            bo = c.tb(pname + "o", j)
            bps = c.tb(f"psum{bk}")
            c.emit("sync", lambda e, j=j, rc=rc, tc=tc: e.dma_start(
                tin[j][:, :], src[rc * PD:(rc + 1) * PD, tc * PD:(tc + 1) * PD]),
                reads=[c.tb(src.name)], writes=[bx], dma=True,
                ch=pname + f"L{j}")
            c.emit("tensor", lambda e, j=j, bk=bk: e.transpose(
                P[bk][:, :].bitcast(BF16)[:, 0:PD], tin[j][:, :], cst["ident_bf"][:, :]),
                reads=[bx, c.tb("cst")], writes=[bps])
            c.emit("vector", lambda e, j=j, bk=bk: e.tensor_copy(
                tout[j][:, :], P[bk][:, :].bitcast(BF16)[:, 0:PD]),
                reads=[bps], writes=[bo])
            c.emit("scalar", lambda e, j=j, rc=rc, tc=tc: e.dma_start(
                dst[tc * PD:(tc + 1) * PD, rc * PD:(rc + 1) * PD], tout[j][:, :]),
                reads=[bo], writes=[c.tb(dst.name, (tc, rc))], dma=True,
                ch=pname + f"S{j}")


def emit_delta(c, sb, cst, P, cfg, lanes, bufs, pname):
    nc = c.nc
    Tb = cfg["Tb"]
    hd2 = cfg["hd2"]
    DKC = hd2 // PD
    NT = Tb // PD
    NCH = Tb // 32
    NG = NCH // 4
    CH = 32
    bcst = c.tb("cst")
    L = []
    for li, ln in enumerate(lanes):
        b, h = ln["b"], ln["h"]
        r0 = h * hd2
        tb0 = b * Tb
        d = {}
        d["qfm"] = sb.alloc([PD, DKC * Tb], BF16)
        d["kfm"] = sb.alloc([PD, DKC * Tb], BF16)
        d["kbfm"] = sb.alloc([PD, DKC * Tb], BF16)
        d["vsb"] = sb.alloc([PD, NT * hd2], BF16)
        d["kbsb"] = sb.alloc([PD, NT * hd2], BF16)
        d["ksb"] = sb.alloc([PD, NT * hd2], BF16)
        d["ubuf"] = sb.alloc([PD, NT * hd2], BF16)
        d["wbuf"] = sb.alloc([PD, NT * hd2], BF16)
        d["wfm"] = sb.alloc([PD, DKC * Tb], BF16)
        d["Rf"] = [sb.alloc([PD, 2 * hd2], F32) for _ in range(2)]
        d["Rb"] = [sb.alloc([PD, 2 * hd2], BF16) for _ in range(2)]
        d["N"] = [sb.alloc([PD, 32], BF16) for _ in range(2)]
        d["Nt"] = [sb.alloc([PD, 32], BF16) for _ in range(2)]
        d["Sf"] = sb.alloc([PD, DKC * hd2], F32)
        d["Sb"] = sb.alloc([PD, DKC * hd2], BF16)
        d["att"] = sb.alloc([PD, 32], BF16)
        d["uh"] = sb.alloc([PD, hd2], BF16)
        d["ob"] = sb.alloc([PD, NT * hd2], BF16)
        pfx = pname + f"l{li}"
        d["pfx"] = pfx
        for nm, srcd, rows in (("qfm", bufs["cqn"], True), ("kfm", bufs["ckn"], True),
                               ("kbfm", bufs["ckb"], True)):
            bb = c.tb(pfx + nm)
            for ci in range(DKC):
                c.emit("sync", lambda e, d=d, nm=nm, srcd=srcd, ci=ci, r0=r0,
                       tb0=tb0: e.dma_start(
                    d[nm][:, ci * Tb:(ci + 1) * Tb],
                    srcd[r0 + ci * PD:r0 + (ci + 1) * PD, tb0:tb0 + Tb]),
                    reads=[c.tb(srcd.name)], writes=[bb], dma=True,
                    ch=pfx + nm)
        for nm, srcd in (("vsb", bufs["t_vb"]), ("kbsb", bufs["t_kb"]),
                         ("ksb", bufs["t_kn"])):
            bb = c.tb(pfx + nm)
            for tt in range(NT):
                c.emit("sync", lambda e, d=d, nm=nm, srcd=srcd, tt=tt, r0=r0,
                       tb0=tb0: e.dma_start(
                    d[nm][:, tt * hd2:(tt + 1) * hd2],
                    srcd[tb0 + tt * PD:tb0 + (tt + 1) * PD, r0:r0 + hd2]),
                    reads=[c.tb(srcd.name)], writes=[bb], dma=True,
                    ch=pfx + nm)
        c.emit("vector", lambda e, d=d: e.memset(d["Sf"][:, :], 0.0),
               reads=[], writes=[c.tb(pfx + "Sf")])
        c.emit("vector", lambda e, d=d: e.memset(d["Sb"][:, :], 0.0),
               reads=[], writes=[c.tb(pfx + "Sb")])
        d["b"], d["h"], d["r0"], d["tb0"], d["li"] = b, h, r0, tb0, li
        L.append(d)

    for d in L:
        pfx = d["pfx"]
        pb = 0 if d["li"] == 0 else 4
        for g in range(NG):
            jr = g % 2
            bR = c.tb(pfx + "R", jr)
            bN = c.tb(pfx + "N", jr)
            bNt = c.tb(pfx + "Nt", jr)
            psG = P[pb]
            bpsG = c.tb(f"psum{pb}")
            psR = P[pb + 1]
            bpsR = c.tb(f"psum{pb + 1}")
            for gm in range(4):
                ch = g * 4 + gm
                for ci in range(DKC):
                    c.emit("tensor", lambda e, d=d, gm=gm, ch=ch, ci=ci,
                           psG=psG: e.matmul(
                        psG[32 * gm:32 * gm + 32, 0:32],
                        d["kbfm"][:, ci * Tb + ch * CH:ci * Tb + ch * CH + 32],
                        d["kfm"][:, ci * Tb + ch * CH:ci * Tb + ch * CH + 32],
                        start=(ci == 0), stop=(ci == DKC - 1),
                        tile_position=(0, 32 * gm)),
                        reads=[c.tb(pfx + "kbfm"), c.tb(pfx + "kfm")],
                        writes=[bpsG])
            c.emit("vector", lambda e, d=d, jr=jr, psG=psG:
                   e.scalar_tensor_tensor(
                d["N"][jr][:, :], psG[:, 0:32], -1.0, cst["maskSL4"][:, :],
                ALU.mult, ALU.mult),
                reads=[bpsG, bcst], writes=[bN])
            c.emit("vector", lambda e, d=d, jr=jr: e.transpose(
                d["Nt"][jr][:, :], d["N"][jr][:, :]),
                reads=[bN], writes=[bNt])
            c.emit("vector", lambda e, d=d, jr=jr, g=g: e.tensor_copy(
                d["Rf"][jr][:, 0:hd2], d["vsb"][:, g * hd2:(g + 1) * hd2]),
                reads=[c.tb(pfx + "vsb")], writes=[bR])
            c.emit("vector", lambda e, d=d, jr=jr, g=g: e.tensor_copy(
                d["Rf"][jr][:, hd2:2 * hd2], d["kbsb"][:, g * hd2:(g + 1) * hd2]),
                reads=[c.tb(pfx + "kbsb")], writes=[bR])
            c.emit("vector", lambda e, d=d, jr=jr: e.tensor_copy(
                d["Rb"][jr][:, :], d["Rf"][jr][:, :]),
                reads=[bR], writes=[bR])
            for it in range(5):
                for gm in range(4):
                    c.emit("tensor", lambda e, d=d, jr=jr, gm=gm, psR=psR:
                           e.matmul(
                        psR[32 * gm:32 * gm + 32, 0:2 * hd2],
                        d["Nt"][jr][32 * gm:32 * gm + 32, :],
                        d["Rb"][jr][32 * gm:32 * gm + 32, :],
                        start=True, stop=True,
                        tile_position=(32 * gm, 32 * gm)),
                        reads=[bNt, bR], writes=[bpsR])
                c.emit("vector", lambda e, d=d, jr=jr, psR=psR: e.tensor_tensor(
                    d["Rf"][jr][:, :], d["Rf"][jr][:, :], psR[:, 0:2 * hd2],
                    ALU.add),
                    reads=[bR, bpsR], writes=[bR])
                c.emit("vector", lambda e, d=d, jr=jr: e.tensor_copy(
                    d["Rb"][jr][:, :], d["Rf"][jr][:, :]),
                    reads=[bR], writes=[bR])
                if it < 4:
                    for gm in range(4):
                        c.emit("tensor", lambda e, d=d, jr=jr, gm=gm, psG=psG:
                               e.matmul(
                            psG[32 * gm:32 * gm + 32, 0:32],
                            d["Nt"][jr][32 * gm:32 * gm + 32, :],
                            d["N"][jr][32 * gm:32 * gm + 32, :],
                            start=True, stop=True,
                            tile_position=(32 * gm, 32 * gm)),
                            reads=[bNt, bN], writes=[bpsG])
                    c.emit("scalar", lambda e, d=d, jr=jr, psG=psG: e.copy(
                        d["N"][jr][:, :], psG[:, 0:32]),
                        reads=[bpsG], writes=[bN])
                    c.emit("vector", lambda e, d=d, jr=jr: e.transpose(
                        d["Nt"][jr][:, :], d["N"][jr][:, :]),
                        reads=[bN], writes=[bNt])
            c.emit("vector", lambda e, d=d, jr=jr, g=g: e.tensor_copy(
                d["ubuf"][:, g * hd2:(g + 1) * hd2], d["Rb"][jr][:, 0:hd2]),
                reads=[bR], writes=[c.tb(pfx + "ubuf")])
            c.emit("vector", lambda e, d=d, jr=jr, g=g: e.tensor_copy(
                d["wbuf"][:, g * hd2:(g + 1) * hd2],
                d["Rb"][jr][:, hd2:2 * hd2]),
                reads=[bR], writes=[c.tb(pfx + "wbuf")])
        for tt in range(NT):
            for ci in range(DKC):
                bk = pb + 2
                bps = c.tb(f"psum{bk}")
                c.emit("tensor", lambda e, d=d, tt=tt, ci=ci, bk=bk:
                       e.transpose(
                    P[bk][:, :].bitcast(BF16)[:, 0:PD],
                    d["wbuf"][:, tt * hd2 + ci * PD:tt * hd2 + (ci + 1) * PD],
                    cst["ident_bf"][:, :]),
                    reads=[c.tb(pfx + "wbuf"), bcst], writes=[bps])
                c.emit("vector", lambda e, d=d, tt=tt, ci=ci, bk=bk:
                       e.tensor_copy(
                    d["wfm"][:, ci * Tb + tt * PD:ci * Tb + (tt + 1) * PD],
                    P[bk][:, :].bitcast(BF16)[:, 0:PD]),
                    reads=[bps], writes=[c.tb(pfx + "wfm")])

    for ch in range(NCH):
        tt = ch // 4
        cr = 32 * (ch % 4)
        for d in L:
            pfx = d["pfx"]
            pb = 0 if d["li"] == 0 else 4
            psG, psU, psO, psD = P[pb], P[pb + 1], P[pb + 2], P[pb + 3]
            bG = c.tb(f"psum{pb}")
            bU = c.tb(f"psum{pb + 1}")
            bO = c.tb(f"psum{pb + 2}")
            bD = c.tb(f"psum{pb + 3}")
            batt = c.tb(pfx + "att")
            buh = c.tb(pfx + "uh")
            bSf = c.tb(pfx + "Sf")
            bSb = c.tb(pfx + "Sb")
            for ci in range(DKC):
                c.emit("tensor", lambda e, d=d, ci=ci, ch=ch, cr=cr, psG=psG:
                       e.matmul(
                    psG[cr:cr + 32, 0:32],
                    d["kfm"][:, ci * Tb + ch * CH:ci * Tb + ch * CH + 32],
                    d["qfm"][:, ci * Tb + ch * CH:ci * Tb + ch * CH + 32],
                    start=(ci == 0), stop=(ci == DKC - 1),
                    tile_position=(0, cr)),
                    reads=[c.tb(pfx + "kfm"), c.tb(pfx + "qfm")], writes=[bG])
            c.emit("vector", lambda e, d=d, cr=cr, psG=psG: e.tensor_tensor(
                d["att"][cr:cr + 32, :], psG[cr:cr + 32, 0:32],
                cst["maskTI4"][cr:cr + 32, :], ALU.mult),
                reads=[bG, bcst], writes=[batt])
            for ci in range(DKC):
                c.emit("tensor", lambda e, d=d, ci=ci, ch=ch, cr=cr, psU=psU:
                       e.matmul(
                    psU[cr:cr + 32, 0:hd2],
                    d["wfm"][:, ci * Tb + ch * CH:ci * Tb + ch * CH + 32],
                    d["Sb"][:, ci * hd2:(ci + 1) * hd2],
                    start=(ci == 0), stop=(ci == DKC - 1),
                    tile_position=(0, cr)),
                    reads=[c.tb(pfx + "wfm"), bSb], writes=[bU])
            c.emit("vector", lambda e, d=d, tt=tt, cr=cr, psU=psU:
                   e.tensor_tensor(
                d["uh"][cr:cr + 32, :],
                d["ubuf"][cr:cr + 32, tt * hd2:(tt + 1) * hd2],
                psU[cr:cr + 32, 0:hd2], ALU.subtract),
                reads=[c.tb(pfx + "ubuf"), bU], writes=[buh])
            for ci in range(DKC):
                c.emit("tensor", lambda e, d=d, ci=ci, ch=ch, cr=cr, psO=psO:
                       e.matmul(
                    psO[cr:cr + 32, 0:hd2],
                    d["qfm"][:, ci * Tb + ch * CH:ci * Tb + ch * CH + 32],
                    d["Sb"][:, ci * hd2:(ci + 1) * hd2],
                    start=(ci == 0), stop=False, tile_position=(0, cr)),
                    reads=[c.tb(pfx + "qfm"), bSb], writes=[bO])
            c.emit("tensor", lambda e, d=d, cr=cr, psO=psO: e.matmul(
                psO[cr:cr + 32, 0:hd2], d["att"][cr:cr + 32, :],
                d["uh"][cr:cr + 32, :], start=False, stop=True,
                tile_position=(cr, cr)),
                reads=[batt, buh], writes=[bO])
            c.emit("scalar", lambda e, d=d, tt=tt, cr=cr, psO=psO: e.copy(
                d["ob"][cr:cr + 32, tt * hd2:(tt + 1) * hd2],
                psO[cr:cr + 32, 0:hd2]),
                reads=[bO], writes=[c.tb(pfx + "ob")])
            for hi in range(DKC):
                c.emit("tensor", lambda e, d=d, tt=tt, cr=cr, hi=hi, psD=psD:
                       e.matmul(
                    psD[:, hi * hd2:(hi + 1) * hd2],
                    d["ksb"][cr:cr + 32, tt * hd2 + hi * PD:tt * hd2 + (hi + 1) * PD],
                    d["uh"][cr:cr + 32, :], start=True, stop=True,
                    tile_position=(cr, 0)),
                    reads=[c.tb(pfx + "ksb"), buh], writes=[bD])
            c.emit("vector", lambda e, d=d, psD=psD: e.tensor_tensor(
                d["Sf"][:, :], d["Sf"][:, :], psD[:, 0:DKC * hd2], ALU.add),
                reads=[bSf, bD], writes=[bSf])
            c.emit("vector", lambda e, d=d: e.tensor_copy(
                d["Sb"][:, :], d["Sf"][:, :]),
                reads=[bSf], writes=[bSb])
    for d in L:
        pfx = d["pfx"]
        for tt in range(NT):
            c.emit("scalar", lambda e, d=d, tt=tt: e.dma_start(
                bufs["delta_tm"][d["tb0"] + tt * PD:d["tb0"] + (tt + 1) * PD,
                                 d["r0"]:d["r0"] + hd2],
                d["ob"][:, tt * hd2:(tt + 1) * hd2]),
                reads=[c.tb(pfx + "ob")],
                writes=[c.tb(bufs["delta_tm"].name, (d["b"], d["h"], tt))],
                dma=True, ch=pfx + "OS")


def emit_branch_stats(c, sb, cst, P, branches, gin, *, T, TSn, D, hd2,
                      pname):
    EC = D // PD
    H_ = D // hd2
    n = float(hd2)
    xa = [sb.alloc([PD, TSn], BF16) for _ in range(2)]
    tp = [sb.alloc([PD, TSn], BF16) for _ in range(2)]
    rows = sb.alloc([H_, 2 * TSn], F32)
    srow = sb.alloc([H_, 4 * TSn], BF16)
    bcst = c.tb("cst")
    Ttot = T
    for s in range(T // TSn):
        t0 = s * TSn
        for br, brd in enumerate(branches):
            bsums = [c.tb(f"psum{4 + k}") for k in range(3)]
            for ci in range(EC):
                j = ci % 2
                bx = c.tb(pname + "x", j)
                btp = c.tb(pname + "t", j)
                c.emit("sync", lambda e, j=j, ci=ci, t0=t0, brd=brd:
                       e.dma_start(xa[j][:, :],
                                   brd[ci * PD:(ci + 1) * PD, t0:t0 + TSn]),
                    reads=[c.tb(brd.name)], writes=[bx], dma=True,
                    ch=pname + f"L{j}")
                c.emit("tensor", lambda e, j=j, ci=ci: e.matmul(
                    P[4][0:H_, 0:TSn], cst["hselD_bf"][:, ci * H_:(ci + 1) * H_],
                    xa[j][:, :], start=(ci == 0), stop=(ci == EC - 1)),
                    reads=[bx, bcst], writes=[bsums[0]])
                c.emit("scalar", lambda e, j=j: e.activation(
                    tp[j][:, :], xa[j][:, :], AF.Abs),
                    reads=[bx], writes=[btp])
                c.emit("tensor", lambda e, j=j, ci=ci: e.matmul(
                    P[5][0:H_, 0:TSn], cst["hselD_bf"][:, ci * H_:(ci + 1) * H_],
                    tp[j][:, :], start=(ci == 0), stop=(ci == EC - 1)),
                    reads=[btp, bcst], writes=[bsums[1]])
                c.emit("scalar", lambda e, j=j: e.square(
                    tp[j][:, :], xa[j][:, :]),
                    reads=[bx], writes=[btp])
                c.emit("tensor", lambda e, j=j, ci=ci: e.matmul(
                    P[6][0:H_, 0:TSn], cst["hselD_bf"][:, ci * H_:(ci + 1) * H_],
                    tp[j][:, :], start=(ci == 0), stop=(ci == EC - 1)),
                    reads=[btp, bcst], writes=[bsums[2]])
            brw = c.tb(pname + "rows")
            bsr = c.tb(pname + "srow")
            c.emit("vector", lambda e: e.tensor_scalar(
                srow[0:H_, 0:TSn], P[4][0:H_, 0:TSn], 1.0 / n, None, ALU.mult),
                reads=[bsums[0]], writes=[bsr])
            c.emit("vector", lambda e: e.tensor_scalar(
                srow[0:H_, 2 * TSn:3 * TSn], P[5][0:H_, 0:TSn], 1.0 / n, None,
                ALU.mult),
                reads=[bsums[1]], writes=[bsr])
            c.emit("scalar", lambda e: e.activation(
                srow[0:H_, 3 * TSn:4 * TSn], P[6][0:H_, 0:TSn], AF.Sqrt,
                bias=np.float32(0.0)),
                reads=[bsums[2]], writes=[bsr])
            c.emit("scalar", lambda e: e.square(
                rows[0:H_, 0:TSn], P[4][0:H_, 0:TSn]),
                reads=[bsums[0]], writes=[brw])
            c.emit("vector", lambda e: e.scalar_tensor_tensor(
                rows[0:H_, TSn:2 * TSn], rows[0:H_, 0:TSn], -1.0 / n,
                P[6][0:H_, 0:TSn], ALU.mult, ALU.add),
                reads=[brw, bsums[2]], writes=[brw])
            c.emit("scalar", lambda e: e.activation(
                srow[0:H_, TSn:2 * TSn], rows[0:H_, TSn:2 * TSn], AF.Sqrt,
                scale=np.float32(1.0 / (n - 1.0))),
                reads=[brw], writes=[bsr])
            for st in range(4):
                r0 = 1024 + 4 * br + st
                c.emit("scalar", lambda e, st=st, r0=r0, t0=t0: e.dma_start(
                    bass.AP(gin.tensor, r0 * Ttot + t0, [[24 * Ttot, H_],
                                                         [1, TSn]]),
                    srow[0:H_, st * TSn:st * TSn + TSn]),
                    reads=[bsr], writes=[c.tb(gin.name, ("st", s, br, st))],
                    dma=True, ch=pname + f"S{st}")


def emit_gates(c, sb, sbk, cst, P, glog, glt, wgate, *, T, pname):
    tl = sb.alloc([1, 32], F32)
    irow = sb.alloc([1, 24], F32)
    gl = [sb.alloc([24, PD], F32) for _ in range(2)]
    wtm = [sb.alloc([PD, 24], F32) for _ in range(2)]
    t1 = [sb.alloc([PD, 24], F32) for _ in range(2)]
    red = [sb.alloc([PD, 8], F32) for _ in range(2)]
    itbc = sb.alloc([PD, 24], F32)
    wout = [sb.alloc([24, PD], F32) for _ in range(2)]
    bcst = c.tb("cst")
    btl = c.tb(pname + "tl")
    bir = c.tb(pname + "ir")
    c.emit("sync", lambda e: e.dma_start(tl[:1, 0:4], glt[:1, :]),
           reads=[c.tb(glt.name)], writes=[btl], dma=True, ch=pname + "L")
    c.emit("scalar", lambda e: e.activation(tl[:1, 4:8], tl[:1, 0:4], AF.Abs),
           reads=[btl], writes=[btl])
    c.emit("scalar", lambda e: e.activation(tl[:1, 8:12], tl[:1, 4:8], AF.Exp,
                                            scale=-1.0),
           reads=[btl], writes=[btl])
    c.emit("scalar", lambda e: e.activation(tl[:1, 12:16], tl[:1, 8:12],
                                            AF.Ln, bias=np.float32(1.0)),
           reads=[btl], writes=[btl])
    c.emit("vector", lambda e: e.tensor_scalar(tl[:1, 16:20], tl[:1, 0:4],
                                               0.0, None, ALU.max),
           reads=[btl], writes=[btl])
    c.emit("vector", lambda e: e.tensor_tensor(tl[:1, 20:24], tl[:1, 12:16],
                                               tl[:1, 16:20], ALU.add),
           reads=[btl], writes=[btl])
    c.emit("vector", lambda e: e.tensor_scalar(tl[:1, 24:28], tl[:1, 20:24],
                                               0.5, None, ALU.add),
           reads=[btl], writes=[btl])
    c.emit("vector", lambda e: e.reciprocal(tl[:1, 28:32], tl[:1, 24:28]),
           reads=[btl], writes=[btl])
    for h in range(4):
        c.emit("vector", lambda e, h=h: e.tensor_copy(
            irow[:1, 6 * h:6 * h + 6],
            tl[:1, 28 + h:29 + h].broadcast_to([1, 6])),
            reads=[btl], writes=[bir])
    bitbc = c.tb(pname + "itbc")
    bps2 = c.tb("psum6")
    c.emit("tensor", lambda e: e.matmul(
        P[6][:, 0:24], cst["ones_row_f"][:1, :], irow[:1, :],
        start=True, stop=True),
        reads=[bir, bcst], writes=[bps2])
    c.emit("scalar", lambda e: e.copy(itbc[:, :], P[6][:, 0:24]),
        reads=[bps2], writes=[bitbc])
    for tc in range(T // PD):
        jt = tc % 2
        bgl = c.tb(pname + "gl", jt)
        bw = c.tb(pname + "w", jt)
        br_ = c.tb(pname + "r", jt)
        c.emit("sync", lambda e, tc=tc, jt=jt: e.dma_start(
            gl[jt][:, :], glog[0:24, tc * PD:(tc + 1) * PD]),
            reads=[c.tb(glog.name)], writes=[bgl], dma=True,
            ch=pname + f"G{jt}")
        bk = 2 + jt
        bps = c.tb(f"psum{bk}")
        c.emit("tensor", lambda e, bk=bk, jt=jt: e.transpose(
            P[bk][:, 0:24], gl[jt][0:24, :], cst["ident_f"][0:24, 0:24]),
            reads=[bgl, bcst], writes=[bps])
        c.emit("vector", lambda e, bk=bk, jt=jt: e.tensor_tensor(
            wtm[jt][:, :], P[bk][:, 0:24], itbc[:, :], ALU.mult),
            reads=[bps, bitbc], writes=[bw])
        c.emit("vector", lambda e, jt=jt: e.tensor_reduce(
            red[jt][:, 0:4], wtm[jt][:, :].rearrange("p (h s) -> p h s", h=4),
            AX.X, ALU.max),
            reads=[bw], writes=[br_])
        c.emit("vector", lambda e, jt=jt: e.tensor_tensor(
            t1[jt][:, :].rearrange("p (h s) -> p h s", h=4),
            wtm[jt][:, :].rearrange("p (h s) -> p h s", h=4),
            red[jt][:, 0:4].unsqueeze(2).broadcast_to([PD, 4, 6]),
            ALU.subtract),
            reads=[bw, br_], writes=[bw])
        c.emit("scalar", lambda e, jt=jt: e.activation(
            t1[jt][:, :], t1[jt][:, :], AF.Exp),
            reads=[bw], writes=[bw])
        c.emit("vector", lambda e, jt=jt: e.tensor_reduce(
            red[jt][:, 4:8], t1[jt][:, :].rearrange("p (h s) -> p h s", h=4),
            AX.X, ALU.add),
            reads=[bw], writes=[br_])
        c.emit("vector", lambda e, jt=jt: e.reciprocal(
            red[jt][:, 4:8], red[jt][:, 4:8]),
            reads=[br_], writes=[br_])
        c.emit("vector", lambda e, jt=jt: e.tensor_tensor(
            wtm[jt][:, :].rearrange("p (h s) -> p h s", h=4),
            t1[jt][:, :].rearrange("p (h s) -> p h s", h=4),
            red[jt][:, 4:8].unsqueeze(2).broadcast_to([PD, 4, 6]), ALU.mult),
            reads=[bw, br_], writes=[bw])
        c.emit("tensor", lambda e, bk=bk, jt=jt: e.transpose(
            P[bk][0:24, 0:PD], wtm[jt][:, :], cst["ident_f"][:, :]),
            reads=[bw, bcst], writes=[bps])
        bwo = c.tb(pname + "wo", jt)
        c.emit("vector", lambda e, bk=bk, jt=jt: e.tensor_copy(
            wout[jt][:, :], P[bk][0:24, 0:PD]),
            reads=[bps], writes=[bwo])
        c.emit("scalar", lambda e, tc=tc, jt=jt: e.dma_start(
            wgate[0:24, tc * PD:(tc + 1) * PD], wout[jt][:, :]),
            reads=[bwo], writes=[c.tb(wgate.name, tc)], dma=True,
            ch=pname + f"S{jt}")


def emit_fuse(c, sb, cst, P, branches, wgate, wg_tb, fused, *, T, TSn, D,
              hd2, pname):
    EC = D // PD
    bt = [[sb.alloc([PD, TSn], BF16) for _ in range(6)] for _ in range(2)]
    wbc = [sb.alloc([PD, 6 * TSn], BF16) for _ in range(2)]
    wrow = [sb.alloc([1, 6 * TSn], F32) for _ in range(2)]
    acc = [sb.alloc([PD, TSn], BF16) for _ in range(2)]
    tmp = [sb.alloc([PD, TSn], BF16) for _ in range(2)]
    bcst = c.tb("cst")
    hpc = PD // hd2 if hd2 < PD else 1
    for s in range(T // TSn):
        t0 = s * TSn
        for hh in range(D // hd2):
            jh = hh % 2
            bwb = c.tb(pname + "wb", jh)
            bwr = c.tb(pname + "wr", jh)
            for s6 in range(6):
                c.emit("sync", lambda e, jh=jh, hh=hh, s6=s6, t0=t0:
                       e.dma_start(wrow[jh][:1, s6 * TSn:(s6 + 1) * TSn],
                                   wgate[6 * hh + s6:6 * hh + s6 + 1,
                                         t0:t0 + TSn]),
                    reads=[c.tb(wgate.name)], writes=[bwr], dma=True,
                    ch=pname + f"W{jh}")
            for s6 in range(6):
                bk = 4 + (s6 % 2)
                bps = c.tb(f"psum{bk}")
                c.emit("tensor", lambda e, jh=jh, s6=s6, bk=bk: e.matmul(
                    P[bk][:, 0:TSn], cst["ones_row_f"][:1, :],
                    wrow[jh][:1, s6 * TSn:(s6 + 1) * TSn],
                    start=True, stop=True),
                    reads=[bwr, bcst], writes=[bps])
                c.emit("scalar", lambda e, jh=jh, s6=s6, bk=bk: e.copy(
                    wbc[jh][:, s6 * TSn:(s6 + 1) * TSn], P[bk][:, 0:TSn]),
                    reads=[bps], writes=[bwb])
            for cc in range(hd2 // PD):
                ci = hh * (hd2 // PD) + cc
                jc = ci % 2
                bacc = c.tb(pname + "acc", jc)
                for s6 in range(6):
                    bx = c.tb(pname + "x", (jc, s6))
                    c.emit("sync", lambda e, jc=jc, s6=s6, ci=ci, t0=t0:
                           e.dma_start(bt[jc][s6][:, :],
                                       branches[s6][ci * PD:(ci + 1) * PD,
                                                    t0:t0 + TSn]),
                        reads=[c.tb(branches[s6].name)], writes=[bx],
                        dma=True, ch=pname + f"L{jc}{s6}")
                for s6 in range(6):
                    bx = c.tb(pname + "x", (jc, s6))
                    if s6 == 0:
                        c.emit("vector", lambda e, jc=jc, jh=jh, s6=s6:
                               e.tensor_tensor(
                            acc[jc][:, :], bt[jc][s6][:, :],
                            wbc[jh][:, s6 * TSn:(s6 + 1) * TSn], ALU.mult),
                            reads=[bx, bwb], writes=[bacc])
                    else:
                        btm = c.tb(pname + "tm", jc)
                        c.emit("vector", lambda e, jc=jc, jh=jh, s6=s6:
                               e.tensor_tensor(
                            tmp[jc][:, :], bt[jc][s6][:, :],
                            wbc[jh][:, s6 * TSn:(s6 + 1) * TSn], ALU.mult),
                            reads=[bx, bwb], writes=[btm])
                        c.emit("vector", lambda e, jc=jc: e.tensor_tensor(
                            acc[jc][:, :], acc[jc][:, :], tmp[jc][:, :],
                            ALU.add),
                            reads=[bacc, btm], writes=[bacc])
                c.emit("scalar", lambda e, jc=jc, ci=ci, t0=t0: e.dma_start(
                    fused[ci * PD:(ci + 1) * PD, t0:t0 + TSn], acc[jc][:, :]),
                    reads=[bacc], writes=[c.tb(fused.name, (s, ci))],
                    dma=True, ch=pname + f"S{jc}")


def sl_l1_attn_b(c, P, cst, cfg, w, scr, res_in, res_out):
    nc = c.nc
    E, T, Tb, TSn, nb = cfg["E"], cfg["T"], cfg["Tb"], cfg["TS"], cfg["nb"]
    hd2 = cfg["hd2"]
    H_ = E // hd2
    sbk = SB(nc)
    beta_sb = sbk.alloc([H_, T], F32)
    sb = SB(nc)
    pn = "db"
    xn = scr["xn"]
    emit_prenorm(c, sb, cst, P, res_in, xn, T, TSn, mode="ln", pname=pn + "ln")
    sb = c.phase(sb)
    cpt = [sb.alloc([PD, TSn], BF16) for _ in range(2)]
    for s in range(T // TSn):
        for ci in range(E // PD):
            j = (s * (E // PD) + ci) % 2
            bx = c.tb(pn + "cp", j)
            c.emit("sync", lambda e, j=j, ci=ci, s=s: e.dma_start(
                cpt[j][:, :], xn[ci * PD:(ci + 1) * PD, s * TSn:(s + 1) * TSn]),
                reads=[c.tb(xn.name)], writes=[bx], dma=True, ch=pn + f"c{j}")
            c.emit("scalar", lambda e, j=j, ci=ci, s=s: e.dma_start(
                scr["gin"][ci * PD:(ci + 1) * PD, s * TSn:(s + 1) * TSn],
                cpt[j][:, :]),
                reads=[bx], writes=[c.tb(scr["gin"].name, ("x", s, ci))],
                dma=True, ch=pn + f"d{j}")
    emit_proj(c, sb, cst, P, xn, w["l1b_qw"], scr["pj3"], T=T, TSn=TSn,
              pname=pn + "pq")
    sb = c.phase(sb)
    emit_proj(c, sb, cst, P, xn, w["l1b_kw"], scr["pk3"], T=T, TSn=TSn,
              pname=pn + "pk")
    sb = c.phase(sb)
    emit_proj(c, sb, cst, P, xn, w["l1b_vw"], scr["pv3"], T=T, TSn=TSn,
              pname=pn + "pv")
    sb = c.phase(sb)
    emit_proj(c, sb, cst, P, xn, w["l1b_bw"], scr["glog"], T=T, TSn=TSn,
              out_dt=F32, pname=pn + "pb")
    sb = c.phase(sb)
    bl = sb.alloc([H_, T], F32)
    bbl = c.tb(pn + "bl")
    c.emit("sync", lambda e: e.dma_start(bl[0:H_, :], scr["glog"][0:H_, :]),
           reads=[c.tb(scr["glog"].name)], writes=[bbl], dma=True,
           ch=pn + "bL")
    c.emit("scalar", lambda e: e.activation(beta_sb[0:H_, :], bl[0:H_, :],
                                            AF.Sigmoid),
           reads=[bbl], writes=[c.tb(pn + "beta")])
    beta_tb = c.tb(pn + "beta")
    emit_fir_conv(c, sb, cst, P, scr["pj3"], scr["cq"], T=T, Tb=Tb, TSn=TSn,
                  taps=4, wdiag=w["l1b_qcd"], act=AF.Silu, pname=pn + "cq",
                  D=E)
    sb = c.phase(sb)
    emit_fir_conv(c, sb, cst, P, scr["pk3"], scr["ck"], T=T, Tb=Tb, TSn=TSn,
                  taps=4, wdiag=w["l1b_kcd"], act=AF.Silu, pname=pn + "ck",
                  D=E)
    sb = c.phase(sb)
    emit_fir_conv(c, sb, cst, P, scr["pv3"], scr["cv"], T=T, Tb=Tb, TSn=TSn,
                  taps=4, wdiag=w["l1b_vcd"], act=AF.Silu, pname=pn + "cv",
                  D=E)
    sb = c.phase(sb)
    beta_tb = c.tb(pn + "beta")
    emit_headnorm(c, sb, cst, P, scr["cq"], [(scr["cqn"], False)], T=T,
                  TSn=TSn, D=E, hd2=hd2, pname=pn + "nq", mode="l2")
    sb = c.phase(sb)
    beta_tb = c.tb(pn + "beta")
    emit_headnorm(c, sb, cst, P, scr["ck"], [(scr["ckn"], False),
                                             (scr["ckb"], True)], T=T,
                  TSn=TSn, D=E, hd2=hd2, pname=pn + "nk", mode="l2",
                  beta_sb=beta_sb, beta_tb=beta_tb)
    sb = c.phase(sb)
    beta_tb = c.tb(pn + "beta")
    emit_headnorm(c, sb, cst, P, scr["cv"], [(scr["cvb"], True)], T=T,
                  TSn=TSn, D=E, hd2=hd2, pname=pn + "nv", mode="none",
                  beta_sb=beta_sb, beta_tb=beta_tb)
    sb = c.phase(sb)
    emit_transpose_pass(c, sb, cst, P, scr["cvb"], scr["t_vb"], R=E, T=T,
                        pname=pn + "tv")
    sb = c.phase(sb)
    emit_transpose_pass(c, sb, cst, P, scr["ckb"], scr["t_kb"], R=E, T=T,
                        pname=pn + "tk")
    sb = c.phase(sb)
    emit_transpose_pass(c, sb, cst, P, scr["ckn"], scr["t_kn"], R=E, T=T,
                        pname=pn + "tn")
    sb = c.phase(sb)
    for kk, nm in ((1, "fb1"), (3, "fb3"), (7, "fb7"), (31, "fb31")):
        emit_fir_conv(c, sb, cst, P, scr["cv"], scr[nm], T=T, Tb=Tb, TSn=TSn,
                      taps=kk, wdiag=w[f"l1b_fir{kk}"], pname=pn + f"f{kk}",
                      D=E)
        sb = c.phase(sb)
    bufs = dict(cqn=scr["cqn"], ckn=scr["ckn"], ckb=scr["ckb"],
                t_vb=scr["t_vb"], t_kb=scr["t_kb"], t_kn=scr["t_kn"],
                delta_tm=scr["dtm"])
    pairs = []
    for b in range(nb):
        for h in range(H_):
            pairs.append(dict(b=b, h=h))
    for pi in range(0, len(pairs), 2):
        lanes = pairs[pi:pi + 2]
        emit_delta(c, sb, cst, P, cfg, lanes, bufs, pn + f"D{pi}")
        sb = c.phase(sb)
    emit_transpose_pass(c, sb, cst, P, scr["dtm"], scr["df"], R=T, T=E,
                        pname=pn + "td")
    sb = c.phase(sb)
    branches = [scr["fb1"], scr["fb3"], scr["fb7"], scr["fb31"], scr["df"],
                scr["cv"]]
    emit_branch_stats(c, sb, cst, P, branches, scr["gin"], T=T, TSn=TSn, D=E,
                      hd2=hd2, pname=pn + "bs")
    sb = c.phase(sb)
    emit_proj(c, sb, cst, P, scr["gin"], w["l1b_mw1"], scr["hg"], T=T,
              TSn=TSn, bias_col=cst["l1b_b1"], act=AF.Gelu, pname=pn + "m1")
    sb = c.phase(sb)
    emit_proj(c, sb, cst, P, scr["hg"], w["l1b_mw2"], scr["glog"], T=T,
              TSn=TSn, bias_col=cst["l1b_b2"], out_dt=F32, pname=pn + "m2")
    sb = c.phase(sb)
    emit_gates(c, sb, sbk, cst, P, scr["glog"], w["l1b_glt"], scr["wgt"],
               T=T, pname=pn + "gt")
    sb = c.phase(sb)
    emit_fuse(c, sb, cst, P, branches, scr["wgt"], c.tb(scr["wgt"].name),
              scr["ff"], T=T, TSn=TSn, D=E, hd2=hd2, pname=pn + "fu")
    sb = c.phase(sb)
    emit_headnorm(c, sb, cst, P, scr["ff"], [(scr["fo"], False)], T=T,
                  TSn=TSn, D=E, hd2=hd2, pname=pn + "on", mode="rmsh",
                  gamma_col=cst["l1b_onorm"])
    sb = c.phase(sb)
    emit_proj(c, sb, cst, P, scr["fo"], w["l1b_ow"], res_out, T=T, TSn=TSn,
              res=res_in, res_out=res_out, out_dt=F32, pname=pn + "oo")
    c.fence(res_out.name)
    c.global_barrier()
    sb.close()
    sbk.close()



def build_const_host(cfg):
    TQ = min(512, cfg["Tb"])
    out = {}
    out["ones_col_f"] = np.ones((PD, 1), np.float32)
    out["ones_row_f"] = np.ones((1, PD), np.float32)
    out["ones_row_bf"] = np.ones((1, PD), BF)
    out["ident_bf"] = np.eye(PD, dtype=BF)
    out["ident_f"] = np.eye(PD, dtype=np.float32)
    kk = np.arange(PD)
    out["tri_lt"] = (kk[:, None] < kk[None, :]).astype(np.float32)
    s64 = np.zeros((PD, PD), np.float32)
    s64[64, :] = 1.0
    out["sel64"] = s64
    p = np.arange(PD, dtype=np.float32)[:, None]
    f = np.arange(TQ, dtype=np.float32)[None, :]
    Tb = cfg["Tb"]
    W = 2 * Tb - PD
    jj = np.arange(W, dtype=np.float32)[None, :]
    x0 = Tb - PD
    out["ABSD"] = np.abs(p - jj + x0).astype(np.float32)
    nrel = TQ // PD
    m = np.zeros((PD, nrel * TQ), BF)
    for rel in range(nrel):
        m[:, rel * TQ:(rel + 1) * TQ] = ((p + rel * PD) <= f).astype(BF)
    out["cmask"] = m
    if cfg.get("l1b"):
        hd2 = cfg["hd2"]
        D_ = cfg["E"]
        EC_ = D_ // PD
        H_ = D_ // hd2
        pp = np.arange(PD)
        hsel = np.zeros((PD, EC_ * H_), np.float32)
        hexp = np.zeros((H_, EC_ * PD), np.float32)
        for ci in range(EC_):
            hh = (ci * PD) // hd2
            hsel[:, ci * H_ + hh] = 1.0
            hexp[hh, ci * PD:(ci + 1) * PD] = 1.0
        out["hselD"] = hsel
        out["hexpD"] = hexp
        out["hselD_bf"] = hsel.astype(BF)
        i32 = np.arange(32)
        sl32 = (i32[None, :] < i32[:, None]).astype(np.float32)
        out["maskSL4"] = np.tile(sl32, (4, 1))
        out["maskTI"] = (i32[:, None] <= i32[None, :]).astype(np.float32)
        out["maskTI4"] = np.tile(out["maskTI"], (4, 1))
    if cfg.get("l1f"):
        Tb_ = cfg["Tb"]
        WIN = cfg["window"]
        DH_ = 64
        half = np.arange(0, DH_, 2, dtype=np.float64)
        inv = 1.0 / 10000.0 ** (half / DH_)
        t = np.arange(Tb_, dtype=np.float64)
        fr = t[:, None] * inv[None, :]
        fr2 = np.concatenate([fr, fr], 1)
        base = (half + 0.4 * DH_) / (1.4 * DH_)
        power = (t - Tb_ // 2) / float(WIN // 2)
        scl = base[None, :] ** power[:, None]
        scl2 = np.concatenate([scl, scl], 1)
        cosf = np.cos(fr2)
        sinf = np.sin(fr2)
        out["xq_c"] = np.tile((cosf * scl2).T.astype(np.float32), (2, 1))
        out["xq_s"] = np.tile((sinf * scl2).T.astype(np.float32), (2, 1))
        out["xk_c"] = np.tile((cosf / scl2).T.astype(np.float32), (2, 1))
        out["xk_s"] = np.tile((sinf / scl2).T.astype(np.float32), (2, 1))
        pp = np.arange(PD)
        out["hsel2"] = (pp[:, None] // 64 == np.arange(2)[None, :]).astype(np.float32)
        out["hexp2"] = np.ascontiguousarray(out["hsel2"].T)
        R = np.zeros((PD, PD), BF)
        for m_ in range(PD):
            if m_ % 64 < 32:
                R[m_ + 32, m_] = -1.0
            else:
                R[m_ - 32, m_] = 1.0
        out["R128"] = R
        nu = WIN // PD
        lm = np.zeros((PD, 2 * nu * WIN), BF)
        pw = np.arange(PD)[:, None]
        fw = np.arange(WIN)[None, :]
        for u in range(-(nu - 1), nu + 1):
            mi = u + nu - 1
            keep = (fw - pw + PD * u >= 0) & (fw - pw + PD * u <= WIN)
            lm[:, mi * WIN:(mi + 1) * WIN] = keep.astype(BF)
        out["lmask"] = lm
    if cfg.get("rope_l0b"):
        Tb_ = cfg["Tb"]
        nb_ = cfg.get("nb", 2)
        hs = 64
        t = np.arange(1, Tb_ + 1, dtype=np.float64)[:, None]
        i = np.arange(hs // 2, dtype=np.float64)
        theta = (1.0 / 10000.0) ** (2.0 * i / hs)
        ang = t * theta
        c32 = np.cos(ang).T.astype(np.float32)
        s32 = np.sin(ang).T.astype(np.float32)
        out["ropecos"] = np.tile(np.tile(c32, (4, 1)), (1, nb_))
        out["ropesin"] = np.tile(np.tile(s32, (4, 1)), (1, nb_))
    return out


def build_program(cfg, wspec, xcols, consts):
    nc = bass.Bass("TRN2", target_bir_lowering=False)
    E, T = cfg["E"], cfg["T"]
    H0 = cfg["l0a_H"]
    x_in = nc.dram_tensor("x_fm", [E, T], F32, kind="ExternalInput").ap()
    y_out = nc.dram_tensor("y_fm", [E, T], F32, kind="ExternalOutput").ap()
    w = {}
    for name, shape in wspec.items():
        w[name] = nc.dram_tensor(name, list(shape), F32,
                                 kind="ExternalInput").ap()
    scr = {}
    dbg0 = set(cfg.get("dbg", ()))

    def sdram(nm, shape, dt):
        kind = "ExternalOutput" if nm in dbg0 else "Internal"
        return nc.dram_tensor("s_" + nm, shape, dt, kind=kind).ap()

    scr["xn"] = sdram("xn", [E, T], BF16)
    scr["qf"] = sdram("qf", [E, T], BF16)
    scr["kf"] = sdram("kf", [E, T], BF16)
    scr["vtm"] = sdram("vtm", [T, E + H0], BF16)
    scr["ao"] = sdram("ao", [E, T], BF16)
    scr["qkro"] = sdram("qkro", [2 * E, T], F32)
    if "pdbg" in dbg0:
        scr["pdbg"] = nc.dram_tensor("s_pdbg", [2048, 1024], BF16,
                                     kind="ExternalOutput").ap()
    if any(s.startswith("l0_ffn") for s in cfg["sublayers"]):
        scr["srow"] = sdram("srow", [1, T], F32)
        scr["xr"] = sdram("xr", [E, T], BF16)
        scr["xr2"] = sdram("xr2", [E, T], BF16)
        scr["h1"] = sdram("h1", [E // 2, T], BF16)
        scr["h2"] = nc.dram_tensor("s_h2", [max(cfg["dh"], cfg.get("hid", 0)), T], BF16).ap()
        scr["resM"] = sdram("resM", [E, T], F32)
    if any(s.startswith("l1_ffn") for s in cfg["sublayers"]):
        if "resM" not in scr:
            scr["resM"] = sdram("resM", [E, T], F32)
            scr["xr"] = sdram("xr", [E, T], BF16)
            scr["xr2"] = sdram("xr2", [E, T], BF16)
            scr["h2"] = nc.dram_tensor("s_h2", [max(cfg.get("dh", 0), cfg["hid"]), T], BF16).ap()
        scr["resN"] = sdram("resN", [E, T], F32)
        scr["resO"] = sdram("resO", [E, T], F32)
        scr["resP"] = sdram("resP", [E, T], F32)
        scr["xg"] = sdram("xg", [E, T], BF16)
        scr["h3"] = sdram("h3", [2 * cfg["hid"], T], BF16)
    dbg = set(cfg.get("dbg", ()))
    if "l1_attn_b" in cfg["sublayers"]:
        for nm, shape, dt in (
                ("pj3", [E, T], BF16), ("pk3", [E, T], BF16),
                ("pv3", [E, T], BF16), ("cq", [E, T], BF16),
                ("ck", [E, T], BF16), ("cv", [E, T], BF16),
                ("cqn", [E, T], BF16), ("ckn", [E, T], BF16),
                ("ckb", [E, T], BF16), ("cvb", [E, T], BF16),
                ("t_vb", [T, E], BF16), ("t_kb", [T, E], BF16),
                ("t_kn", [T, E], BF16), ("dtm", [T, E], BF16),
                ("df", [E, T], BF16), ("fb1", [E, T], BF16),
                ("fb3", [E, T], BF16), ("fb7", [E, T], BF16),
                ("fb31", [E, T], BF16), ("gin", [E + 96, T], BF16),
                ("hg", [E, T], BF16), ("glog", [24, T], F32),
                ("wgt", [24, T], F32), ("ff", [E, T], BF16),
                ("fo", [E, T], BF16)):
            kind = "ExternalOutput" if nm in dbg else "Internal"
            scr[nm] = nc.dram_tensor("s_" + nm, shape, dt, kind=kind).ap()
    resA = sdram("resA", [E, T], F32)
    resB = sdram("resB", [E, T], F32)

    c = Ctx(nc)
    sbg = SB(nc)
    stack = ExitStack()
    P = [stack.enter_context(nc.psum_tensor(f"pb{i}", [PD, 512], F32))
         for i in range(8)]
    cst = {"_lazy": {}}
    bcst = c.tb("cst")
    cpool_vals = cfg.get("cpool", [])
    if cpool_vals:
        cp = nc.dram_tensor("cst_cpool", [PD, len(cpool_vals)], F32,
                            kind="ExternalInput").ap()
        cpt = sbg.alloc([PD, len(cpool_vals)], F32, name="c_cpool")
        c.emit("sync", lambda e: e.dma_start(cpt[:, :], cp),
               reads=[], writes=[c.tb("cst", "cpool")], dma=True, ch="CST")
        for i, v in enumerate(cpool_vals):
            nc.const_aps.register(F32, float(v), cpt[:, i:i + 1])
    for name, arr in consts.items():
        dt = BF16 if arr.dtype == BF else F32
        d = nc.dram_tensor("cst_" + name, list(arr.shape), dt,
                           kind="ExternalInput").ap()
        if name in BIG_CONSTS:
            cst["_lazy"][name] = (d, arr.shape, dt)
            continue
        tl = sbg.alloc(list(arr.shape), dt, name="c_" + name)
        c.emit("sync", lambda e, tl=tl, d=d: e.dma_start(tl[:, :], d),
               reads=[], writes=[c.tb("cst", name)], dma=True, ch="CST")
        cst[name] = tl
    for name, shape in xcols.items():
        d = nc.dram_tensor("xc_" + name, list(shape), F32,
                           kind="ExternalInput").ap()
        tl = sbg.alloc(list(shape), F32, name="c_" + name)
        c.emit("sync", lambda e, tl=tl, d=d: e.dma_start(tl[:, :], d),
               reads=[], writes=[c.tb("cst", name)], dma=True, ch="CST")
        cst[name] = tl
    c.fence("cst")
    c.global_barrier()

    seq = cfg["sublayers"]
    bufs = [resA, resB]
    cur = x_in
    for i, sl in enumerate(seq):
        res_out = y_out if i == len(seq) - 1 else bufs[i % 2]
        if sl == "l0_attn_a":
            sl_l0_attn_a(c, P, cst, cfg, w, scr, cur, res_out)
        elif sl == "l1_attn_a":
            sl_l1_attn_a(c, P, cst, cfg, w, scr, cur, res_out)
        elif sl == "l0_attn_b":
            sl_l0_attn_b(c, P, cst, cfg, w, scr, cur, res_out)
        elif sl.startswith("l0_ffn"):
            sl_l0_ffn(c, P, cst, cfg, w, scr, cur, res_out, sl[-1])
        elif sl.startswith("l1_ffn"):
            sl_l1_ffn(c, P, cst, cfg, w, scr, cur, res_out, sl[-1])
        elif sl == "l1_attn_b":
            sl_l1_attn_b(c, P, cst, cfg, w, scr, cur, res_out)
        else:
            raise KeyError(sl)
        cur = res_out
    c.finalize()
    stack.close()
    sbg.close()
    return nc


def col_layout(b):
    n = b.shape[0]
    oc = cdiv(n, PD)
    pad = np.zeros(oc * PD, np.float32)
    pad[:n] = b
    return np.ascontiguousarray(pad.reshape(oc, PD).T)


def vtm_weights(WvT, bv, H, hd):
    E = WvT.shape[0]
    wve = np.zeros((E, H * (hd + 1)), np.float32)
    bve = np.zeros((1, H * (hd + 1)), np.float32)
    for h in range(H):
        wve[:, h * (hd + 1):h * (hd + 1) + hd] = WvT[:, h * hd:(h + 1) * hd]
        bve[0, h * (hd + 1):h * (hd + 1) + hd] = bv[h * hd:(h + 1) * hd]
        bve[0, h * (hd + 1) + hd] = 1.0
    return wve, bve


def host_prep(cfg, rr):
    E = cfg["E"]
    ws, xc = {}, {}
    seq = cfg["sublayers"]
    f32 = np.float32
    if "l0_attn_a" in seq:
        H = cfg["l0a_H"]
        hd = E // H
        Wqkv = rr["L0_attn_a_Wqkv_w"]
        bqkv = rr["L0_attn_a_Wqkv_b"]
        ws["l0a_wq"] = np.ascontiguousarray(Wqkv[0:E].T, f32)
        ws["l0a_wk"] = np.ascontiguousarray(Wqkv[E:2 * E].T, f32)
        wve, bve = vtm_weights(np.ascontiguousarray(Wqkv[2 * E:3 * E].T, f32),
                               bqkv[2 * E:3 * E], H, hd)
        ws["l0a_wve"] = wve
        ws["l0a_bve"] = bve
        ws["l0a_wo"] = np.ascontiguousarray(rr["L0_attn_a_out_w"].T, f32)
        ws["l0a_dd"] = rr["L0_attn_a_distance_decay"].reshape(1, 1).astype(f32)
        xc["l0a_bq"] = col_layout(bqkv[0:E])
        xc["l0a_bk"] = col_layout(bqkv[E:2 * E])
        xc["l0a_bo"] = col_layout(rr["L0_attn_a_out_b"])
    if "l1_attn_a" in seq:
        H = E // 64
        WqT = np.ascontiguousarray(rr["L1_attn_a_Wq"].T, f32)
        WkT = np.ascontiguousarray(rr["L1_attn_a_Wk"].T, f32)
        WvT = np.ascontiguousarray(rr["L1_attn_a_Wv"].T, f32)
        kvh = WkT.shape[1] // 64
        rep = H // kvh
        dup = lambda W: np.repeat(W.reshape(E, kvh, 64), rep, axis=1).reshape(E, H * 64)
        ws["l1a_wq"] = WqT
        ws["l1a_wk"] = np.ascontiguousarray(dup(WkT))
        wve, bve = vtm_weights(np.ascontiguousarray(dup(WvT)),
                               np.zeros(H * 64, f32), H, 64)
        ws["l1a_wve"] = wve
        ws["l1a_bve"] = bve
        ws["l1a_wo"] = np.ascontiguousarray(rr["L1_attn_a_Wo"].T, f32)
    if "l0_attn_b" in seq:
        H = E // 64
        def eo(W):
            WT = np.ascontiguousarray(W.T, f32).reshape(E, H * 32, 2)
            return WT[..., 0], WT[..., 1]
        qe, qo = eo(rr["L0_attn_b_Wq"])
        ke, ko = eo(rr["L0_attn_b_Wk"])
        ws["l0b_weo"] = np.ascontiguousarray(np.concatenate([qe, qo, ke, ko], 1))
        bq = rr["L0_attn_b_bq"].reshape(H * 32, 2)
        bk = rr["L0_attn_b_bk"].reshape(H * 32, 2)
        xc["l0b_beo"] = col_layout(np.concatenate(
            [bq[:, 0], bq[:, 1], bk[:, 0], bk[:, 1]]))
        wve, bve = vtm_weights(np.ascontiguousarray(rr["L0_attn_b_Wv"].T, f32),
                               rr["L0_attn_b_bv"], H, 64)
        ws["l0b_wve"] = wve
        ws["l0b_bve"] = bve
        ws["l0b_wo"] = np.ascontiguousarray(rr["L0_attn_b_Wo"].T, f32)
        xc["l0b_bo"] = col_layout(rr["L0_attn_b_bo"])
    if any(s.startswith("l0_ffn") for s in seq):
        ws["l0f_rt"] = rr["L0_ffn_routing_token"].reshape(E, 1).astype(f32)
        ws["l0f_lw1"] = rr["L0_ffn_light_w1"].astype(f32)
        ws["l0f_lw2"] = rr["L0_ffn_light_w2"].astype(f32)
        ws["l0f_hw1"] = rr["L0_ffn_heavy_w1"].astype(f32)
        ws["l0f_hw2"] = rr["L0_ffn_heavy_w2"].astype(f32)
        xc["l0f_lb1"] = col_layout(rr["L0_ffn_light_b1"])
        xc["l0f_lb2"] = col_layout(rr["L0_ffn_light_b2"])
        xc["l0f_hb1"] = col_layout(rr["L0_ffn_heavy_b1"])
        xc["l0f_hb2"] = col_layout(rr["L0_ffn_heavy_b2"])
        xc["l0f_lgam"] = col_layout(rr["L0_ffn_light_gamma"])
        xc["l0f_hgam"] = col_layout(rr["L0_ffn_heavy_gamma"])
    if any(s.startswith("l1_ffn") for s in seq):
        I2 = E // 2
        H = I2 // 64
        wqkv = rr["L1_ffn_wqkv"].astype(f32)
        for L in range(2):
            ws[f"l1f_wqk{L}"] = np.ascontiguousarray(wqkv[L][:, 0:2 * I2])
            wv, bv = vtm_weights(np.ascontiguousarray(wqkv[L][:, 2 * I2:3 * I2]),
                                 np.zeros(I2, f32), H, 64)
            ws[f"l1f_wv{L}"] = wv
            if L == 0:
                ws["l1f_bve"] = bv
            ws[f"l1f_wo{L}"] = rr["L1_ffn_wo"][L].astype(f32)
            ws[f"l1f_w1{L}"] = rr["L1_ffn_w1"][L].astype(f32)
            ws[f"l1f_w2{L}"] = rr["L1_ffn_w2"][L].astype(f32)
            xc[f"l1f_qs{L}"] = np.tile(rr["L1_ffn_q_scale"][L], 2)[:, None].astype(f32)
            xc[f"l1f_ks{L}"] = np.tile(rr["L1_ffn_k_scale"][L], 2)[:, None].astype(f32)
            xc[f"l1f_lg{L}"] = col_layout(rr["L1_ffn_ln_g"][L])
            xc[f"l1f_lb{L}"] = col_layout(rr["L1_ffn_ln_b"][L])
            xc[f"l1f_b1{L}"] = col_layout(rr["L1_ffn_b1"][L])
            xc[f"l1f_b2{L}"] = col_layout(rr["L1_ffn_b2"][L])
    if "l1_attn_b" in seq:
        def conv_diag(wk):
            E_, k = wk.shape
            o = np.zeros((E_, k * PD), f32)
            for ci in range(E_ // PD):
                for jj in range(k):
                    blk = o[ci * PD:(ci + 1) * PD, jj * PD:(jj + 1) * PD]
                    np.fill_diagonal(blk, wk[ci * PD:(ci + 1) * PD, jj])
            return o
        ws["l1b_qw"] = rr["L1_attn_b_q_w"].astype(f32)
        ws["l1b_kw"] = rr["L1_attn_b_k_w"].astype(f32)
        ws["l1b_vw"] = rr["L1_attn_b_v_w"].astype(f32)
        ws["l1b_bw"] = rr["L1_attn_b_b_w"].astype(f32)
        ws["l1b_qcd"] = conv_diag(rr["L1_attn_b_qc_w"][:, 0, :].astype(f32))
        ws["l1b_kcd"] = conv_diag(rr["L1_attn_b_kc_w"][:, 0, :].astype(f32))
        ws["l1b_vcd"] = conv_diag(rr["L1_attn_b_vc_w"][:, 0, :].astype(f32))
        for kk in (1, 3, 7, 31):
            ws[f"l1b_fir{kk}"] = conv_diag(
                rr[f"L1_attn_b_fir_w{kk}"][:, 0, :].astype(f32))
        ws["l1b_mw1"] = rr["L1_attn_b_mlp_w1"].astype(f32)
        ws["l1b_mw2"] = rr["L1_attn_b_mlp_w2"].astype(f32)
        ws["l1b_ow"] = rr["L1_attn_b_o_w"].astype(f32)
        ws["l1b_glt"] = rr["L1_attn_b_gate_log_temp"].reshape(1, 4).astype(f32)
        xc["l1b_b1"] = col_layout(rr["L1_attn_b_mlp_b1"])
        xc["l1b_b2"] = col_layout(rr["L1_attn_b_mlp_b2"])
        hd2_ = rr["L1_attn_b_onorm_w"].shape[0]
        rep = E // hd2_
        xc["l1b_onorm"] = np.ascontiguousarray(
            np.tile(rr["L1_attn_b_onorm_w"], rep).reshape(E // PD, PD).T
            .astype(f32))
    return ws, xc



def full_cfg():
    return dict(E=1024, T=4096, Tb=2048, nb=2, TS=512, l0a_H=16, C=20.0,
                nheavy=1024, keff=1152.0, dh=4096, hid=2730, window=512,
                hd2=256, l1b=True, l1f=True, rope_l0b=True,
                cpool=[0.0, 1e-5, 1.0],
                sublayers=["l0_attn_a", "l0_ffn_1", "l0_attn_b", "l0_ffn_2",
                           "l1_attn_a", "l1_ffn_1", "l1_attn_b", "l1_ffn_2"])


def build_full():
    cfg = full_cfg()
    consts = build_const_host(cfg)
    return cfg, consts


def kernel(**inputs):
    cfg, consts = build_full()
    ws, xc = host_prep(cfg, inputs)
    nc = build_program(cfg, {k: v.shape for k, v in ws.items()},
                       {k: v.shape for k, v in xc.items()}, consts)
    x = inputs["L0_attn_a_x"]
    base = {}
    base["cst_cpool"] = np.tile(np.array(cfg["cpool"], np.float32)[None, :],
                                (PD, 1))
    for k, v in ws.items():
        base[k] = np.ascontiguousarray(v)
    for k, v in xc.items():
        base["xc_" + k] = np.ascontiguousarray(v)
    for k, v in consts.items():
        base["cst_" + k] = np.ascontiguousarray(v)
    in_maps = []
    for ci in range(8):
        m = dict(base)
        xs = np.asarray(x[2 * ci:2 * ci + 2], np.float32)
        m["x_fm"] = np.ascontiguousarray(xs.reshape(4096, 1024).T)
        in_maps.append(m)
    res = run_bass_kernel_spmd(nc, in_maps, core_ids=list(range(8)))
    out = np.empty((16, 2048, 1024), np.float32)
    for ci in range(8):
        y = res.results[ci]["y_fm"]
        out[2 * ci:2 * ci + 2] = np.moveaxis(
            np.asarray(y).reshape(1024, 2, 2048), 0, 2)
    return out


# version 13
# speedup vs baseline: 1.1902x; 1.1902x over previous
import math
from contextlib import ExitStack
import numpy as np
import ml_dtypes
import concourse.bass as bass
import concourse.mybir as mybir
from concourse.bass_utils import run_bass_kernel_spmd

F32 = mybir.dt.float32
BF16 = mybir.dt.bfloat16
AF = mybir.ActivationFunctionType
ALU = mybir.AluOpType
AX = mybir.AxisListType
BF = ml_dtypes.bfloat16

PD = 128


class TBuf:
    __slots__ = ("w", "r")
    def __init__(self):
        self.w = {}
        self.r = {}

def _merge(dst, src):
    for k, v in src.items():
        if dst.get(k, -1) < v:
            dst[k] = v

class Ctx:
    SEM_OF = {"tensor": "T", "vector": "V", "scalar": "S", "gpsimd": "G", "sync": "Y"}
    SECTIONS = ("tensor", "vector", "scalar", "gpsimd", "sync")
    NDMA = 64

    SEM_LIM = 30000

    def __init__(self, nc):
        self.nc = nc
        self.ops = {s: [] for s in self.SECTIONS}
        self.cnt = {}
        self.cur = {}
        for s in ("T", "V", "S", "G", "Y"):
            self._new_sem(s)
        for i in range(self.NDMA):
            self._new_sem(f"D{i}")
        self.regions = {}
        self.floor = {}
        self.chmap = {}
        self.chprev = {}
        self.chfree = {"hw": list(range(self.NDMA - 12)),
                       "sw": list(range(self.NDMA - 12, self.NDMA))}

    def _new_sem(self, base):
        n = 0
        name = base
        while name in self.cnt:
            n += 1
            name = f"{base}.{n}"
        self.cnt[name] = 0
        self.cur[base] = name
        return name

    def _sem_for(self, base, inc):
        name = self.cur[base]
        if self.cnt[name] + inc > self.SEM_LIM:
            name = self._new_sem(base)
        return name

    def tb(self, name, key=None):
        reg = self.regions.setdefault(name, {})
        if key not in reg:
            t = TBuf()
            base = reg.get(None)
            if base is not None and key is not None:
                _merge(t.w, base.w)
                _merge(t.r, base.r)
            reg[key] = t
        return reg[key]

    def fence(self, name):
        reg = self.regions.get(name)
        if not reg:
            return
        base = TBuf()
        for t in reg.values():
            _merge(base.w, t.w)
            _merge(base.r, t.r)
        self.regions[name] = {None: base}

    def phase(self, sb):
        self.global_barrier()
        sb.close()
        return SB(self.nc)

    def global_barrier(self):
        self.floor = dict(self.cnt)
        self.regions = {}
        self.chmap = {}
        self.chprev = {}
        self.chfree = {"hw": list(range(self.NDMA - 12)),
                       "sw": list(range(self.NDMA - 12, self.NDMA))}

    def _chan(self, ch, section):
        kind = "sw" if section == "gpsimd" else "hw"
        if ch not in self.chmap:
            if not self.chfree[kind]:
                raise RuntimeError("out of DMA channel sems")
            idx = self.chfree[kind].pop(0)
            self.chmap[ch] = (f"D{idx}", kind)
        sem, k0 = self.chmap[ch]
        assert k0 == kind, f"channel {ch} used from both SWDGE and HWDGE"
        return sem

    def emit(self, section, fns, reads=(), writes=(), dma=False, ch=None):
        if callable(fns):
            fns = [fns]
        if dma:
            assert ch is not None, "dma emit needs a channel"
            semname = self._sem_for(self._chan(ch, section), 16)
        else:
            semname = self._sem_for(self.SEM_OF[section], 1)
        deps = dict(self.floor)
        for b in reads:
            _merge(deps, b.w)
        for b in writes:
            _merge(deps, b.w)
            _merge(deps, b.r)
        if dma:
            prev = self.chprev.get(ch)
            if prev is not None:
                ptok, pwrites = prev
                _merge(deps, ptok)
                for pb in pwrites:
                    _merge(deps, pb.r)
        if section == "tensor" and not dma:
            deps.pop("T", None)
        inc = 16 if dma else 1
        self.cnt[semname] += inc
        tok = {semname: self.cnt[semname]}
        self.ops[section].append((fns, deps, semname, inc))
        for b in writes:
            b.w = dict(tok)
            b.r = {}
        for b in reads:
            _merge(b.r, tok)
        if dma:
            self.chprev[ch] = (dict(tok), list(writes))
        return tok

    def finalize(self, final_waits=None):
        nc = self.nc
        if final_waits is None:
            final_waits = dict(self.cnt)
        stack = ExitStack()
        sems = {s: stack.enter_context(nc.semaphore(f"sem_{s}")) for s in self.cnt}
        blk = stack.enter_context(nc.Block())

        def body(section):
            def run(eng):
                waited = {}
                for fns, deps, semname, inc in self.ops[section]:
                    for ds in sorted(deps):
                        dv = deps[ds]
                        if dv > 0 and waited.get(ds, -1) < dv:
                            eng.wait_ge(sems[ds], dv)
                            waited[ds] = dv
                    inst = None
                    for fn in fns:
                        inst = fn(eng)
                    inst.then_inc(sems[semname], inc)
                if section == "sync":
                    for ds in sorted(final_waits):
                        dv = final_waits[ds]
                        if dv > 0 and waited.get(ds, -1) < dv:
                            eng.wait_ge(sems[ds], dv)
                            waited[ds] = dv
            return run

        blk.tensor(body("tensor"))
        blk.vector(body("vector"))
        blk.scalar(body("scalar"))
        blk.gpsimd(body("gpsimd"))
        blk.sync(body("sync"))
        stack.close()


class SB:
    _uid = [0]
    def __init__(self, nc):
        self.nc = nc
        self.stack = ExitStack()

    def alloc(self, shape, dtype, name=None):
        SB._uid[0] += 1
        name = name or f"sb_{SB._uid[0]}"
        return self.stack.enter_context(self.nc.sbuf_tensor(name, shape, dtype))

    def close(self):
        self.stack.close()


def cdiv(a, b):
    return (a + b - 1) // b


BIG_CONSTS = {"ABSD", "xq_c", "xq_s", "xk_c", "xk_s", "ropecos", "ropesin",
              "lmask", "cmask"}


def cst_lazy(c, sb, cst, name):
    d, shape, dt = cst["_lazy"][name]
    t = sb.alloc(list(shape), dt)
    c.emit("sync", lambda e, t=t, d=d: e.dma_start(t[:, :], d),
           reads=[], writes=[c.tb("cst")], dma=True, ch="CZ" + name)
    return t



def emit_prenorm(c, sb, cst, P, src, dst, T, TSn, mode="ln", gamma_col=None,
                 pname="pn", src_bf=False, beta_col=None, shift_half=None,
                 dst_f32=None):
    E = src.shape[0]
    EC = E // PD
    nslab = T // TSn
    assert not (src_bf and mode == "ln")
    xdt = BF16 if src_bf else F32
    xa = [sb.alloc([PD, EC * TSn], xdt) for _ in range(3)]
    sqa = [sb.alloc([PD, TSn], F32) for _ in range(2)]
    rows = [sb.alloc([1, 3 * TSn], F32) for _ in range(2)]
    xnb = [sb.alloc([PD, EC * TSn], BF16) for _ in range(2)]
    tnorm = [sb.alloc([PD, TSn], F32) for _ in range(2)]
    src_r = src.rearrange("(ci p) t -> p ci t", p=PD)
    dst_r = dst.rearrange("(ci p) t -> p ci t", p=PD)
    bcst = c.tb("cst")
    eps = 1e-5
    tail = None
    for i in range(nslab):
        j = i % 2
        ja = i % 3
        t0 = i * TSn
        psA = P[4 + j]
        psB = P[6 + j]
        bx = c.tb(pname + "x", ja)
        bsq = c.tb(pname + "sq", j)
        brow = c.tb(pname + "row", j)
        bxn = c.tb(pname + "xn", j)
        btn = c.tb(pname + "tn", j)
        bpA = c.tb(f"psum{4 + j}")
        bpB = c.tb(f"psum{6 + j}")
        EH = EC // 2
        if shift_half is None:
            for hf in range(2):
                c.emit("sync", lambda e, ja=ja, t0=t0, hf=hf: e.dma_start(
                    xa[ja][:, hf * EH * TSn:(hf + 1) * EH * TSn]
                    .rearrange("p (ci t) -> p ci t", ci=EH),
                    src_r[:, hf * EH:(hf + 1) * EH, t0:t0 + TSn]),
                    reads=[c.tb(src.name)], writes=[bx], dma=True,
                    ch=pname + f"L{ja}{hf}")
        else:
            Tb_ = shift_half
            c.emit("sync", lambda e, ja=ja, t0=t0: e.dma_start(
                xa[ja][:, 0:EH * TSn].rearrange("p (ci t) -> p ci t", ci=EH),
                src_r[:, 0:EH, t0:t0 + TSn]),
                reads=[c.tb(src.name)], writes=[bx], dma=True,
                ch=pname + f"L{ja}0")
            if t0 % Tb_ == 0:
                c.emit("sync", lambda e, ja=ja, t0=t0: e.dma_start(
                    xa[ja][:, EH * TSn:].rearrange(
                        "p (ci t) -> p ci t", ci=EH)[:, :, 1:TSn],
                    src_r[:, EH:, t0:t0 + TSn - 1]),
                    reads=[c.tb(src.name)], writes=[bx], dma=True,
                    ch=pname + f"L{ja}1")
                c.emit("vector", lambda e, ja=ja: e.memset(
                    xa[ja][:, :].rearrange("p (ci t) -> p ci t", ci=EC)
                    [:, EH:, 0:1], 0.0),
                    reads=[], writes=[bx])
            else:
                c.emit("sync", lambda e, ja=ja, t0=t0: e.dma_start(
                    xa[ja][:, EH * TSn:].rearrange("p (ci t) -> p ci t", ci=EH),
                    src_r[:, EH:, t0 - 1:t0 + TSn - 1]),
                    reads=[c.tb(src.name)], writes=[bx], dma=True,
                    ch=pname + f"L{ja}1")
        if mode == "ln":
            for ci in range(EC):
                c.emit("tensor", lambda e, ja=ja, ci=ci, psA=psA: e.matmul(
                    psA[:1, 0:TSn], cst["ones_col_f"][:, 0:1],
                    xa[ja][:, ci * TSn:(ci + 1) * TSn],
                    start=(ci == 0), stop=(ci == EC - 1)),
                    reads=[bx, bcst], writes=[bpA])
        for ci in range(EC):
            c.emit("scalar", lambda e, ja=ja, j=j, ci=ci: e.square(
                sqa[j][:, :], xa[ja][:, ci * TSn:(ci + 1) * TSn]),
                reads=[bx], writes=[bsq])
            c.emit("tensor", lambda e, j=j, ci=ci, psB=psB: e.matmul(
                psB[:1, 0:TSn], cst["ones_col_f"][:, 0:1], sqa[j][:, :],
                start=(ci == 0), stop=(ci == EC - 1)),
                reads=[bsq, bcst], writes=[bpB])
        A = rows[j][:1, 0:TSn]
        Brow = rows[j][:1, TSn:2 * TSn]
        TMP = rows[j][:1, 2 * TSn:3 * TSn]
        if mode == "ln":
            c.emit("vector", lambda e, Brow=Brow, psA=psA: e.tensor_scalar(
                Brow, psA[:1, 0:TSn], -1.0 / E, None, ALU.mult),
                reads=[bpA], writes=[brow])
            c.emit("vector", lambda e, Brow=Brow, TMP=TMP: e.tensor_tensor(
                TMP, Brow, Brow, ALU.mult),
                reads=[brow], writes=[brow])
            c.emit("vector", lambda e, TMP=TMP, psB=psB: e.scalar_tensor_tensor(
                TMP, psB[:1, 0:TSn], 1.0 / E, TMP, ALU.mult, ALU.subtract),
                reads=[bpB, brow], writes=[brow])
            c.emit("scalar", lambda e, TMP=TMP: e.activation(
                TMP, TMP, AF.Ln, bias=np.float32(eps)),
                reads=[brow], writes=[brow])
            c.emit("scalar", lambda e, A=A, TMP=TMP: e.activation(
                A, TMP, AF.Exp, scale=-0.5),
                reads=[brow], writes=[brow])
        else:
            c.emit("vector", lambda e, TMP=TMP, psB=psB: e.tensor_scalar(
                TMP, psB[:1, 0:TSn], 1e-24, None, ALU.max),
                reads=[bpB], writes=[brow])
            c.emit("scalar", lambda e, TMP=TMP: e.activation(
                TMP, TMP, AF.Ln, scale=np.float32(1.0 / E)),
                reads=[brow], writes=[brow])
            c.emit("scalar", lambda e, A=A, TMP=TMP: e.activation(
                A, TMP, AF.Exp, scale=-0.5),
                reads=[brow], writes=[brow])
        if tail is not None:
            tail()

        def mk_tail(i=i, j=j, ja=ja, t0=t0, psA=psA, psB=psB, bx=bx, brow=brow,
                    bpA=bpA, bpB=bpB, bxn=bxn, btn=btn, A=A, Brow=Brow):
            def run():
                c.emit("tensor", lambda e: e.matmul(
                    psA[:, 0:TSn], cst["ones_row_f"][:1, :], A,
                    start=True, stop=True),
                    reads=[brow, bcst], writes=[bpA])
                if mode == "ln":
                    c.emit("tensor", lambda e: e.matmul(
                        psB[:, 0:TSn], cst["ones_row_f"][:1, :], Brow,
                        start=True, stop=True),
                        reads=[brow, bcst], writes=[bpB])
                for ci in range(EC):
                    if mode == "ln":
                        c.emit("vector", lambda e, ci=ci: e.tensor_tensor(
                            tnorm[j][:, :], xa[ja][:, ci * TSn:(ci + 1) * TSn],
                            psB[:, 0:TSn], ALU.add),
                            reads=[bx, bpB], writes=[btn])
                        if gamma_col is None:
                            c.emit("vector", lambda e, ci=ci:
                                   e.tensor_tensor(
                                xnb[j][:, ci * TSn:(ci + 1) * TSn],
                                tnorm[j][:, :], psA[:, 0:TSn], ALU.mult),
                                reads=[btn, bpA], writes=[bxn])
                        else:
                            c.emit("vector", lambda e, ci=ci:
                                   e.tensor_tensor(
                                tnorm[j][:, :], tnorm[j][:, :], psA[:, 0:TSn],
                                ALU.mult),
                                reads=[btn, bpA], writes=[btn])
                            c.emit("vector", lambda e, ci=ci: e.tensor_scalar(
                                xnb[j][:, ci * TSn:(ci + 1) * TSn],
                                tnorm[j][:, :],
                                gamma_col[:, ci:ci + 1], beta_col[:, ci:ci + 1],
                                ALU.mult, ALU.add),
                                reads=[btn, c.tb("cst")], writes=[bxn])
                    else:
                        c.emit("vector", lambda e, ci=ci:
                               e.scalar_tensor_tensor(
                            xnb[j][:, ci * TSn:(ci + 1) * TSn],
                            xa[ja][:, ci * TSn:(ci + 1) * TSn],
                            gamma_col[:, ci:ci + 1], psA[:, 0:TSn],
                            ALU.mult, ALU.mult),
                            reads=[bx, bpA, bcst], writes=[bxn])
                c.emit("scalar", lambda e: e.dma_start(
                    dst_r[:, :, t0:t0 + TSn],
                    xnb[j][:, :].rearrange("p (ci t) -> p ci t", ci=EC)),
                    reads=[bxn], writes=[c.tb(dst.name, ("sl", i))], dma=True,
                    ch=pname + f"S{j}")
                if dst_f32 is not None:
                    c.emit("gpsimd", lambda e: e.dma_start(
                        dst_f32.rearrange("(ci p) t -> p ci t", p=PD)
                        [:, :, t0:t0 + TSn],
                        xnb[j][:, :].rearrange("p (ci t) -> p ci t", ci=EC)),
                        reads=[bxn], writes=[c.tb(dst_f32.name, ("sl", i))],
                        dma=True, ch=pname + f"F{j}")
            return run
        tail = mk_tail()
    tail()



WOB = 512


def emit_proj_wload(c, sb, W, *, pname):
    din, dout = W.shape
    ncin = cdiv(din, PD)
    nob = cdiv(dout, WOB)
    wsb = sb.alloc([PD, ncin * dout], BF16)
    stg = [sb.alloc([PD, WOB], F32) for _ in range(4)]
    bws = [[c.tb(pname + "w", (ci, ob)) for ob in range(nob)]
           for ci in range(ncin)]
    k = 0
    for ob in range(nob):
        o0, o1 = ob * WOB, min((ob + 1) * WOB, dout)
        for ci in range(ncin):
            r0, r1 = ci * PD, min((ci + 1) * PD, din)
            j = k % 4
            k += 1
            bst = c.tb(pname + "wst", j)
            c.emit("sync", lambda e, j=j, r0=r0, r1=r1, o0=o0, o1=o1:
                   e.dma_start(stg[j][0:r1 - r0, 0:o1 - o0], W[r0:r1, o0:o1]),
                reads=[c.tb(W.name)], writes=[bst], dma=True,
                ch=pname + f"W{j % 2}")
            c.emit("vector", lambda e, j=j, ci=ci, r0=r0, r1=r1, o0=o0, o1=o1:
                   e.tensor_copy(wsb[0:r1 - r0, ci * dout + o0:ci * dout + o1],
                                 stg[j][0:r1 - r0, 0:o1 - o0]),
                reads=[bst], writes=[bws[ci][ob]])
    return wsb, bws


def emit_proj(c, sb, cst, P, xn, W, out, *, T, TSn, bias_col=None, act=None,
              out_dt=BF16, res=None, res_out=None, mask_sb=None, mask_tb=None,
              banks=(0, 1, 2, 3), pname="pj", shift_half=None, res2=None,
              also_bf=None, wpre=None):
    din, dout = W.shape
    ncin = cdiv(din, PD)
    nco = cdiv(dout, PD)
    if wpre is None:
        wsb, bws = emit_proj_wload(c, sb, W, pname=pname)
    else:
        wsb, bws = wpre
    NXH = 2 if ncin * dout * 2 > 48 * 1024 else 3
    xh = [sb.alloc([PD, ncin * TSn], BF16) for _ in range(NXH)]
    ot = [sb.alloc([PD, TSn], out_dt) for _ in range(4)]
    rt = [sb.alloc([PD, TSn], F32) for _ in range(2)]
    t1 = [sb.alloc([PD, TSn], F32) for _ in range(2)]
    nslab = T // TSn
    nbk = len(banks)
    k = 0
    for s in range(nslab):
        t0 = s * TSn
        js = s % NXH
        bxh = c.tb(pname + "xh", js)
        for ci in range(ncin):
            r0, r1 = ci * PD, min((ci + 1) * PD, din)
            shifted = shift_half is not None and ci >= ncin // 2
            if not shifted:
                c.emit("sync", lambda e, ci=ci, r0=r0, r1=r1, js=js, t0=t0:
                       e.dma_start(xh[js][0:r1 - r0, ci * TSn:(ci + 1) * TSn],
                                   xn[r0:r1, t0:t0 + TSn]),
                    reads=[c.tb(xn.name)], writes=[bxh], dma=True,
                    ch=pname + f"X{js}{ci % 2}")
            elif t0 % shift_half == 0:
                c.emit("sync", lambda e, ci=ci, r0=r0, r1=r1, js=js, t0=t0:
                       e.dma_start(
                    xh[js][0:r1 - r0, ci * TSn + 1:(ci + 1) * TSn],
                    xn[r0:r1, t0:t0 + TSn - 1]),
                    reads=[c.tb(xn.name)], writes=[bxh], dma=True,
                    ch=pname + f"X{js}{ci % 2}")
                c.emit("vector", lambda e, ci=ci, r0=r0, r1=r1, js=js: e.memset(
                    xh[js][0:r1 - r0, ci * TSn:ci * TSn + 1], 0.0),
                    reads=[], writes=[bxh])
            else:
                c.emit("sync", lambda e, ci=ci, r0=r0, r1=r1, js=js, t0=t0:
                       e.dma_start(xh[js][0:r1 - r0, ci * TSn:(ci + 1) * TSn],
                                   xn[r0:r1, t0 - 1:t0 + TSn - 1]),
                    reads=[c.tb(xn.name)], writes=[bxh], dma=True,
                    ch=pname + f"X{js}{ci % 2}")
        for oc in range(nco):
            o0, o1 = oc * PD, min((oc + 1) * PD, dout)
            mc = o1 - o0
            bk = banks[k % nbk]
            jr = k % 2
            ko = k % 4
            k += 1
            ps = P[bk]
            bps = c.tb(f"psum{bk}")
            mms = [
                (lambda e, ci=ci, js=js, first=(ci == 0), last=(ci == ncin - 1),
                 mc=mc, o0=o0, ps=ps: e.matmul(
                    ps[0:mc, 0:TSn],
                    wsb[0:min(PD, din - ci * PD), ci * dout + o0:ci * dout + o0 + mc],
                    xh[js][0:min(PD, din - ci * PD), ci * TSn:(ci + 1) * TSn],
                    start=first, stop=last))
                for ci in range(ncin)]
            c.emit("tensor", mms,
                   reads=[bws[ci][o0 // WOB] for ci in range(ncin)] + [bxh],
                   writes=[bps])
            fn = act if act is not None else AF.Identity
            bias_ap = bias_col[0:mc, oc:oc + 1] if bias_col is not None else 0.0
            if res is None and mask_sb is None:
                bo = c.tb(pname + "ot", ko)
                c.emit("scalar", lambda e, ps=ps, mc=mc, fn=fn, bias_ap=bias_ap,
                       ko=ko: e.activation(
                    ot[ko][0:mc, :], ps[0:mc, 0:TSn], fn, bias=bias_ap),
                    reads=[bps, c.tb("cst")], writes=[bo])
                c.emit("scalar", lambda e, ko=ko, mc=mc, o0=o0, o1=o1, t0=t0:
                       e.dma_start(out[o0:o1, t0:t0 + TSn], ot[ko][0:mc, :]),
                    reads=[bo], writes=[c.tb(out.name, ("t", s, oc))], dma=True,
                    ch=pname + f"S{ko}")
            else:
                bt1 = c.tb(pname + "t1", jr)
                c.emit("scalar", lambda e, ps=ps, mc=mc, fn=fn, bias_ap=bias_ap,
                       jr=jr: e.activation(
                    t1[jr][0:mc, :], ps[0:mc, 0:TSn], fn, bias=bias_ap),
                    reads=[bps, c.tb("cst")], writes=[bt1])
                if mask_sb is not None:
                    c.emit("vector", lambda e, jr=jr, mc=mc, t0=t0: e.tensor_tensor(
                        t1[jr][0:mc, :], t1[jr][0:mc, :], mask_sb[0:mc, t0:t0 + TSn],
                        ALU.mult),
                        reads=[bt1, mask_tb], writes=[bt1])
                if res is not None:
                    brt = c.tb(pname + "rt", jr)
                    c.emit("sync", lambda e, jr=jr, o0=o0, o1=o1, t0=t0: e.dma_start(
                        rt[jr][0:o1 - o0, :], res[o0:o1, t0:t0 + TSn]),
                        reads=[c.tb(res.name)], writes=[brt], dma=True,
                        ch=pname + f"R{jr}")
                    bo2 = c.tb(pname + "o2", ko)
                    c.emit("vector", lambda e, jr=jr, mc=mc, ko=ko: e.tensor_tensor(
                        ot[ko][0:mc, :], t1[jr][0:mc, :], rt[jr][0:mc, :], ALU.add),
                        reads=[bt1, brt], writes=[bo2])
                    dst = res_out if res_out is not None else out
                    c.emit("scalar", lambda e, ko=ko, mc=mc, o0=o0, o1=o1, t0=t0:
                           e.dma_start(dst[o0:o1, t0:t0 + TSn], ot[ko][0:mc, :]),
                        reads=[bo2], writes=[c.tb(dst.name, ("t", s, oc))],
                        dma=True, ch=pname + f"S{ko}")
                    if also_bf is not None:
                        c.emit("gpsimd", lambda e, ko=ko, mc=mc, o0=o0, o1=o1,
                               t0=t0: e.dma_start(
                            also_bf[o0:o1, t0:t0 + TSn], ot[ko][0:mc, :]),
                            reads=[bo2],
                            writes=[c.tb(also_bf.name, ("t", s, oc))],
                            dma=True, ch=pname + f"B{ko % 2}")
                else:
                    c.emit("scalar", lambda e, jr=jr, mc=mc, o0=o0, o1=o1, t0=t0:
                           e.dma_start(out[o0:o1, t0:t0 + TSn], t1[jr][0:mc, :]),
                        reads=[bt1], writes=[c.tb(out.name, ("t", s, oc))],
                        dma=True, ch=pname + f"St{jr}")


def emit_vtm_proj(c, sb, cst, P, xn, W, out, *, T, TSn, bias_row=None,
                  pname="vt", banks=(0, 1), shift_half=None, wpre=None):
    din, dout = W.shape
    ncin = din // PD
    ktn = T // PD
    nno = cdiv(dout, TSn)
    if wpre is None:
        wsb, bws = emit_proj_wload(c, sb, W, pname=pname)
    else:
        wsb, bws = wpre
    brow = None
    if bias_row is not None:
        bsb = sb.alloc([1, dout], BF16)
        brow = c.tb(pname + "b")
        c.emit("gpsimd", lambda e: e.dma_start(bsb[:1, :], bias_row[:1, :]),
               reads=[c.tb(bias_row.name)], writes=[brow], dma=True,
               ch=pname + "W")
    xk = [sb.alloc([PD, ncin * PD], BF16) for _ in range(2)]
    ot = [sb.alloc([PD, dout], BF16) for _ in range(2)]
    xn_r = xn.rearrange("(ci p) t -> p ci t", p=PD)
    for kt in range(ktn):
        j = kt % 2
        bxk = c.tb(pname + "xk", j)
        bot = c.tb(pname + "ot", j)
        nl = ncin // 2 if shift_half is not None else ncin
        c.emit("sync", lambda e, kt=kt, j=j, nl=nl: e.dma_start(
            xk[j][:, 0:nl * PD].rearrange("p (ci q) -> p ci q", ci=nl),
            xn_r[:, 0:nl, kt * PD:(kt + 1) * PD]),
            reads=[c.tb(xn.name)], writes=[bxk], dma=True, ch=pname + f"X{j}")
        if shift_half is not None:
            t0 = kt * PD
            if t0 % shift_half == 0:
                c.emit("sync", lambda e, j=j, nl=nl, t0=t0: e.dma_start(
                    xk[j][:, nl * PD:].rearrange(
                        "p (ci q) -> p ci q", ci=nl)[:, :, 1:PD],
                    xn_r[:, nl:, t0:t0 + PD - 1]),
                    reads=[c.tb(xn.name)], writes=[bxk], dma=True,
                    ch=pname + f"X{j}")
                c.emit("vector", lambda e, j=j, nl=nl: e.memset(
                    xk[j][:, :].rearrange("p (ci q) -> p ci q", ci=ncin)
                    [:, nl:, 0:1], 0.0),
                    reads=[], writes=[bxk])
            else:
                c.emit("sync", lambda e, j=j, nl=nl, t0=t0: e.dma_start(
                    xk[j][:, nl * PD:].rearrange("p (ci q) -> p ci q", ci=nl),
                    xn_r[:, nl:, t0 - 1:t0 + PD - 1]),
                    reads=[c.tb(xn.name)], writes=[bxk], dma=True,
                    ch=pname + f"X{j}")
        for no in range(nno):
            n0, n1 = no * TSn, min((no + 1) * TSn, dout)
            bk = banks[(kt * nno + no) % len(banks)]
            ps = P[bk]
            bps = c.tb(f"psum{bk}")
            mms = [
                (lambda e, ci=ci, j=j, n0=n0, n1=n1, ps=ps, first=(ci == 0):
                 e.matmul(
                    ps[:, 0:n1 - n0], xk[j][:, ci * PD:(ci + 1) * PD],
                    wsb[:, ci * dout + n0:ci * dout + n1],
                    start=first, stop=(bias_row is None and ci == ncin - 1)))
                for ci in range(ncin)]
            if bias_row is not None:
                mms.append(lambda e, n0=n0, n1=n1, ps=ps: e.matmul(
                    ps[:, 0:n1 - n0], cst["ones_row_bf"][:1, 0:PD],
                    bsb[:1, n0:n1], start=False, stop=True))
            rd = ([bws[ci][n0 // WOB] for ci in range(ncin)] + [bxk]
                  + ([brow] if brow is not None else []))
            c.emit("tensor", mms, reads=rd, writes=[bps])
            c.emit("vector", lambda e, j=j, ps=ps, n0=n0, n1=n1: e.tensor_copy(
                ot[j][:, n0:n1], ps[:, 0:n1 - n0]),
                reads=[bps], writes=[bot])
        c.emit("scalar", lambda e, kt=kt, j=j: e.dma_start(
            out[kt * PD:(kt + 1) * PD, 0:dout], ot[j][:, :]),
            reads=[bot], writes=[c.tb(out.name, ("kt", kt))], dma=True,
            ch=pname + f"S{j}")



def emit_attn(c, sb, cst, P, *, qf, kf, vtm, out, Tb, nb, Hq, hdq, hdv,
              TQ, scale, Cshift, causal, decay=False, pname="at", local=False):
    HK = Hq * hdq
    KC = cdiv(HK, PD)
    Vw = Hq * (hdv + 1)
    ktn = Tb // PD
    nqb = Tb // TQ
    ksb = sb.alloc([PD, KC * Tb], BF16)
    vsb = sb.alloc([PD, ktn * Vw], BF16)
    qsb = [sb.alloc([PD, KC * TQ], BF16) for _ in range(2)]
    NPB = 4
    pbuf = [sb.alloc([PD, TQ], BF16) for _ in range(NPB)]
    sdec = [sb.alloc([PD, TQ], F32) for _ in range(2)]
    rrow = [sb.alloc([hdv + 1, TQ], F32) for _ in range(2)]
    rr2 = [sb.alloc([hdv + 1, TQ], F32) for _ in range(2)]
    rb = [sb.alloc([PD, TQ], F32) for _ in range(2)]
    obuf = [sb.alloc([PD, TQ], BF16) for _ in range(2)]
    absd_t = cst_lazy(c, sb, cst, "ABSD") if decay else None
    cmask_t = cst_lazy(c, sb, cst, "cmask") if causal else None
    lmask_t = cst_lazy(c, sb, cst, "lmask") if local else None
    pdbg = cst.get("_pdbg")
    SNM = [0, 1, 4, 7] if decay else [0, 1, 4]
    NSB = len(SNM)
    POB, PON = [P[2], P[5]], [2, 5]
    PBB, PBN = [P[3], P[6]], [3, 6]
    decT = None
    if decay:
        Wd = absd_t.shape[1]
        decT = sb.alloc([PD, Wd], F32)
        bdecT = c.tb(pname + "decT")
        for u0 in range(0, Wd, 512):
            u1 = min(u0 + 512, Wd)
            c.emit("scalar", lambda e, u0=u0, u1=u1: e.activation(
                decT[:, u0:u1], absd_t[:, u0:u1], AF.Exp,
                scale=cst["negabsdd"][:, 0:1]),
                reads=[c.tb("cst")], writes=[bdecT])
    for b in range(nb):
        tb0 = b * Tb
        bk_ = c.tb(pname + "k", b)
        bv_ = c.tb(pname + "v", b)
        for ci in range(KC):
            r0, r1 = ci * PD, min((ci + 1) * PD, HK)
            c.emit("sync", lambda e, ci=ci, r0=r0, r1=r1, tb0=tb0, ksb=ksb:
                   e.dma_start(
                ksb[0:r1 - r0, ci * Tb:(ci + 1) * Tb], kf[r0:r1, tb0:tb0 + Tb]),
                reads=[c.tb(kf.name)], writes=[bk_], dma=True,
                ch=pname + f"K{ci % 4}")
        for kt in range(ktn):
            c.emit("sync", lambda e, kt=kt, tb0=tb0, vsb=vsb: e.dma_start(
                vsb[:, kt * Vw:(kt + 1) * Vw],
                vtm[tb0 + kt * PD:tb0 + (kt + 1) * PD, 0:Vw]),
                reads=[c.tb(vtm.name)], writes=[bv_], dma=True,
                ch=pname + f"V{kt % 4}")
        pend = None
        for qb in range(nqb):
            q0 = tb0 + qb * TQ
            jq = qb % 2
            bq_ = c.tb(pname + "q", (b, jq))
            for ci in range(KC):
                r0, r1 = ci * PD, min((ci + 1) * PD, HK)
                c.emit("sync", lambda e, ci=ci, r0=r0, r1=r1, q0=q0, jq=jq:
                       e.dma_start(qsb[jq][0:r1 - r0, ci * TQ:(ci + 1) * TQ],
                                   qf[r0:r1, q0:q0 + TQ]),
                    reads=[c.tb(qf.name)], writes=[bq_], dma=True,
                    ch=pname + f"Q{jq}{ci % 2}")
            for h in range(Hq):
                kchunk, kbase = divmod(h * hdq, PD)
                nu = TQ // PD
                if local:
                    kt_beg = max(0, (qb - 1) * nu)
                    kt_lim = min(ktn, (qb + 1) * nu)
                else:
                    kt_beg = 0
                    kt_lim = ktn if not causal else min(ktn, ((qb + 1) * TQ) // PD)
                ndiag = nu if causal else 0
                kts = list(range(kt_beg, kt_lim))
                n = len(kts)
                jh = h % 2
                po = POB[jh]
                bo_ps = c.tb(pname + "pso", jh)

                def mk_S(i, jq=jq, bq_=bq_, bk_=bk_, kchunk=kchunk,
                         kbase=kbase, kts=kts):
                    kt = kts[i]
                    js = P[SNM[i % NSB]]
                    bjs = c.tb(f"psum{SNM[i % NSB]}")
                    c.emit("tensor", lambda e, kt=kt, jq=jq, js=js,
                           kchunk=kchunk, kbase=kbase: e.matmul(
                        js[:, 0:TQ],
                        ksb[kbase:kbase + hdq,
                            kchunk * Tb + kt * PD:kchunk * Tb + (kt + 1) * PD],
                        qsb[jq][kbase:kbase + hdq,
                                kchunk * TQ:(kchunk + 1) * TQ],
                        start=True, stop=True, tile_position=(kbase, 0)),
                        reads=[bk_, bq_], writes=[bjs])

                for i in range(n):
                    kt = kts[i]
                    if i == 0:
                        mk_S(0)
                        if n > 1:
                            mk_S(1)
                    jp = i % NPB
                    js = P[SNM[i % NSB]]
                    bjs = c.tb(f"psum{SNM[i % NSB]}")
                    bpb = c.tb(pname + "pb", jp)
                    is_diag = causal and kt >= kt_lim - ndiag
                    if decay:
                        jd = i % 2
                        bsd = c.tb(pname + "sd", jd)
                        s0 = (Tb - PD) - (kt * PD - qb * TQ)
                        c.emit("vector", lambda e, js=js, jd=jd, s0=s0:
                               e.scalar_tensor_tensor(
                            sdec[jd][:, :], js[:, 0:TQ], scale,
                            decT[:, s0:s0 + TQ], ALU.mult, ALU.mult),
                            reads=[bjs, bdecT], writes=[bsd])
                        c.emit("scalar", lambda e, jd=jd, jp=jp: e.activation(
                            pbuf[jp][:, :], sdec[jd][:, :], AF.Exp,
                            bias=np.float32(-Cshift)),
                            reads=[bsd], writes=[bpb])
                    else:
                        c.emit("scalar", lambda e, js=js, jp=jp: e.activation(
                            pbuf[jp][:, :], js[:, 0:TQ], AF.Exp,
                            bias=np.float32(-Cshift), scale=scale),
                            reads=[bjs], writes=[bpb])
                    if local:
                        u = nu * qb - kt
                        mi = u + nu - 1
                        c.emit("vector", lambda e, jp=jp, mi=mi: e.tensor_tensor(
                            pbuf[jp][:, :], pbuf[jp][:, :],
                            lmask_t[:, mi * TQ:(mi + 1) * TQ], ALU.mult),
                            reads=[bpb, c.tb("cst")], writes=[bpb])
                    elif is_diag:
                        rel = kt - (qb * TQ) // PD
                        c.emit("vector", lambda e, jp=jp, rel=rel: e.tensor_tensor(
                            pbuf[jp][:, :], pbuf[jp][:, :],
                            cmask_t[:, rel * TQ:(rel + 1) * TQ], ALU.mult),
                            reads=[bpb, c.tb("cst")], writes=[bpb])
                    if i + 2 < n:
                        mk_S(i + 2)
                    c.emit("tensor", lambda e, kt=kt, jp=jp, h=h, po=po,
                           first=(i == 0), last=(i == n - 1):
                           e.matmul(
                        po[0:hdv + 1, 0:TQ],
                        vsb[:, kt * Vw + h * (hdv + 1):kt * Vw + (h + 1) * (hdv + 1)],
                        pbuf[jp][:, :], start=first, stop=last),
                        reads=[bv_, bpb], writes=[bo_ps])
                    if pdbg is not None and b == 0 and qb <= 1 and h == 0:
                        c.emit("scalar", lambda e, kt=kt, jp=jp, qb=qb:
                               e.dma_start(
                            pdbg[kt * PD:(kt + 1) * PD,
                                 qb * TQ:qb * TQ + TQ], pbuf[jp][:, :]),
                            reads=[bpb], writes=[c.tb("s_pdbg", (qb, kt))],
                            dma=True, ch=pname + "PD")
                    if i == 1 and pend is not None:
                        pend()
                        pend = None
                if pend is not None:
                    pend()
                    pend = None

                def mk_tail(jh=jh, po=po, bo_ps=bo_ps, h=h, q0=q0, b=b, qb=qb):
                    def tail():
                        brr = c.tb(pname + "rr", jh)
                        c.emit("vector", lambda e: e.tensor_scalar(
                            rrow[jh][0:hdv + 1, :], po[0:hdv + 1, 0:TQ], 1e-30,
                            None, ALU.max),
                            reads=[bo_ps], writes=[brr])
                        if pdbg is not None and b == 0 and qb == 0 and h == 0:
                            c.emit("gpsimd", lambda e: e.dma_start(
                                pdbg[1024:1089, 0:TQ], rrow[jh][0:hdv + 1, :]),
                                reads=[brr], writes=[c.tb("s_pdbg", "rr")],
                                dma=True, ch=pname + "PR")
                        brr2 = c.tb(pname + "rr2", jh)
                        c.emit("scalar", lambda e: e.activation(
                            rr2[jh][0:hdv + 1, :], rrow[jh][0:hdv + 1, :],
                            AF.Ln),
                            reads=[brr], writes=[brr2])
                        c.emit("scalar", lambda e: e.activation(
                            rr2[jh][0:hdv + 1, :], rr2[jh][0:hdv + 1, :],
                            AF.Exp, scale=-1.0),
                            reads=[brr2], writes=[brr2])
                        bps3 = c.tb(f"psum{PBN[jh]}")
                        c.emit("tensor", lambda e: e.matmul(
                            PBB[jh][0:hdv, 0:TQ], cst["sel64"][0:hdv + 1, 0:hdv],
                            rr2[jh][0:hdv + 1, :], start=True, stop=True),
                            reads=[brr2, c.tb("cst")], writes=[bps3])
                        brb = c.tb(pname + "rb", jh)
                        c.emit("scalar", lambda e: e.copy(
                            rb[jh][0:hdv, :], PBB[jh][0:hdv, 0:TQ]),
                            reads=[bps3], writes=[brb])
                        bob = c.tb(pname + "ob", jh)
                        c.emit("vector", lambda e: e.tensor_tensor(
                            obuf[jh][0:hdv, :], po[0:hdv, 0:TQ], rb[jh][0:hdv, :],
                            ALU.mult),
                            reads=[bo_ps, brb], writes=[bob])
                        c.emit("scalar", lambda e: e.dma_start(
                            out[h * hdv:(h + 1) * hdv, q0:q0 + TQ],
                            obuf[jh][0:hdv, :]),
                            reads=[bob], writes=[c.tb(out.name, ("o", b, qb, h))],
                            dma=True, ch=pname + f"S{jh}")
                    return tail
                pend = mk_tail()
        if pend is not None:
            pend()
            pend = None



def emit_negabs_scalar(c, sb, cst, P, src_dram, name):
    t = sb.alloc([1, 1], F32, name="nab_t")
    col = sb.alloc([PD, 1], F32, name="nab_c")
    bt = c.tb("nab", name)
    c.emit("sync", lambda e: e.dma_start(t[:1, :1], src_dram[:1, :1]),
           reads=[c.tb(src_dram.name)], writes=[bt], dma=True, ch="NA")
    c.emit("scalar", lambda e: e.activation(t[:1, :1], t[:1, :1], AF.Abs),
           reads=[bt], writes=[bt])
    c.emit("vector", lambda e: e.tensor_scalar(t[:1, :1], t[:1, :1], -1.0, None,
                                               ALU.mult),
           reads=[bt], writes=[bt])
    bp = c.tb("psum7")
    c.emit("tensor", lambda e: e.matmul(P[7][:, 0:1], cst["ones_row_f"][:1, :],
                                        t[:1, :1], start=True, stop=True),
           reads=[bt, c.tb("cst")], writes=[bp])
    c.emit("scalar", lambda e: e.copy(col[:, 0:1], P[7][:, 0:1]),
           reads=[bp], writes=[c.tb("cst")])
    cst[name] = col



def sl_l0_attn_a(c, P, cst, cfg, w, scr, res_in, res_out):
    nc = c.nc
    E, T, Tb, nb, TSn = cfg["E"], cfg["T"], cfg["Tb"], cfg["nb"], cfg["TS"]
    H = cfg["l0a_H"]
    hd = E // H
    sbk = SB(nc)
    sb = SB(nc)
    emit_negabs_scalar(c, sbk, cst, P, w["l0a_dd"], "negabsdd")
    xn = scr["xn"]
    emit_prenorm(c, sb, cst, P, res_in, xn, T, TSn, mode="ln", pname="aaln")
    c.fence(xn.name)
    sb = c.phase(sb)
    emit_proj(c, sb, cst, P, xn, w["l0a_wq"], scr["qf"], T=T, TSn=TSn,
              bias_col=cst["l0a_bq"], pname="aaq")
    sb = c.phase(sb)
    emit_proj(c, sb, cst, P, xn, w["l0a_wk"], scr["kf"], T=T, TSn=TSn,
              bias_col=cst["l0a_bk"], pname="aak")
    sb = c.phase(sb)
    emit_vtm_proj(c, sb, cst, P, xn, w["l0a_wve"], scr["vtm"], T=T, TSn=TSn,
                  bias_row=w["l0a_bve"], pname="aav")
    sb = c.phase(sb)
    TQ = min(512, Tb)
    emit_attn(c, sb, cst, P, qf=scr["qf"], kf=scr["kf"], vtm=scr["vtm"],
              out=scr["ao"], Tb=Tb, nb=nb, Hq=H, hdq=hd, hdv=hd, TQ=TQ,
              scale=1.0 / math.sqrt(hd), Cshift=cfg["C"], causal=False,
              decay=True, pname="aat")
    sb = c.phase(sb)
    emit_proj(c, sb, cst, P, scr["ao"], w["l0a_wo"], res_out, T=T, TSn=TSn,
              bias_col=cst["l0a_bo"], res=res_in, res_out=res_out,
              out_dt=F32, pname="aao")
    c.fence(res_out.name)
    c.global_barrier()
    sb.close()
    sbk.close()


def emit_rope_evens(c, sb, cst, src, dst, *, T, rows, coff, pname):
    RC = rows // PD
    rc_t = cst_lazy(c, sb, cst, "ropecos")
    rs_t = cst_lazy(c, sb, cst, "ropesin")
    TSn = min(512, T)
    a = [sb.alloc([PD, TSn], F32) for _ in range(2)]
    b_ = [sb.alloc([PD, TSn], F32) for _ in range(2)]
    t1 = [sb.alloc([PD, TSn], F32) for _ in range(2)]
    t2 = [sb.alloc([PD, TSn], F32) for _ in range(2)]
    ob = [sb.alloc([PD, TSn], BF16) for _ in range(2)]
    k = 0
    for s in range(T // TSn):
        t0 = s * TSn
        for ci in range(RC):
            j = k % 2
            k += 1
            ba = c.tb(pname + "a", j)
            bb = c.tb(pname + "b", j)
            bt1 = c.tb(pname + "t1", j)
            bt2 = c.tb(pname + "t2", j)
            bo = c.tb(pname + "o", j)
            r0 = coff * rows + ci * PD
            r1 = (coff + 1) * rows + ci * PD
            c.emit("sync", lambda e, j=j, r0=r0, t0=t0: e.dma_start(
                a[j][:, :], src[r0:r0 + PD, t0:t0 + TSn]),
                reads=[c.tb(src.name)], writes=[ba], dma=True,
                ch=pname + f"A{j}")
            c.emit("sync", lambda e, j=j, r1=r1, t0=t0: e.dma_start(
                b_[j][:, :], src[r1:r1 + PD, t0:t0 + TSn]),
                reads=[c.tb(src.name)], writes=[bb], dma=True,
                ch=pname + f"B{j}")
            c.emit("vector", lambda e, j=j, t0=t0: e.tensor_tensor(
                t1[j][:, :], a[j][:, :], rc_t[:, t0:t0 + TSn], ALU.mult),
                reads=[ba, c.tb("cst")], writes=[bt1])
            c.emit("vector", lambda e, j=j, t0=t0: e.tensor_tensor(
                t2[j][:, :], b_[j][:, :], rs_t[:, t0:t0 + TSn], ALU.mult),
                reads=[bb, c.tb("cst")], writes=[bt2])
            c.emit("vector", lambda e, j=j: e.tensor_tensor(
                ob[j][:, :], t1[j][:, :], t2[j][:, :], ALU.subtract),
                reads=[bt1, bt2], writes=[bo])
            c.emit("scalar", lambda e, j=j, ci=ci, t0=t0: e.dma_start(
                dst[ci * PD:(ci + 1) * PD, t0:t0 + TSn], ob[j][:, :]),
                reads=[bo], writes=[c.tb(dst.name, (s, ci))], dma=True,
                ch=pname + f"S{j}")


def sl_l1_attn_a(c, P, cst, cfg, w, scr, res_in, res_out):
    nc = c.nc
    E, T, Tb, nb, TSn = cfg["E"], cfg["T"], cfg["Tb"], cfg["nb"], cfg["TS"]
    H = E // 64
    hd = 64
    sb = SB(nc)
    xn = scr["xn"]
    emit_prenorm(c, sb, cst, P, res_in, xn, T, TSn, mode="ln", pname="baln")
    c.fence(xn.name)
    sb = c.phase(sb)
    emit_proj(c, sb, cst, P, xn, w["l1a_wq"], scr["qf"], T=T, TSn=TSn,
              pname="baq")
    sb = c.phase(sb)
    emit_proj(c, sb, cst, P, xn, w["l1a_wk"], scr["kf"], T=T, TSn=TSn,
              pname="bak")
    sb = c.phase(sb)
    emit_vtm_proj(c, sb, cst, P, xn, w["l1a_wve"], scr["vtm"], T=T, TSn=TSn,
                  bias_row=w["l1a_bve"], pname="bav")
    sb = c.phase(sb)
    TQ = min(512, Tb)
    emit_attn(c, sb, cst, P, qf=scr["qf"], kf=scr["kf"], vtm=scr["vtm"],
              out=scr["ao"], Tb=Tb, nb=nb, Hq=H, hdq=hd, hdv=hd, TQ=TQ,
              scale=1.0 / math.sqrt(hd), Cshift=cfg["C"], causal=True,
              decay=False, pname="bat")
    sb = c.phase(sb)
    emit_proj(c, sb, cst, P, scr["ao"], w["l1a_wo"], res_out, T=T, TSn=TSn,
              res=res_in, res_out=res_out, out_dt=F32, pname="bao")
    c.fence(res_out.name)
    c.global_barrier()
    sb.close()


def sl_l0_attn_b(c, P, cst, cfg, w, scr, res_in, res_out):
    nc = c.nc
    E, T, Tb, nb, TSn = cfg["E"], cfg["T"], cfg["Tb"], cfg["nb"], cfg["TS"]
    H = E // 64
    hdq, hdv = 32, 64
    sb = SB(nc)
    xn = scr["xn"]
    emit_prenorm(c, sb, cst, P, res_in, xn, T, TSn, mode="ln", pname="cbln")
    c.fence(xn.name)
    sb = c.phase(sb)
    emit_proj(c, sb, cst, P, xn, w["l0b_weo"], scr["qkro"], T=T, TSn=TSn,
              bias_col=cst["l0b_beo"], out_dt=F32, pname="cbp")
    sb = c.phase(sb)
    emit_rope_evens(c, sb, cst, scr["qkro"], scr["qf"], T=T, rows=E // 2,
                    coff=0, pname="cbrq")
    emit_rope_evens(c, sb, cst, scr["qkro"], scr["kf"], T=T, rows=E // 2,
                    coff=2, pname="cbrk")
    sb = c.phase(sb)
    emit_vtm_proj(c, sb, cst, P, xn, w["l0b_wve"], scr["vtm"], T=T, TSn=TSn,
                  bias_row=w["l0b_bve"], pname="cbv")
    sb = c.phase(sb)
    TQ = min(512, Tb)
    emit_attn(c, sb, cst, P, qf=scr["qf"], kf=scr["kf"], vtm=scr["vtm"],
              out=scr["ao"], Tb=Tb, nb=nb, Hq=H, hdq=hdq, hdv=hdv, TQ=TQ,
              scale=1.0 / math.sqrt(hdv), Cshift=cfg["C"], causal=True,
              decay=False, pname="cbt")
    sb = c.phase(sb)
    emit_proj(c, sb, cst, P, scr["ao"], w["l0b_wo"], res_out, T=T, TSn=TSn,
              bias_col=cst["l0b_bo"], res=res_in, res_out=res_out,
              out_dt=F32, pname="cbo")
    c.fence(res_out.name)
    c.global_barrier()
    sb.close()


def emit_routing(c, sb, cst, P, srow, mask_sb, cfg, pname="rt"):
    T, Tb, nb = cfg["T"], cfg["Tb"], cfg["nb"]
    CC = Tb // PD
    NC = nb * CC
    KH = float(cfg["nheavy"])
    logk = float(np.log(cfg["keff"]))
    assert NC <= 32 and Tb % 512 == 0 or True
    sr = sb.alloc([1, T], F32)
    s_tm = sb.alloc([PD, NC], F32)
    b_tm = sb.alloc([PD, NC], F32)
    t1 = sb.alloc([PD, NC], F32)
    e_tm = sb.alloc([PD, NC], F32)
    fold = sb.alloc([PD, 2], F32)
    foldB = sb.alloc([32, 32], F32)
    vt32 = sb.alloc([32, 32], F32)
    rows = sb.alloc([1, 16], F32)
    mcol = sb.alloc([PD, 1], F32)
    wrow = sb.alloc([1, T], F32)
    wb = sb.alloc([PD, Tb], F32)
    gtmp = sb.alloc([PD, Tb], F32)
    g_tm = sb.alloc([PD, NC], F32)
    pos_tm = sb.alloc([PD, NC], F32)
    m1 = sb.alloc([PD, NC], F32)
    cp = sb.alloc([1, 2 * NC], F32)
    mrow = sb.alloc([1, T], F32)
    b_ = {k: c.tb(pname + k) for k in
          ("sr", "stm", "btm", "t1", "etm", "fold", "foldB", "vt", "rows",
           "mcol", "wrow", "wb", "g", "pos", "m1", "cp", "mrow", "msb")}
    bcst = c.tb("cst")
    M = rows[:1, 0:1]
    NEGM = rows[:1, 1:2]
    LNS = rows[:1, 2:2 + nb]
    AROW = rows[:1, 8:8 + nb]
    TMPR = rows[:1, 12:12 + nb]
    c.emit("sync", lambda e: e.dma_start(sr[:1, :], srow[:1, :]),
           reads=[c.tb(srow.name)], writes=[b_["sr"]], dma=True, ch=pname + "L")
    c.emit("vector", lambda e: e.memset(foldB[:, :], 0.0),
           reads=[], writes=[b_["foldB"]])
    bps3 = c.tb("psum3")
    for col in range(NC):
        c.emit("tensor", lambda e, col=col: e.transpose(
            P[3][:, col:col + 1], sr[:1, col * PD:(col + 1) * PD],
            cst["ident_f"][0:1, 0:1]),
            reads=[b_["sr"], bcst], writes=[bps3])
    c.emit("vector", lambda e: e.tensor_copy(s_tm[:, :], P[3][:, 0:NC]),
           reads=[bps3], writes=[b_["stm"]])
    c.emit("vector", lambda e: e.tensor_scalar(
        b_tm[:, :], s_tm[:, :], -1.0, None, ALU.mult),
        reads=[b_["stm"]], writes=[b_["btm"]])
    apb = sb.alloc([PD, nb], F32)
    c.emit("vector", lambda e: e.memset(apb[:, :], 0.0),
           reads=[], writes=[b_["mcol"]])
    c.emit("vector", lambda e: e.memset(rows[:1, :], 0.0),
           reads=[], writes=[b_["rows"]])
    for it in range(50):
        c.emit("vector", lambda e: e.tensor_tensor(
            t1[:, :], s_tm[:, :], b_tm[:, :], ALU.add),
            reads=[b_["stm"], b_["btm"]], writes=[b_["t1"]])
        c.emit("vector", lambda e: e.tensor_tensor(
            e_tm[:, :].rearrange("p (b cc) -> p b cc", b=nb),
            t1[:, :].rearrange("p (b cc) -> p b cc", b=nb),
            apb[:, :].unsqueeze(2).broadcast_to([PD, nb, CC]), ALU.add),
            reads=[b_["t1"], b_["mcol"]], writes=[b_["etm"]])
        c.emit("scalar", lambda e: e.activation(
            e_tm[:, :], e_tm[:, :], AF.Exp),
            reads=[b_["etm"]], writes=[b_["etm"]])
        bps4 = c.tb("psum4")
        c.emit("tensor", lambda e: e.matmul(
            P[4][:1, 0:NC], cst["ones_col_f"][:, 0:1], e_tm[:, :],
            start=True, stop=True),
            reads=[b_["etm"], bcst], writes=[bps4])
        c.emit("vector", lambda e: e.tensor_reduce(
            LNS, P[4][:1, 0:NC].rearrange("p (b cc) -> p b cc", b=nb),
            AX.X, ALU.add),
            reads=[bps4], writes=[b_["rows"]])
        c.emit("scalar", lambda e: e.activation(LNS, LNS, AF.Ln),
            reads=[b_["rows"]], writes=[b_["rows"]])
        c.emit("vector", lambda e: e.tensor_tensor(
            TMPR, LNS, AROW, ALU.subtract),
            reads=[b_["rows"]], writes=[b_["rows"]])
        c.emit("vector", lambda e: e.tensor_scalar(
            AROW, TMPR, -1.0, float(logk), ALU.mult, ALU.add),
            reads=[b_["rows"]], writes=[b_["rows"]])
        bps2b = c.tb("psum2")
        c.emit("tensor", lambda e: e.matmul(
            P[2][:, 1:1 + nb], cst["ones_row_f"][:1, :], AROW, start=True,
            stop=True),
            reads=[b_["rows"], bcst], writes=[bps2b])
        c.emit("scalar", lambda e: e.copy(apb[:, :], P[2][:, 1:1 + nb]),
            reads=[bps2b], writes=[b_["mcol"]])
        c.emit("vector", lambda e: e.tensor_tensor(
            t1[:, :].rearrange("p (b cc) -> p b cc", b=nb),
            s_tm[:, :].rearrange("p (b cc) -> p b cc", b=nb),
            apb[:, :].unsqueeze(2).broadcast_to([PD, nb, CC]),
            ALU.add),
            reads=[b_["stm"], b_["mcol"]], writes=[b_["t1"]])
        if it < 49:
            c.emit("vector", lambda e: e.tensor_scalar(
                b_tm[:, :], t1[:, :], -1.0, 0.0, ALU.mult, ALU.min),
                reads=[b_["t1"]], writes=[b_["btm"]])
    c.emit("vector", lambda e: e.tensor_scalar(
        pos_tm[:, :], t1[:, :], 0.0, None, ALU.is_ge),
        reads=[b_["t1"]], writes=[b_["pos"]])
    bps5 = c.tb("psum5")
    c.emit("tensor", lambda e: e.matmul(
        P[5][:, 0:NC], cst["tri_lt"][:, :], pos_tm[:, :], start=True,
        stop=False),
        reads=[b_["pos"], bcst], writes=[bps5])
    bps4b = c.tb("psum4")
    c.emit("tensor", lambda e: e.matmul(
        P[4][:1, 32:32 + NC], cst["ones_col_f"][:, 0:1], pos_tm[:, :],
        start=True, stop=True),
        reads=[b_["pos"], bcst], writes=[bps4b])
    c.emit("vector", lambda e: e.tensor_copy(cp[:1, 0:NC],
                                             P[4][:1, 32:32 + NC]),
        reads=[bps4b], writes=[b_["cp"]])
    sh = 1
    while sh < CC:
        c.emit("vector", lambda e, sh=sh: e.tensor_tensor(
            cp[:1, 0:NC].rearrange("p (b cc) -> p b cc", b=nb)[:, :, sh:CC],
            cp[:1, 0:NC].rearrange("p (b cc) -> p b cc", b=nb)[:, :, sh:CC],
            cp[:1, 0:NC].rearrange("p (b cc) -> p b cc", b=nb)[:, :, 0:CC - sh],
            ALU.add),
            reads=[b_["cp"]], writes=[b_["cp"]])
        sh *= 2
    c.emit("vector", lambda e: e.tensor_tensor(
        cp[:1, NC:2 * NC], cp[:1, 0:NC], P[4][:1, 32:32 + NC], ALU.subtract),
        reads=[b_["cp"], bps4b], writes=[b_["cp"]])
    c.emit("tensor", lambda e: e.matmul(
        P[5][:, 0:NC], cst["ones_row_f"][:1, :], cp[:1, NC:2 * NC],
        start=False, stop=True),
        reads=[b_["cp"], bcst], writes=[bps5])
    c.emit("vector", lambda e: e.tensor_scalar(
        m1[:, :], P[5][:, 0:NC], KH, None, ALU.is_lt),
        reads=[bps5], writes=[b_["m1"]])
    c.emit("vector", lambda e: e.tensor_tensor(
        m1[:, :], m1[:, :], pos_tm[:, :], ALU.mult),
        reads=[b_["m1"], b_["pos"]], writes=[b_["m1"]])
    for b in range(nb):
        c.emit("vector", lambda e, b=b: e.tensor_tensor(
            wrow[:1, b * Tb:(b + 1) * Tb], sr[:1, b * Tb:(b + 1) * Tb],
            AROW[:1, b:b + 1].broadcast_to([1, Tb]), ALU.add),
            reads=[b_["sr"], b_["rows"]], writes=[b_["wrow"]])
    for b in range(nb):
        bps6 = c.tb("psum6")
        for q in range(cdiv(Tb, 512)):
            wq = min(512, Tb - q * 512)
            c.emit("tensor", lambda e, b=b, q=q, wq=wq: e.matmul(
                P[6][:, 0:wq], cst["ones_row_f"][:1, :],
                wrow[:1, b * Tb + q * 512:b * Tb + q * 512 + wq],
                start=True, stop=True),
                reads=[b_["wrow"], bcst], writes=[bps6])
            c.emit("vector", lambda e, q=q, wq=wq: e.tensor_copy(
                wb[:, q * 512:q * 512 + wq], P[6][:, 0:wq]),
                reads=[bps6], writes=[b_["wb"]])
        for cc in range(CC):
            col = b * CC + cc
            c.emit("vector", lambda e, col=col: e.tensor_scalar(
                gtmp[:, 0:Tb], wb[:, 0:Tb], t1[:, col:col + 1], 0.0,
                ALU.is_gt, ALU.add, accum_out=g_tm[:, col:col + 1]),
                reads=[b_["wb"], b_["t1"]], writes=[b_["g"]])
    c.emit("vector", lambda e: e.tensor_scalar(
        g_tm[:, :], g_tm[:, :], KH, None, ALU.is_lt),
        reads=[b_["g"]], writes=[b_["g"]])
    c.emit("vector", lambda e: e.tensor_scalar(
        pos_tm[:, :], pos_tm[:, :], -1.0, 1.0, ALU.mult, ALU.add),
        reads=[b_["pos"]], writes=[b_["pos"]])
    c.emit("vector", lambda e: e.tensor_tensor(
        g_tm[:, :], g_tm[:, :], pos_tm[:, :], ALU.mult),
        reads=[b_["g"], b_["pos"]], writes=[b_["g"]])
    c.emit("vector", lambda e: e.tensor_tensor(
        m1[:, :], m1[:, :], g_tm[:, :], ALU.add),
        reads=[b_["m1"], b_["g"]], writes=[b_["m1"]])
    bps2c = c.tb("psum2")
    for col in range(NC):
        c.emit("tensor", lambda e, col=col: e.transpose(
            P[2][0:1, 0:PD], m1[:, col:col + 1], cst["ident_f"][:, :]),
            reads=[b_["m1"], bcst], writes=[bps2c])
        c.emit("vector", lambda e, col=col: e.tensor_copy(
            mrow[:1, col * PD:(col + 1) * PD], P[2][0:1, 0:PD]),
            reads=[bps2c], writes=[b_["mrow"]])
    bmsb = b_["msb"]
    for q in range(cdiv(T, 512)):
        wq = min(512, T - q * 512)
        bps7 = c.tb("psum7")
        c.emit("tensor", lambda e, q=q, wq=wq: e.matmul(
            P[7][:, 0:wq], cst["ones_row_f"][:1, :],
            mrow[:1, q * 512:q * 512 + wq], start=True, stop=True),
            reads=[b_["mrow"], bcst], writes=[bps7])
        c.emit("vector", lambda e, q=q, wq=wq: e.tensor_copy(
            mask_sb[:, q * 512:q * 512 + wq], P[7][:, 0:wq]),
            reads=[bps7], writes=[bmsb])
    return bmsb


def sl_l0_ffn(c, P, cst, cfg, w, scr, res_in, res_out, tag):
    nc = c.nc
    E, T, TSn = cfg["E"], cfg["T"], cfg["TS"]
    dl = E // 2
    dh = cfg["dh"]
    sbk = SB(nc)
    mask_sb = sbk.alloc([PD, T], F32)
    sb = SB(nc)
    pn = "f" + tag
    xn = scr["xn"]
    emit_prenorm(c, sb, cst, P, res_in, xn, T, TSn, mode="ln", pname=pn + "ln")
    c.fence(xn.name)
    sb = c.phase(sb)
    emit_proj(c, sb, cst, P, xn, w["l0f_rt"], scr["srow"], T=T, TSn=TSn,
              out_dt=F32, pname=pn + "s")
    c.fence(scr["srow"].name)
    emit_routing(c, sb, cst, P, scr["srow"], mask_sb, cfg, pname=pn + "r")
    mask_keep = dict(self_w=None)
    mk = c.tb("l0f_mask")
    _merge(mk.w, c.tb(pn + "rmsb").w)
    sb = c.phase(sb)
    emit_prenorm(c, sb, cst, P, xn, scr["xr"], T, TSn, mode="rms",
                 gamma_col=cst["l0f_lgam"], pname=pn + "rl", src_bf=True)
    sb = c.phase(sb)
    emit_proj(c, sb, cst, P, scr["xr"], w["l0f_lw1"], scr["h1"], T=T, TSn=TSn,
              bias_col=cst["l0f_lb1"], act=AF.Gelu, pname=pn + "l1")
    sb = c.phase(sb)
    emit_proj(c, sb, cst, P, scr["h1"], w["l0f_lw2"], scr["resM"], T=T,
              TSn=TSn, bias_col=cst["l0f_lb2"], res=res_in,
              res_out=scr["resM"], out_dt=F32, pname=pn + "l2")
    sb = c.phase(sb)
    emit_prenorm(c, sb, cst, P, xn, scr["xr2"], T, TSn, mode="rms",
                 gamma_col=cst["l0f_hgam"], pname=pn + "rh", src_bf=True)
    sb = c.phase(sb)
    emit_proj(c, sb, cst, P, scr["xr2"], w["l0f_hw1"], scr["h2"], T=T,
              TSn=TSn, bias_col=cst["l0f_hb1"], act=AF.Gelu, pname=pn + "h1")
    sb = c.phase(sb)
    emit_proj(c, sb, cst, P, scr["h2"], w["l0f_hw2"], res_out, T=T, TSn=TSn,
              bias_col=cst["l0f_hb2"], res=scr["resM"], res_out=res_out,
              mask_sb=mask_sb, mask_tb=c.tb("l0f_mask"), out_dt=F32,
              pname=pn + "h2")
    c.fence(res_out.name)
    c.global_barrier()
    sb.close()
    sbk.close()


def emit_qknorm_rope(c, sb, cst, P, src, dst, *, T, Tb, E2, sc_col, xc, xs,
                     pname, roff=0):
    EC = E2 // PD
    H = E2 // 64
    TSn = min(512, Tb)
    nslab = T // TSn
    xc_t = cst_lazy(c, sb, cst, xc)
    xs_t = cst_lazy(c, sb, cst, xs)
    xa = [sb.alloc([PD, TSn], F32) for _ in range(2)]
    sq = [sb.alloc([PD, TSn], F32) for _ in range(2)]
    rr_ = [sb.alloc([8, TSn], F32) for _ in range(2)]
    qn = [sb.alloc([PD, TSn], BF16) for _ in range(2)]
    t1 = [sb.alloc([PD, TSn], F32) for _ in range(2)]
    t2 = [sb.alloc([PD, TSn], F32) for _ in range(2)]
    ob = [sb.alloc([PD, TSn], BF16) for _ in range(2)]
    bcst = c.tb("cst")
    for i in range(nslab):
        t0 = i * TSn
        tb_off = t0 % Tb
        for ci in range(EC):
            j = (i * EC + ci) % 2
            bx = c.tb(pname + "x", j)
            bsq = c.tb(pname + "sq", j)
            brr = c.tb(pname + "rr", j)
            bqn = c.tb(pname + "qn", j)
            bt1 = c.tb(pname + "t1", j)
            bt2 = c.tb(pname + "t2", j)
            bo = c.tb(pname + "o", j)
            c.emit("sync", lambda e, j=j, ci=ci, t0=t0: e.dma_start(
                xa[j][:, :], src[roff + ci * PD:roff + (ci + 1) * PD,
                                 t0:t0 + TSn]),
                reads=[c.tb(src.name)], writes=[bx], dma=True,
                ch=pname + f"L{j}")
            c.emit("scalar", lambda e, j=j: e.square(sq[j][:, :], xa[j][:, :]),
                reads=[bx], writes=[bsq])
            bps = c.tb("psum" + str(4 + j))
            c.emit("tensor", lambda e, j=j, ci=ci: e.matmul(
                P[4 + j][0:2, 0:TSn], cst["hsel2"][:, 0:2], sq[j][:, :],
                start=True, stop=True),
                reads=[bsq, bcst], writes=[bps])
            c.emit("vector", lambda e, j=j: e.tensor_scalar(
                rr_[j][0:2, :], P[4 + j][0:2, 0:TSn], 1e-24, None, ALU.max),
                reads=[bps], writes=[brr])
            c.emit("scalar", lambda e, j=j: e.activation(
                rr_[j][0:2, :], rr_[j][0:2, :], AF.Ln),
                reads=[brr], writes=[brr])
            c.emit("scalar", lambda e, j=j: e.activation(
                rr_[j][0:2, :], rr_[j][0:2, :], AF.Exp, scale=-0.5),
                reads=[brr], writes=[brr])
            c.emit("tensor", lambda e, j=j: e.matmul(
                P[4 + j][:, 0:TSn], cst["hexp2"][0:2, :], rr_[j][0:2, :],
                start=True, stop=True),
                reads=[brr, bcst], writes=[bps])
            c.emit("vector", lambda e, j=j: e.scalar_tensor_tensor(
                qn[j][:, :], xa[j][:, :], cst[sc_col][:, 0:1],
                P[4 + j][:, 0:TSn], ALU.mult, ALU.mult),
                reads=[bx, bps, bcst], writes=[bqn])
            bps2 = c.tb("psum" + str(6 + j))
            c.emit("tensor", lambda e, j=j: e.matmul(
                P[6 + j][:, 0:TSn], cst["R128"][:, :], qn[j][:, :],
                start=True, stop=True),
                reads=[bqn, bcst], writes=[bps2])
            c.emit("vector", lambda e, j=j, tb_off=tb_off: e.tensor_tensor(
                t1[j][:, :], qn[j][:, :], xc_t[:, tb_off:tb_off + TSn],
                ALU.mult),
                reads=[bqn, bcst], writes=[bt1])
            c.emit("vector", lambda e, j=j, tb_off=tb_off: e.tensor_tensor(
                t2[j][:, :], P[6 + j][:, 0:TSn],
                xs_t[:, tb_off:tb_off + TSn], ALU.mult),
                reads=[bps2, bcst], writes=[bt2])
            c.emit("vector", lambda e, j=j: e.tensor_tensor(
                ob[j][:, :], t1[j][:, :], t2[j][:, :], ALU.add),
                reads=[bt1, bt2], writes=[bo])
            c.emit("scalar", lambda e, j=j, ci=ci, t0=t0: e.dma_start(
                dst[ci * PD:(ci + 1) * PD, t0:t0 + TSn], ob[j][:, :]),
                reads=[bo], writes=[c.tb(dst.name, (i, ci))], dma=True,
                ch=pname + f"S{j}")


def emit_geglu(c, sb, cst, P, src, dst, *, T, TSn, HID, pname):
    NCH = cdiv(HID, PD)
    at = [sb.alloc([PD, TSn], BF16) for _ in range(4)]
    gt = [sb.alloc([PD, TSn], BF16) for _ in range(4)]
    gg = [sb.alloc([PD, TSn], F32) for _ in range(4)]
    ot = [sb.alloc([PD, TSn], BF16) for _ in range(4)]
    k = 0
    for s in range(T // TSn):
        t0 = s * TSn
        for ci in range(NCH):
            r0, r1 = ci * PD, min((ci + 1) * PD, HID)
            j = k % 4
            k += 1
            ba = c.tb(pname + "a", j)
            bg = c.tb(pname + "g", j)
            bgg = c.tb(pname + "gg", j)
            bo = c.tb(pname + "o", j)
            c.emit("sync", lambda e, j=j, r0=r0, r1=r1, t0=t0: e.dma_start(
                at[j][0:r1 - r0, :], src[r0:r1, t0:t0 + TSn]),
                reads=[c.tb(src.name)], writes=[ba], dma=True,
                ch=pname + f"A{j}")
            c.emit("sync", lambda e, j=j, r0=r0, r1=r1, t0=t0: e.dma_start(
                gt[j][0:r1 - r0, :], src[HID + r0:HID + r1, t0:t0 + TSn]),
                reads=[c.tb(src.name)], writes=[bg], dma=True,
                ch=pname + f"G{j}")
            c.emit("scalar", lambda e, j=j, mc=r1 - r0: e.activation(
                gg[j][0:mc, :], gt[j][0:mc, :], AF.Gelu),
                reads=[bg], writes=[bgg])
            c.emit("vector", lambda e, j=j, mc=r1 - r0: e.tensor_tensor(
                ot[j][0:mc, :], at[j][0:mc, :], gg[j][0:mc, :], ALU.mult),
                reads=[ba, bgg], writes=[bo])
            c.emit("scalar", lambda e, j=j, r0=r0, r1=r1, t0=t0: e.dma_start(
                dst[r0:r1, t0:t0 + TSn], ot[j][0:r1 - r0, :]),
                reads=[bo], writes=[c.tb(dst.name, (s, ci))], dma=True,
                ch=pname + f"S{j}")


def emit_add_pass(c, sb, cst, P, a, b, out, *, T, TSn, pname):
    E = a.shape[0]
    EC = E // PD
    at = [sb.alloc([PD, EC * TSn], F32) for _ in range(2)]
    bt = [sb.alloc([PD, EC * TSn], F32) for _ in range(2)]
    ot = [sb.alloc([PD, EC * TSn], F32) for _ in range(2)]
    a_r = a.rearrange("(ci p) t -> p ci t", p=PD)
    b_r = b.rearrange("(ci p) t -> p ci t", p=PD)
    o_r = out.rearrange("(ci p) t -> p ci t", p=PD)
    for i in range(T // TSn):
        j = i % 2
        t0 = i * TSn
        ba = c.tb(pname + "a", j)
        bb = c.tb(pname + "b", j)
        bo = c.tb(pname + "o", j)
        c.emit("sync", lambda e, j=j, t0=t0: e.dma_start(
            at[j][:, :].rearrange("p (ci t) -> p ci t", ci=EC),
            a_r[:, :, t0:t0 + TSn]),
            reads=[c.tb(a.name)], writes=[ba], dma=True, ch=pname + f"A{j}")
        c.emit("sync", lambda e, j=j, t0=t0: e.dma_start(
            bt[j][:, :].rearrange("p (ci t) -> p ci t", ci=EC),
            b_r[:, :, t0:t0 + TSn]),
            reads=[c.tb(b.name)], writes=[bb], dma=True, ch=pname + f"B{j}")
        for ci in range(EC):
            c.emit("vector", lambda e, j=j, ci=ci: e.tensor_tensor(
                ot[j][:, ci * TSn:(ci + 1) * TSn],
                at[j][:, ci * TSn:(ci + 1) * TSn],
                bt[j][:, ci * TSn:(ci + 1) * TSn], ALU.add),
                reads=[ba, bb], writes=[bo])
        c.emit("scalar", lambda e, j=j, t0=t0: e.dma_start(
            o_r[:, :, t0:t0 + TSn],
            ot[j][:, :].rearrange("p (ci t) -> p ci t", ci=EC)),
            reads=[bo], writes=[c.tb(out.name, i)], dma=True,
            ch=pname + f"S{j}")


def sl_l1_ffn(c, P, cst, cfg, w, scr, res_in, res_out, tag):
    nc = c.nc
    E, T, Tb, TSn, nb = cfg["E"], cfg["T"], cfg["Tb"], cfg["TS"], cfg["nb"]
    INNER = E // 2
    H = INNER // 64
    HID = cfg["hid"]
    WIN = cfg["window"]
    sb = SB(nc)
    pn = "g" + tag
    xn = scr["xn"]
    emit_prenorm(c, sb, cst, P, res_in, xn, T, TSn, mode="ln",
                 pname=pn + "ln", dst_f32=scr["resM"])
    sb = c.phase(sb)
    xcur_bf, xcur_f = xn, scr["resM"]
    for L in range(cfg.get("l1f_layers", 2)):
        pl = pn + str(L)
        emit_proj(c, sb, cst, P, xcur_bf, w[f"l1f_wqk{L}"], scr["qkro"],
                  T=T, TSn=TSn, out_dt=F32, pname=pl + "qk", shift_half=Tb)
        sb = c.phase(sb)
        emit_vtm_proj(c, sb, cst, P, xcur_bf, w[f"l1f_wv{L}"], scr["vtm"],
                      T=T, TSn=TSn, bias_row=w["l1f_bve"], pname=pl + "v",
                      shift_half=Tb)
        sb = c.phase(sb)
        emit_qknorm_rope(c, sb, cst, P, scr["qkro"], scr["qf"], T=T, Tb=Tb,
                         E2=INNER, sc_col=f"l1f_qs{L}", xc="xq_c", xs="xq_s",
                         pname=pl + "rq", roff=0)
        emit_qknorm_rope(c, sb, cst, P, scr["qkro"], scr["kf"], T=T, Tb=Tb,
                         E2=INNER, sc_col=f"l1f_ks{L}", xc="xk_c", xs="xk_s",
                         pname=pl + "rk", roff=INNER)
        sb = c.phase(sb)
        if L == 0 and "pdbg" in scr:
            cst["_pdbg"] = scr["pdbg"]
        emit_attn(c, sb, cst, P, qf=scr["qf"], kf=scr["kf"], vtm=scr["vtm"],
                  out=scr["ao"], Tb=Tb, nb=nb, Hq=H, hdq=64,
                  hdv=64, TQ=WIN, scale=8.0, Cshift=cfg["C"], causal=False,
                  local=True, pname=pl + "t")
        cst.pop("_pdbg", None)
        sb = c.phase(sb)
        xmid_f = scr["resN"]
        emit_proj(c, sb, cst, P, scr["ao"], w[f"l1f_wo{L}"],
                  xmid_f, T=T, TSn=TSn, res=xcur_f, res_out=xmid_f,
                  out_dt=F32, pname=pl + "o")
        sb = c.phase(sb)
        emit_prenorm(c, sb, cst, P, xmid_f, scr["xg"], T, TSn, mode="ln",
                     gamma_col=cst[f"l1f_lg{L}"], beta_col=cst[f"l1f_lb{L}"],
                     pname=pl + "mn", shift_half=Tb)
        sb = c.phase(sb)
        emit_proj(c, sb, cst, P, scr["xg"], w[f"l1f_w1{L}"], scr["h3"],
                  T=T, TSn=TSn, bias_col=cst[f"l1f_b1{L}"], pname=pl + "w1")
        sb = c.phase(sb)
        emit_geglu(c, sb, cst, P, scr["h3"], scr["h2"], T=T, TSn=TSn,
                   HID=HID, pname=pl + "gg")
        sb = c.phase(sb)
        if L == 0:
            nxt_bf, nxt_f = scr["xr2"], scr["resO"]
        else:
            nxt_bf, nxt_f = None, scr["resP"]
        emit_proj(c, sb, cst, P, scr["h2"], w[f"l1f_w2{L}"], nxt_f,
                  T=T, TSn=TSn, bias_col=cst[f"l1f_b2{L}"], res=xmid_f,
                  res_out=nxt_f, out_dt=F32, pname=pl + "w2",
                  also_bf=nxt_bf)
        sb = c.phase(sb)
        xcur_bf, xcur_f = nxt_bf, nxt_f
    emit_add_pass(c, sb, cst, P, res_in, scr["resP"], res_out, T=T, TSn=TSn,
                  pname=pn + "fin")
    c.fence(res_out.name)
    c.global_barrier()
    sb.close()



def emit_fir_conv(c, sb, cst, P, x32, out, *, T, Tb, TSn, taps, wdiag, pname,
                  act=None, out_dt=BF16, extra_out=None, D=None):
    k = taps
    DD = D if D is not None else x32.shape[0]
    EC = DD // PD
    nslab = T // TSn
    wsb = sb.alloc([PD, EC * k * PD], BF16)
    bw = c.tb(pname + "w")
    for ci in range(EC):
        c.emit("gpsimd", lambda e, ci=ci: e.dma_start(
            wsb[:, ci * k * PD:(ci + 1) * k * PD],
            wdiag[ci * PD:(ci + 1) * PD, :]),
            reads=[c.tb(wdiag.name)], writes=[bw], dma=True, ch=pname + "W")
    xt = [sb.alloc([PD, TSn + 31], BF16) for _ in range(2)]
    ot = [sb.alloc([PD, TSn], out_dt) for _ in range(2)]
    km1 = k - 1
    for s in range(nslab):
        t0 = s * TSn
        for ci in range(EC):
            j = (s * EC + ci) % 2
            bx = c.tb(pname + "x", j)
            bo = c.tb(pname + "o", j)
            if km1 == 0:
                c.emit("sync", lambda e, j=j, ci=ci, t0=t0: e.dma_start(
                    xt[j][:, 0:TSn], x32[ci * PD:(ci + 1) * PD, t0:t0 + TSn]),
                    reads=[c.tb(x32.name)], writes=[bx], dma=True,
                    ch=pname + f"L{j}")
            elif t0 % Tb == 0:
                c.emit("vector", lambda e, j=j: e.memset(xt[j][:, 0:km1], 0.0),
                       reads=[], writes=[bx])
                c.emit("sync", lambda e, j=j, ci=ci, t0=t0: e.dma_start(
                    xt[j][:, km1:km1 + TSn], x32[ci * PD:(ci + 1) * PD,
                                                 t0:t0 + TSn]),
                    reads=[c.tb(x32.name)], writes=[bx], dma=True,
                    ch=pname + f"L{j}")
            else:
                c.emit("sync", lambda e, j=j, ci=ci, t0=t0: e.dma_start(
                    xt[j][:, 0:km1 + TSn], x32[ci * PD:(ci + 1) * PD,
                                               t0 - km1:t0 + TSn]),
                    reads=[c.tb(x32.name)], writes=[bx], dma=True,
                    ch=pname + f"L{j}")
            bk = (s * EC + ci) % 2
            ps = P[bk]
            bps = c.tb(f"psum{bk}")
            mms = [
                (lambda e, j=j, ci=ci, kk=kk, first=(kk == 0),
                 last=(kk == k - 1), ps=ps: e.matmul(
                    ps[:, 0:TSn], wsb[:, (ci * k + kk) * PD:(ci * k + kk + 1) * PD],
                    xt[j][:, kk:kk + TSn], start=first, stop=last))
                for kk in range(k)]
            c.emit("tensor", mms, reads=[bw, bx], writes=[bps])
            fn = act if act is not None else AF.Identity
            c.emit("scalar", lambda e, j=j, ps=ps, fn=fn: e.activation(
                ot[j][:, :], ps[:, 0:TSn], fn),
                reads=[bps], writes=[bo])
            c.emit("scalar", lambda e, j=j, ci=ci, t0=t0: e.dma_start(
                out[ci * PD:(ci + 1) * PD, t0:t0 + TSn], ot[j][:, :]),
                reads=[bo], writes=[c.tb(out.name, (s, ci))], dma=True,
                ch=pname + f"S{j}")
            if extra_out is not None:
                c.emit("gpsimd", lambda e, j=j, ci=ci, t0=t0: e.dma_start(
                    extra_out[ci * PD:(ci + 1) * PD, t0:t0 + TSn], ot[j][:, :]),
                    reads=[bo], writes=[c.tb(extra_out.name, (s, ci))],
                    dma=True, ch=pname + f"E{j}")


def emit_headnorm(c, sb, cst, P, src, dsts, *, T, TSn, D, hd2, pname,
                  mode="l2", beta_sb=None, beta_tb=None, gamma_col=None):
    EC = D // PD
    H_ = D // hd2
    nslab = T // TSn
    xa = [sb.alloc([PD, EC * TSn], BF16) for _ in range(2)]
    sq = sb.alloc([PD, TSn], F32)
    rr_ = [sb.alloc([max(H_, 1), TSn], F32) for _ in range(2)]
    nb_ = [sb.alloc([PD, TSn], BF16) for _ in range(2)]
    bb_ = [sb.alloc([PD, TSn], BF16) for _ in range(2)]
    ob = [[sb.alloc([PD, TSn], BF16) for _ in range(len(dsts))]
          for _ in range(2)]
    bcst = c.tb("cst")
    for s in range(nslab):
        t0 = s * TSn
        js = s % 2
        bx = c.tb(pname + "x", js)
        psH = P[4 + js]
        bpsH = c.tb(f"psum{4 + js}")
        bsq = c.tb(pname + "sq")
        brr = c.tb(pname + "rr", js)
        for ci in range(EC):
            c.emit("sync", lambda e, js=js, ci=ci, t0=t0: e.dma_start(
                xa[js][:, ci * TSn:(ci + 1) * TSn],
                src[ci * PD:(ci + 1) * PD, t0:t0 + TSn]),
                reads=[c.tb(src.name)], writes=[bx], dma=True,
                ch=pname + f"L{js}")
        if mode != "none":
            for ci in range(EC):
                c.emit("scalar", lambda e, js=js, ci=ci: e.square(
                    sq[:, :], xa[js][:, ci * TSn:(ci + 1) * TSn]),
                    reads=[bx], writes=[bsq])
                c.emit("tensor", lambda e, ci=ci, psH=psH: e.matmul(
                    psH[0:H_, 0:TSn], cst["hselD"][:, ci * H_:(ci + 1) * H_],
                    sq[:, :], start=(ci == 0), stop=(ci == EC - 1)),
                    reads=[bsq, bcst], writes=[bpsH])
            if mode == "l2":
                c.emit("scalar", lambda e, js=js, psH=psH: e.activation(
                    rr_[js][0:H_, :], psH[0:H_, 0:TSn], AF.Ln,
                    bias=np.float32(1e-6)),
                    reads=[bpsH], writes=[brr])
            else:
                c.emit("scalar", lambda e, js=js, psH=psH: e.activation(
                    rr_[js][0:H_, :], psH[0:H_, 0:TSn], AF.Ln,
                    bias=np.float32(1e-5), scale=np.float32(1.0 / hd2)),
                    reads=[bpsH], writes=[brr])
            c.emit("scalar", lambda e, js=js: e.activation(
                rr_[js][0:H_, :], rr_[js][0:H_, :], AF.Exp, scale=-0.5),
                reads=[brr], writes=[brr])
        for ci in range(EC):
            jc = ci % 2
            bn = c.tb(pname + "bn", jc)
            bbb = c.tb(pname + "bb", jc)
            if mode != "none":
                psB = P[jc]
                bpsB = c.tb(f"psum{jc}")
                c.emit("tensor", lambda e, js=js, ci=ci, psB=psB: e.matmul(
                    psB[:, 0:TSn], cst["hexpD"][0:H_, ci * PD:(ci + 1) * PD],
                    rr_[js][0:H_, :], start=True, stop=True),
                    reads=[brr, bcst], writes=[bpsB])
                c.emit("scalar", lambda e, jc=jc, psB=psB: e.copy(
                    nb_[jc][:, :], psB[:, 0:TSn]),
                    reads=[bpsB], writes=[bn])
            if any(mb for _, mb in dsts):
                psB2 = P[2 + jc]
                bps2 = c.tb(f"psum{2 + jc}")
                c.emit("tensor", lambda e, ci=ci, t0=t0, psB2=psB2: e.matmul(
                    psB2[:, 0:TSn], cst["hexpD"][0:H_, ci * PD:(ci + 1) * PD],
                    beta_sb[0:H_, t0:t0 + TSn], start=True, stop=True),
                    reads=[beta_tb, bcst], writes=[bps2])
                c.emit("scalar", lambda e, jc=jc, psB2=psB2: e.copy(
                    bb_[jc][:, :], psB2[:, 0:TSn]),
                    reads=[bps2], writes=[bbb])
            for di, (dst, mul_beta) in enumerate(dsts):
                bo = c.tb(pname + "o", (jc, di))
                obt = ob[jc][di]
                if mode == "none":
                    c.emit("vector", lambda e, js=js, ci=ci, jc=jc, obt=obt:
                           e.tensor_tensor(
                        obt[:, :], xa[js][:, ci * TSn:(ci + 1) * TSn],
                        bb_[jc][:, :], ALU.mult),
                        reads=[bx, bbb], writes=[bo])
                elif gamma_col is not None:
                    c.emit("vector", lambda e, js=js, ci=ci, jc=jc, obt=obt:
                           e.scalar_tensor_tensor(
                        obt[:, :], xa[js][:, ci * TSn:(ci + 1) * TSn],
                        gamma_col[:, ci:ci + 1], nb_[jc][:, :], ALU.mult,
                        ALU.mult),
                        reads=[bx, bn, bcst], writes=[bo])
                else:
                    c.emit("vector", lambda e, js=js, ci=ci, jc=jc, obt=obt:
                           e.tensor_tensor(
                        obt[:, :], xa[js][:, ci * TSn:(ci + 1) * TSn],
                        nb_[jc][:, :], ALU.mult),
                        reads=[bx, bn], writes=[bo])
                if mul_beta and mode != "none":
                    c.emit("vector", lambda e, jc=jc, obt=obt: e.tensor_tensor(
                        obt[:, :], obt[:, :], bb_[jc][:, :], ALU.mult),
                        reads=[bo, bbb], writes=[bo])
                c.emit("scalar", lambda e, ci=ci, t0=t0, dst=dst, obt=obt:
                       e.dma_start(
                    dst[ci * PD:(ci + 1) * PD, t0:t0 + TSn], obt[:, :]),
                    reads=[bo], writes=[c.tb(dst.name, (s, ci))], dma=True,
                    ch=pname + f"S{jc}{di}")


def emit_transpose_pass(c, sb, cst, P, src, dst, *, R, T, pname):
    RC = R // PD
    TC = T // PD
    tin = [sb.alloc([PD, PD], BF16) for _ in range(2)]
    tout = [sb.alloc([PD, PD], BF16) for _ in range(2)]
    k = 0
    for tc in range(TC):
        for rc in range(RC):
            j = k % 2
            bk = 2 + k % 2
            k += 1
            bx = c.tb(pname + "x", j)
            bo = c.tb(pname + "o", j)
            bps = c.tb(f"psum{bk}")
            c.emit("sync", lambda e, j=j, rc=rc, tc=tc: e.dma_start(
                tin[j][:, :], src[rc * PD:(rc + 1) * PD, tc * PD:(tc + 1) * PD]),
                reads=[c.tb(src.name)], writes=[bx], dma=True,
                ch=pname + f"L{j}")
            c.emit("tensor", lambda e, j=j, bk=bk: e.transpose(
                P[bk][:, :].bitcast(BF16)[:, 0:PD], tin[j][:, :], cst["ident_bf"][:, :]),
                reads=[bx, c.tb("cst")], writes=[bps])
            c.emit("vector", lambda e, j=j, bk=bk: e.tensor_copy(
                tout[j][:, :], P[bk][:, :].bitcast(BF16)[:, 0:PD]),
                reads=[bps], writes=[bo])
            c.emit("scalar", lambda e, j=j, rc=rc, tc=tc: e.dma_start(
                dst[tc * PD:(tc + 1) * PD, rc * PD:(rc + 1) * PD], tout[j][:, :]),
                reads=[bo], writes=[c.tb(dst.name, (tc, rc))], dma=True,
                ch=pname + f"S{j}")


def emit_delta(c, sb, cst, P, cfg, lanes, bufs, pname):
    nc = c.nc
    Tb = cfg["Tb"]
    hd2 = cfg["hd2"]
    DKC = hd2 // PD
    NT = Tb // PD
    NCH = Tb // 32
    NG = NCH // 4
    CH = 32
    bcst = c.tb("cst")
    L = []
    for li, ln in enumerate(lanes):
        b, h = ln["b"], ln["h"]
        r0 = h * hd2
        tb0 = b * Tb
        d = {}
        d["qfm"] = sb.alloc([PD, DKC * Tb], BF16)
        d["kfm"] = sb.alloc([PD, DKC * Tb], BF16)
        d["kbfm"] = sb.alloc([PD, DKC * Tb], BF16)
        d["vsb"] = sb.alloc([PD, NT * hd2], BF16)
        d["kbsb"] = sb.alloc([PD, NT * hd2], BF16)
        d["ksb"] = sb.alloc([PD, NT * hd2], BF16)
        d["ubuf"] = sb.alloc([PD, NT * hd2], BF16)
        d["wbuf"] = sb.alloc([PD, NT * hd2], BF16)
        d["wfm"] = sb.alloc([PD, DKC * Tb], BF16)
        d["Rf"] = [sb.alloc([PD, 2 * hd2], F32) for _ in range(2)]
        d["Rb"] = [sb.alloc([PD, 2 * hd2], BF16) for _ in range(2)]
        d["N"] = [sb.alloc([PD, 32], BF16) for _ in range(2)]
        d["Nt"] = [sb.alloc([PD, 32], BF16) for _ in range(2)]
        d["Sf"] = sb.alloc([PD, DKC * hd2], F32)
        d["Sb"] = sb.alloc([PD, DKC * hd2], BF16)
        d["att"] = sb.alloc([PD, 32], BF16)
        d["uh"] = sb.alloc([PD, hd2], BF16)
        d["ob"] = sb.alloc([PD, NT * hd2], BF16)
        pfx = pname + f"l{li}"
        d["pfx"] = pfx
        for nm, srcd, rows in (("qfm", bufs["cqn"], True), ("kfm", bufs["ckn"], True),
                               ("kbfm", bufs["ckb"], True)):
            bb = c.tb(pfx + nm)
            for ci in range(DKC):
                c.emit("sync", lambda e, d=d, nm=nm, srcd=srcd, ci=ci, r0=r0,
                       tb0=tb0: e.dma_start(
                    d[nm][:, ci * Tb:(ci + 1) * Tb],
                    srcd[r0 + ci * PD:r0 + (ci + 1) * PD, tb0:tb0 + Tb]),
                    reads=[c.tb(srcd.name)], writes=[bb], dma=True,
                    ch=pfx + nm)
        for nm, srcd in (("vsb", bufs["t_vb"]), ("kbsb", bufs["t_kb"]),
                         ("ksb", bufs["t_kn"])):
            bb = c.tb(pfx + nm)
            for tt in range(NT):
                c.emit("sync", lambda e, d=d, nm=nm, srcd=srcd, tt=tt, r0=r0,
                       tb0=tb0: e.dma_start(
                    d[nm][:, tt * hd2:(tt + 1) * hd2],
                    srcd[tb0 + tt * PD:tb0 + (tt + 1) * PD, r0:r0 + hd2]),
                    reads=[c.tb(srcd.name)], writes=[bb], dma=True,
                    ch=pfx + nm)
        c.emit("vector", lambda e, d=d: e.memset(d["Sf"][:, :], 0.0),
               reads=[], writes=[c.tb(pfx + "Sf")])
        c.emit("vector", lambda e, d=d: e.memset(d["Sb"][:, :], 0.0),
               reads=[], writes=[c.tb(pfx + "Sb")])
        d["b"], d["h"], d["r0"], d["tb0"], d["li"] = b, h, r0, tb0, li
        L.append(d)

    for d in L:
        pfx = d["pfx"]
        pb = 0 if d["li"] == 0 else 4
        for g in range(NG):
            jr = g % 2
            bR = c.tb(pfx + "R", jr)
            bN = c.tb(pfx + "N", jr)
            bNt = c.tb(pfx + "Nt", jr)
            psG = P[pb]
            bpsG = c.tb(f"psum{pb}")
            psR = P[pb + 1]
            bpsR = c.tb(f"psum{pb + 1}")
            for gm in range(4):
                ch = g * 4 + gm
                for ci in range(DKC):
                    c.emit("tensor", lambda e, d=d, gm=gm, ch=ch, ci=ci,
                           psG=psG: e.matmul(
                        psG[32 * gm:32 * gm + 32, 0:32],
                        d["kbfm"][:, ci * Tb + ch * CH:ci * Tb + ch * CH + 32],
                        d["kfm"][:, ci * Tb + ch * CH:ci * Tb + ch * CH + 32],
                        start=(ci == 0), stop=(ci == DKC - 1),
                        tile_position=(0, 32 * gm)),
                        reads=[c.tb(pfx + "kbfm"), c.tb(pfx + "kfm")],
                        writes=[bpsG])
            c.emit("vector", lambda e, d=d, jr=jr, psG=psG:
                   e.scalar_tensor_tensor(
                d["N"][jr][:, :], psG[:, 0:32], -1.0, cst["maskSL4"][:, :],
                ALU.mult, ALU.mult),
                reads=[bpsG, bcst], writes=[bN])
            c.emit("vector", lambda e, d=d, jr=jr: e.transpose(
                d["Nt"][jr][:, :], d["N"][jr][:, :]),
                reads=[bN], writes=[bNt])
            c.emit("vector", lambda e, d=d, jr=jr, g=g: e.tensor_copy(
                d["Rf"][jr][:, 0:hd2], d["vsb"][:, g * hd2:(g + 1) * hd2]),
                reads=[c.tb(pfx + "vsb")], writes=[bR])
            c.emit("vector", lambda e, d=d, jr=jr, g=g: e.tensor_copy(
                d["Rf"][jr][:, hd2:2 * hd2], d["kbsb"][:, g * hd2:(g + 1) * hd2]),
                reads=[c.tb(pfx + "kbsb")], writes=[bR])
            c.emit("vector", lambda e, d=d, jr=jr: e.tensor_copy(
                d["Rb"][jr][:, :], d["Rf"][jr][:, :]),
                reads=[bR], writes=[bR])
            for it in range(5):
                for gm in range(4):
                    c.emit("tensor", lambda e, d=d, jr=jr, gm=gm, psR=psR:
                           e.matmul(
                        psR[32 * gm:32 * gm + 32, 0:2 * hd2],
                        d["Nt"][jr][32 * gm:32 * gm + 32, :],
                        d["Rb"][jr][32 * gm:32 * gm + 32, :],
                        start=True, stop=True,
                        tile_position=(32 * gm, 32 * gm)),
                        reads=[bNt, bR], writes=[bpsR])
                c.emit("vector", lambda e, d=d, jr=jr, psR=psR: e.tensor_tensor(
                    d["Rf"][jr][:, :], d["Rf"][jr][:, :], psR[:, 0:2 * hd2],
                    ALU.add),
                    reads=[bR, bpsR], writes=[bR])
                c.emit("vector", lambda e, d=d, jr=jr: e.tensor_copy(
                    d["Rb"][jr][:, :], d["Rf"][jr][:, :]),
                    reads=[bR], writes=[bR])
                if it < 4:
                    for gm in range(4):
                        c.emit("tensor", lambda e, d=d, jr=jr, gm=gm, psG=psG:
                               e.matmul(
                            psG[32 * gm:32 * gm + 32, 0:32],
                            d["Nt"][jr][32 * gm:32 * gm + 32, :],
                            d["N"][jr][32 * gm:32 * gm + 32, :],
                            start=True, stop=True,
                            tile_position=(32 * gm, 32 * gm)),
                            reads=[bNt, bN], writes=[bpsG])
                    c.emit("scalar", lambda e, d=d, jr=jr, psG=psG: e.copy(
                        d["N"][jr][:, :], psG[:, 0:32]),
                        reads=[bpsG], writes=[bN])
                    c.emit("vector", lambda e, d=d, jr=jr: e.transpose(
                        d["Nt"][jr][:, :], d["N"][jr][:, :]),
                        reads=[bN], writes=[bNt])
            c.emit("vector", lambda e, d=d, jr=jr, g=g: e.tensor_copy(
                d["ubuf"][:, g * hd2:(g + 1) * hd2], d["Rb"][jr][:, 0:hd2]),
                reads=[bR], writes=[c.tb(pfx + "ubuf")])
            c.emit("vector", lambda e, d=d, jr=jr, g=g: e.tensor_copy(
                d["wbuf"][:, g * hd2:(g + 1) * hd2],
                d["Rb"][jr][:, hd2:2 * hd2]),
                reads=[bR], writes=[c.tb(pfx + "wbuf")])
        for tt in range(NT):
            for ci in range(DKC):
                bk = pb + 2
                bps = c.tb(f"psum{bk}")
                c.emit("tensor", lambda e, d=d, tt=tt, ci=ci, bk=bk:
                       e.transpose(
                    P[bk][:, :].bitcast(BF16)[:, 0:PD],
                    d["wbuf"][:, tt * hd2 + ci * PD:tt * hd2 + (ci + 1) * PD],
                    cst["ident_bf"][:, :]),
                    reads=[c.tb(pfx + "wbuf"), bcst], writes=[bps])
                c.emit("vector", lambda e, d=d, tt=tt, ci=ci, bk=bk:
                       e.tensor_copy(
                    d["wfm"][:, ci * Tb + tt * PD:ci * Tb + (tt + 1) * PD],
                    P[bk][:, :].bitcast(BF16)[:, 0:PD]),
                    reads=[bps], writes=[c.tb(pfx + "wfm")])

    for ch in range(NCH):
        tt = ch // 4
        cr = 32 * (ch % 4)
        for d in L:
            pfx = d["pfx"]
            pb = 0 if d["li"] == 0 else 4
            psG, psU, psO, psD = P[pb], P[pb + 1], P[pb + 2], P[pb + 3]
            bG = c.tb(f"psum{pb}")
            bU = c.tb(f"psum{pb + 1}")
            bO = c.tb(f"psum{pb + 2}")
            bD = c.tb(f"psum{pb + 3}")
            batt = c.tb(pfx + "att")
            buh = c.tb(pfx + "uh")
            bSf = c.tb(pfx + "Sf")
            bSb = c.tb(pfx + "Sb")
            for ci in range(DKC):
                c.emit("tensor", lambda e, d=d, ci=ci, ch=ch, cr=cr, psG=psG:
                       e.matmul(
                    psG[cr:cr + 32, 0:32],
                    d["kfm"][:, ci * Tb + ch * CH:ci * Tb + ch * CH + 32],
                    d["qfm"][:, ci * Tb + ch * CH:ci * Tb + ch * CH + 32],
                    start=(ci == 0), stop=(ci == DKC - 1),
                    tile_position=(0, cr)),
                    reads=[c.tb(pfx + "kfm"), c.tb(pfx + "qfm")], writes=[bG])
            c.emit("vector", lambda e, d=d, cr=cr, psG=psG: e.tensor_tensor(
                d["att"][cr:cr + 32, :], psG[cr:cr + 32, 0:32],
                cst["maskTI4"][cr:cr + 32, :], ALU.mult),
                reads=[bG, bcst], writes=[batt])
            for ci in range(DKC):
                c.emit("tensor", lambda e, d=d, ci=ci, ch=ch, cr=cr, psU=psU:
                       e.matmul(
                    psU[cr:cr + 32, 0:hd2],
                    d["wfm"][:, ci * Tb + ch * CH:ci * Tb + ch * CH + 32],
                    d["Sb"][:, ci * hd2:(ci + 1) * hd2],
                    start=(ci == 0), stop=(ci == DKC - 1),
                    tile_position=(0, cr)),
                    reads=[c.tb(pfx + "wfm"), bSb], writes=[bU])
            c.emit("vector", lambda e, d=d, tt=tt, cr=cr, psU=psU:
                   e.tensor_tensor(
                d["uh"][cr:cr + 32, :],
                d["ubuf"][cr:cr + 32, tt * hd2:(tt + 1) * hd2],
                psU[cr:cr + 32, 0:hd2], ALU.subtract),
                reads=[c.tb(pfx + "ubuf"), bU], writes=[buh])
            for ci in range(DKC):
                c.emit("tensor", lambda e, d=d, ci=ci, ch=ch, cr=cr, psO=psO:
                       e.matmul(
                    psO[cr:cr + 32, 0:hd2],
                    d["qfm"][:, ci * Tb + ch * CH:ci * Tb + ch * CH + 32],
                    d["Sb"][:, ci * hd2:(ci + 1) * hd2],
                    start=(ci == 0), stop=False, tile_position=(0, cr)),
                    reads=[c.tb(pfx + "qfm"), bSb], writes=[bO])
            c.emit("tensor", lambda e, d=d, cr=cr, psO=psO: e.matmul(
                psO[cr:cr + 32, 0:hd2], d["att"][cr:cr + 32, :],
                d["uh"][cr:cr + 32, :], start=False, stop=True,
                tile_position=(cr, cr)),
                reads=[batt, buh], writes=[bO])
            c.emit("scalar", lambda e, d=d, tt=tt, cr=cr, psO=psO: e.copy(
                d["ob"][cr:cr + 32, tt * hd2:(tt + 1) * hd2],
                psO[cr:cr + 32, 0:hd2]),
                reads=[bO], writes=[c.tb(pfx + "ob")])
            for hi in range(DKC):
                c.emit("tensor", lambda e, d=d, tt=tt, cr=cr, hi=hi, psD=psD:
                       e.matmul(
                    psD[:, hi * hd2:(hi + 1) * hd2],
                    d["ksb"][cr:cr + 32, tt * hd2 + hi * PD:tt * hd2 + (hi + 1) * PD],
                    d["uh"][cr:cr + 32, :], start=True, stop=True,
                    tile_position=(cr, 0)),
                    reads=[c.tb(pfx + "ksb"), buh], writes=[bD])
            c.emit("vector", lambda e, d=d, psD=psD: e.tensor_tensor(
                d["Sf"][:, :], d["Sf"][:, :], psD[:, 0:DKC * hd2], ALU.add),
                reads=[bSf, bD], writes=[bSf])
            c.emit("vector", lambda e, d=d: e.tensor_copy(
                d["Sb"][:, :], d["Sf"][:, :]),
                reads=[bSf], writes=[bSb])
    for d in L:
        pfx = d["pfx"]
        for tt in range(NT):
            c.emit("scalar", lambda e, d=d, tt=tt: e.dma_start(
                bufs["delta_tm"][d["tb0"] + tt * PD:d["tb0"] + (tt + 1) * PD,
                                 d["r0"]:d["r0"] + hd2],
                d["ob"][:, tt * hd2:(tt + 1) * hd2]),
                reads=[c.tb(pfx + "ob")],
                writes=[c.tb(bufs["delta_tm"].name, (d["b"], d["h"], tt))],
                dma=True, ch=pfx + "OS")


def emit_branch_stats(c, sb, cst, P, branches, gin, *, T, TSn, D, hd2,
                      pname):
    EC = D // PD
    H_ = D // hd2
    n = float(hd2)
    xa = [sb.alloc([PD, TSn], BF16) for _ in range(2)]
    tp = [sb.alloc([PD, TSn], BF16) for _ in range(2)]
    rows = sb.alloc([H_, 2 * TSn], F32)
    srow = sb.alloc([H_, 4 * TSn], BF16)
    bcst = c.tb("cst")
    Ttot = T
    for s in range(T // TSn):
        t0 = s * TSn
        for br, brd in enumerate(branches):
            bsums = [c.tb(f"psum{4 + k}") for k in range(3)]
            for ci in range(EC):
                j = ci % 2
                bx = c.tb(pname + "x", j)
                btp = c.tb(pname + "t", j)
                c.emit("sync", lambda e, j=j, ci=ci, t0=t0, brd=brd:
                       e.dma_start(xa[j][:, :],
                                   brd[ci * PD:(ci + 1) * PD, t0:t0 + TSn]),
                    reads=[c.tb(brd.name)], writes=[bx], dma=True,
                    ch=pname + f"L{j}")
                c.emit("tensor", lambda e, j=j, ci=ci: e.matmul(
                    P[4][0:H_, 0:TSn], cst["hselD_bf"][:, ci * H_:(ci + 1) * H_],
                    xa[j][:, :], start=(ci == 0), stop=(ci == EC - 1)),
                    reads=[bx, bcst], writes=[bsums[0]])
                c.emit("scalar", lambda e, j=j: e.activation(
                    tp[j][:, :], xa[j][:, :], AF.Abs),
                    reads=[bx], writes=[btp])
                c.emit("tensor", lambda e, j=j, ci=ci: e.matmul(
                    P[5][0:H_, 0:TSn], cst["hselD_bf"][:, ci * H_:(ci + 1) * H_],
                    tp[j][:, :], start=(ci == 0), stop=(ci == EC - 1)),
                    reads=[btp, bcst], writes=[bsums[1]])
                c.emit("scalar", lambda e, j=j: e.square(
                    tp[j][:, :], xa[j][:, :]),
                    reads=[bx], writes=[btp])
                c.emit("tensor", lambda e, j=j, ci=ci: e.matmul(
                    P[6][0:H_, 0:TSn], cst["hselD_bf"][:, ci * H_:(ci + 1) * H_],
                    tp[j][:, :], start=(ci == 0), stop=(ci == EC - 1)),
                    reads=[btp, bcst], writes=[bsums[2]])
            brw = c.tb(pname + "rows")
            bsr = c.tb(pname + "srow")
            c.emit("vector", lambda e: e.tensor_scalar(
                srow[0:H_, 0:TSn], P[4][0:H_, 0:TSn], 1.0 / n, None, ALU.mult),
                reads=[bsums[0]], writes=[bsr])
            c.emit("vector", lambda e: e.tensor_scalar(
                srow[0:H_, 2 * TSn:3 * TSn], P[5][0:H_, 0:TSn], 1.0 / n, None,
                ALU.mult),
                reads=[bsums[1]], writes=[bsr])
            c.emit("scalar", lambda e: e.activation(
                srow[0:H_, 3 * TSn:4 * TSn], P[6][0:H_, 0:TSn], AF.Sqrt,
                bias=np.float32(0.0)),
                reads=[bsums[2]], writes=[bsr])
            c.emit("scalar", lambda e: e.square(
                rows[0:H_, 0:TSn], P[4][0:H_, 0:TSn]),
                reads=[bsums[0]], writes=[brw])
            c.emit("vector", lambda e: e.scalar_tensor_tensor(
                rows[0:H_, TSn:2 * TSn], rows[0:H_, 0:TSn], -1.0 / n,
                P[6][0:H_, 0:TSn], ALU.mult, ALU.add),
                reads=[brw, bsums[2]], writes=[brw])
            c.emit("scalar", lambda e: e.activation(
                srow[0:H_, TSn:2 * TSn], rows[0:H_, TSn:2 * TSn], AF.Sqrt,
                scale=np.float32(1.0 / (n - 1.0))),
                reads=[brw], writes=[bsr])
            for st in range(4):
                r0 = 1024 + 4 * br + st
                c.emit("scalar", lambda e, st=st, r0=r0, t0=t0: e.dma_start(
                    bass.AP(gin.tensor, r0 * Ttot + t0, [[24 * Ttot, H_],
                                                         [1, TSn]]),
                    srow[0:H_, st * TSn:st * TSn + TSn]),
                    reads=[bsr], writes=[c.tb(gin.name, ("st", s, br, st))],
                    dma=True, ch=pname + f"S{st}")


def emit_gates(c, sb, sbk, cst, P, glog, glt, wgate, *, T, pname):
    tl = sb.alloc([1, 32], F32)
    irow = sb.alloc([1, 24], F32)
    gl = [sb.alloc([24, PD], F32) for _ in range(2)]
    wtm = [sb.alloc([PD, 24], F32) for _ in range(2)]
    t1 = [sb.alloc([PD, 24], F32) for _ in range(2)]
    red = [sb.alloc([PD, 8], F32) for _ in range(2)]
    itbc = sb.alloc([PD, 24], F32)
    wout = [sb.alloc([24, PD], F32) for _ in range(2)]
    bcst = c.tb("cst")
    btl = c.tb(pname + "tl")
    bir = c.tb(pname + "ir")
    c.emit("sync", lambda e: e.dma_start(tl[:1, 0:4], glt[:1, :]),
           reads=[c.tb(glt.name)], writes=[btl], dma=True, ch=pname + "L")
    c.emit("scalar", lambda e: e.activation(tl[:1, 4:8], tl[:1, 0:4], AF.Abs),
           reads=[btl], writes=[btl])
    c.emit("scalar", lambda e: e.activation(tl[:1, 8:12], tl[:1, 4:8], AF.Exp,
                                            scale=-1.0),
           reads=[btl], writes=[btl])
    c.emit("scalar", lambda e: e.activation(tl[:1, 12:16], tl[:1, 8:12],
                                            AF.Ln, bias=np.float32(1.0)),
           reads=[btl], writes=[btl])
    c.emit("vector", lambda e: e.tensor_scalar(tl[:1, 16:20], tl[:1, 0:4],
                                               0.0, None, ALU.max),
           reads=[btl], writes=[btl])
    c.emit("vector", lambda e: e.tensor_tensor(tl[:1, 20:24], tl[:1, 12:16],
                                               tl[:1, 16:20], ALU.add),
           reads=[btl], writes=[btl])
    c.emit("vector", lambda e: e.tensor_scalar(tl[:1, 24:28], tl[:1, 20:24],
                                               0.5, None, ALU.add),
           reads=[btl], writes=[btl])
    c.emit("vector", lambda e: e.reciprocal(tl[:1, 28:32], tl[:1, 24:28]),
           reads=[btl], writes=[btl])
    for h in range(4):
        c.emit("vector", lambda e, h=h: e.tensor_copy(
            irow[:1, 6 * h:6 * h + 6],
            tl[:1, 28 + h:29 + h].broadcast_to([1, 6])),
            reads=[btl], writes=[bir])
    bitbc = c.tb(pname + "itbc")
    bps2 = c.tb("psum6")
    c.emit("tensor", lambda e: e.matmul(
        P[6][:, 0:24], cst["ones_row_f"][:1, :], irow[:1, :],
        start=True, stop=True),
        reads=[bir, bcst], writes=[bps2])
    c.emit("scalar", lambda e: e.copy(itbc[:, :], P[6][:, 0:24]),
        reads=[bps2], writes=[bitbc])
    for tc in range(T // PD):
        jt = tc % 2
        bgl = c.tb(pname + "gl", jt)
        bw = c.tb(pname + "w", jt)
        br_ = c.tb(pname + "r", jt)
        c.emit("sync", lambda e, tc=tc, jt=jt: e.dma_start(
            gl[jt][:, :], glog[0:24, tc * PD:(tc + 1) * PD]),
            reads=[c.tb(glog.name)], writes=[bgl], dma=True,
            ch=pname + f"G{jt}")
        bk = 2 + jt
        bps = c.tb(f"psum{bk}")
        c.emit("tensor", lambda e, bk=bk, jt=jt: e.transpose(
            P[bk][:, 0:24], gl[jt][0:24, :], cst["ident_f"][0:24, 0:24]),
            reads=[bgl, bcst], writes=[bps])
        c.emit("vector", lambda e, bk=bk, jt=jt: e.tensor_tensor(
            wtm[jt][:, :], P[bk][:, 0:24], itbc[:, :], ALU.mult),
            reads=[bps, bitbc], writes=[bw])
        c.emit("vector", lambda e, jt=jt: e.tensor_reduce(
            red[jt][:, 0:4], wtm[jt][:, :].rearrange("p (h s) -> p h s", h=4),
            AX.X, ALU.max),
            reads=[bw], writes=[br_])
        c.emit("vector", lambda e, jt=jt: e.tensor_tensor(
            t1[jt][:, :].rearrange("p (h s) -> p h s", h=4),
            wtm[jt][:, :].rearrange("p (h s) -> p h s", h=4),
            red[jt][:, 0:4].unsqueeze(2).broadcast_to([PD, 4, 6]),
            ALU.subtract),
            reads=[bw, br_], writes=[bw])
        c.emit("scalar", lambda e, jt=jt: e.activation(
            t1[jt][:, :], t1[jt][:, :], AF.Exp),
            reads=[bw], writes=[bw])
        c.emit("vector", lambda e, jt=jt: e.tensor_reduce(
            red[jt][:, 4:8], t1[jt][:, :].rearrange("p (h s) -> p h s", h=4),
            AX.X, ALU.add),
            reads=[bw], writes=[br_])
        c.emit("vector", lambda e, jt=jt: e.reciprocal(
            red[jt][:, 4:8], red[jt][:, 4:8]),
            reads=[br_], writes=[br_])
        c.emit("vector", lambda e, jt=jt: e.tensor_tensor(
            wtm[jt][:, :].rearrange("p (h s) -> p h s", h=4),
            t1[jt][:, :].rearrange("p (h s) -> p h s", h=4),
            red[jt][:, 4:8].unsqueeze(2).broadcast_to([PD, 4, 6]), ALU.mult),
            reads=[bw, br_], writes=[bw])
        c.emit("tensor", lambda e, bk=bk, jt=jt: e.transpose(
            P[bk][0:24, 0:PD], wtm[jt][:, :], cst["ident_f"][:, :]),
            reads=[bw, bcst], writes=[bps])
        bwo = c.tb(pname + "wo", jt)
        c.emit("vector", lambda e, bk=bk, jt=jt: e.tensor_copy(
            wout[jt][:, :], P[bk][0:24, 0:PD]),
            reads=[bps], writes=[bwo])
        c.emit("scalar", lambda e, tc=tc, jt=jt: e.dma_start(
            wgate[0:24, tc * PD:(tc + 1) * PD], wout[jt][:, :]),
            reads=[bwo], writes=[c.tb(wgate.name, tc)], dma=True,
            ch=pname + f"S{jt}")


def emit_fuse(c, sb, cst, P, branches, wgate, wg_tb, fused, *, T, TSn, D,
              hd2, pname):
    EC = D // PD
    bt = [[sb.alloc([PD, TSn], BF16) for _ in range(6)] for _ in range(2)]
    wbc = [sb.alloc([PD, 6 * TSn], BF16) for _ in range(2)]
    wrow = [sb.alloc([1, 6 * TSn], F32) for _ in range(2)]
    acc = [sb.alloc([PD, TSn], BF16) for _ in range(2)]
    tmp = [sb.alloc([PD, TSn], BF16) for _ in range(2)]
    bcst = c.tb("cst")
    hpc = PD // hd2 if hd2 < PD else 1
    for s in range(T // TSn):
        t0 = s * TSn
        for hh in range(D // hd2):
            jh = hh % 2
            bwb = c.tb(pname + "wb", jh)
            bwr = c.tb(pname + "wr", jh)
            for s6 in range(6):
                c.emit("sync", lambda e, jh=jh, hh=hh, s6=s6, t0=t0:
                       e.dma_start(wrow[jh][:1, s6 * TSn:(s6 + 1) * TSn],
                                   wgate[6 * hh + s6:6 * hh + s6 + 1,
                                         t0:t0 + TSn]),
                    reads=[c.tb(wgate.name)], writes=[bwr], dma=True,
                    ch=pname + f"W{jh}")
            for s6 in range(6):
                bk = 4 + (s6 % 2)
                bps = c.tb(f"psum{bk}")
                c.emit("tensor", lambda e, jh=jh, s6=s6, bk=bk: e.matmul(
                    P[bk][:, 0:TSn], cst["ones_row_f"][:1, :],
                    wrow[jh][:1, s6 * TSn:(s6 + 1) * TSn],
                    start=True, stop=True),
                    reads=[bwr, bcst], writes=[bps])
                c.emit("scalar", lambda e, jh=jh, s6=s6, bk=bk: e.copy(
                    wbc[jh][:, s6 * TSn:(s6 + 1) * TSn], P[bk][:, 0:TSn]),
                    reads=[bps], writes=[bwb])
            for cc in range(hd2 // PD):
                ci = hh * (hd2 // PD) + cc
                jc = ci % 2
                bacc = c.tb(pname + "acc", jc)
                for s6 in range(6):
                    bx = c.tb(pname + "x", (jc, s6))
                    c.emit("sync", lambda e, jc=jc, s6=s6, ci=ci, t0=t0:
                           e.dma_start(bt[jc][s6][:, :],
                                       branches[s6][ci * PD:(ci + 1) * PD,
                                                    t0:t0 + TSn]),
                        reads=[c.tb(branches[s6].name)], writes=[bx],
                        dma=True, ch=pname + f"L{jc}{s6}")
                for s6 in range(6):
                    bx = c.tb(pname + "x", (jc, s6))
                    if s6 == 0:
                        c.emit("vector", lambda e, jc=jc, jh=jh, s6=s6:
                               e.tensor_tensor(
                            acc[jc][:, :], bt[jc][s6][:, :],
                            wbc[jh][:, s6 * TSn:(s6 + 1) * TSn], ALU.mult),
                            reads=[bx, bwb], writes=[bacc])
                    else:
                        btm = c.tb(pname + "tm", jc)
                        c.emit("vector", lambda e, jc=jc, jh=jh, s6=s6:
                               e.tensor_tensor(
                            tmp[jc][:, :], bt[jc][s6][:, :],
                            wbc[jh][:, s6 * TSn:(s6 + 1) * TSn], ALU.mult),
                            reads=[bx, bwb], writes=[btm])
                        c.emit("vector", lambda e, jc=jc: e.tensor_tensor(
                            acc[jc][:, :], acc[jc][:, :], tmp[jc][:, :],
                            ALU.add),
                            reads=[bacc, btm], writes=[bacc])
                c.emit("scalar", lambda e, jc=jc, ci=ci, t0=t0: e.dma_start(
                    fused[ci * PD:(ci + 1) * PD, t0:t0 + TSn], acc[jc][:, :]),
                    reads=[bacc], writes=[c.tb(fused.name, (s, ci))],
                    dma=True, ch=pname + f"S{jc}")


def sl_l1_attn_b(c, P, cst, cfg, w, scr, res_in, res_out):
    nc = c.nc
    E, T, Tb, TSn, nb = cfg["E"], cfg["T"], cfg["Tb"], cfg["TS"], cfg["nb"]
    hd2 = cfg["hd2"]
    H_ = E // hd2
    sbk = SB(nc)
    beta_sb = sbk.alloc([H_, T], F32)
    sb = SB(nc)
    pn = "db"
    xn = scr["xn"]
    emit_prenorm(c, sb, cst, P, res_in, xn, T, TSn, mode="ln", pname=pn + "ln")
    sb = c.phase(sb)
    cpt = [sb.alloc([PD, TSn], BF16) for _ in range(2)]
    for s in range(T // TSn):
        for ci in range(E // PD):
            j = (s * (E // PD) + ci) % 2
            bx = c.tb(pn + "cp", j)
            c.emit("sync", lambda e, j=j, ci=ci, s=s: e.dma_start(
                cpt[j][:, :], xn[ci * PD:(ci + 1) * PD, s * TSn:(s + 1) * TSn]),
                reads=[c.tb(xn.name)], writes=[bx], dma=True, ch=pn + f"c{j}")
            c.emit("scalar", lambda e, j=j, ci=ci, s=s: e.dma_start(
                scr["gin"][ci * PD:(ci + 1) * PD, s * TSn:(s + 1) * TSn],
                cpt[j][:, :]),
                reads=[bx], writes=[c.tb(scr["gin"].name, ("x", s, ci))],
                dma=True, ch=pn + f"d{j}")
    emit_proj(c, sb, cst, P, xn, w["l1b_qw"], scr["pj3"], T=T, TSn=TSn,
              pname=pn + "pq")
    sb = c.phase(sb)
    emit_proj(c, sb, cst, P, xn, w["l1b_kw"], scr["pk3"], T=T, TSn=TSn,
              pname=pn + "pk")
    sb = c.phase(sb)
    emit_proj(c, sb, cst, P, xn, w["l1b_vw"], scr["pv3"], T=T, TSn=TSn,
              pname=pn + "pv")
    sb = c.phase(sb)
    emit_proj(c, sb, cst, P, xn, w["l1b_bw"], scr["glog"], T=T, TSn=TSn,
              out_dt=F32, pname=pn + "pb")
    sb = c.phase(sb)
    bl = sb.alloc([H_, T], F32)
    bbl = c.tb(pn + "bl")
    c.emit("sync", lambda e: e.dma_start(bl[0:H_, :], scr["glog"][0:H_, :]),
           reads=[c.tb(scr["glog"].name)], writes=[bbl], dma=True,
           ch=pn + "bL")
    c.emit("scalar", lambda e: e.activation(beta_sb[0:H_, :], bl[0:H_, :],
                                            AF.Sigmoid),
           reads=[bbl], writes=[c.tb(pn + "beta")])
    beta_tb = c.tb(pn + "beta")
    emit_fir_conv(c, sb, cst, P, scr["pj3"], scr["cq"], T=T, Tb=Tb, TSn=TSn,
                  taps=4, wdiag=w["l1b_qcd"], act=AF.Silu, pname=pn + "cq",
                  D=E)
    sb = c.phase(sb)
    emit_fir_conv(c, sb, cst, P, scr["pk3"], scr["ck"], T=T, Tb=Tb, TSn=TSn,
                  taps=4, wdiag=w["l1b_kcd"], act=AF.Silu, pname=pn + "ck",
                  D=E)
    sb = c.phase(sb)
    emit_fir_conv(c, sb, cst, P, scr["pv3"], scr["cv"], T=T, Tb=Tb, TSn=TSn,
                  taps=4, wdiag=w["l1b_vcd"], act=AF.Silu, pname=pn + "cv",
                  D=E)
    sb = c.phase(sb)
    beta_tb = c.tb(pn + "beta")
    emit_headnorm(c, sb, cst, P, scr["cq"], [(scr["cqn"], False)], T=T,
                  TSn=TSn, D=E, hd2=hd2, pname=pn + "nq", mode="l2")
    sb = c.phase(sb)
    beta_tb = c.tb(pn + "beta")
    emit_headnorm(c, sb, cst, P, scr["ck"], [(scr["ckn"], False),
                                             (scr["ckb"], True)], T=T,
                  TSn=TSn, D=E, hd2=hd2, pname=pn + "nk", mode="l2",
                  beta_sb=beta_sb, beta_tb=beta_tb)
    sb = c.phase(sb)
    beta_tb = c.tb(pn + "beta")
    emit_headnorm(c, sb, cst, P, scr["cv"], [(scr["cvb"], True)], T=T,
                  TSn=TSn, D=E, hd2=hd2, pname=pn + "nv", mode="none",
                  beta_sb=beta_sb, beta_tb=beta_tb)
    sb = c.phase(sb)
    emit_transpose_pass(c, sb, cst, P, scr["cvb"], scr["t_vb"], R=E, T=T,
                        pname=pn + "tv")
    sb = c.phase(sb)
    emit_transpose_pass(c, sb, cst, P, scr["ckb"], scr["t_kb"], R=E, T=T,
                        pname=pn + "tk")
    sb = c.phase(sb)
    emit_transpose_pass(c, sb, cst, P, scr["ckn"], scr["t_kn"], R=E, T=T,
                        pname=pn + "tn")
    sb = c.phase(sb)
    for kk, nm in ((1, "fb1"), (3, "fb3"), (7, "fb7"), (31, "fb31")):
        emit_fir_conv(c, sb, cst, P, scr["cv"], scr[nm], T=T, Tb=Tb, TSn=TSn,
                      taps=kk, wdiag=w[f"l1b_fir{kk}"], pname=pn + f"f{kk}",
                      D=E)
        sb = c.phase(sb)
    bufs = dict(cqn=scr["cqn"], ckn=scr["ckn"], ckb=scr["ckb"],
                t_vb=scr["t_vb"], t_kb=scr["t_kb"], t_kn=scr["t_kn"],
                delta_tm=scr["dtm"])
    pairs = []
    for b in range(nb):
        for h in range(H_):
            pairs.append(dict(b=b, h=h))
    for pi in range(0, len(pairs), 2):
        lanes = pairs[pi:pi + 2]
        emit_delta(c, sb, cst, P, cfg, lanes, bufs, pn + f"D{pi}")
        sb = c.phase(sb)
    emit_transpose_pass(c, sb, cst, P, scr["dtm"], scr["df"], R=T, T=E,
                        pname=pn + "td")
    sb = c.phase(sb)
    branches = [scr["fb1"], scr["fb3"], scr["fb7"], scr["fb31"], scr["df"],
                scr["cv"]]
    emit_branch_stats(c, sb, cst, P, branches, scr["gin"], T=T, TSn=TSn, D=E,
                      hd2=hd2, pname=pn + "bs")
    sb = c.phase(sb)
    emit_proj(c, sb, cst, P, scr["gin"], w["l1b_mw1"], scr["hg"], T=T,
              TSn=TSn, bias_col=cst["l1b_b1"], act=AF.Gelu, pname=pn + "m1")
    sb = c.phase(sb)
    emit_proj(c, sb, cst, P, scr["hg"], w["l1b_mw2"], scr["glog"], T=T,
              TSn=TSn, bias_col=cst["l1b_b2"], out_dt=F32, pname=pn + "m2")
    sb = c.phase(sb)
    emit_gates(c, sb, sbk, cst, P, scr["glog"], w["l1b_glt"], scr["wgt"],
               T=T, pname=pn + "gt")
    sb = c.phase(sb)
    emit_fuse(c, sb, cst, P, branches, scr["wgt"], c.tb(scr["wgt"].name),
              scr["ff"], T=T, TSn=TSn, D=E, hd2=hd2, pname=pn + "fu")
    sb = c.phase(sb)
    emit_headnorm(c, sb, cst, P, scr["ff"], [(scr["fo"], False)], T=T,
                  TSn=TSn, D=E, hd2=hd2, pname=pn + "on", mode="rmsh",
                  gamma_col=cst["l1b_onorm"])
    sb = c.phase(sb)
    emit_proj(c, sb, cst, P, scr["fo"], w["l1b_ow"], res_out, T=T, TSn=TSn,
              res=res_in, res_out=res_out, out_dt=F32, pname=pn + "oo")
    c.fence(res_out.name)
    c.global_barrier()
    sb.close()
    sbk.close()



def build_const_host(cfg):
    TQ = min(512, cfg["Tb"])
    out = {}
    out["ones_col_f"] = np.ones((PD, 1), np.float32)
    out["ones_row_f"] = np.ones((1, PD), np.float32)
    out["ones_row_bf"] = np.ones((1, PD), BF)
    out["ident_bf"] = np.eye(PD, dtype=BF)
    out["ident_f"] = np.eye(PD, dtype=np.float32)
    kk = np.arange(PD)
    out["tri_lt"] = (kk[:, None] < kk[None, :]).astype(np.float32)
    s64 = np.zeros((PD, PD), np.float32)
    s64[64, :] = 1.0
    out["sel64"] = s64
    p = np.arange(PD, dtype=np.float32)[:, None]
    f = np.arange(TQ, dtype=np.float32)[None, :]
    Tb = cfg["Tb"]
    W = 2 * Tb - PD
    jj = np.arange(W, dtype=np.float32)[None, :]
    x0 = Tb - PD
    out["ABSD"] = np.abs(p - jj + x0).astype(np.float32)
    nrel = TQ // PD
    m = np.zeros((PD, nrel * TQ), BF)
    for rel in range(nrel):
        m[:, rel * TQ:(rel + 1) * TQ] = ((p + rel * PD) <= f).astype(BF)
    out["cmask"] = m
    if cfg.get("l1b"):
        hd2 = cfg["hd2"]
        D_ = cfg["E"]
        EC_ = D_ // PD
        H_ = D_ // hd2
        pp = np.arange(PD)
        hsel = np.zeros((PD, EC_ * H_), np.float32)
        hexp = np.zeros((H_, EC_ * PD), np.float32)
        for ci in range(EC_):
            hh = (ci * PD) // hd2
            hsel[:, ci * H_ + hh] = 1.0
            hexp[hh, ci * PD:(ci + 1) * PD] = 1.0
        out["hselD"] = hsel
        out["hexpD"] = hexp
        out["hselD_bf"] = hsel.astype(BF)
        i32 = np.arange(32)
        sl32 = (i32[None, :] < i32[:, None]).astype(np.float32)
        out["maskSL4"] = np.tile(sl32, (4, 1))
        out["maskTI"] = (i32[:, None] <= i32[None, :]).astype(np.float32)
        out["maskTI4"] = np.tile(out["maskTI"], (4, 1))
    if cfg.get("l1f"):
        Tb_ = cfg["Tb"]
        WIN = cfg["window"]
        DH_ = 64
        half = np.arange(0, DH_, 2, dtype=np.float64)
        inv = 1.0 / 10000.0 ** (half / DH_)
        t = np.arange(Tb_, dtype=np.float64)
        fr = t[:, None] * inv[None, :]
        fr2 = np.concatenate([fr, fr], 1)
        base = (half + 0.4 * DH_) / (1.4 * DH_)
        power = (t - Tb_ // 2) / float(WIN // 2)
        scl = base[None, :] ** power[:, None]
        scl2 = np.concatenate([scl, scl], 1)
        cosf = np.cos(fr2)
        sinf = np.sin(fr2)
        out["xq_c"] = np.tile((cosf * scl2).T.astype(np.float32), (2, 1))
        out["xq_s"] = np.tile((sinf * scl2).T.astype(np.float32), (2, 1))
        out["xk_c"] = np.tile((cosf / scl2).T.astype(np.float32), (2, 1))
        out["xk_s"] = np.tile((sinf / scl2).T.astype(np.float32), (2, 1))
        pp = np.arange(PD)
        out["hsel2"] = (pp[:, None] // 64 == np.arange(2)[None, :]).astype(np.float32)
        out["hexp2"] = np.ascontiguousarray(out["hsel2"].T)
        R = np.zeros((PD, PD), BF)
        for m_ in range(PD):
            if m_ % 64 < 32:
                R[m_ + 32, m_] = -1.0
            else:
                R[m_ - 32, m_] = 1.0
        out["R128"] = R
        nu = WIN // PD
        lm = np.zeros((PD, 2 * nu * WIN), BF)
        pw = np.arange(PD)[:, None]
        fw = np.arange(WIN)[None, :]
        for u in range(-(nu - 1), nu + 1):
            mi = u + nu - 1
            keep = (fw - pw + PD * u >= 0) & (fw - pw + PD * u <= WIN)
            lm[:, mi * WIN:(mi + 1) * WIN] = keep.astype(BF)
        out["lmask"] = lm
    if cfg.get("rope_l0b"):
        Tb_ = cfg["Tb"]
        nb_ = cfg.get("nb", 2)
        hs = 64
        t = np.arange(1, Tb_ + 1, dtype=np.float64)[:, None]
        i = np.arange(hs // 2, dtype=np.float64)
        theta = (1.0 / 10000.0) ** (2.0 * i / hs)
        ang = t * theta
        c32 = np.cos(ang).T.astype(np.float32)
        s32 = np.sin(ang).T.astype(np.float32)
        out["ropecos"] = np.tile(np.tile(c32, (4, 1)), (1, nb_))
        out["ropesin"] = np.tile(np.tile(s32, (4, 1)), (1, nb_))
    return out


def build_program(cfg, wspec, xcols, consts):
    nc = bass.Bass("TRN2", target_bir_lowering=False)
    E, T = cfg["E"], cfg["T"]
    H0 = cfg["l0a_H"]
    x_in = nc.dram_tensor("x_fm", [E, T], F32, kind="ExternalInput").ap()
    y_out = nc.dram_tensor("y_fm", [E, T], F32, kind="ExternalOutput").ap()
    w = {}
    for name, shape in wspec.items():
        w[name] = nc.dram_tensor(name, list(shape), F32,
                                 kind="ExternalInput").ap()
    scr = {}
    dbg0 = set(cfg.get("dbg", ()))

    def sdram(nm, shape, dt):
        kind = "ExternalOutput" if nm in dbg0 else "Internal"
        return nc.dram_tensor("s_" + nm, shape, dt, kind=kind).ap()

    scr["xn"] = sdram("xn", [E, T], BF16)
    scr["qf"] = sdram("qf", [E, T], BF16)
    scr["kf"] = sdram("kf", [E, T], BF16)
    scr["vtm"] = sdram("vtm", [T, E + H0], BF16)
    scr["ao"] = sdram("ao", [E, T], BF16)
    scr["qkro"] = sdram("qkro", [2 * E, T], F32)
    if "pdbg" in dbg0:
        scr["pdbg"] = nc.dram_tensor("s_pdbg", [2048, 1024], BF16,
                                     kind="ExternalOutput").ap()
    if any(s.startswith("l0_ffn") for s in cfg["sublayers"]):
        scr["srow"] = sdram("srow", [1, T], F32)
        scr["xr"] = sdram("xr", [E, T], BF16)
        scr["xr2"] = sdram("xr2", [E, T], BF16)
        scr["h1"] = sdram("h1", [E // 2, T], BF16)
        scr["h2"] = nc.dram_tensor("s_h2", [max(cfg["dh"], cfg.get("hid", 0)), T], BF16).ap()
        scr["resM"] = sdram("resM", [E, T], F32)
    if any(s.startswith("l1_ffn") for s in cfg["sublayers"]):
        if "resM" not in scr:
            scr["resM"] = sdram("resM", [E, T], F32)
            scr["xr"] = sdram("xr", [E, T], BF16)
            scr["xr2"] = sdram("xr2", [E, T], BF16)
            scr["h2"] = nc.dram_tensor("s_h2", [max(cfg.get("dh", 0), cfg["hid"]), T], BF16).ap()
        scr["resN"] = sdram("resN", [E, T], F32)
        scr["resO"] = sdram("resO", [E, T], F32)
        scr["resP"] = sdram("resP", [E, T], F32)
        scr["xg"] = sdram("xg", [E, T], BF16)
        scr["h3"] = sdram("h3", [2 * cfg["hid"], T], BF16)
    dbg = set(cfg.get("dbg", ()))
    if "l1_attn_b" in cfg["sublayers"]:
        for nm, shape, dt in (
                ("pj3", [E, T], BF16), ("pk3", [E, T], BF16),
                ("pv3", [E, T], BF16), ("cq", [E, T], BF16),
                ("ck", [E, T], BF16), ("cv", [E, T], BF16),
                ("cqn", [E, T], BF16), ("ckn", [E, T], BF16),
                ("ckb", [E, T], BF16), ("cvb", [E, T], BF16),
                ("t_vb", [T, E], BF16), ("t_kb", [T, E], BF16),
                ("t_kn", [T, E], BF16), ("dtm", [T, E], BF16),
                ("df", [E, T], BF16), ("fb1", [E, T], BF16),
                ("fb3", [E, T], BF16), ("fb7", [E, T], BF16),
                ("fb31", [E, T], BF16), ("gin", [E + 96, T], BF16),
                ("hg", [E, T], BF16), ("glog", [24, T], F32),
                ("wgt", [24, T], F32), ("ff", [E, T], BF16),
                ("fo", [E, T], BF16)):
            kind = "ExternalOutput" if nm in dbg else "Internal"
            scr[nm] = nc.dram_tensor("s_" + nm, shape, dt, kind=kind).ap()
    resA = sdram("resA", [E, T], F32)
    resB = sdram("resB", [E, T], F32)

    c = Ctx(nc)
    sbg = SB(nc)
    stack = ExitStack()
    P = [stack.enter_context(nc.psum_tensor(f"pb{i}", [PD, 512], F32))
         for i in range(8)]
    cst = {"_lazy": {}}
    bcst = c.tb("cst")
    cpool_vals = cfg.get("cpool", [])
    if cpool_vals:
        cp = nc.dram_tensor("cst_cpool", [PD, len(cpool_vals)], F32,
                            kind="ExternalInput").ap()
        cpt = sbg.alloc([PD, len(cpool_vals)], F32, name="c_cpool")
        c.emit("sync", lambda e: e.dma_start(cpt[:, :], cp),
               reads=[], writes=[c.tb("cst", "cpool")], dma=True, ch="CST")
        for i, v in enumerate(cpool_vals):
            nc.const_aps.register(F32, float(v), cpt[:, i:i + 1])
    for name, arr in consts.items():
        dt = BF16 if arr.dtype == BF else F32
        d = nc.dram_tensor("cst_" + name, list(arr.shape), dt,
                           kind="ExternalInput").ap()
        if name in BIG_CONSTS:
            cst["_lazy"][name] = (d, arr.shape, dt)
            continue
        tl = sbg.alloc(list(arr.shape), dt, name="c_" + name)
        c.emit("sync", lambda e, tl=tl, d=d: e.dma_start(tl[:, :], d),
               reads=[], writes=[c.tb("cst", name)], dma=True, ch="CST")
        cst[name] = tl
    for name, shape in xcols.items():
        d = nc.dram_tensor("xc_" + name, list(shape), F32,
                           kind="ExternalInput").ap()
        tl = sbg.alloc(list(shape), F32, name="c_" + name)
        c.emit("sync", lambda e, tl=tl, d=d: e.dma_start(tl[:, :], d),
               reads=[], writes=[c.tb("cst", name)], dma=True, ch="CST")
        cst[name] = tl
    c.fence("cst")
    c.global_barrier()

    seq = cfg["sublayers"]
    bufs = [resA, resB]
    cur = x_in
    for i, sl in enumerate(seq):
        res_out = y_out if i == len(seq) - 1 else bufs[i % 2]
        if sl == "l0_attn_a":
            sl_l0_attn_a(c, P, cst, cfg, w, scr, cur, res_out)
        elif sl == "l1_attn_a":
            sl_l1_attn_a(c, P, cst, cfg, w, scr, cur, res_out)
        elif sl == "l0_attn_b":
            sl_l0_attn_b(c, P, cst, cfg, w, scr, cur, res_out)
        elif sl.startswith("l0_ffn"):
            sl_l0_ffn(c, P, cst, cfg, w, scr, cur, res_out, sl[-1])
        elif sl.startswith("l1_ffn"):
            sl_l1_ffn(c, P, cst, cfg, w, scr, cur, res_out, sl[-1])
        elif sl == "l1_attn_b":
            sl_l1_attn_b(c, P, cst, cfg, w, scr, cur, res_out)
        else:
            raise KeyError(sl)
        cur = res_out
    c.finalize()
    stack.close()
    sbg.close()
    return nc


def col_layout(b):
    n = b.shape[0]
    oc = cdiv(n, PD)
    pad = np.zeros(oc * PD, np.float32)
    pad[:n] = b
    return np.ascontiguousarray(pad.reshape(oc, PD).T)


def vtm_weights(WvT, bv, H, hd):
    E = WvT.shape[0]
    wve = np.zeros((E, H * (hd + 1)), np.float32)
    bve = np.zeros((1, H * (hd + 1)), np.float32)
    for h in range(H):
        wve[:, h * (hd + 1):h * (hd + 1) + hd] = WvT[:, h * hd:(h + 1) * hd]
        bve[0, h * (hd + 1):h * (hd + 1) + hd] = bv[h * hd:(h + 1) * hd]
        bve[0, h * (hd + 1) + hd] = 1.0
    return wve, bve


def host_prep(cfg, rr):
    E = cfg["E"]
    ws, xc = {}, {}
    seq = cfg["sublayers"]
    f32 = np.float32
    if "l0_attn_a" in seq:
        H = cfg["l0a_H"]
        hd = E // H
        Wqkv = rr["L0_attn_a_Wqkv_w"]
        bqkv = rr["L0_attn_a_Wqkv_b"]
        ws["l0a_wq"] = np.ascontiguousarray(Wqkv[0:E].T, f32)
        ws["l0a_wk"] = np.ascontiguousarray(Wqkv[E:2 * E].T, f32)
        wve, bve = vtm_weights(np.ascontiguousarray(Wqkv[2 * E:3 * E].T, f32),
                               bqkv[2 * E:3 * E], H, hd)
        ws["l0a_wve"] = wve
        ws["l0a_bve"] = bve
        ws["l0a_wo"] = np.ascontiguousarray(rr["L0_attn_a_out_w"].T, f32)
        ws["l0a_dd"] = rr["L0_attn_a_distance_decay"].reshape(1, 1).astype(f32)
        xc["l0a_bq"] = col_layout(bqkv[0:E])
        xc["l0a_bk"] = col_layout(bqkv[E:2 * E])
        xc["l0a_bo"] = col_layout(rr["L0_attn_a_out_b"])
    if "l1_attn_a" in seq:
        H = E // 64
        WqT = np.ascontiguousarray(rr["L1_attn_a_Wq"].T, f32)
        WkT = np.ascontiguousarray(rr["L1_attn_a_Wk"].T, f32)
        WvT = np.ascontiguousarray(rr["L1_attn_a_Wv"].T, f32)
        kvh = WkT.shape[1] // 64
        rep = H // kvh
        dup = lambda W: np.repeat(W.reshape(E, kvh, 64), rep, axis=1).reshape(E, H * 64)
        ws["l1a_wq"] = WqT
        ws["l1a_wk"] = np.ascontiguousarray(dup(WkT))
        wve, bve = vtm_weights(np.ascontiguousarray(dup(WvT)),
                               np.zeros(H * 64, f32), H, 64)
        ws["l1a_wve"] = wve
        ws["l1a_bve"] = bve
        ws["l1a_wo"] = np.ascontiguousarray(rr["L1_attn_a_Wo"].T, f32)
    if "l0_attn_b" in seq:
        H = E // 64
        def eo(W):
            WT = np.ascontiguousarray(W.T, f32).reshape(E, H * 32, 2)
            return WT[..., 0], WT[..., 1]
        qe, qo = eo(rr["L0_attn_b_Wq"])
        ke, ko = eo(rr["L0_attn_b_Wk"])
        ws["l0b_weo"] = np.ascontiguousarray(np.concatenate([qe, qo, ke, ko], 1))
        bq = rr["L0_attn_b_bq"].reshape(H * 32, 2)
        bk = rr["L0_attn_b_bk"].reshape(H * 32, 2)
        xc["l0b_beo"] = col_layout(np.concatenate(
            [bq[:, 0], bq[:, 1], bk[:, 0], bk[:, 1]]))
        wve, bve = vtm_weights(np.ascontiguousarray(rr["L0_attn_b_Wv"].T, f32),
                               rr["L0_attn_b_bv"], H, 64)
        ws["l0b_wve"] = wve
        ws["l0b_bve"] = bve
        ws["l0b_wo"] = np.ascontiguousarray(rr["L0_attn_b_Wo"].T, f32)
        xc["l0b_bo"] = col_layout(rr["L0_attn_b_bo"])
    if any(s.startswith("l0_ffn") for s in seq):
        ws["l0f_rt"] = rr["L0_ffn_routing_token"].reshape(E, 1).astype(f32)
        ws["l0f_lw1"] = rr["L0_ffn_light_w1"].astype(f32)
        ws["l0f_lw2"] = rr["L0_ffn_light_w2"].astype(f32)
        ws["l0f_hw1"] = rr["L0_ffn_heavy_w1"].astype(f32)
        ws["l0f_hw2"] = rr["L0_ffn_heavy_w2"].astype(f32)
        xc["l0f_lb1"] = col_layout(rr["L0_ffn_light_b1"])
        xc["l0f_lb2"] = col_layout(rr["L0_ffn_light_b2"])
        xc["l0f_hb1"] = col_layout(rr["L0_ffn_heavy_b1"])
        xc["l0f_hb2"] = col_layout(rr["L0_ffn_heavy_b2"])
        xc["l0f_lgam"] = col_layout(rr["L0_ffn_light_gamma"])
        xc["l0f_hgam"] = col_layout(rr["L0_ffn_heavy_gamma"])
    if any(s.startswith("l1_ffn") for s in seq):
        I2 = E // 2
        H = I2 // 64
        wqkv = rr["L1_ffn_wqkv"].astype(f32)
        for L in range(2):
            ws[f"l1f_wqk{L}"] = np.ascontiguousarray(wqkv[L][:, 0:2 * I2])
            wv, bv = vtm_weights(np.ascontiguousarray(wqkv[L][:, 2 * I2:3 * I2]),
                                 np.zeros(I2, f32), H, 64)
            ws[f"l1f_wv{L}"] = wv
            if L == 0:
                ws["l1f_bve"] = bv
            ws[f"l1f_wo{L}"] = rr["L1_ffn_wo"][L].astype(f32)
            ws[f"l1f_w1{L}"] = rr["L1_ffn_w1"][L].astype(f32)
            ws[f"l1f_w2{L}"] = rr["L1_ffn_w2"][L].astype(f32)
            xc[f"l1f_qs{L}"] = np.tile(rr["L1_ffn_q_scale"][L], 2)[:, None].astype(f32)
            xc[f"l1f_ks{L}"] = np.tile(rr["L1_ffn_k_scale"][L], 2)[:, None].astype(f32)
            xc[f"l1f_lg{L}"] = col_layout(rr["L1_ffn_ln_g"][L])
            xc[f"l1f_lb{L}"] = col_layout(rr["L1_ffn_ln_b"][L])
            xc[f"l1f_b1{L}"] = col_layout(rr["L1_ffn_b1"][L])
            xc[f"l1f_b2{L}"] = col_layout(rr["L1_ffn_b2"][L])
    if "l1_attn_b" in seq:
        def conv_diag(wk):
            E_, k = wk.shape
            o = np.zeros((E_, k * PD), f32)
            for ci in range(E_ // PD):
                for jj in range(k):
                    blk = o[ci * PD:(ci + 1) * PD, jj * PD:(jj + 1) * PD]
                    np.fill_diagonal(blk, wk[ci * PD:(ci + 1) * PD, jj])
            return o
        ws["l1b_qw"] = rr["L1_attn_b_q_w"].astype(f32)
        ws["l1b_kw"] = rr["L1_attn_b_k_w"].astype(f32)
        ws["l1b_vw"] = rr["L1_attn_b_v_w"].astype(f32)
        ws["l1b_bw"] = rr["L1_attn_b_b_w"].astype(f32)
        ws["l1b_qcd"] = conv_diag(rr["L1_attn_b_qc_w"][:, 0, :].astype(f32))
        ws["l1b_kcd"] = conv_diag(rr["L1_attn_b_kc_w"][:, 0, :].astype(f32))
        ws["l1b_vcd"] = conv_diag(rr["L1_attn_b_vc_w"][:, 0, :].astype(f32))
        for kk in (1, 3, 7, 31):
            ws[f"l1b_fir{kk}"] = conv_diag(
                rr[f"L1_attn_b_fir_w{kk}"][:, 0, :].astype(f32))
        ws["l1b_mw1"] = rr["L1_attn_b_mlp_w1"].astype(f32)
        ws["l1b_mw2"] = rr["L1_attn_b_mlp_w2"].astype(f32)
        ws["l1b_ow"] = rr["L1_attn_b_o_w"].astype(f32)
        ws["l1b_glt"] = rr["L1_attn_b_gate_log_temp"].reshape(1, 4).astype(f32)
        xc["l1b_b1"] = col_layout(rr["L1_attn_b_mlp_b1"])
        xc["l1b_b2"] = col_layout(rr["L1_attn_b_mlp_b2"])
        hd2_ = rr["L1_attn_b_onorm_w"].shape[0]
        rep = E // hd2_
        xc["l1b_onorm"] = np.ascontiguousarray(
            np.tile(rr["L1_attn_b_onorm_w"], rep).reshape(E // PD, PD).T
            .astype(f32))
    return ws, xc



def full_cfg():
    return dict(E=1024, T=4096, Tb=2048, nb=2, TS=512, l0a_H=16, C=20.0,
                nheavy=1024, keff=1152.0, dh=4096, hid=2730, window=512,
                hd2=256, l1b=True, l1f=True, rope_l0b=True,
                cpool=[0.0, 1e-5, 1.0],
                sublayers=["l0_attn_a", "l0_ffn_1", "l0_attn_b", "l0_ffn_2",
                           "l1_attn_a", "l1_ffn_1", "l1_attn_b", "l1_ffn_2"])


def build_full():
    cfg = full_cfg()
    consts = build_const_host(cfg)
    return cfg, consts


def kernel(**inputs):
    cfg, consts = build_full()
    ws, xc = host_prep(cfg, inputs)
    nc = build_program(cfg, {k: v.shape for k, v in ws.items()},
                       {k: v.shape for k, v in xc.items()}, consts)
    x = inputs["L0_attn_a_x"]
    base = {}
    base["cst_cpool"] = np.tile(np.array(cfg["cpool"], np.float32)[None, :],
                                (PD, 1))
    for k, v in ws.items():
        base[k] = np.ascontiguousarray(v)
    for k, v in xc.items():
        base["xc_" + k] = np.ascontiguousarray(v)
    for k, v in consts.items():
        base["cst_" + k] = np.ascontiguousarray(v)
    in_maps = []
    for ci in range(8):
        m = dict(base)
        xs = np.asarray(x[2 * ci:2 * ci + 2], np.float32)
        m["x_fm"] = np.ascontiguousarray(xs.reshape(4096, 1024).T)
        in_maps.append(m)
    res = run_bass_kernel_spmd(nc, in_maps, core_ids=list(range(8)))
    out = np.empty((16, 2048, 1024), np.float32)
    for ci in range(8):
        y = res.results[ci]["y_fm"]
        out[2 * ci:2 * ci + 2] = np.moveaxis(
            np.asarray(y).reshape(1024, 2, 2048), 0, 2)
    return out


# version 14
# speedup vs baseline: 1.2350x; 1.0376x over previous
import math
from contextlib import ExitStack
import numpy as np
import ml_dtypes
import concourse.bass as bass
import concourse.mybir as mybir
from concourse.bass_utils import run_bass_kernel_spmd

F32 = mybir.dt.float32
BF16 = mybir.dt.bfloat16
AF = mybir.ActivationFunctionType
ALU = mybir.AluOpType
AX = mybir.AxisListType
BF = ml_dtypes.bfloat16

PD = 128


class TBuf:
    __slots__ = ("w", "r")
    def __init__(self):
        self.w = {}
        self.r = {}

def _merge(dst, src):
    for k, v in src.items():
        if dst.get(k, -1) < v:
            dst[k] = v

class Ctx:
    SEM_OF = {"tensor": "T", "vector": "V", "scalar": "S", "gpsimd": "G", "sync": "Y"}
    SECTIONS = ("tensor", "vector", "scalar", "gpsimd", "sync")
    NDMA = 64

    SEM_LIM = 30000

    def __init__(self, nc):
        self.nc = nc
        self.ops = {s: [] for s in self.SECTIONS}
        self.cnt = {}
        self.cur = {}
        for s in ("T", "V", "S", "G", "Y"):
            self._new_sem(s)
        for i in range(self.NDMA):
            self._new_sem(f"D{i}")
        self.regions = {}
        self.floor = {}
        self.chmap = {}
        self.chprev = {}
        self.chfree = {"hw": list(range(self.NDMA - 12)),
                       "sw": list(range(self.NDMA - 12, self.NDMA))}

    def _new_sem(self, base):
        n = 0
        name = base
        while name in self.cnt:
            n += 1
            name = f"{base}.{n}"
        self.cnt[name] = 0
        self.cur[base] = name
        return name

    def _sem_for(self, base, inc):
        name = self.cur[base]
        if self.cnt[name] + inc > self.SEM_LIM:
            name = self._new_sem(base)
        return name

    def tb(self, name, key=None):
        reg = self.regions.setdefault(name, {})
        if key not in reg:
            t = TBuf()
            base = reg.get(None)
            if base is not None and key is not None:
                _merge(t.w, base.w)
                _merge(t.r, base.r)
            reg[key] = t
        return reg[key]

    def fence(self, name):
        reg = self.regions.get(name)
        if not reg:
            return
        base = TBuf()
        for t in reg.values():
            _merge(base.w, t.w)
            _merge(base.r, t.r)
        self.regions[name] = {None: base}

    def phase(self, sb):
        self.global_barrier()
        sb.close()
        return SB(self.nc)

    def global_barrier(self):
        self.floor = dict(self.cnt)
        self.regions = {}
        self.chmap = {}
        self.chprev = {}
        self.chfree = {"hw": list(range(self.NDMA - 12)),
                       "sw": list(range(self.NDMA - 12, self.NDMA))}

    def _chan(self, ch, section):
        kind = "sw" if section == "gpsimd" else "hw"
        if ch not in self.chmap:
            if not self.chfree[kind]:
                raise RuntimeError("out of DMA channel sems")
            idx = self.chfree[kind].pop(0)
            self.chmap[ch] = (f"D{idx}", kind)
        sem, k0 = self.chmap[ch]
        assert k0 == kind, f"channel {ch} used from both SWDGE and HWDGE"
        return sem

    def emit(self, section, fns, reads=(), writes=(), dma=False, ch=None):
        if callable(fns):
            fns = [fns]
        if dma:
            assert ch is not None, "dma emit needs a channel"
            semname = self._sem_for(self._chan(ch, section), 16)
        else:
            semname = self._sem_for(self.SEM_OF[section], 1)
        deps = dict(self.floor)
        for b in reads:
            _merge(deps, b.w)
        for b in writes:
            _merge(deps, b.w)
            _merge(deps, b.r)
        if dma:
            prev = self.chprev.get(ch)
            if prev is not None:
                ptok, pwrites = prev
                _merge(deps, ptok)
                for pb in pwrites:
                    _merge(deps, pb.r)
        if section == "tensor" and not dma:
            deps.pop("T", None)
        inc = 16 if dma else 1
        self.cnt[semname] += inc
        tok = {semname: self.cnt[semname]}
        self.ops[section].append((fns, deps, semname, inc))
        for b in writes:
            b.w = dict(tok)
            b.r = {}
        for b in reads:
            _merge(b.r, tok)
        if dma:
            self.chprev[ch] = (dict(tok), list(writes))
        return tok

    def finalize(self, final_waits=None):
        nc = self.nc
        if final_waits is None:
            final_waits = dict(self.cnt)
        stack = ExitStack()
        sems = {s: stack.enter_context(nc.semaphore(f"sem_{s}")) for s in self.cnt}
        blk = stack.enter_context(nc.Block())

        def body(section):
            def run(eng):
                waited = {}
                for fns, deps, semname, inc in self.ops[section]:
                    for ds in sorted(deps):
                        dv = deps[ds]
                        if dv > 0 and waited.get(ds, -1) < dv:
                            eng.wait_ge(sems[ds], dv)
                            waited[ds] = dv
                    inst = None
                    for fn in fns:
                        inst = fn(eng)
                    inst.then_inc(sems[semname], inc)
                if section == "sync":
                    for ds in sorted(final_waits):
                        dv = final_waits[ds]
                        if dv > 0 and waited.get(ds, -1) < dv:
                            eng.wait_ge(sems[ds], dv)
                            waited[ds] = dv
            return run

        blk.tensor(body("tensor"))
        blk.vector(body("vector"))
        blk.scalar(body("scalar"))
        blk.gpsimd(body("gpsimd"))
        blk.sync(body("sync"))
        stack.close()


class SB:
    _uid = [0]
    def __init__(self, nc):
        self.nc = nc
        self.stack = ExitStack()

    def alloc(self, shape, dtype, name=None):
        SB._uid[0] += 1
        name = name or f"sb_{SB._uid[0]}"
        return self.stack.enter_context(self.nc.sbuf_tensor(name, shape, dtype))

    def close(self):
        self.stack.close()


def cdiv(a, b):
    return (a + b - 1) // b


BIG_CONSTS = {"ABSD", "xq_c", "xq_s", "xk_c", "xk_s", "ropecos", "ropesin",
              "lmask", "cmask"}


def cst_lazy(c, sb, cst, name):
    d, shape, dt = cst["_lazy"][name]
    t = sb.alloc(list(shape), dt)
    c.emit("sync", lambda e, t=t, d=d: e.dma_start(t[:, :], d),
           reads=[], writes=[c.tb("cst")], dma=True, ch="CZ" + name)
    return t



def emit_prenorm(c, sb, cst, P, src, dst, T, TSn, mode="ln", gamma_col=None,
                 pname="pn", src_bf=False, beta_col=None, shift_half=None,
                 dst_f32=None):
    E = src.shape[0]
    EC = E // PD
    nslab = T // TSn
    assert not (src_bf and mode == "ln")
    xdt = BF16 if src_bf else F32
    xa = [sb.alloc([PD, EC * TSn], xdt) for _ in range(3)]
    sqa = [sb.alloc([PD, TSn], F32) for _ in range(2)]
    rows = [sb.alloc([1, 3 * TSn], F32) for _ in range(2)]
    xnb = [sb.alloc([PD, EC * TSn], BF16) for _ in range(2)]
    tnorm = [sb.alloc([PD, TSn], F32) for _ in range(2)]
    src_r = src.rearrange("(ci p) t -> p ci t", p=PD)
    dst_r = dst.rearrange("(ci p) t -> p ci t", p=PD)
    bcst = c.tb("cst")
    eps = 1e-5
    tail = None
    for i in range(nslab):
        j = i % 2
        ja = i % 3
        t0 = i * TSn
        psA = P[4 + j]
        psB = P[6 + j]
        bx = c.tb(pname + "x", ja)
        bsq = c.tb(pname + "sq", j)
        brow = c.tb(pname + "row", j)
        bxn = c.tb(pname + "xn", j)
        btn = c.tb(pname + "tn", j)
        bpA = c.tb(f"psum{4 + j}")
        bpB = c.tb(f"psum{6 + j}")
        EH = EC // 2
        if shift_half is None:
            for hf in range(2):
                c.emit("sync", lambda e, ja=ja, t0=t0, hf=hf: e.dma_start(
                    xa[ja][:, hf * EH * TSn:(hf + 1) * EH * TSn]
                    .rearrange("p (ci t) -> p ci t", ci=EH),
                    src_r[:, hf * EH:(hf + 1) * EH, t0:t0 + TSn]),
                    reads=[c.tb(src.name)], writes=[bx], dma=True,
                    ch=pname + f"L{ja}{hf}")
        else:
            Tb_ = shift_half
            c.emit("sync", lambda e, ja=ja, t0=t0: e.dma_start(
                xa[ja][:, 0:EH * TSn].rearrange("p (ci t) -> p ci t", ci=EH),
                src_r[:, 0:EH, t0:t0 + TSn]),
                reads=[c.tb(src.name)], writes=[bx], dma=True,
                ch=pname + f"L{ja}0")
            if t0 % Tb_ == 0:
                c.emit("sync", lambda e, ja=ja, t0=t0: e.dma_start(
                    xa[ja][:, EH * TSn:].rearrange(
                        "p (ci t) -> p ci t", ci=EH)[:, :, 1:TSn],
                    src_r[:, EH:, t0:t0 + TSn - 1]),
                    reads=[c.tb(src.name)], writes=[bx], dma=True,
                    ch=pname + f"L{ja}1")
                c.emit("vector", lambda e, ja=ja: e.memset(
                    xa[ja][:, :].rearrange("p (ci t) -> p ci t", ci=EC)
                    [:, EH:, 0:1], 0.0),
                    reads=[], writes=[bx])
            else:
                c.emit("sync", lambda e, ja=ja, t0=t0: e.dma_start(
                    xa[ja][:, EH * TSn:].rearrange("p (ci t) -> p ci t", ci=EH),
                    src_r[:, EH:, t0 - 1:t0 + TSn - 1]),
                    reads=[c.tb(src.name)], writes=[bx], dma=True,
                    ch=pname + f"L{ja}1")
        if mode == "ln":
            for ci in range(EC):
                c.emit("tensor", lambda e, ja=ja, ci=ci, psA=psA: e.matmul(
                    psA[:1, 0:TSn], cst["ones_col_f"][:, 0:1],
                    xa[ja][:, ci * TSn:(ci + 1) * TSn],
                    start=(ci == 0), stop=(ci == EC - 1)),
                    reads=[bx, bcst], writes=[bpA])
        for ci in range(EC):
            c.emit("scalar", lambda e, ja=ja, j=j, ci=ci: e.square(
                sqa[j][:, :], xa[ja][:, ci * TSn:(ci + 1) * TSn]),
                reads=[bx], writes=[bsq])
            c.emit("tensor", lambda e, j=j, ci=ci, psB=psB: e.matmul(
                psB[:1, 0:TSn], cst["ones_col_f"][:, 0:1], sqa[j][:, :],
                start=(ci == 0), stop=(ci == EC - 1)),
                reads=[bsq, bcst], writes=[bpB])
        A = rows[j][:1, 0:TSn]
        Brow = rows[j][:1, TSn:2 * TSn]
        TMP = rows[j][:1, 2 * TSn:3 * TSn]
        if mode == "ln":
            c.emit("vector", lambda e, Brow=Brow, psA=psA: e.tensor_scalar(
                Brow, psA[:1, 0:TSn], -1.0 / E, None, ALU.mult),
                reads=[bpA], writes=[brow])
            c.emit("vector", lambda e, Brow=Brow, TMP=TMP: e.tensor_tensor(
                TMP, Brow, Brow, ALU.mult),
                reads=[brow], writes=[brow])
            c.emit("vector", lambda e, TMP=TMP, psB=psB: e.scalar_tensor_tensor(
                TMP, psB[:1, 0:TSn], 1.0 / E, TMP, ALU.mult, ALU.subtract),
                reads=[bpB, brow], writes=[brow])
            c.emit("scalar", lambda e, TMP=TMP: e.activation(
                TMP, TMP, AF.Ln, bias=np.float32(eps)),
                reads=[brow], writes=[brow])
            c.emit("scalar", lambda e, A=A, TMP=TMP: e.activation(
                A, TMP, AF.Exp, scale=-0.5),
                reads=[brow], writes=[brow])
        else:
            c.emit("vector", lambda e, TMP=TMP, psB=psB: e.tensor_scalar(
                TMP, psB[:1, 0:TSn], 1e-24, None, ALU.max),
                reads=[bpB], writes=[brow])
            c.emit("scalar", lambda e, TMP=TMP: e.activation(
                TMP, TMP, AF.Ln, scale=np.float32(1.0 / E)),
                reads=[brow], writes=[brow])
            c.emit("scalar", lambda e, A=A, TMP=TMP: e.activation(
                A, TMP, AF.Exp, scale=-0.5),
                reads=[brow], writes=[brow])
        if tail is not None:
            tail()

        def mk_tail(i=i, j=j, ja=ja, t0=t0, psA=psA, psB=psB, bx=bx, brow=brow,
                    bpA=bpA, bpB=bpB, bxn=bxn, btn=btn, A=A, Brow=Brow):
            def run():
                c.emit("tensor", lambda e: e.matmul(
                    psA[:, 0:TSn], cst["ones_row_f"][:1, :], A,
                    start=True, stop=True),
                    reads=[brow, bcst], writes=[bpA])
                if mode == "ln":
                    c.emit("tensor", lambda e: e.matmul(
                        psB[:, 0:TSn], cst["ones_row_f"][:1, :], Brow,
                        start=True, stop=True),
                        reads=[brow, bcst], writes=[bpB])
                for ci in range(EC):
                    if mode == "ln":
                        c.emit("vector", lambda e, ci=ci: e.tensor_tensor(
                            tnorm[j][:, :], xa[ja][:, ci * TSn:(ci + 1) * TSn],
                            psB[:, 0:TSn], ALU.add),
                            reads=[bx, bpB], writes=[btn])
                        if gamma_col is None:
                            c.emit("vector", lambda e, ci=ci:
                                   e.tensor_tensor(
                                xnb[j][:, ci * TSn:(ci + 1) * TSn],
                                tnorm[j][:, :], psA[:, 0:TSn], ALU.mult),
                                reads=[btn, bpA], writes=[bxn])
                        else:
                            c.emit("vector", lambda e, ci=ci:
                                   e.tensor_tensor(
                                tnorm[j][:, :], tnorm[j][:, :], psA[:, 0:TSn],
                                ALU.mult),
                                reads=[btn, bpA], writes=[btn])
                            c.emit("vector", lambda e, ci=ci: e.tensor_scalar(
                                xnb[j][:, ci * TSn:(ci + 1) * TSn],
                                tnorm[j][:, :],
                                gamma_col[:, ci:ci + 1], beta_col[:, ci:ci + 1],
                                ALU.mult, ALU.add),
                                reads=[btn, c.tb("cst")], writes=[bxn])
                    else:
                        c.emit("vector", lambda e, ci=ci:
                               e.scalar_tensor_tensor(
                            xnb[j][:, ci * TSn:(ci + 1) * TSn],
                            xa[ja][:, ci * TSn:(ci + 1) * TSn],
                            gamma_col[:, ci:ci + 1], psA[:, 0:TSn],
                            ALU.mult, ALU.mult),
                            reads=[bx, bpA, bcst], writes=[bxn])
                c.emit("scalar", lambda e: e.dma_start(
                    dst_r[:, :, t0:t0 + TSn],
                    xnb[j][:, :].rearrange("p (ci t) -> p ci t", ci=EC)),
                    reads=[bxn], writes=[c.tb(dst.name, ("sl", i))], dma=True,
                    ch=pname + f"S{j}")
                if dst_f32 is not None:
                    c.emit("gpsimd", lambda e: e.dma_start(
                        dst_f32.rearrange("(ci p) t -> p ci t", p=PD)
                        [:, :, t0:t0 + TSn],
                        xnb[j][:, :].rearrange("p (ci t) -> p ci t", ci=EC)),
                        reads=[bxn], writes=[c.tb(dst_f32.name, ("sl", i))],
                        dma=True, ch=pname + f"F{j}")
            return run
        tail = mk_tail()
    tail()



WOB = 512


def emit_proj_wload(c, sb, W, *, pname):
    din, dout = W.shape
    ncin = cdiv(din, PD)
    nob = cdiv(dout, WOB)
    wsb = sb.alloc([PD, ncin * dout], BF16)
    stg = [sb.alloc([PD, WOB], F32) for _ in range(4)]
    bws = [[c.tb(pname + "w", (ci, ob)) for ob in range(nob)]
           for ci in range(ncin)]
    k = 0
    for ob in range(nob):
        o0, o1 = ob * WOB, min((ob + 1) * WOB, dout)
        for ci in range(ncin):
            r0, r1 = ci * PD, min((ci + 1) * PD, din)
            j = k % 4
            k += 1
            bst = c.tb(pname + "wst", j)
            c.emit("sync", lambda e, j=j, r0=r0, r1=r1, o0=o0, o1=o1:
                   e.dma_start(stg[j][0:r1 - r0, 0:o1 - o0], W[r0:r1, o0:o1]),
                reads=[c.tb(W.name)], writes=[bst], dma=True,
                ch=pname + f"W{j % 2}")
            c.emit("vector", lambda e, j=j, ci=ci, r0=r0, r1=r1, o0=o0, o1=o1:
                   e.tensor_copy(wsb[0:r1 - r0, ci * dout + o0:ci * dout + o1],
                                 stg[j][0:r1 - r0, 0:o1 - o0]),
                reads=[bst], writes=[bws[ci][ob]])
    return wsb, bws


def emit_proj(c, sb, cst, P, xn, W, out, *, T, TSn, bias_col=None, act=None,
              out_dt=BF16, res=None, res_out=None, mask_sb=None, mask_tb=None,
              banks=(0, 1, 2, 3), pname="pj", shift_half=None, res2=None,
              also_bf=None, wpre=None, geglu_hid=None):
    din, dout = W.shape
    ncin = cdiv(din, PD)
    nco = cdiv(dout, PD)
    if wpre is None:
        wsb, bws = emit_proj_wload(c, sb, W, pname=pname)
    else:
        wsb, bws = wpre
    ggb = ([sb.alloc([PD, TSn], F32) for _ in range(2)]
           if geglu_hid is not None else None)
    NXH = 2 if ncin * dout * 2 > 48 * 1024 else 3
    xh = [sb.alloc([PD, ncin * TSn], BF16) for _ in range(NXH)]
    ot = [sb.alloc([PD, TSn], out_dt) for _ in range(4)]
    rt = [sb.alloc([PD, TSn], F32) for _ in range(2)]
    t1 = [sb.alloc([PD, TSn], F32) for _ in range(2)]
    nslab = T // TSn
    nbk = len(banks)
    k = 0
    for s in range(nslab):
        t0 = s * TSn
        js = s % NXH
        bxh = c.tb(pname + "xh", js)
        for ci in range(ncin):
            r0, r1 = ci * PD, min((ci + 1) * PD, din)
            shifted = shift_half is not None and ci >= ncin // 2
            if not shifted:
                c.emit("sync", lambda e, ci=ci, r0=r0, r1=r1, js=js, t0=t0:
                       e.dma_start(xh[js][0:r1 - r0, ci * TSn:(ci + 1) * TSn],
                                   xn[r0:r1, t0:t0 + TSn]),
                    reads=[c.tb(xn.name)], writes=[bxh], dma=True,
                    ch=pname + f"X{js}{ci % 2}")
            elif t0 % shift_half == 0:
                c.emit("sync", lambda e, ci=ci, r0=r0, r1=r1, js=js, t0=t0:
                       e.dma_start(
                    xh[js][0:r1 - r0, ci * TSn + 1:(ci + 1) * TSn],
                    xn[r0:r1, t0:t0 + TSn - 1]),
                    reads=[c.tb(xn.name)], writes=[bxh], dma=True,
                    ch=pname + f"X{js}{ci % 2}")
                c.emit("vector", lambda e, ci=ci, r0=r0, r1=r1, js=js: e.memset(
                    xh[js][0:r1 - r0, ci * TSn:ci * TSn + 1], 0.0),
                    reads=[], writes=[bxh])
            else:
                c.emit("sync", lambda e, ci=ci, r0=r0, r1=r1, js=js, t0=t0:
                       e.dma_start(xh[js][0:r1 - r0, ci * TSn:(ci + 1) * TSn],
                                   xn[r0:r1, t0 - 1:t0 + TSn - 1]),
                    reads=[c.tb(xn.name)], writes=[bxh], dma=True,
                    ch=pname + f"X{js}{ci % 2}")
        if geglu_hid is not None:
            ncoh = cdiv(geglu_hid, PD)
            off = 0
            for oc in range(ncoh):
                mc = min(PD, geglu_hid - oc * PD)
                offA, offG = off, off + mc
                off += 2 * mc
                jr = oc % 2
                ko = k % 4
                k += 1
                bkA, bkG = banks[(2 * oc) % 4], banks[(2 * oc + 1) % 4]
                psA2, psG = P[bkA], P[bkG]
                bpsA, bpsG = c.tb(f"psum{bkA}"), c.tb(f"psum{bkG}")
                bgg = c.tb(pname + "gg", jr)
                bo = c.tb(pname + "ot", ko)
                for o0w, ps, bps in ((offA, psA2, bpsA), (offG, psG, bpsG)):
                    mms = [
                        (lambda e, ci=ci, js=js, first=(ci == 0),
                         last=(ci == ncin - 1), mc=mc, o0w=o0w, ps=ps:
                         e.matmul(
                            ps[0:mc, 0:TSn],
                            wsb[0:min(PD, din - ci * PD),
                                ci * dout + o0w:ci * dout + o0w + mc],
                            xh[js][0:min(PD, din - ci * PD),
                                   ci * TSn:(ci + 1) * TSn],
                            start=first, stop=last))
                        for ci in range(ncin)]
                    wtbs = [bws[ci][b] for ci in range(ncin)
                            for b in sorted({o0w // WOB, (o0w + mc - 1) // WOB})]
                    c.emit("tensor", mms, reads=wtbs + [bxh], writes=[bps])
                c.emit("scalar", lambda e, mc=mc, oc=oc, jr=jr, psG=psG:
                       e.activation(
                    ggb[jr][0:mc, :], psG[0:mc, 0:TSn], AF.Gelu,
                    bias=bias_col[0:mc, 2 * oc + 1:2 * oc + 2]),
                    reads=[bpsG, c.tb("cst")], writes=[bgg])
                c.emit("vector", lambda e, mc=mc, oc=oc, jr=jr, ko=ko,
                       psA2=psA2: e.scalar_tensor_tensor(
                    ot[ko][0:mc, :], psA2[0:mc, 0:TSn],
                    bias_col[0:mc, 2 * oc:2 * oc + 1], ggb[jr][0:mc, :],
                    ALU.add, ALU.mult),
                    reads=[bpsA, bgg, c.tb("cst")], writes=[bo])
                c.emit("scalar", lambda e, ko=ko, mc=mc, oc=oc, t0=t0:
                       e.dma_start(out[oc * PD:oc * PD + mc, t0:t0 + TSn],
                                   ot[ko][0:mc, :]),
                    reads=[bo], writes=[c.tb(out.name, ("t", s, oc))],
                    dma=True, ch=pname + f"S{ko}")
            continue
        for oc in range(nco):
            o0, o1 = oc * PD, min((oc + 1) * PD, dout)
            mc = o1 - o0
            bk = banks[k % nbk]
            jr = k % 2
            ko = k % 4
            k += 1
            ps = P[bk]
            bps = c.tb(f"psum{bk}")
            mms = [
                (lambda e, ci=ci, js=js, first=(ci == 0), last=(ci == ncin - 1),
                 mc=mc, o0=o0, ps=ps: e.matmul(
                    ps[0:mc, 0:TSn],
                    wsb[0:min(PD, din - ci * PD), ci * dout + o0:ci * dout + o0 + mc],
                    xh[js][0:min(PD, din - ci * PD), ci * TSn:(ci + 1) * TSn],
                    start=first, stop=last))
                for ci in range(ncin)]
            c.emit("tensor", mms,
                   reads=[bws[ci][o0 // WOB] for ci in range(ncin)] + [bxh],
                   writes=[bps])
            fn = act if act is not None else AF.Identity
            bias_ap = bias_col[0:mc, oc:oc + 1] if bias_col is not None else 0.0
            if res is None and mask_sb is None:
                bo = c.tb(pname + "ot", ko)
                c.emit("scalar", lambda e, ps=ps, mc=mc, fn=fn, bias_ap=bias_ap,
                       ko=ko: e.activation(
                    ot[ko][0:mc, :], ps[0:mc, 0:TSn], fn, bias=bias_ap),
                    reads=[bps, c.tb("cst")], writes=[bo])
                c.emit("scalar", lambda e, ko=ko, mc=mc, o0=o0, o1=o1, t0=t0:
                       e.dma_start(out[o0:o1, t0:t0 + TSn], ot[ko][0:mc, :]),
                    reads=[bo], writes=[c.tb(out.name, ("t", s, oc))], dma=True,
                    ch=pname + f"S{ko}")
            else:
                bt1 = c.tb(pname + "t1", jr)
                c.emit("scalar", lambda e, ps=ps, mc=mc, fn=fn, bias_ap=bias_ap,
                       jr=jr: e.activation(
                    t1[jr][0:mc, :], ps[0:mc, 0:TSn], fn, bias=bias_ap),
                    reads=[bps, c.tb("cst")], writes=[bt1])
                if mask_sb is not None:
                    c.emit("vector", lambda e, jr=jr, mc=mc, t0=t0: e.tensor_tensor(
                        t1[jr][0:mc, :], t1[jr][0:mc, :], mask_sb[0:mc, t0:t0 + TSn],
                        ALU.mult),
                        reads=[bt1, mask_tb], writes=[bt1])
                if res is not None:
                    brt = c.tb(pname + "rt", jr)
                    c.emit("sync", lambda e, jr=jr, o0=o0, o1=o1, t0=t0: e.dma_start(
                        rt[jr][0:o1 - o0, :], res[o0:o1, t0:t0 + TSn]),
                        reads=[c.tb(res.name)], writes=[brt], dma=True,
                        ch=pname + f"R{jr}")
                    bo2 = c.tb(pname + "o2", ko)
                    c.emit("vector", lambda e, jr=jr, mc=mc, ko=ko: e.tensor_tensor(
                        ot[ko][0:mc, :], t1[jr][0:mc, :], rt[jr][0:mc, :], ALU.add),
                        reads=[bt1, brt], writes=[bo2])
                    dst = res_out if res_out is not None else out
                    c.emit("scalar", lambda e, ko=ko, mc=mc, o0=o0, o1=o1, t0=t0:
                           e.dma_start(dst[o0:o1, t0:t0 + TSn], ot[ko][0:mc, :]),
                        reads=[bo2], writes=[c.tb(dst.name, ("t", s, oc))],
                        dma=True, ch=pname + f"S{ko}")
                    if also_bf is not None:
                        c.emit("gpsimd", lambda e, ko=ko, mc=mc, o0=o0, o1=o1,
                               t0=t0: e.dma_start(
                            also_bf[o0:o1, t0:t0 + TSn], ot[ko][0:mc, :]),
                            reads=[bo2],
                            writes=[c.tb(also_bf.name, ("t", s, oc))],
                            dma=True, ch=pname + f"B{ko % 2}")
                else:
                    c.emit("scalar", lambda e, jr=jr, mc=mc, o0=o0, o1=o1, t0=t0:
                           e.dma_start(out[o0:o1, t0:t0 + TSn], t1[jr][0:mc, :]),
                        reads=[bt1], writes=[c.tb(out.name, ("t", s, oc))],
                        dma=True, ch=pname + f"St{jr}")


def emit_vtm_proj(c, sb, cst, P, xn, W, out, *, T, TSn, bias_row=None,
                  pname="vt", banks=(0, 1), shift_half=None, wpre=None):
    din, dout = W.shape
    ncin = din // PD
    ktn = T // PD
    nno = cdiv(dout, TSn)
    if wpre is None:
        wsb, bws = emit_proj_wload(c, sb, W, pname=pname)
    else:
        wsb, bws = wpre
    brow = None
    if bias_row is not None:
        bsb = sb.alloc([1, dout], BF16)
        brow = c.tb(pname + "b")
        c.emit("gpsimd", lambda e: e.dma_start(bsb[:1, :], bias_row[:1, :]),
               reads=[c.tb(bias_row.name)], writes=[brow], dma=True,
               ch=pname + "W")
    xk = [sb.alloc([PD, ncin * PD], BF16) for _ in range(2)]
    ot = [sb.alloc([PD, dout], BF16) for _ in range(2)]
    xn_r = xn.rearrange("(ci p) t -> p ci t", p=PD)
    for kt in range(ktn):
        j = kt % 2
        bxk = c.tb(pname + "xk", j)
        bot = c.tb(pname + "ot", j)
        nl = ncin // 2 if shift_half is not None else ncin
        c.emit("sync", lambda e, kt=kt, j=j, nl=nl: e.dma_start(
            xk[j][:, 0:nl * PD].rearrange("p (ci q) -> p ci q", ci=nl),
            xn_r[:, 0:nl, kt * PD:(kt + 1) * PD]),
            reads=[c.tb(xn.name)], writes=[bxk], dma=True, ch=pname + f"X{j}")
        if shift_half is not None:
            t0 = kt * PD
            if t0 % shift_half == 0:
                c.emit("sync", lambda e, j=j, nl=nl, t0=t0: e.dma_start(
                    xk[j][:, nl * PD:].rearrange(
                        "p (ci q) -> p ci q", ci=nl)[:, :, 1:PD],
                    xn_r[:, nl:, t0:t0 + PD - 1]),
                    reads=[c.tb(xn.name)], writes=[bxk], dma=True,
                    ch=pname + f"X{j}")
                c.emit("vector", lambda e, j=j, nl=nl: e.memset(
                    xk[j][:, :].rearrange("p (ci q) -> p ci q", ci=ncin)
                    [:, nl:, 0:1], 0.0),
                    reads=[], writes=[bxk])
            else:
                c.emit("sync", lambda e, j=j, nl=nl, t0=t0: e.dma_start(
                    xk[j][:, nl * PD:].rearrange("p (ci q) -> p ci q", ci=nl),
                    xn_r[:, nl:, t0 - 1:t0 + PD - 1]),
                    reads=[c.tb(xn.name)], writes=[bxk], dma=True,
                    ch=pname + f"X{j}")
        for no in range(nno):
            n0, n1 = no * TSn, min((no + 1) * TSn, dout)
            bk = banks[(kt * nno + no) % len(banks)]
            ps = P[bk]
            bps = c.tb(f"psum{bk}")
            mms = [
                (lambda e, ci=ci, j=j, n0=n0, n1=n1, ps=ps, first=(ci == 0):
                 e.matmul(
                    ps[:, 0:n1 - n0], xk[j][:, ci * PD:(ci + 1) * PD],
                    wsb[:, ci * dout + n0:ci * dout + n1],
                    start=first, stop=(bias_row is None and ci == ncin - 1)))
                for ci in range(ncin)]
            if bias_row is not None:
                mms.append(lambda e, n0=n0, n1=n1, ps=ps: e.matmul(
                    ps[:, 0:n1 - n0], cst["ones_row_bf"][:1, 0:PD],
                    bsb[:1, n0:n1], start=False, stop=True))
            rd = ([bws[ci][n0 // WOB] for ci in range(ncin)] + [bxk]
                  + ([brow] if brow is not None else []))
            c.emit("tensor", mms, reads=rd, writes=[bps])
            c.emit("vector", lambda e, j=j, ps=ps, n0=n0, n1=n1: e.tensor_copy(
                ot[j][:, n0:n1], ps[:, 0:n1 - n0]),
                reads=[bps], writes=[bot])
        c.emit("scalar", lambda e, kt=kt, j=j: e.dma_start(
            out[kt * PD:(kt + 1) * PD, 0:dout], ot[j][:, :]),
            reads=[bot], writes=[c.tb(out.name, ("kt", kt))], dma=True,
            ch=pname + f"S{j}")



def emit_attn(c, sb, cst, P, *, qf, kf, vtm, out, Tb, nb, Hq, hdq, hdv,
              TQ, scale, Cshift, causal, decay=False, pname="at", local=False):
    HK = Hq * hdq
    KC = cdiv(HK, PD)
    Vw = Hq * (hdv + 1)
    ktn = Tb // PD
    nqb = Tb // TQ
    ksb = sb.alloc([PD, KC * Tb], BF16)
    vsb = sb.alloc([PD, ktn * Vw], BF16)
    qsb = [sb.alloc([PD, KC * TQ], BF16) for _ in range(2)]
    NPB = 4
    pbuf = [sb.alloc([PD, TQ], BF16) for _ in range(NPB)]
    sdec = [sb.alloc([PD, TQ], F32) for _ in range(2)]
    rrow = [sb.alloc([hdv + 1, TQ], F32) for _ in range(2)]
    rr2 = [sb.alloc([hdv + 1, TQ], F32) for _ in range(2)]
    rb = [sb.alloc([PD, TQ], F32) for _ in range(2)]
    obuf = [sb.alloc([PD, TQ], BF16) for _ in range(2)]
    absd_t = cst_lazy(c, sb, cst, "ABSD") if decay else None
    cmask_t = cst_lazy(c, sb, cst, "cmask") if causal else None
    lmask_t = cst_lazy(c, sb, cst, "lmask") if local else None
    pdbg = cst.get("_pdbg")
    SNM = [0, 1, 4, 7] if decay else [0, 1, 4]
    NSB = len(SNM)
    POB, PON = [P[2], P[5]], [2, 5]
    PBB, PBN = [P[3], P[6]], [3, 6]
    decT = None
    if decay:
        Wd = absd_t.shape[1]
        decT = sb.alloc([PD, Wd], F32)
        bdecT = c.tb(pname + "decT")
        for u0 in range(0, Wd, 512):
            u1 = min(u0 + 512, Wd)
            c.emit("scalar", lambda e, u0=u0, u1=u1: e.activation(
                decT[:, u0:u1], absd_t[:, u0:u1], AF.Exp,
                scale=cst["negabsdd"][:, 0:1]),
                reads=[c.tb("cst")], writes=[bdecT])
    for b in range(nb):
        tb0 = b * Tb
        bk_ = c.tb(pname + "k", b)
        bv_ = c.tb(pname + "v", b)
        for ci in range(KC):
            r0, r1 = ci * PD, min((ci + 1) * PD, HK)
            c.emit("sync", lambda e, ci=ci, r0=r0, r1=r1, tb0=tb0, ksb=ksb:
                   e.dma_start(
                ksb[0:r1 - r0, ci * Tb:(ci + 1) * Tb], kf[r0:r1, tb0:tb0 + Tb]),
                reads=[c.tb(kf.name)], writes=[bk_], dma=True,
                ch=pname + f"K{ci % 4}")
        for kt in range(ktn):
            c.emit("sync", lambda e, kt=kt, tb0=tb0, vsb=vsb: e.dma_start(
                vsb[:, kt * Vw:(kt + 1) * Vw],
                vtm[tb0 + kt * PD:tb0 + (kt + 1) * PD, 0:Vw]),
                reads=[c.tb(vtm.name)], writes=[bv_], dma=True,
                ch=pname + f"V{kt % 4}")
        pend = None
        for qb in range(nqb):
            q0 = tb0 + qb * TQ
            jq = qb % 2
            bq_ = c.tb(pname + "q", (b, jq))
            for ci in range(KC):
                r0, r1 = ci * PD, min((ci + 1) * PD, HK)
                c.emit("sync", lambda e, ci=ci, r0=r0, r1=r1, q0=q0, jq=jq:
                       e.dma_start(qsb[jq][0:r1 - r0, ci * TQ:(ci + 1) * TQ],
                                   qf[r0:r1, q0:q0 + TQ]),
                    reads=[c.tb(qf.name)], writes=[bq_], dma=True,
                    ch=pname + f"Q{jq}{ci % 2}")
            for h in range(Hq):
                kchunk, kbase = divmod(h * hdq, PD)
                nu = TQ // PD
                if local:
                    kt_beg = max(0, (qb - 1) * nu)
                    kt_lim = min(ktn, (qb + 1) * nu)
                else:
                    kt_beg = 0
                    kt_lim = ktn if not causal else min(ktn, ((qb + 1) * TQ) // PD)
                ndiag = nu if causal else 0
                kts = list(range(kt_beg, kt_lim))
                n = len(kts)
                jh = h % 2
                po = POB[jh]
                bo_ps = c.tb(pname + "pso", jh)

                def mk_S(i, jq=jq, bq_=bq_, bk_=bk_, kchunk=kchunk,
                         kbase=kbase, kts=kts):
                    kt = kts[i]
                    js = P[SNM[i % NSB]]
                    bjs = c.tb(f"psum{SNM[i % NSB]}")
                    c.emit("tensor", lambda e, kt=kt, jq=jq, js=js,
                           kchunk=kchunk, kbase=kbase: e.matmul(
                        js[:, 0:TQ],
                        ksb[kbase:kbase + hdq,
                            kchunk * Tb + kt * PD:kchunk * Tb + (kt + 1) * PD],
                        qsb[jq][kbase:kbase + hdq,
                                kchunk * TQ:(kchunk + 1) * TQ],
                        start=True, stop=True, tile_position=(kbase, 0)),
                        reads=[bk_, bq_], writes=[bjs])

                for i in range(n):
                    kt = kts[i]
                    if i == 0:
                        mk_S(0)
                        if n > 1:
                            mk_S(1)
                    jp = i % NPB
                    js = P[SNM[i % NSB]]
                    bjs = c.tb(f"psum{SNM[i % NSB]}")
                    bpb = c.tb(pname + "pb", jp)
                    is_diag = causal and kt >= kt_lim - ndiag
                    if decay:
                        jd = i % 2
                        bsd = c.tb(pname + "sd", jd)
                        s0 = (Tb - PD) - (kt * PD - qb * TQ)
                        c.emit("vector", lambda e, js=js, jd=jd, s0=s0:
                               e.scalar_tensor_tensor(
                            sdec[jd][:, :], js[:, 0:TQ], scale,
                            decT[:, s0:s0 + TQ], ALU.mult, ALU.mult),
                            reads=[bjs, bdecT], writes=[bsd])
                        c.emit("scalar", lambda e, jd=jd, jp=jp: e.activation(
                            pbuf[jp][:, :], sdec[jd][:, :], AF.Exp,
                            bias=np.float32(-Cshift)),
                            reads=[bsd], writes=[bpb])
                    else:
                        c.emit("scalar", lambda e, js=js, jp=jp: e.activation(
                            pbuf[jp][:, :], js[:, 0:TQ], AF.Exp,
                            bias=np.float32(-Cshift), scale=scale),
                            reads=[bjs], writes=[bpb])
                    if local:
                        u = nu * qb - kt
                        mi = u + nu - 1
                        c.emit("vector", lambda e, jp=jp, mi=mi: e.tensor_tensor(
                            pbuf[jp][:, :], pbuf[jp][:, :],
                            lmask_t[:, mi * TQ:(mi + 1) * TQ], ALU.mult),
                            reads=[bpb, c.tb("cst")], writes=[bpb])
                    elif is_diag:
                        rel = kt - (qb * TQ) // PD
                        c.emit("vector", lambda e, jp=jp, rel=rel: e.tensor_tensor(
                            pbuf[jp][:, :], pbuf[jp][:, :],
                            cmask_t[:, rel * TQ:(rel + 1) * TQ], ALU.mult),
                            reads=[bpb, c.tb("cst")], writes=[bpb])
                    if i + 2 < n:
                        mk_S(i + 2)
                    c.emit("tensor", lambda e, kt=kt, jp=jp, h=h, po=po,
                           first=(i == 0), last=(i == n - 1):
                           e.matmul(
                        po[0:hdv + 1, 0:TQ],
                        vsb[:, kt * Vw + h * (hdv + 1):kt * Vw + (h + 1) * (hdv + 1)],
                        pbuf[jp][:, :], start=first, stop=last),
                        reads=[bv_, bpb], writes=[bo_ps])
                    if pdbg is not None and b == 0 and qb <= 1 and h == 0:
                        c.emit("scalar", lambda e, kt=kt, jp=jp, qb=qb:
                               e.dma_start(
                            pdbg[kt * PD:(kt + 1) * PD,
                                 qb * TQ:qb * TQ + TQ], pbuf[jp][:, :]),
                            reads=[bpb], writes=[c.tb("s_pdbg", (qb, kt))],
                            dma=True, ch=pname + "PD")
                    if i == 1 and pend is not None:
                        pend()
                        pend = None
                if pend is not None:
                    pend()
                    pend = None

                def mk_tail(jh=jh, po=po, bo_ps=bo_ps, h=h, q0=q0, b=b, qb=qb):
                    def tail():
                        brr = c.tb(pname + "rr", jh)
                        c.emit("vector", lambda e: e.tensor_scalar(
                            rrow[jh][0:hdv + 1, :], po[0:hdv + 1, 0:TQ], 1e-30,
                            None, ALU.max),
                            reads=[bo_ps], writes=[brr])
                        if pdbg is not None and b == 0 and qb == 0 and h == 0:
                            c.emit("gpsimd", lambda e: e.dma_start(
                                pdbg[1024:1089, 0:TQ], rrow[jh][0:hdv + 1, :]),
                                reads=[brr], writes=[c.tb("s_pdbg", "rr")],
                                dma=True, ch=pname + "PR")
                        brr2 = c.tb(pname + "rr2", jh)
                        c.emit("scalar", lambda e: e.activation(
                            rr2[jh][0:hdv + 1, :], rrow[jh][0:hdv + 1, :],
                            AF.Ln),
                            reads=[brr], writes=[brr2])
                        c.emit("scalar", lambda e: e.activation(
                            rr2[jh][0:hdv + 1, :], rr2[jh][0:hdv + 1, :],
                            AF.Exp, scale=-1.0),
                            reads=[brr2], writes=[brr2])
                        bps3 = c.tb(f"psum{PBN[jh]}")
                        c.emit("tensor", lambda e: e.matmul(
                            PBB[jh][0:hdv, 0:TQ], cst["sel64"][0:hdv + 1, 0:hdv],
                            rr2[jh][0:hdv + 1, :], start=True, stop=True),
                            reads=[brr2, c.tb("cst")], writes=[bps3])
                        brb = c.tb(pname + "rb", jh)
                        c.emit("scalar", lambda e: e.copy(
                            rb[jh][0:hdv, :], PBB[jh][0:hdv, 0:TQ]),
                            reads=[bps3], writes=[brb])
                        bob = c.tb(pname + "ob", jh)
                        c.emit("vector", lambda e: e.tensor_tensor(
                            obuf[jh][0:hdv, :], po[0:hdv, 0:TQ], rb[jh][0:hdv, :],
                            ALU.mult),
                            reads=[bo_ps, brb], writes=[bob])
                        c.emit("scalar", lambda e: e.dma_start(
                            out[h * hdv:(h + 1) * hdv, q0:q0 + TQ],
                            obuf[jh][0:hdv, :]),
                            reads=[bob], writes=[c.tb(out.name, ("o", b, qb, h))],
                            dma=True, ch=pname + f"S{jh}")
                    return tail
                pend = mk_tail()
        if pend is not None:
            pend()
            pend = None



def emit_negabs_scalar(c, sb, cst, P, src_dram, name):
    t = sb.alloc([1, 1], F32, name="nab_t")
    col = sb.alloc([PD, 1], F32, name="nab_c")
    bt = c.tb("nab", name)
    c.emit("sync", lambda e: e.dma_start(t[:1, :1], src_dram[:1, :1]),
           reads=[c.tb(src_dram.name)], writes=[bt], dma=True, ch="NA")
    c.emit("scalar", lambda e: e.activation(t[:1, :1], t[:1, :1], AF.Abs),
           reads=[bt], writes=[bt])
    c.emit("vector", lambda e: e.tensor_scalar(t[:1, :1], t[:1, :1], -1.0, None,
                                               ALU.mult),
           reads=[bt], writes=[bt])
    bp = c.tb("psum7")
    c.emit("tensor", lambda e: e.matmul(P[7][:, 0:1], cst["ones_row_f"][:1, :],
                                        t[:1, :1], start=True, stop=True),
           reads=[bt, c.tb("cst")], writes=[bp])
    c.emit("scalar", lambda e: e.copy(col[:, 0:1], P[7][:, 0:1]),
           reads=[bp], writes=[c.tb("cst")])
    cst[name] = col



def sl_l0_attn_a(c, P, cst, cfg, w, scr, res_in, res_out):
    nc = c.nc
    E, T, Tb, nb, TSn = cfg["E"], cfg["T"], cfg["Tb"], cfg["nb"], cfg["TS"]
    H = cfg["l0a_H"]
    hd = E // H
    sbk = SB(nc)
    sb = SB(nc)
    emit_negabs_scalar(c, sbk, cst, P, w["l0a_dd"], "negabsdd")
    xn = scr["xn"]
    emit_prenorm(c, sb, cst, P, res_in, xn, T, TSn, mode="ln", pname="aaln")
    c.fence(xn.name)
    sb = c.phase(sb)
    emit_proj(c, sb, cst, P, xn, w["l0a_wq"], scr["qf"], T=T, TSn=TSn,
              bias_col=cst["l0a_bq"], pname="aaq")
    sb = c.phase(sb)
    emit_proj(c, sb, cst, P, xn, w["l0a_wk"], scr["kf"], T=T, TSn=TSn,
              bias_col=cst["l0a_bk"], pname="aak")
    sb = c.phase(sb)
    emit_vtm_proj(c, sb, cst, P, xn, w["l0a_wve"], scr["vtm"], T=T, TSn=TSn,
                  bias_row=w["l0a_bve"], pname="aav")
    sb = c.phase(sb)
    TQ = min(512, Tb)
    emit_attn(c, sb, cst, P, qf=scr["qf"], kf=scr["kf"], vtm=scr["vtm"],
              out=scr["ao"], Tb=Tb, nb=nb, Hq=H, hdq=hd, hdv=hd, TQ=TQ,
              scale=1.0 / math.sqrt(hd), Cshift=cfg["C"], causal=False,
              decay=True, pname="aat")
    sb = c.phase(sb)
    emit_proj(c, sb, cst, P, scr["ao"], w["l0a_wo"], res_out, T=T, TSn=TSn,
              bias_col=cst["l0a_bo"], res=res_in, res_out=res_out,
              out_dt=F32, pname="aao")
    c.fence(res_out.name)
    c.global_barrier()
    sb.close()
    sbk.close()


def emit_rope_evens(c, sb, cst, src, dst, *, T, rows, coff, pname):
    RC = rows // PD
    rc_t = cst_lazy(c, sb, cst, "ropecos")
    rs_t = cst_lazy(c, sb, cst, "ropesin")
    TSn = min(512, T)
    a = [sb.alloc([PD, TSn], F32) for _ in range(2)]
    b_ = [sb.alloc([PD, TSn], F32) for _ in range(2)]
    t1 = [sb.alloc([PD, TSn], F32) for _ in range(2)]
    t2 = [sb.alloc([PD, TSn], F32) for _ in range(2)]
    ob = [sb.alloc([PD, TSn], BF16) for _ in range(2)]
    k = 0
    for s in range(T // TSn):
        t0 = s * TSn
        for ci in range(RC):
            j = k % 2
            k += 1
            ba = c.tb(pname + "a", j)
            bb = c.tb(pname + "b", j)
            bt1 = c.tb(pname + "t1", j)
            bt2 = c.tb(pname + "t2", j)
            bo = c.tb(pname + "o", j)
            r0 = coff * rows + ci * PD
            r1 = (coff + 1) * rows + ci * PD
            c.emit("sync", lambda e, j=j, r0=r0, t0=t0: e.dma_start(
                a[j][:, :], src[r0:r0 + PD, t0:t0 + TSn]),
                reads=[c.tb(src.name)], writes=[ba], dma=True,
                ch=pname + f"A{j}")
            c.emit("sync", lambda e, j=j, r1=r1, t0=t0: e.dma_start(
                b_[j][:, :], src[r1:r1 + PD, t0:t0 + TSn]),
                reads=[c.tb(src.name)], writes=[bb], dma=True,
                ch=pname + f"B{j}")
            c.emit("vector", lambda e, j=j, t0=t0: e.tensor_tensor(
                t1[j][:, :], a[j][:, :], rc_t[:, t0:t0 + TSn], ALU.mult),
                reads=[ba, c.tb("cst")], writes=[bt1])
            c.emit("vector", lambda e, j=j, t0=t0: e.tensor_tensor(
                t2[j][:, :], b_[j][:, :], rs_t[:, t0:t0 + TSn], ALU.mult),
                reads=[bb, c.tb("cst")], writes=[bt2])
            c.emit("vector", lambda e, j=j: e.tensor_tensor(
                ob[j][:, :], t1[j][:, :], t2[j][:, :], ALU.subtract),
                reads=[bt1, bt2], writes=[bo])
            c.emit("scalar", lambda e, j=j, ci=ci, t0=t0: e.dma_start(
                dst[ci * PD:(ci + 1) * PD, t0:t0 + TSn], ob[j][:, :]),
                reads=[bo], writes=[c.tb(dst.name, (s, ci))], dma=True,
                ch=pname + f"S{j}")


def sl_l1_attn_a(c, P, cst, cfg, w, scr, res_in, res_out):
    nc = c.nc
    E, T, Tb, nb, TSn = cfg["E"], cfg["T"], cfg["Tb"], cfg["nb"], cfg["TS"]
    H = E // 64
    hd = 64
    sb = SB(nc)
    xn = scr["xn"]
    emit_prenorm(c, sb, cst, P, res_in, xn, T, TSn, mode="ln", pname="baln")
    c.fence(xn.name)
    sb = c.phase(sb)
    emit_proj(c, sb, cst, P, xn, w["l1a_wq"], scr["qf"], T=T, TSn=TSn,
              pname="baq")
    sb = c.phase(sb)
    emit_proj(c, sb, cst, P, xn, w["l1a_wk"], scr["kf"], T=T, TSn=TSn,
              pname="bak")
    sb = c.phase(sb)
    emit_vtm_proj(c, sb, cst, P, xn, w["l1a_wve"], scr["vtm"], T=T, TSn=TSn,
                  bias_row=w["l1a_bve"], pname="bav")
    sb = c.phase(sb)
    TQ = min(512, Tb)
    emit_attn(c, sb, cst, P, qf=scr["qf"], kf=scr["kf"], vtm=scr["vtm"],
              out=scr["ao"], Tb=Tb, nb=nb, Hq=H, hdq=hd, hdv=hd, TQ=TQ,
              scale=1.0 / math.sqrt(hd), Cshift=cfg["C"], causal=True,
              decay=False, pname="bat")
    sb = c.phase(sb)
    emit_proj(c, sb, cst, P, scr["ao"], w["l1a_wo"], res_out, T=T, TSn=TSn,
              res=res_in, res_out=res_out, out_dt=F32, pname="bao")
    c.fence(res_out.name)
    c.global_barrier()
    sb.close()


def sl_l0_attn_b(c, P, cst, cfg, w, scr, res_in, res_out):
    nc = c.nc
    E, T, Tb, nb, TSn = cfg["E"], cfg["T"], cfg["Tb"], cfg["nb"], cfg["TS"]
    H = E // 64
    hdq, hdv = 32, 64
    sb = SB(nc)
    xn = scr["xn"]
    emit_prenorm(c, sb, cst, P, res_in, xn, T, TSn, mode="ln", pname="cbln")
    c.fence(xn.name)
    sb = c.phase(sb)
    emit_proj(c, sb, cst, P, xn, w["l0b_weo"], scr["qkro"], T=T, TSn=TSn,
              bias_col=cst["l0b_beo"], out_dt=F32, pname="cbp")
    sb = c.phase(sb)
    emit_rope_evens(c, sb, cst, scr["qkro"], scr["qf"], T=T, rows=E // 2,
                    coff=0, pname="cbrq")
    emit_rope_evens(c, sb, cst, scr["qkro"], scr["kf"], T=T, rows=E // 2,
                    coff=2, pname="cbrk")
    sb = c.phase(sb)
    emit_vtm_proj(c, sb, cst, P, xn, w["l0b_wve"], scr["vtm"], T=T, TSn=TSn,
                  bias_row=w["l0b_bve"], pname="cbv")
    sb = c.phase(sb)
    TQ = min(512, Tb)
    emit_attn(c, sb, cst, P, qf=scr["qf"], kf=scr["kf"], vtm=scr["vtm"],
              out=scr["ao"], Tb=Tb, nb=nb, Hq=H, hdq=hdq, hdv=hdv, TQ=TQ,
              scale=1.0 / math.sqrt(hdv), Cshift=cfg["C"], causal=True,
              decay=False, pname="cbt")
    sb = c.phase(sb)
    emit_proj(c, sb, cst, P, scr["ao"], w["l0b_wo"], res_out, T=T, TSn=TSn,
              bias_col=cst["l0b_bo"], res=res_in, res_out=res_out,
              out_dt=F32, pname="cbo")
    c.fence(res_out.name)
    c.global_barrier()
    sb.close()


def emit_routing(c, sb, cst, P, srow, mask_sb, cfg, pname="rt"):
    T, Tb, nb = cfg["T"], cfg["Tb"], cfg["nb"]
    CC = Tb // PD
    NC = nb * CC
    KH = float(cfg["nheavy"])
    logk = float(np.log(cfg["keff"]))
    assert NC <= 32 and Tb % 512 == 0 or True
    sr = sb.alloc([1, T], F32)
    s_tm = sb.alloc([PD, NC], F32)
    b_tm = sb.alloc([PD, NC], F32)
    t1 = sb.alloc([PD, NC], F32)
    e_tm = sb.alloc([PD, NC], F32)
    fold = sb.alloc([PD, 2], F32)
    foldB = sb.alloc([32, 32], F32)
    vt32 = sb.alloc([32, 32], F32)
    rows = sb.alloc([1, 16], F32)
    mcol = sb.alloc([PD, 1], F32)
    wrow = sb.alloc([1, T], F32)
    wb = sb.alloc([PD, Tb], F32)
    gtmp = sb.alloc([PD, Tb], F32)
    g_tm = sb.alloc([PD, NC], F32)
    pos_tm = sb.alloc([PD, NC], F32)
    m1 = sb.alloc([PD, NC], F32)
    cp = sb.alloc([1, 2 * NC], F32)
    mrow = sb.alloc([1, T], F32)
    b_ = {k: c.tb(pname + k) for k in
          ("sr", "stm", "btm", "t1", "etm", "fold", "foldB", "vt", "rows",
           "mcol", "wrow", "wb", "g", "pos", "m1", "cp", "mrow", "msb")}
    bcst = c.tb("cst")
    M = rows[:1, 0:1]
    NEGM = rows[:1, 1:2]
    LNS = rows[:1, 2:2 + nb]
    AROW = rows[:1, 8:8 + nb]
    TMPR = rows[:1, 12:12 + nb]
    c.emit("sync", lambda e: e.dma_start(sr[:1, :], srow[:1, :]),
           reads=[c.tb(srow.name)], writes=[b_["sr"]], dma=True, ch=pname + "L")
    c.emit("vector", lambda e: e.memset(foldB[:, :], 0.0),
           reads=[], writes=[b_["foldB"]])
    bps3 = c.tb("psum3")
    for col in range(NC):
        c.emit("tensor", lambda e, col=col: e.transpose(
            P[3][:, col:col + 1], sr[:1, col * PD:(col + 1) * PD],
            cst["ident_f"][0:1, 0:1]),
            reads=[b_["sr"], bcst], writes=[bps3])
    c.emit("vector", lambda e: e.tensor_copy(s_tm[:, :], P[3][:, 0:NC]),
           reads=[bps3], writes=[b_["stm"]])
    c.emit("vector", lambda e: e.tensor_scalar(
        b_tm[:, :], s_tm[:, :], -1.0, None, ALU.mult),
        reads=[b_["stm"]], writes=[b_["btm"]])
    apb = sb.alloc([PD, nb], F32)
    c.emit("vector", lambda e: e.memset(apb[:, :], 0.0),
           reads=[], writes=[b_["mcol"]])
    c.emit("vector", lambda e: e.memset(rows[:1, :], 0.0),
           reads=[], writes=[b_["rows"]])
    for it in range(50):
        c.emit("vector", lambda e: e.tensor_tensor(
            t1[:, :], s_tm[:, :], b_tm[:, :], ALU.add),
            reads=[b_["stm"], b_["btm"]], writes=[b_["t1"]])
        c.emit("vector", lambda e: e.tensor_tensor(
            e_tm[:, :].rearrange("p (b cc) -> p b cc", b=nb),
            t1[:, :].rearrange("p (b cc) -> p b cc", b=nb),
            apb[:, :].unsqueeze(2).broadcast_to([PD, nb, CC]), ALU.add),
            reads=[b_["t1"], b_["mcol"]], writes=[b_["etm"]])
        c.emit("scalar", lambda e: e.activation(
            e_tm[:, :], e_tm[:, :], AF.Exp),
            reads=[b_["etm"]], writes=[b_["etm"]])
        bps4 = c.tb("psum4")
        c.emit("tensor", lambda e: e.matmul(
            P[4][:1, 0:NC], cst["ones_col_f"][:, 0:1], e_tm[:, :],
            start=True, stop=True),
            reads=[b_["etm"], bcst], writes=[bps4])
        c.emit("vector", lambda e: e.tensor_reduce(
            LNS, P[4][:1, 0:NC].rearrange("p (b cc) -> p b cc", b=nb),
            AX.X, ALU.add),
            reads=[bps4], writes=[b_["rows"]])
        c.emit("scalar", lambda e: e.activation(LNS, LNS, AF.Ln),
            reads=[b_["rows"]], writes=[b_["rows"]])
        c.emit("vector", lambda e: e.tensor_tensor(
            TMPR, LNS, AROW, ALU.subtract),
            reads=[b_["rows"]], writes=[b_["rows"]])
        c.emit("vector", lambda e: e.tensor_scalar(
            AROW, TMPR, -1.0, float(logk), ALU.mult, ALU.add),
            reads=[b_["rows"]], writes=[b_["rows"]])
        bps2b = c.tb("psum2")
        c.emit("tensor", lambda e: e.matmul(
            P[2][:, 1:1 + nb], cst["ones_row_f"][:1, :], AROW, start=True,
            stop=True),
            reads=[b_["rows"], bcst], writes=[bps2b])
        c.emit("scalar", lambda e: e.copy(apb[:, :], P[2][:, 1:1 + nb]),
            reads=[bps2b], writes=[b_["mcol"]])
        c.emit("vector", lambda e: e.tensor_tensor(
            t1[:, :].rearrange("p (b cc) -> p b cc", b=nb),
            s_tm[:, :].rearrange("p (b cc) -> p b cc", b=nb),
            apb[:, :].unsqueeze(2).broadcast_to([PD, nb, CC]),
            ALU.add),
            reads=[b_["stm"], b_["mcol"]], writes=[b_["t1"]])
        if it < 49:
            c.emit("vector", lambda e: e.tensor_scalar(
                b_tm[:, :], t1[:, :], -1.0, 0.0, ALU.mult, ALU.min),
                reads=[b_["t1"]], writes=[b_["btm"]])
    c.emit("vector", lambda e: e.tensor_scalar(
        pos_tm[:, :], t1[:, :], 0.0, None, ALU.is_ge),
        reads=[b_["t1"]], writes=[b_["pos"]])
    bps5 = c.tb("psum5")
    c.emit("tensor", lambda e: e.matmul(
        P[5][:, 0:NC], cst["tri_lt"][:, :], pos_tm[:, :], start=True,
        stop=False),
        reads=[b_["pos"], bcst], writes=[bps5])
    bps4b = c.tb("psum4")
    c.emit("tensor", lambda e: e.matmul(
        P[4][:1, 32:32 + NC], cst["ones_col_f"][:, 0:1], pos_tm[:, :],
        start=True, stop=True),
        reads=[b_["pos"], bcst], writes=[bps4b])
    c.emit("vector", lambda e: e.tensor_copy(cp[:1, 0:NC],
                                             P[4][:1, 32:32 + NC]),
        reads=[bps4b], writes=[b_["cp"]])
    sh = 1
    while sh < CC:
        c.emit("vector", lambda e, sh=sh: e.tensor_tensor(
            cp[:1, 0:NC].rearrange("p (b cc) -> p b cc", b=nb)[:, :, sh:CC],
            cp[:1, 0:NC].rearrange("p (b cc) -> p b cc", b=nb)[:, :, sh:CC],
            cp[:1, 0:NC].rearrange("p (b cc) -> p b cc", b=nb)[:, :, 0:CC - sh],
            ALU.add),
            reads=[b_["cp"]], writes=[b_["cp"]])
        sh *= 2
    c.emit("vector", lambda e: e.tensor_tensor(
        cp[:1, NC:2 * NC], cp[:1, 0:NC], P[4][:1, 32:32 + NC], ALU.subtract),
        reads=[b_["cp"], bps4b], writes=[b_["cp"]])
    c.emit("tensor", lambda e: e.matmul(
        P[5][:, 0:NC], cst["ones_row_f"][:1, :], cp[:1, NC:2 * NC],
        start=False, stop=True),
        reads=[b_["cp"], bcst], writes=[bps5])
    c.emit("vector", lambda e: e.tensor_scalar(
        m1[:, :], P[5][:, 0:NC], KH, None, ALU.is_lt),
        reads=[bps5], writes=[b_["m1"]])
    c.emit("vector", lambda e: e.tensor_tensor(
        m1[:, :], m1[:, :], pos_tm[:, :], ALU.mult),
        reads=[b_["m1"], b_["pos"]], writes=[b_["m1"]])
    for b in range(nb):
        c.emit("vector", lambda e, b=b: e.tensor_tensor(
            wrow[:1, b * Tb:(b + 1) * Tb], sr[:1, b * Tb:(b + 1) * Tb],
            AROW[:1, b:b + 1].broadcast_to([1, Tb]), ALU.add),
            reads=[b_["sr"], b_["rows"]], writes=[b_["wrow"]])
    for b in range(nb):
        bps6 = c.tb("psum6")
        for q in range(cdiv(Tb, 512)):
            wq = min(512, Tb - q * 512)
            c.emit("tensor", lambda e, b=b, q=q, wq=wq: e.matmul(
                P[6][:, 0:wq], cst["ones_row_f"][:1, :],
                wrow[:1, b * Tb + q * 512:b * Tb + q * 512 + wq],
                start=True, stop=True),
                reads=[b_["wrow"], bcst], writes=[bps6])
            c.emit("vector", lambda e, q=q, wq=wq: e.tensor_copy(
                wb[:, q * 512:q * 512 + wq], P[6][:, 0:wq]),
                reads=[bps6], writes=[b_["wb"]])
        for cc in range(CC):
            col = b * CC + cc
            c.emit("vector", lambda e, col=col: e.tensor_scalar(
                gtmp[:, 0:Tb], wb[:, 0:Tb], t1[:, col:col + 1], 0.0,
                ALU.is_gt, ALU.add, accum_out=g_tm[:, col:col + 1]),
                reads=[b_["wb"], b_["t1"]], writes=[b_["g"]])
    c.emit("vector", lambda e: e.tensor_scalar(
        g_tm[:, :], g_tm[:, :], KH, None, ALU.is_lt),
        reads=[b_["g"]], writes=[b_["g"]])
    c.emit("vector", lambda e: e.tensor_scalar(
        pos_tm[:, :], pos_tm[:, :], -1.0, 1.0, ALU.mult, ALU.add),
        reads=[b_["pos"]], writes=[b_["pos"]])
    c.emit("vector", lambda e: e.tensor_tensor(
        g_tm[:, :], g_tm[:, :], pos_tm[:, :], ALU.mult),
        reads=[b_["g"], b_["pos"]], writes=[b_["g"]])
    c.emit("vector", lambda e: e.tensor_tensor(
        m1[:, :], m1[:, :], g_tm[:, :], ALU.add),
        reads=[b_["m1"], b_["g"]], writes=[b_["m1"]])
    bps2c = c.tb("psum2")
    for col in range(NC):
        c.emit("tensor", lambda e, col=col: e.transpose(
            P[2][0:1, 0:PD], m1[:, col:col + 1], cst["ident_f"][:, :]),
            reads=[b_["m1"], bcst], writes=[bps2c])
        c.emit("vector", lambda e, col=col: e.tensor_copy(
            mrow[:1, col * PD:(col + 1) * PD], P[2][0:1, 0:PD]),
            reads=[bps2c], writes=[b_["mrow"]])
    bmsb = b_["msb"]
    for q in range(cdiv(T, 512)):
        wq = min(512, T - q * 512)
        bps7 = c.tb("psum7")
        c.emit("tensor", lambda e, q=q, wq=wq: e.matmul(
            P[7][:, 0:wq], cst["ones_row_f"][:1, :],
            mrow[:1, q * 512:q * 512 + wq], start=True, stop=True),
            reads=[b_["mrow"], bcst], writes=[bps7])
        c.emit("vector", lambda e, q=q, wq=wq: e.tensor_copy(
            mask_sb[:, q * 512:q * 512 + wq], P[7][:, 0:wq]),
            reads=[bps7], writes=[bmsb])
    return bmsb


def sl_l0_ffn(c, P, cst, cfg, w, scr, res_in, res_out, tag):
    nc = c.nc
    E, T, TSn = cfg["E"], cfg["T"], cfg["TS"]
    dl = E // 2
    dh = cfg["dh"]
    sbk = SB(nc)
    mask_sb = sbk.alloc([PD, T], F32)
    sb = SB(nc)
    pn = "f" + tag
    xn = scr["xn"]
    emit_prenorm(c, sb, cst, P, res_in, xn, T, TSn, mode="ln", pname=pn + "ln")
    c.fence(xn.name)
    sb = c.phase(sb)
    emit_proj(c, sb, cst, P, xn, w["l0f_rt"], scr["srow"], T=T, TSn=TSn,
              out_dt=F32, pname=pn + "s")
    c.fence(scr["srow"].name)
    emit_routing(c, sb, cst, P, scr["srow"], mask_sb, cfg, pname=pn + "r")
    mask_keep = dict(self_w=None)
    mk = c.tb("l0f_mask")
    _merge(mk.w, c.tb(pn + "rmsb").w)
    sb = c.phase(sb)
    emit_prenorm(c, sb, cst, P, xn, scr["xr"], T, TSn, mode="rms",
                 gamma_col=cst["l0f_lgam"], pname=pn + "rl", src_bf=True)
    sb = c.phase(sb)
    emit_proj(c, sb, cst, P, scr["xr"], w["l0f_lw1"], scr["h1"], T=T, TSn=TSn,
              bias_col=cst["l0f_lb1"], act=AF.Gelu, pname=pn + "l1")
    sb = c.phase(sb)
    emit_proj(c, sb, cst, P, scr["h1"], w["l0f_lw2"], scr["resM"], T=T,
              TSn=TSn, bias_col=cst["l0f_lb2"], res=res_in,
              res_out=scr["resM"], out_dt=F32, pname=pn + "l2")
    sb = c.phase(sb)
    emit_prenorm(c, sb, cst, P, xn, scr["xr2"], T, TSn, mode="rms",
                 gamma_col=cst["l0f_hgam"], pname=pn + "rh", src_bf=True)
    sb = c.phase(sb)
    emit_proj(c, sb, cst, P, scr["xr2"], w["l0f_hw1"], scr["h2"], T=T,
              TSn=TSn, bias_col=cst["l0f_hb1"], act=AF.Gelu, pname=pn + "h1")
    sb = c.phase(sb)
    emit_proj(c, sb, cst, P, scr["h2"], w["l0f_hw2"], res_out, T=T, TSn=TSn,
              bias_col=cst["l0f_hb2"], res=scr["resM"], res_out=res_out,
              mask_sb=mask_sb, mask_tb=c.tb("l0f_mask"), out_dt=F32,
              pname=pn + "h2")
    c.fence(res_out.name)
    c.global_barrier()
    sb.close()
    sbk.close()


def emit_qknorm_rope(c, sb, cst, P, src, dst, *, T, Tb, E2, sc_col, xc, xs,
                     pname, roff=0):
    EC = E2 // PD
    H = E2 // 64
    TSn = min(512, Tb)
    nslab = T // TSn
    xc_t = cst_lazy(c, sb, cst, xc)
    xs_t = cst_lazy(c, sb, cst, xs)
    xa = [sb.alloc([PD, TSn], F32) for _ in range(2)]
    sq = [sb.alloc([PD, TSn], F32) for _ in range(2)]
    rr_ = [sb.alloc([8, TSn], F32) for _ in range(2)]
    qn = [sb.alloc([PD, TSn], BF16) for _ in range(2)]
    t1 = [sb.alloc([PD, TSn], F32) for _ in range(2)]
    t2 = [sb.alloc([PD, TSn], F32) for _ in range(2)]
    ob = [sb.alloc([PD, TSn], BF16) for _ in range(2)]
    bcst = c.tb("cst")
    for i in range(nslab):
        t0 = i * TSn
        tb_off = t0 % Tb
        for ci in range(EC):
            j = (i * EC + ci) % 2
            bx = c.tb(pname + "x", j)
            bsq = c.tb(pname + "sq", j)
            brr = c.tb(pname + "rr", j)
            bqn = c.tb(pname + "qn", j)
            bt1 = c.tb(pname + "t1", j)
            bt2 = c.tb(pname + "t2", j)
            bo = c.tb(pname + "o", j)
            c.emit("sync", lambda e, j=j, ci=ci, t0=t0: e.dma_start(
                xa[j][:, :], src[roff + ci * PD:roff + (ci + 1) * PD,
                                 t0:t0 + TSn]),
                reads=[c.tb(src.name)], writes=[bx], dma=True,
                ch=pname + f"L{j}")
            c.emit("scalar", lambda e, j=j: e.square(sq[j][:, :], xa[j][:, :]),
                reads=[bx], writes=[bsq])
            bps = c.tb("psum" + str(4 + j))
            c.emit("tensor", lambda e, j=j, ci=ci: e.matmul(
                P[4 + j][0:2, 0:TSn], cst["hsel2"][:, 0:2], sq[j][:, :],
                start=True, stop=True),
                reads=[bsq, bcst], writes=[bps])
            c.emit("vector", lambda e, j=j: e.tensor_scalar(
                rr_[j][0:2, :], P[4 + j][0:2, 0:TSn], 1e-24, None, ALU.max),
                reads=[bps], writes=[brr])
            c.emit("scalar", lambda e, j=j: e.activation(
                rr_[j][0:2, :], rr_[j][0:2, :], AF.Ln),
                reads=[brr], writes=[brr])
            c.emit("scalar", lambda e, j=j: e.activation(
                rr_[j][0:2, :], rr_[j][0:2, :], AF.Exp, scale=-0.5),
                reads=[brr], writes=[brr])
            c.emit("tensor", lambda e, j=j: e.matmul(
                P[4 + j][:, 0:TSn], cst["hexp2"][0:2, :], rr_[j][0:2, :],
                start=True, stop=True),
                reads=[brr, bcst], writes=[bps])
            c.emit("vector", lambda e, j=j: e.scalar_tensor_tensor(
                qn[j][:, :], xa[j][:, :], cst[sc_col][:, 0:1],
                P[4 + j][:, 0:TSn], ALU.mult, ALU.mult),
                reads=[bx, bps, bcst], writes=[bqn])
            bps2 = c.tb("psum" + str(6 + j))
            c.emit("tensor", lambda e, j=j: e.matmul(
                P[6 + j][:, 0:TSn], cst["R128"][:, :], qn[j][:, :],
                start=True, stop=True),
                reads=[bqn, bcst], writes=[bps2])
            c.emit("vector", lambda e, j=j, tb_off=tb_off: e.tensor_tensor(
                t1[j][:, :], qn[j][:, :], xc_t[:, tb_off:tb_off + TSn],
                ALU.mult),
                reads=[bqn, bcst], writes=[bt1])
            c.emit("vector", lambda e, j=j, tb_off=tb_off: e.tensor_tensor(
                t2[j][:, :], P[6 + j][:, 0:TSn],
                xs_t[:, tb_off:tb_off + TSn], ALU.mult),
                reads=[bps2, bcst], writes=[bt2])
            c.emit("vector", lambda e, j=j: e.tensor_tensor(
                ob[j][:, :], t1[j][:, :], t2[j][:, :], ALU.add),
                reads=[bt1, bt2], writes=[bo])
            c.emit("scalar", lambda e, j=j, ci=ci, t0=t0: e.dma_start(
                dst[ci * PD:(ci + 1) * PD, t0:t0 + TSn], ob[j][:, :]),
                reads=[bo], writes=[c.tb(dst.name, (i, ci))], dma=True,
                ch=pname + f"S{j}")


def emit_geglu(c, sb, cst, P, src, dst, *, T, TSn, HID, pname):
    NCH = cdiv(HID, PD)
    at = [sb.alloc([PD, TSn], BF16) for _ in range(4)]
    gt = [sb.alloc([PD, TSn], BF16) for _ in range(4)]
    gg = [sb.alloc([PD, TSn], F32) for _ in range(4)]
    ot = [sb.alloc([PD, TSn], BF16) for _ in range(4)]
    k = 0
    for s in range(T // TSn):
        t0 = s * TSn
        for ci in range(NCH):
            r0, r1 = ci * PD, min((ci + 1) * PD, HID)
            j = k % 4
            k += 1
            ba = c.tb(pname + "a", j)
            bg = c.tb(pname + "g", j)
            bgg = c.tb(pname + "gg", j)
            bo = c.tb(pname + "o", j)
            c.emit("sync", lambda e, j=j, r0=r0, r1=r1, t0=t0: e.dma_start(
                at[j][0:r1 - r0, :], src[r0:r1, t0:t0 + TSn]),
                reads=[c.tb(src.name)], writes=[ba], dma=True,
                ch=pname + f"A{j}")
            c.emit("sync", lambda e, j=j, r0=r0, r1=r1, t0=t0: e.dma_start(
                gt[j][0:r1 - r0, :], src[HID + r0:HID + r1, t0:t0 + TSn]),
                reads=[c.tb(src.name)], writes=[bg], dma=True,
                ch=pname + f"G{j}")
            c.emit("scalar", lambda e, j=j, mc=r1 - r0: e.activation(
                gg[j][0:mc, :], gt[j][0:mc, :], AF.Gelu),
                reads=[bg], writes=[bgg])
            c.emit("vector", lambda e, j=j, mc=r1 - r0: e.tensor_tensor(
                ot[j][0:mc, :], at[j][0:mc, :], gg[j][0:mc, :], ALU.mult),
                reads=[ba, bgg], writes=[bo])
            c.emit("scalar", lambda e, j=j, r0=r0, r1=r1, t0=t0: e.dma_start(
                dst[r0:r1, t0:t0 + TSn], ot[j][0:r1 - r0, :]),
                reads=[bo], writes=[c.tb(dst.name, (s, ci))], dma=True,
                ch=pname + f"S{j}")


def emit_add_pass(c, sb, cst, P, a, b, out, *, T, TSn, pname):
    E = a.shape[0]
    EC = E // PD
    at = [sb.alloc([PD, EC * TSn], F32) for _ in range(2)]
    bt = [sb.alloc([PD, EC * TSn], F32) for _ in range(2)]
    ot = [sb.alloc([PD, EC * TSn], F32) for _ in range(2)]
    a_r = a.rearrange("(ci p) t -> p ci t", p=PD)
    b_r = b.rearrange("(ci p) t -> p ci t", p=PD)
    o_r = out.rearrange("(ci p) t -> p ci t", p=PD)
    for i in range(T // TSn):
        j = i % 2
        t0 = i * TSn
        ba = c.tb(pname + "a", j)
        bb = c.tb(pname + "b", j)
        bo = c.tb(pname + "o", j)
        c.emit("sync", lambda e, j=j, t0=t0: e.dma_start(
            at[j][:, :].rearrange("p (ci t) -> p ci t", ci=EC),
            a_r[:, :, t0:t0 + TSn]),
            reads=[c.tb(a.name)], writes=[ba], dma=True, ch=pname + f"A{j}")
        c.emit("sync", lambda e, j=j, t0=t0: e.dma_start(
            bt[j][:, :].rearrange("p (ci t) -> p ci t", ci=EC),
            b_r[:, :, t0:t0 + TSn]),
            reads=[c.tb(b.name)], writes=[bb], dma=True, ch=pname + f"B{j}")
        for ci in range(EC):
            c.emit("vector", lambda e, j=j, ci=ci: e.tensor_tensor(
                ot[j][:, ci * TSn:(ci + 1) * TSn],
                at[j][:, ci * TSn:(ci + 1) * TSn],
                bt[j][:, ci * TSn:(ci + 1) * TSn], ALU.add),
                reads=[ba, bb], writes=[bo])
        c.emit("scalar", lambda e, j=j, t0=t0: e.dma_start(
            o_r[:, :, t0:t0 + TSn],
            ot[j][:, :].rearrange("p (ci t) -> p ci t", ci=EC)),
            reads=[bo], writes=[c.tb(out.name, i)], dma=True,
            ch=pname + f"S{j}")


def sl_l1_ffn(c, P, cst, cfg, w, scr, res_in, res_out, tag):
    nc = c.nc
    E, T, Tb, TSn, nb = cfg["E"], cfg["T"], cfg["Tb"], cfg["TS"], cfg["nb"]
    INNER = E // 2
    H = INNER // 64
    HID = cfg["hid"]
    WIN = cfg["window"]
    sb = SB(nc)
    pn = "g" + tag
    xn = scr["xn"]
    emit_prenorm(c, sb, cst, P, res_in, xn, T, TSn, mode="ln",
                 pname=pn + "ln", dst_f32=scr["resM"])
    sb = c.phase(sb)
    xcur_bf, xcur_f = xn, scr["resM"]
    for L in range(cfg.get("l1f_layers", 2)):
        pl = pn + str(L)
        emit_proj(c, sb, cst, P, xcur_bf, w[f"l1f_wqk{L}"], scr["qkro"],
                  T=T, TSn=TSn, out_dt=F32, pname=pl + "qk", shift_half=Tb)
        sb = c.phase(sb)
        emit_vtm_proj(c, sb, cst, P, xcur_bf, w[f"l1f_wv{L}"], scr["vtm"],
                      T=T, TSn=TSn, bias_row=w["l1f_bve"], pname=pl + "v",
                      shift_half=Tb)
        sb = c.phase(sb)
        emit_qknorm_rope(c, sb, cst, P, scr["qkro"], scr["qf"], T=T, Tb=Tb,
                         E2=INNER, sc_col=f"l1f_qs{L}", xc="xq_c", xs="xq_s",
                         pname=pl + "rq", roff=0)
        emit_qknorm_rope(c, sb, cst, P, scr["qkro"], scr["kf"], T=T, Tb=Tb,
                         E2=INNER, sc_col=f"l1f_ks{L}", xc="xk_c", xs="xk_s",
                         pname=pl + "rk", roff=INNER)
        sb = c.phase(sb)
        if L == 0 and "pdbg" in scr:
            cst["_pdbg"] = scr["pdbg"]
        emit_attn(c, sb, cst, P, qf=scr["qf"], kf=scr["kf"], vtm=scr["vtm"],
                  out=scr["ao"], Tb=Tb, nb=nb, Hq=H, hdq=64,
                  hdv=64, TQ=WIN, scale=8.0, Cshift=cfg["C"], causal=False,
                  local=True, pname=pl + "t")
        cst.pop("_pdbg", None)
        sb = c.phase(sb)
        xmid_f = scr["resN"]
        emit_proj(c, sb, cst, P, scr["ao"], w[f"l1f_wo{L}"],
                  xmid_f, T=T, TSn=TSn, res=xcur_f, res_out=xmid_f,
                  out_dt=F32, pname=pl + "o")
        sb = c.phase(sb)
        emit_prenorm(c, sb, cst, P, xmid_f, scr["xg"], T, TSn, mode="ln",
                     gamma_col=cst[f"l1f_lg{L}"], beta_col=cst[f"l1f_lb{L}"],
                     pname=pl + "mn", shift_half=Tb)
        sb = c.phase(sb)
        emit_proj(c, sb, cst, P, scr["xg"], w[f"l1f_w1{L}"], scr["h2"],
                  T=T, TSn=TSn, bias_col=cst[f"l1f_b1{L}"], pname=pl + "w1",
                  geglu_hid=HID)
        sb = c.phase(sb)
        if L == 0:
            nxt_bf, nxt_f = scr["xr2"], scr["resO"]
        else:
            nxt_bf, nxt_f = None, scr["resP"]
        emit_proj(c, sb, cst, P, scr["h2"], w[f"l1f_w2{L}"], nxt_f,
                  T=T, TSn=TSn, bias_col=cst[f"l1f_b2{L}"], res=xmid_f,
                  res_out=nxt_f, out_dt=F32, pname=pl + "w2",
                  also_bf=nxt_bf)
        sb = c.phase(sb)
        xcur_bf, xcur_f = nxt_bf, nxt_f
    emit_add_pass(c, sb, cst, P, res_in, scr["resP"], res_out, T=T, TSn=TSn,
                  pname=pn + "fin")
    c.fence(res_out.name)
    c.global_barrier()
    sb.close()



def emit_fir_conv(c, sb, cst, P, x32, out, *, T, Tb, TSn, taps, wdiag, pname,
                  act=None, out_dt=BF16, extra_out=None, D=None):
    k = taps
    DD = D if D is not None else x32.shape[0]
    EC = DD // PD
    nslab = T // TSn
    wsb = sb.alloc([PD, EC * k * PD], BF16)
    bw = c.tb(pname + "w")
    for ci in range(EC):
        c.emit("gpsimd", lambda e, ci=ci: e.dma_start(
            wsb[:, ci * k * PD:(ci + 1) * k * PD],
            wdiag[ci * PD:(ci + 1) * PD, :]),
            reads=[c.tb(wdiag.name)], writes=[bw], dma=True, ch=pname + "W")
    xt = [sb.alloc([PD, TSn + 31], BF16) for _ in range(2)]
    ot = [sb.alloc([PD, TSn], out_dt) for _ in range(2)]
    km1 = k - 1
    for s in range(nslab):
        t0 = s * TSn
        for ci in range(EC):
            j = (s * EC + ci) % 2
            bx = c.tb(pname + "x", j)
            bo = c.tb(pname + "o", j)
            if km1 == 0:
                c.emit("sync", lambda e, j=j, ci=ci, t0=t0: e.dma_start(
                    xt[j][:, 0:TSn], x32[ci * PD:(ci + 1) * PD, t0:t0 + TSn]),
                    reads=[c.tb(x32.name)], writes=[bx], dma=True,
                    ch=pname + f"L{j}")
            elif t0 % Tb == 0:
                c.emit("vector", lambda e, j=j: e.memset(xt[j][:, 0:km1], 0.0),
                       reads=[], writes=[bx])
                c.emit("sync", lambda e, j=j, ci=ci, t0=t0: e.dma_start(
                    xt[j][:, km1:km1 + TSn], x32[ci * PD:(ci + 1) * PD,
                                                 t0:t0 + TSn]),
                    reads=[c.tb(x32.name)], writes=[bx], dma=True,
                    ch=pname + f"L{j}")
            else:
                c.emit("sync", lambda e, j=j, ci=ci, t0=t0: e.dma_start(
                    xt[j][:, 0:km1 + TSn], x32[ci * PD:(ci + 1) * PD,
                                               t0 - km1:t0 + TSn]),
                    reads=[c.tb(x32.name)], writes=[bx], dma=True,
                    ch=pname + f"L{j}")
            bk = (s * EC + ci) % 2
            ps = P[bk]
            bps = c.tb(f"psum{bk}")
            mms = [
                (lambda e, j=j, ci=ci, kk=kk, first=(kk == 0),
                 last=(kk == k - 1), ps=ps: e.matmul(
                    ps[:, 0:TSn], wsb[:, (ci * k + kk) * PD:(ci * k + kk + 1) * PD],
                    xt[j][:, kk:kk + TSn], start=first, stop=last))
                for kk in range(k)]
            c.emit("tensor", mms, reads=[bw, bx], writes=[bps])
            fn = act if act is not None else AF.Identity
            c.emit("scalar", lambda e, j=j, ps=ps, fn=fn: e.activation(
                ot[j][:, :], ps[:, 0:TSn], fn),
                reads=[bps], writes=[bo])
            c.emit("scalar", lambda e, j=j, ci=ci, t0=t0: e.dma_start(
                out[ci * PD:(ci + 1) * PD, t0:t0 + TSn], ot[j][:, :]),
                reads=[bo], writes=[c.tb(out.name, (s, ci))], dma=True,
                ch=pname + f"S{j}")
            if extra_out is not None:
                c.emit("gpsimd", lambda e, j=j, ci=ci, t0=t0: e.dma_start(
                    extra_out[ci * PD:(ci + 1) * PD, t0:t0 + TSn], ot[j][:, :]),
                    reads=[bo], writes=[c.tb(extra_out.name, (s, ci))],
                    dma=True, ch=pname + f"E{j}")


def emit_headnorm(c, sb, cst, P, src, dsts, *, T, TSn, D, hd2, pname,
                  mode="l2", beta_sb=None, beta_tb=None, gamma_col=None):
    EC = D // PD
    H_ = D // hd2
    nslab = T // TSn
    xa = [sb.alloc([PD, EC * TSn], BF16) for _ in range(2)]
    sq = sb.alloc([PD, TSn], F32)
    rr_ = [sb.alloc([max(H_, 1), TSn], F32) for _ in range(2)]
    nb_ = [sb.alloc([PD, TSn], BF16) for _ in range(2)]
    bb_ = [sb.alloc([PD, TSn], BF16) for _ in range(2)]
    ob = [[sb.alloc([PD, TSn], BF16) for _ in range(len(dsts))]
          for _ in range(2)]
    bcst = c.tb("cst")
    for s in range(nslab):
        t0 = s * TSn
        js = s % 2
        bx = c.tb(pname + "x", js)
        psH = P[4 + js]
        bpsH = c.tb(f"psum{4 + js}")
        bsq = c.tb(pname + "sq")
        brr = c.tb(pname + "rr", js)
        for ci in range(EC):
            c.emit("sync", lambda e, js=js, ci=ci, t0=t0: e.dma_start(
                xa[js][:, ci * TSn:(ci + 1) * TSn],
                src[ci * PD:(ci + 1) * PD, t0:t0 + TSn]),
                reads=[c.tb(src.name)], writes=[bx], dma=True,
                ch=pname + f"L{js}")
        if mode != "none":
            for ci in range(EC):
                c.emit("scalar", lambda e, js=js, ci=ci: e.square(
                    sq[:, :], xa[js][:, ci * TSn:(ci + 1) * TSn]),
                    reads=[bx], writes=[bsq])
                c.emit("tensor", lambda e, ci=ci, psH=psH: e.matmul(
                    psH[0:H_, 0:TSn], cst["hselD"][:, ci * H_:(ci + 1) * H_],
                    sq[:, :], start=(ci == 0), stop=(ci == EC - 1)),
                    reads=[bsq, bcst], writes=[bpsH])
            if mode == "l2":
                c.emit("scalar", lambda e, js=js, psH=psH: e.activation(
                    rr_[js][0:H_, :], psH[0:H_, 0:TSn], AF.Ln,
                    bias=np.float32(1e-6)),
                    reads=[bpsH], writes=[brr])
            else:
                c.emit("scalar", lambda e, js=js, psH=psH: e.activation(
                    rr_[js][0:H_, :], psH[0:H_, 0:TSn], AF.Ln,
                    bias=np.float32(1e-5), scale=np.float32(1.0 / hd2)),
                    reads=[bpsH], writes=[brr])
            c.emit("scalar", lambda e, js=js: e.activation(
                rr_[js][0:H_, :], rr_[js][0:H_, :], AF.Exp, scale=-0.5),
                reads=[brr], writes=[brr])
        for ci in range(EC):
            jc = ci % 2
            bn = c.tb(pname + "bn", jc)
            bbb = c.tb(pname + "bb", jc)
            if mode != "none":
                psB = P[jc]
                bpsB = c.tb(f"psum{jc}")
                c.emit("tensor", lambda e, js=js, ci=ci, psB=psB: e.matmul(
                    psB[:, 0:TSn], cst["hexpD"][0:H_, ci * PD:(ci + 1) * PD],
                    rr_[js][0:H_, :], start=True, stop=True),
                    reads=[brr, bcst], writes=[bpsB])
                c.emit("scalar", lambda e, jc=jc, psB=psB: e.copy(
                    nb_[jc][:, :], psB[:, 0:TSn]),
                    reads=[bpsB], writes=[bn])
            if any(mb for _, mb in dsts):
                psB2 = P[2 + jc]
                bps2 = c.tb(f"psum{2 + jc}")
                c.emit("tensor", lambda e, ci=ci, t0=t0, psB2=psB2: e.matmul(
                    psB2[:, 0:TSn], cst["hexpD"][0:H_, ci * PD:(ci + 1) * PD],
                    beta_sb[0:H_, t0:t0 + TSn], start=True, stop=True),
                    reads=[beta_tb, bcst], writes=[bps2])
                c.emit("scalar", lambda e, jc=jc, psB2=psB2: e.copy(
                    bb_[jc][:, :], psB2[:, 0:TSn]),
                    reads=[bps2], writes=[bbb])
            for di, (dst, mul_beta) in enumerate(dsts):
                bo = c.tb(pname + "o", (jc, di))
                obt = ob[jc][di]
                if mode == "none":
                    c.emit("vector", lambda e, js=js, ci=ci, jc=jc, obt=obt:
                           e.tensor_tensor(
                        obt[:, :], xa[js][:, ci * TSn:(ci + 1) * TSn],
                        bb_[jc][:, :], ALU.mult),
                        reads=[bx, bbb], writes=[bo])
                elif gamma_col is not None:
                    c.emit("vector", lambda e, js=js, ci=ci, jc=jc, obt=obt:
                           e.scalar_tensor_tensor(
                        obt[:, :], xa[js][:, ci * TSn:(ci + 1) * TSn],
                        gamma_col[:, ci:ci + 1], nb_[jc][:, :], ALU.mult,
                        ALU.mult),
                        reads=[bx, bn, bcst], writes=[bo])
                else:
                    c.emit("vector", lambda e, js=js, ci=ci, jc=jc, obt=obt:
                           e.tensor_tensor(
                        obt[:, :], xa[js][:, ci * TSn:(ci + 1) * TSn],
                        nb_[jc][:, :], ALU.mult),
                        reads=[bx, bn], writes=[bo])
                if mul_beta and mode != "none":
                    c.emit("vector", lambda e, jc=jc, obt=obt: e.tensor_tensor(
                        obt[:, :], obt[:, :], bb_[jc][:, :], ALU.mult),
                        reads=[bo, bbb], writes=[bo])
                c.emit("scalar", lambda e, ci=ci, t0=t0, dst=dst, obt=obt:
                       e.dma_start(
                    dst[ci * PD:(ci + 1) * PD, t0:t0 + TSn], obt[:, :]),
                    reads=[bo], writes=[c.tb(dst.name, (s, ci))], dma=True,
                    ch=pname + f"S{jc}{di}")


def emit_transpose_pass(c, sb, cst, P, src, dst, *, R, T, pname):
    RC = R // PD
    TC = T // PD
    tin = [sb.alloc([PD, PD], BF16) for _ in range(2)]
    tout = [sb.alloc([PD, PD], BF16) for _ in range(2)]
    k = 0
    for tc in range(TC):
        for rc in range(RC):
            j = k % 2
            bk = 2 + k % 2
            k += 1
            bx = c.tb(pname + "x", j)
            bo = c.tb(pname + "o", j)
            bps = c.tb(f"psum{bk}")
            c.emit("sync", lambda e, j=j, rc=rc, tc=tc: e.dma_start(
                tin[j][:, :], src[rc * PD:(rc + 1) * PD, tc * PD:(tc + 1) * PD]),
                reads=[c.tb(src.name)], writes=[bx], dma=True,
                ch=pname + f"L{j}")
            c.emit("tensor", lambda e, j=j, bk=bk: e.transpose(
                P[bk][:, :].bitcast(BF16)[:, 0:PD], tin[j][:, :], cst["ident_bf"][:, :]),
                reads=[bx, c.tb("cst")], writes=[bps])
            c.emit("vector", lambda e, j=j, bk=bk: e.tensor_copy(
                tout[j][:, :], P[bk][:, :].bitcast(BF16)[:, 0:PD]),
                reads=[bps], writes=[bo])
            c.emit("scalar", lambda e, j=j, rc=rc, tc=tc: e.dma_start(
                dst[tc * PD:(tc + 1) * PD, rc * PD:(rc + 1) * PD], tout[j][:, :]),
                reads=[bo], writes=[c.tb(dst.name, (tc, rc))], dma=True,
                ch=pname + f"S{j}")


def emit_delta(c, sb, cst, P, cfg, lanes, bufs, pname):
    nc = c.nc
    Tb = cfg["Tb"]
    hd2 = cfg["hd2"]
    DKC = hd2 // PD
    NT = Tb // PD
    NCH = Tb // 32
    NG = NCH // 4
    CH = 32
    bcst = c.tb("cst")
    L = []
    for li, ln in enumerate(lanes):
        b, h = ln["b"], ln["h"]
        r0 = h * hd2
        tb0 = b * Tb
        d = {}
        d["qfm"] = sb.alloc([PD, DKC * Tb], BF16)
        d["kfm"] = sb.alloc([PD, DKC * Tb], BF16)
        d["kbfm"] = sb.alloc([PD, DKC * Tb], BF16)
        d["vsb"] = sb.alloc([PD, NT * hd2], BF16)
        d["kbsb"] = sb.alloc([PD, NT * hd2], BF16)
        d["ksb"] = sb.alloc([PD, NT * hd2], BF16)
        d["ubuf"] = sb.alloc([PD, NT * hd2], BF16)
        d["wbuf"] = sb.alloc([PD, NT * hd2], BF16)
        d["wfm"] = sb.alloc([PD, DKC * Tb], BF16)
        d["Rf"] = [sb.alloc([PD, 2 * hd2], F32) for _ in range(2)]
        d["Rb"] = [sb.alloc([PD, 2 * hd2], BF16) for _ in range(2)]
        d["N"] = [sb.alloc([PD, 32], BF16) for _ in range(2)]
        d["Nt"] = [sb.alloc([PD, 32], BF16) for _ in range(2)]
        d["Sf"] = sb.alloc([PD, DKC * hd2], F32)
        d["Sb"] = sb.alloc([PD, DKC * hd2], BF16)
        d["att"] = sb.alloc([PD, 32], BF16)
        d["uh"] = sb.alloc([PD, hd2], BF16)
        d["ob"] = sb.alloc([PD, NT * hd2], BF16)
        pfx = pname + f"l{li}"
        d["pfx"] = pfx
        for nm, srcd, rows in (("qfm", bufs["cqn"], True), ("kfm", bufs["ckn"], True),
                               ("kbfm", bufs["ckb"], True)):
            bb = c.tb(pfx + nm)
            for ci in range(DKC):
                c.emit("sync", lambda e, d=d, nm=nm, srcd=srcd, ci=ci, r0=r0,
                       tb0=tb0: e.dma_start(
                    d[nm][:, ci * Tb:(ci + 1) * Tb],
                    srcd[r0 + ci * PD:r0 + (ci + 1) * PD, tb0:tb0 + Tb]),
                    reads=[c.tb(srcd.name)], writes=[bb], dma=True,
                    ch=pfx + nm)
        for nm, srcd in (("vsb", bufs["t_vb"]), ("kbsb", bufs["t_kb"]),
                         ("ksb", bufs["t_kn"])):
            bb = c.tb(pfx + nm)
            for tt in range(NT):
                c.emit("sync", lambda e, d=d, nm=nm, srcd=srcd, tt=tt, r0=r0,
                       tb0=tb0: e.dma_start(
                    d[nm][:, tt * hd2:(tt + 1) * hd2],
                    srcd[tb0 + tt * PD:tb0 + (tt + 1) * PD, r0:r0 + hd2]),
                    reads=[c.tb(srcd.name)], writes=[bb], dma=True,
                    ch=pfx + nm)
        c.emit("vector", lambda e, d=d: e.memset(d["Sf"][:, :], 0.0),
               reads=[], writes=[c.tb(pfx + "Sf")])
        c.emit("vector", lambda e, d=d: e.memset(d["Sb"][:, :], 0.0),
               reads=[], writes=[c.tb(pfx + "Sb")])
        d["b"], d["h"], d["r0"], d["tb0"], d["li"] = b, h, r0, tb0, li
        L.append(d)

    for d in L:
        pfx = d["pfx"]
        pb = 0 if d["li"] == 0 else 4
        for g in range(NG):
            jr = g % 2
            bR = c.tb(pfx + "R", jr)
            bN = c.tb(pfx + "N", jr)
            bNt = c.tb(pfx + "Nt", jr)
            psG = P[pb]
            bpsG = c.tb(f"psum{pb}")
            psR = P[pb + 1]
            bpsR = c.tb(f"psum{pb + 1}")
            for gm in range(4):
                ch = g * 4 + gm
                for ci in range(DKC):
                    c.emit("tensor", lambda e, d=d, gm=gm, ch=ch, ci=ci,
                           psG=psG: e.matmul(
                        psG[32 * gm:32 * gm + 32, 0:32],
                        d["kbfm"][:, ci * Tb + ch * CH:ci * Tb + ch * CH + 32],
                        d["kfm"][:, ci * Tb + ch * CH:ci * Tb + ch * CH + 32],
                        start=(ci == 0), stop=(ci == DKC - 1),
                        tile_position=(0, 32 * gm)),
                        reads=[c.tb(pfx + "kbfm"), c.tb(pfx + "kfm")],
                        writes=[bpsG])
            c.emit("vector", lambda e, d=d, jr=jr, psG=psG:
                   e.scalar_tensor_tensor(
                d["N"][jr][:, :], psG[:, 0:32], -1.0, cst["maskSL4"][:, :],
                ALU.mult, ALU.mult),
                reads=[bpsG, bcst], writes=[bN])
            c.emit("vector", lambda e, d=d, jr=jr: e.transpose(
                d["Nt"][jr][:, :], d["N"][jr][:, :]),
                reads=[bN], writes=[bNt])
            c.emit("vector", lambda e, d=d, jr=jr, g=g: e.tensor_copy(
                d["Rf"][jr][:, 0:hd2], d["vsb"][:, g * hd2:(g + 1) * hd2]),
                reads=[c.tb(pfx + "vsb")], writes=[bR])
            c.emit("vector", lambda e, d=d, jr=jr, g=g: e.tensor_copy(
                d["Rf"][jr][:, hd2:2 * hd2], d["kbsb"][:, g * hd2:(g + 1) * hd2]),
                reads=[c.tb(pfx + "kbsb")], writes=[bR])
            c.emit("vector", lambda e, d=d, jr=jr: e.tensor_copy(
                d["Rb"][jr][:, :], d["Rf"][jr][:, :]),
                reads=[bR], writes=[bR])
            for it in range(5):
                for gm in range(4):
                    c.emit("tensor", lambda e, d=d, jr=jr, gm=gm, psR=psR:
                           e.matmul(
                        psR[32 * gm:32 * gm + 32, 0:2 * hd2],
                        d["Nt"][jr][32 * gm:32 * gm + 32, :],
                        d["Rb"][jr][32 * gm:32 * gm + 32, :],
                        start=True, stop=True,
                        tile_position=(32 * gm, 32 * gm)),
                        reads=[bNt, bR], writes=[bpsR])
                c.emit("vector", lambda e, d=d, jr=jr, psR=psR: e.tensor_tensor(
                    d["Rf"][jr][:, :], d["Rf"][jr][:, :], psR[:, 0:2 * hd2],
                    ALU.add),
                    reads=[bR, bpsR], writes=[bR])
                c.emit("vector", lambda e, d=d, jr=jr: e.tensor_copy(
                    d["Rb"][jr][:, :], d["Rf"][jr][:, :]),
                    reads=[bR], writes=[bR])
                if it < 4:
                    for gm in range(4):
                        c.emit("tensor", lambda e, d=d, jr=jr, gm=gm, psG=psG:
                               e.matmul(
                            psG[32 * gm:32 * gm + 32, 0:32],
                            d["Nt"][jr][32 * gm:32 * gm + 32, :],
                            d["N"][jr][32 * gm:32 * gm + 32, :],
                            start=True, stop=True,
                            tile_position=(32 * gm, 32 * gm)),
                            reads=[bNt, bN], writes=[bpsG])
                    c.emit("scalar", lambda e, d=d, jr=jr, psG=psG: e.copy(
                        d["N"][jr][:, :], psG[:, 0:32]),
                        reads=[bpsG], writes=[bN])
                    c.emit("vector", lambda e, d=d, jr=jr: e.transpose(
                        d["Nt"][jr][:, :], d["N"][jr][:, :]),
                        reads=[bN], writes=[bNt])
            c.emit("vector", lambda e, d=d, jr=jr, g=g: e.tensor_copy(
                d["ubuf"][:, g * hd2:(g + 1) * hd2], d["Rb"][jr][:, 0:hd2]),
                reads=[bR], writes=[c.tb(pfx + "ubuf")])
            c.emit("vector", lambda e, d=d, jr=jr, g=g: e.tensor_copy(
                d["wbuf"][:, g * hd2:(g + 1) * hd2],
                d["Rb"][jr][:, hd2:2 * hd2]),
                reads=[bR], writes=[c.tb(pfx + "wbuf")])
        for tt in range(NT):
            for ci in range(DKC):
                bk = pb + 2
                bps = c.tb(f"psum{bk}")
                c.emit("tensor", lambda e, d=d, tt=tt, ci=ci, bk=bk:
                       e.transpose(
                    P[bk][:, :].bitcast(BF16)[:, 0:PD],
                    d["wbuf"][:, tt * hd2 + ci * PD:tt * hd2 + (ci + 1) * PD],
                    cst["ident_bf"][:, :]),
                    reads=[c.tb(pfx + "wbuf"), bcst], writes=[bps])
                c.emit("vector", lambda e, d=d, tt=tt, ci=ci, bk=bk:
                       e.tensor_copy(
                    d["wfm"][:, ci * Tb + tt * PD:ci * Tb + (tt + 1) * PD],
                    P[bk][:, :].bitcast(BF16)[:, 0:PD]),
                    reads=[bps], writes=[c.tb(pfx + "wfm")])

    for ch in range(NCH):
        tt = ch // 4
        cr = 32 * (ch % 4)
        for d in L:
            pfx = d["pfx"]
            pb = 0 if d["li"] == 0 else 4
            psG, psU, psO, psD = P[pb], P[pb + 1], P[pb + 2], P[pb + 3]
            bG = c.tb(f"psum{pb}")
            bU = c.tb(f"psum{pb + 1}")
            bO = c.tb(f"psum{pb + 2}")
            bD = c.tb(f"psum{pb + 3}")
            batt = c.tb(pfx + "att")
            buh = c.tb(pfx + "uh")
            bSf = c.tb(pfx + "Sf")
            bSb = c.tb(pfx + "Sb")
            for ci in range(DKC):
                c.emit("tensor", lambda e, d=d, ci=ci, ch=ch, cr=cr, psG=psG:
                       e.matmul(
                    psG[cr:cr + 32, 0:32],
                    d["kfm"][:, ci * Tb + ch * CH:ci * Tb + ch * CH + 32],
                    d["qfm"][:, ci * Tb + ch * CH:ci * Tb + ch * CH + 32],
                    start=(ci == 0), stop=(ci == DKC - 1),
                    tile_position=(0, cr)),
                    reads=[c.tb(pfx + "kfm"), c.tb(pfx + "qfm")], writes=[bG])
            c.emit("vector", lambda e, d=d, cr=cr, psG=psG: e.tensor_tensor(
                d["att"][cr:cr + 32, :], psG[cr:cr + 32, 0:32],
                cst["maskTI4"][cr:cr + 32, :], ALU.mult),
                reads=[bG, bcst], writes=[batt])
            for ci in range(DKC):
                c.emit("tensor", lambda e, d=d, ci=ci, ch=ch, cr=cr, psU=psU:
                       e.matmul(
                    psU[cr:cr + 32, 0:hd2],
                    d["wfm"][:, ci * Tb + ch * CH:ci * Tb + ch * CH + 32],
                    d["Sb"][:, ci * hd2:(ci + 1) * hd2],
                    start=(ci == 0), stop=(ci == DKC - 1),
                    tile_position=(0, cr)),
                    reads=[c.tb(pfx + "wfm"), bSb], writes=[bU])
            c.emit("vector", lambda e, d=d, tt=tt, cr=cr, psU=psU:
                   e.tensor_tensor(
                d["uh"][cr:cr + 32, :],
                d["ubuf"][cr:cr + 32, tt * hd2:(tt + 1) * hd2],
                psU[cr:cr + 32, 0:hd2], ALU.subtract),
                reads=[c.tb(pfx + "ubuf"), bU], writes=[buh])
            for ci in range(DKC):
                c.emit("tensor", lambda e, d=d, ci=ci, ch=ch, cr=cr, psO=psO:
                       e.matmul(
                    psO[cr:cr + 32, 0:hd2],
                    d["qfm"][:, ci * Tb + ch * CH:ci * Tb + ch * CH + 32],
                    d["Sb"][:, ci * hd2:(ci + 1) * hd2],
                    start=(ci == 0), stop=False, tile_position=(0, cr)),
                    reads=[c.tb(pfx + "qfm"), bSb], writes=[bO])
            c.emit("tensor", lambda e, d=d, cr=cr, psO=psO: e.matmul(
                psO[cr:cr + 32, 0:hd2], d["att"][cr:cr + 32, :],
                d["uh"][cr:cr + 32, :], start=False, stop=True,
                tile_position=(cr, cr)),
                reads=[batt, buh], writes=[bO])
            c.emit("scalar", lambda e, d=d, tt=tt, cr=cr, psO=psO: e.copy(
                d["ob"][cr:cr + 32, tt * hd2:(tt + 1) * hd2],
                psO[cr:cr + 32, 0:hd2]),
                reads=[bO], writes=[c.tb(pfx + "ob")])
            for hi in range(DKC):
                c.emit("tensor", lambda e, d=d, tt=tt, cr=cr, hi=hi, psD=psD:
                       e.matmul(
                    psD[:, hi * hd2:(hi + 1) * hd2],
                    d["ksb"][cr:cr + 32, tt * hd2 + hi * PD:tt * hd2 + (hi + 1) * PD],
                    d["uh"][cr:cr + 32, :], start=True, stop=True,
                    tile_position=(cr, 0)),
                    reads=[c.tb(pfx + "ksb"), buh], writes=[bD])
            c.emit("vector", lambda e, d=d, psD=psD: e.tensor_tensor(
                d["Sf"][:, :], d["Sf"][:, :], psD[:, 0:DKC * hd2], ALU.add),
                reads=[bSf, bD], writes=[bSf])
            c.emit("vector", lambda e, d=d: e.tensor_copy(
                d["Sb"][:, :], d["Sf"][:, :]),
                reads=[bSf], writes=[bSb])
    for d in L:
        pfx = d["pfx"]
        for tt in range(NT):
            c.emit("scalar", lambda e, d=d, tt=tt: e.dma_start(
                bufs["delta_tm"][d["tb0"] + tt * PD:d["tb0"] + (tt + 1) * PD,
                                 d["r0"]:d["r0"] + hd2],
                d["ob"][:, tt * hd2:(tt + 1) * hd2]),
                reads=[c.tb(pfx + "ob")],
                writes=[c.tb(bufs["delta_tm"].name, (d["b"], d["h"], tt))],
                dma=True, ch=pfx + "OS")


def emit_branch_stats(c, sb, cst, P, branches, gin, *, T, TSn, D, hd2,
                      pname):
    EC = D // PD
    H_ = D // hd2
    n = float(hd2)
    xa = [sb.alloc([PD, TSn], BF16) for _ in range(2)]
    tp = [sb.alloc([PD, TSn], BF16) for _ in range(2)]
    rows = sb.alloc([H_, 2 * TSn], F32)
    srow = sb.alloc([H_, 4 * TSn], BF16)
    bcst = c.tb("cst")
    Ttot = T
    for s in range(T // TSn):
        t0 = s * TSn
        for br, brd in enumerate(branches):
            bsums = [c.tb(f"psum{4 + k}") for k in range(3)]
            for ci in range(EC):
                j = ci % 2
                bx = c.tb(pname + "x", j)
                btp = c.tb(pname + "t", j)
                c.emit("sync", lambda e, j=j, ci=ci, t0=t0, brd=brd:
                       e.dma_start(xa[j][:, :],
                                   brd[ci * PD:(ci + 1) * PD, t0:t0 + TSn]),
                    reads=[c.tb(brd.name)], writes=[bx], dma=True,
                    ch=pname + f"L{j}")
                c.emit("tensor", lambda e, j=j, ci=ci: e.matmul(
                    P[4][0:H_, 0:TSn], cst["hselD_bf"][:, ci * H_:(ci + 1) * H_],
                    xa[j][:, :], start=(ci == 0), stop=(ci == EC - 1)),
                    reads=[bx, bcst], writes=[bsums[0]])
                c.emit("scalar", lambda e, j=j: e.activation(
                    tp[j][:, :], xa[j][:, :], AF.Abs),
                    reads=[bx], writes=[btp])
                c.emit("tensor", lambda e, j=j, ci=ci: e.matmul(
                    P[5][0:H_, 0:TSn], cst["hselD_bf"][:, ci * H_:(ci + 1) * H_],
                    tp[j][:, :], start=(ci == 0), stop=(ci == EC - 1)),
                    reads=[btp, bcst], writes=[bsums[1]])
                c.emit("scalar", lambda e, j=j: e.square(
                    tp[j][:, :], xa[j][:, :]),
                    reads=[bx], writes=[btp])
                c.emit("tensor", lambda e, j=j, ci=ci: e.matmul(
                    P[6][0:H_, 0:TSn], cst["hselD_bf"][:, ci * H_:(ci + 1) * H_],
                    tp[j][:, :], start=(ci == 0), stop=(ci == EC - 1)),
                    reads=[btp, bcst], writes=[bsums[2]])
            brw = c.tb(pname + "rows")
            bsr = c.tb(pname + "srow")
            c.emit("vector", lambda e: e.tensor_scalar(
                srow[0:H_, 0:TSn], P[4][0:H_, 0:TSn], 1.0 / n, None, ALU.mult),
                reads=[bsums[0]], writes=[bsr])
            c.emit("vector", lambda e: e.tensor_scalar(
                srow[0:H_, 2 * TSn:3 * TSn], P[5][0:H_, 0:TSn], 1.0 / n, None,
                ALU.mult),
                reads=[bsums[1]], writes=[bsr])
            c.emit("scalar", lambda e: e.activation(
                srow[0:H_, 3 * TSn:4 * TSn], P[6][0:H_, 0:TSn], AF.Sqrt,
                bias=np.float32(0.0)),
                reads=[bsums[2]], writes=[bsr])
            c.emit("scalar", lambda e: e.square(
                rows[0:H_, 0:TSn], P[4][0:H_, 0:TSn]),
                reads=[bsums[0]], writes=[brw])
            c.emit("vector", lambda e: e.scalar_tensor_tensor(
                rows[0:H_, TSn:2 * TSn], rows[0:H_, 0:TSn], -1.0 / n,
                P[6][0:H_, 0:TSn], ALU.mult, ALU.add),
                reads=[brw, bsums[2]], writes=[brw])
            c.emit("scalar", lambda e: e.activation(
                srow[0:H_, TSn:2 * TSn], rows[0:H_, TSn:2 * TSn], AF.Sqrt,
                scale=np.float32(1.0 / (n - 1.0))),
                reads=[brw], writes=[bsr])
            for st in range(4):
                r0 = 1024 + 4 * br + st
                c.emit("scalar", lambda e, st=st, r0=r0, t0=t0: e.dma_start(
                    bass.AP(gin.tensor, r0 * Ttot + t0, [[24 * Ttot, H_],
                                                         [1, TSn]]),
                    srow[0:H_, st * TSn:st * TSn + TSn]),
                    reads=[bsr], writes=[c.tb(gin.name, ("st", s, br, st))],
                    dma=True, ch=pname + f"S{st}")


def emit_gates(c, sb, sbk, cst, P, glog, glt, wgate, *, T, pname):
    tl = sb.alloc([1, 32], F32)
    irow = sb.alloc([1, 24], F32)
    gl = [sb.alloc([24, PD], F32) for _ in range(2)]
    wtm = [sb.alloc([PD, 24], F32) for _ in range(2)]
    t1 = [sb.alloc([PD, 24], F32) for _ in range(2)]
    red = [sb.alloc([PD, 8], F32) for _ in range(2)]
    itbc = sb.alloc([PD, 24], F32)
    wout = [sb.alloc([24, PD], F32) for _ in range(2)]
    bcst = c.tb("cst")
    btl = c.tb(pname + "tl")
    bir = c.tb(pname + "ir")
    c.emit("sync", lambda e: e.dma_start(tl[:1, 0:4], glt[:1, :]),
           reads=[c.tb(glt.name)], writes=[btl], dma=True, ch=pname + "L")
    c.emit("scalar", lambda e: e.activation(tl[:1, 4:8], tl[:1, 0:4], AF.Abs),
           reads=[btl], writes=[btl])
    c.emit("scalar", lambda e: e.activation(tl[:1, 8:12], tl[:1, 4:8], AF.Exp,
                                            scale=-1.0),
           reads=[btl], writes=[btl])
    c.emit("scalar", lambda e: e.activation(tl[:1, 12:16], tl[:1, 8:12],
                                            AF.Ln, bias=np.float32(1.0)),
           reads=[btl], writes=[btl])
    c.emit("vector", lambda e: e.tensor_scalar(tl[:1, 16:20], tl[:1, 0:4],
                                               0.0, None, ALU.max),
           reads=[btl], writes=[btl])
    c.emit("vector", lambda e: e.tensor_tensor(tl[:1, 20:24], tl[:1, 12:16],
                                               tl[:1, 16:20], ALU.add),
           reads=[btl], writes=[btl])
    c.emit("vector", lambda e: e.tensor_scalar(tl[:1, 24:28], tl[:1, 20:24],
                                               0.5, None, ALU.add),
           reads=[btl], writes=[btl])
    c.emit("vector", lambda e: e.reciprocal(tl[:1, 28:32], tl[:1, 24:28]),
           reads=[btl], writes=[btl])
    for h in range(4):
        c.emit("vector", lambda e, h=h: e.tensor_copy(
            irow[:1, 6 * h:6 * h + 6],
            tl[:1, 28 + h:29 + h].broadcast_to([1, 6])),
            reads=[btl], writes=[bir])
    bitbc = c.tb(pname + "itbc")
    bps2 = c.tb("psum6")
    c.emit("tensor", lambda e: e.matmul(
        P[6][:, 0:24], cst["ones_row_f"][:1, :], irow[:1, :],
        start=True, stop=True),
        reads=[bir, bcst], writes=[bps2])
    c.emit("scalar", lambda e: e.copy(itbc[:, :], P[6][:, 0:24]),
        reads=[bps2], writes=[bitbc])
    for tc in range(T // PD):
        jt = tc % 2
        bgl = c.tb(pname + "gl", jt)
        bw = c.tb(pname + "w", jt)
        br_ = c.tb(pname + "r", jt)
        c.emit("sync", lambda e, tc=tc, jt=jt: e.dma_start(
            gl[jt][:, :], glog[0:24, tc * PD:(tc + 1) * PD]),
            reads=[c.tb(glog.name)], writes=[bgl], dma=True,
            ch=pname + f"G{jt}")
        bk = 2 + jt
        bps = c.tb(f"psum{bk}")
        c.emit("tensor", lambda e, bk=bk, jt=jt: e.transpose(
            P[bk][:, 0:24], gl[jt][0:24, :], cst["ident_f"][0:24, 0:24]),
            reads=[bgl, bcst], writes=[bps])
        c.emit("vector", lambda e, bk=bk, jt=jt: e.tensor_tensor(
            wtm[jt][:, :], P[bk][:, 0:24], itbc[:, :], ALU.mult),
            reads=[bps, bitbc], writes=[bw])
        c.emit("vector", lambda e, jt=jt: e.tensor_reduce(
            red[jt][:, 0:4], wtm[jt][:, :].rearrange("p (h s) -> p h s", h=4),
            AX.X, ALU.max),
            reads=[bw], writes=[br_])
        c.emit("vector", lambda e, jt=jt: e.tensor_tensor(
            t1[jt][:, :].rearrange("p (h s) -> p h s", h=4),
            wtm[jt][:, :].rearrange("p (h s) -> p h s", h=4),
            red[jt][:, 0:4].unsqueeze(2).broadcast_to([PD, 4, 6]),
            ALU.subtract),
            reads=[bw, br_], writes=[bw])
        c.emit("scalar", lambda e, jt=jt: e.activation(
            t1[jt][:, :], t1[jt][:, :], AF.Exp),
            reads=[bw], writes=[bw])
        c.emit("vector", lambda e, jt=jt: e.tensor_reduce(
            red[jt][:, 4:8], t1[jt][:, :].rearrange("p (h s) -> p h s", h=4),
            AX.X, ALU.add),
            reads=[bw], writes=[br_])
        c.emit("vector", lambda e, jt=jt: e.reciprocal(
            red[jt][:, 4:8], red[jt][:, 4:8]),
            reads=[br_], writes=[br_])
        c.emit("vector", lambda e, jt=jt: e.tensor_tensor(
            wtm[jt][:, :].rearrange("p (h s) -> p h s", h=4),
            t1[jt][:, :].rearrange("p (h s) -> p h s", h=4),
            red[jt][:, 4:8].unsqueeze(2).broadcast_to([PD, 4, 6]), ALU.mult),
            reads=[bw, br_], writes=[bw])
        c.emit("tensor", lambda e, bk=bk, jt=jt: e.transpose(
            P[bk][0:24, 0:PD], wtm[jt][:, :], cst["ident_f"][:, :]),
            reads=[bw, bcst], writes=[bps])
        bwo = c.tb(pname + "wo", jt)
        c.emit("vector", lambda e, bk=bk, jt=jt: e.tensor_copy(
            wout[jt][:, :], P[bk][0:24, 0:PD]),
            reads=[bps], writes=[bwo])
        c.emit("scalar", lambda e, tc=tc, jt=jt: e.dma_start(
            wgate[0:24, tc * PD:(tc + 1) * PD], wout[jt][:, :]),
            reads=[bwo], writes=[c.tb(wgate.name, tc)], dma=True,
            ch=pname + f"S{jt}")


def emit_fuse(c, sb, cst, P, branches, wgate, wg_tb, fused, *, T, TSn, D,
              hd2, pname):
    EC = D // PD
    bt = [[sb.alloc([PD, TSn], BF16) for _ in range(6)] for _ in range(2)]
    wbc = [sb.alloc([PD, 6 * TSn], BF16) for _ in range(2)]
    wrow = [sb.alloc([1, 6 * TSn], F32) for _ in range(2)]
    acc = [sb.alloc([PD, TSn], BF16) for _ in range(2)]
    tmp = [sb.alloc([PD, TSn], BF16) for _ in range(2)]
    bcst = c.tb("cst")
    hpc = PD // hd2 if hd2 < PD else 1
    for s in range(T // TSn):
        t0 = s * TSn
        for hh in range(D // hd2):
            jh = hh % 2
            bwb = c.tb(pname + "wb", jh)
            bwr = c.tb(pname + "wr", jh)
            for s6 in range(6):
                c.emit("sync", lambda e, jh=jh, hh=hh, s6=s6, t0=t0:
                       e.dma_start(wrow[jh][:1, s6 * TSn:(s6 + 1) * TSn],
                                   wgate[6 * hh + s6:6 * hh + s6 + 1,
                                         t0:t0 + TSn]),
                    reads=[c.tb(wgate.name)], writes=[bwr], dma=True,
                    ch=pname + f"W{jh}")
            for s6 in range(6):
                bk = 4 + (s6 % 2)
                bps = c.tb(f"psum{bk}")
                c.emit("tensor", lambda e, jh=jh, s6=s6, bk=bk: e.matmul(
                    P[bk][:, 0:TSn], cst["ones_row_f"][:1, :],
                    wrow[jh][:1, s6 * TSn:(s6 + 1) * TSn],
                    start=True, stop=True),
                    reads=[bwr, bcst], writes=[bps])
                c.emit("scalar", lambda e, jh=jh, s6=s6, bk=bk: e.copy(
                    wbc[jh][:, s6 * TSn:(s6 + 1) * TSn], P[bk][:, 0:TSn]),
                    reads=[bps], writes=[bwb])
            for cc in range(hd2 // PD):
                ci = hh * (hd2 // PD) + cc
                jc = ci % 2
                bacc = c.tb(pname + "acc", jc)
                for s6 in range(6):
                    bx = c.tb(pname + "x", (jc, s6))
                    c.emit("sync", lambda e, jc=jc, s6=s6, ci=ci, t0=t0:
                           e.dma_start(bt[jc][s6][:, :],
                                       branches[s6][ci * PD:(ci + 1) * PD,
                                                    t0:t0 + TSn]),
                        reads=[c.tb(branches[s6].name)], writes=[bx],
                        dma=True, ch=pname + f"L{jc}{s6}")
                for s6 in range(6):
                    bx = c.tb(pname + "x", (jc, s6))
                    if s6 == 0:
                        c.emit("vector", lambda e, jc=jc, jh=jh, s6=s6:
                               e.tensor_tensor(
                            acc[jc][:, :], bt[jc][s6][:, :],
                            wbc[jh][:, s6 * TSn:(s6 + 1) * TSn], ALU.mult),
                            reads=[bx, bwb], writes=[bacc])
                    else:
                        btm = c.tb(pname + "tm", jc)
                        c.emit("vector", lambda e, jc=jc, jh=jh, s6=s6:
                               e.tensor_tensor(
                            tmp[jc][:, :], bt[jc][s6][:, :],
                            wbc[jh][:, s6 * TSn:(s6 + 1) * TSn], ALU.mult),
                            reads=[bx, bwb], writes=[btm])
                        c.emit("vector", lambda e, jc=jc: e.tensor_tensor(
                            acc[jc][:, :], acc[jc][:, :], tmp[jc][:, :],
                            ALU.add),
                            reads=[bacc, btm], writes=[bacc])
                c.emit("scalar", lambda e, jc=jc, ci=ci, t0=t0: e.dma_start(
                    fused[ci * PD:(ci + 1) * PD, t0:t0 + TSn], acc[jc][:, :]),
                    reads=[bacc], writes=[c.tb(fused.name, (s, ci))],
                    dma=True, ch=pname + f"S{jc}")


def sl_l1_attn_b(c, P, cst, cfg, w, scr, res_in, res_out):
    nc = c.nc
    E, T, Tb, TSn, nb = cfg["E"], cfg["T"], cfg["Tb"], cfg["TS"], cfg["nb"]
    hd2 = cfg["hd2"]
    H_ = E // hd2
    sbk = SB(nc)
    beta_sb = sbk.alloc([H_, T], F32)
    sb = SB(nc)
    pn = "db"
    xn = scr["xn"]
    emit_prenorm(c, sb, cst, P, res_in, xn, T, TSn, mode="ln", pname=pn + "ln")
    sb = c.phase(sb)
    cpt = [sb.alloc([PD, TSn], BF16) for _ in range(2)]
    for s in range(T // TSn):
        for ci in range(E // PD):
            j = (s * (E // PD) + ci) % 2
            bx = c.tb(pn + "cp", j)
            c.emit("sync", lambda e, j=j, ci=ci, s=s: e.dma_start(
                cpt[j][:, :], xn[ci * PD:(ci + 1) * PD, s * TSn:(s + 1) * TSn]),
                reads=[c.tb(xn.name)], writes=[bx], dma=True, ch=pn + f"c{j}")
            c.emit("scalar", lambda e, j=j, ci=ci, s=s: e.dma_start(
                scr["gin"][ci * PD:(ci + 1) * PD, s * TSn:(s + 1) * TSn],
                cpt[j][:, :]),
                reads=[bx], writes=[c.tb(scr["gin"].name, ("x", s, ci))],
                dma=True, ch=pn + f"d{j}")
    emit_proj(c, sb, cst, P, xn, w["l1b_qw"], scr["pj3"], T=T, TSn=TSn,
              pname=pn + "pq")
    sb = c.phase(sb)
    emit_proj(c, sb, cst, P, xn, w["l1b_kw"], scr["pk3"], T=T, TSn=TSn,
              pname=pn + "pk")
    sb = c.phase(sb)
    emit_proj(c, sb, cst, P, xn, w["l1b_vw"], scr["pv3"], T=T, TSn=TSn,
              pname=pn + "pv")
    sb = c.phase(sb)
    emit_proj(c, sb, cst, P, xn, w["l1b_bw"], scr["glog"], T=T, TSn=TSn,
              out_dt=F32, pname=pn + "pb")
    sb = c.phase(sb)
    bl = sb.alloc([H_, T], F32)
    bbl = c.tb(pn + "bl")
    c.emit("sync", lambda e: e.dma_start(bl[0:H_, :], scr["glog"][0:H_, :]),
           reads=[c.tb(scr["glog"].name)], writes=[bbl], dma=True,
           ch=pn + "bL")
    c.emit("scalar", lambda e: e.activation(beta_sb[0:H_, :], bl[0:H_, :],
                                            AF.Sigmoid),
           reads=[bbl], writes=[c.tb(pn + "beta")])
    beta_tb = c.tb(pn + "beta")
    emit_fir_conv(c, sb, cst, P, scr["pj3"], scr["cq"], T=T, Tb=Tb, TSn=TSn,
                  taps=4, wdiag=w["l1b_qcd"], act=AF.Silu, pname=pn + "cq",
                  D=E)
    sb = c.phase(sb)
    emit_fir_conv(c, sb, cst, P, scr["pk3"], scr["ck"], T=T, Tb=Tb, TSn=TSn,
                  taps=4, wdiag=w["l1b_kcd"], act=AF.Silu, pname=pn + "ck",
                  D=E)
    sb = c.phase(sb)
    emit_fir_conv(c, sb, cst, P, scr["pv3"], scr["cv"], T=T, Tb=Tb, TSn=TSn,
                  taps=4, wdiag=w["l1b_vcd"], act=AF.Silu, pname=pn + "cv",
                  D=E)
    sb = c.phase(sb)
    beta_tb = c.tb(pn + "beta")
    emit_headnorm(c, sb, cst, P, scr["cq"], [(scr["cqn"], False)], T=T,
                  TSn=TSn, D=E, hd2=hd2, pname=pn + "nq", mode="l2")
    sb = c.phase(sb)
    beta_tb = c.tb(pn + "beta")
    emit_headnorm(c, sb, cst, P, scr["ck"], [(scr["ckn"], False),
                                             (scr["ckb"], True)], T=T,
                  TSn=TSn, D=E, hd2=hd2, pname=pn + "nk", mode="l2",
                  beta_sb=beta_sb, beta_tb=beta_tb)
    sb = c.phase(sb)
    beta_tb = c.tb(pn + "beta")
    emit_headnorm(c, sb, cst, P, scr["cv"], [(scr["cvb"], True)], T=T,
                  TSn=TSn, D=E, hd2=hd2, pname=pn + "nv", mode="none",
                  beta_sb=beta_sb, beta_tb=beta_tb)
    sb = c.phase(sb)
    emit_transpose_pass(c, sb, cst, P, scr["cvb"], scr["t_vb"], R=E, T=T,
                        pname=pn + "tv")
    sb = c.phase(sb)
    emit_transpose_pass(c, sb, cst, P, scr["ckb"], scr["t_kb"], R=E, T=T,
                        pname=pn + "tk")
    sb = c.phase(sb)
    emit_transpose_pass(c, sb, cst, P, scr["ckn"], scr["t_kn"], R=E, T=T,
                        pname=pn + "tn")
    sb = c.phase(sb)
    for kk, nm in ((1, "fb1"), (3, "fb3"), (7, "fb7"), (31, "fb31")):
        emit_fir_conv(c, sb, cst, P, scr["cv"], scr[nm], T=T, Tb=Tb, TSn=TSn,
                      taps=kk, wdiag=w[f"l1b_fir{kk}"], pname=pn + f"f{kk}",
                      D=E)
        sb = c.phase(sb)
    bufs = dict(cqn=scr["cqn"], ckn=scr["ckn"], ckb=scr["ckb"],
                t_vb=scr["t_vb"], t_kb=scr["t_kb"], t_kn=scr["t_kn"],
                delta_tm=scr["dtm"])
    pairs = []
    for b in range(nb):
        for h in range(H_):
            pairs.append(dict(b=b, h=h))
    for pi in range(0, len(pairs), 2):
        lanes = pairs[pi:pi + 2]
        emit_delta(c, sb, cst, P, cfg, lanes, bufs, pn + f"D{pi}")
        sb = c.phase(sb)
    emit_transpose_pass(c, sb, cst, P, scr["dtm"], scr["df"], R=T, T=E,
                        pname=pn + "td")
    sb = c.phase(sb)
    branches = [scr["fb1"], scr["fb3"], scr["fb7"], scr["fb31"], scr["df"],
                scr["cv"]]
    emit_branch_stats(c, sb, cst, P, branches, scr["gin"], T=T, TSn=TSn, D=E,
                      hd2=hd2, pname=pn + "bs")
    sb = c.phase(sb)
    emit_proj(c, sb, cst, P, scr["gin"], w["l1b_mw1"], scr["hg"], T=T,
              TSn=TSn, bias_col=cst["l1b_b1"], act=AF.Gelu, pname=pn + "m1")
    sb = c.phase(sb)
    emit_proj(c, sb, cst, P, scr["hg"], w["l1b_mw2"], scr["glog"], T=T,
              TSn=TSn, bias_col=cst["l1b_b2"], out_dt=F32, pname=pn + "m2")
    sb = c.phase(sb)
    emit_gates(c, sb, sbk, cst, P, scr["glog"], w["l1b_glt"], scr["wgt"],
               T=T, pname=pn + "gt")
    sb = c.phase(sb)
    emit_fuse(c, sb, cst, P, branches, scr["wgt"], c.tb(scr["wgt"].name),
              scr["ff"], T=T, TSn=TSn, D=E, hd2=hd2, pname=pn + "fu")
    sb = c.phase(sb)
    emit_headnorm(c, sb, cst, P, scr["ff"], [(scr["fo"], False)], T=T,
                  TSn=TSn, D=E, hd2=hd2, pname=pn + "on", mode="rmsh",
                  gamma_col=cst["l1b_onorm"])
    sb = c.phase(sb)
    emit_proj(c, sb, cst, P, scr["fo"], w["l1b_ow"], res_out, T=T, TSn=TSn,
              res=res_in, res_out=res_out, out_dt=F32, pname=pn + "oo")
    c.fence(res_out.name)
    c.global_barrier()
    sb.close()
    sbk.close()



def build_const_host(cfg):
    TQ = min(512, cfg["Tb"])
    out = {}
    out["ones_col_f"] = np.ones((PD, 1), np.float32)
    out["ones_row_f"] = np.ones((1, PD), np.float32)
    out["ones_row_bf"] = np.ones((1, PD), BF)
    out["ident_bf"] = np.eye(PD, dtype=BF)
    out["ident_f"] = np.eye(PD, dtype=np.float32)
    kk = np.arange(PD)
    out["tri_lt"] = (kk[:, None] < kk[None, :]).astype(np.float32)
    s64 = np.zeros((PD, PD), np.float32)
    s64[64, :] = 1.0
    out["sel64"] = s64
    p = np.arange(PD, dtype=np.float32)[:, None]
    f = np.arange(TQ, dtype=np.float32)[None, :]
    Tb = cfg["Tb"]
    W = 2 * Tb - PD
    jj = np.arange(W, dtype=np.float32)[None, :]
    x0 = Tb - PD
    out["ABSD"] = np.abs(p - jj + x0).astype(np.float32)
    nrel = TQ // PD
    m = np.zeros((PD, nrel * TQ), BF)
    for rel in range(nrel):
        m[:, rel * TQ:(rel + 1) * TQ] = ((p + rel * PD) <= f).astype(BF)
    out["cmask"] = m
    if cfg.get("l1b"):
        hd2 = cfg["hd2"]
        D_ = cfg["E"]
        EC_ = D_ // PD
        H_ = D_ // hd2
        pp = np.arange(PD)
        hsel = np.zeros((PD, EC_ * H_), np.float32)
        hexp = np.zeros((H_, EC_ * PD), np.float32)
        for ci in range(EC_):
            hh = (ci * PD) // hd2
            hsel[:, ci * H_ + hh] = 1.0
            hexp[hh, ci * PD:(ci + 1) * PD] = 1.0
        out["hselD"] = hsel
        out["hexpD"] = hexp
        out["hselD_bf"] = hsel.astype(BF)
        i32 = np.arange(32)
        sl32 = (i32[None, :] < i32[:, None]).astype(np.float32)
        out["maskSL4"] = np.tile(sl32, (4, 1))
        out["maskTI"] = (i32[:, None] <= i32[None, :]).astype(np.float32)
        out["maskTI4"] = np.tile(out["maskTI"], (4, 1))
    if cfg.get("l1f"):
        Tb_ = cfg["Tb"]
        WIN = cfg["window"]
        DH_ = 64
        half = np.arange(0, DH_, 2, dtype=np.float64)
        inv = 1.0 / 10000.0 ** (half / DH_)
        t = np.arange(Tb_, dtype=np.float64)
        fr = t[:, None] * inv[None, :]
        fr2 = np.concatenate([fr, fr], 1)
        base = (half + 0.4 * DH_) / (1.4 * DH_)
        power = (t - Tb_ // 2) / float(WIN // 2)
        scl = base[None, :] ** power[:, None]
        scl2 = np.concatenate([scl, scl], 1)
        cosf = np.cos(fr2)
        sinf = np.sin(fr2)
        out["xq_c"] = np.tile((cosf * scl2).T.astype(np.float32), (2, 1))
        out["xq_s"] = np.tile((sinf * scl2).T.astype(np.float32), (2, 1))
        out["xk_c"] = np.tile((cosf / scl2).T.astype(np.float32), (2, 1))
        out["xk_s"] = np.tile((sinf / scl2).T.astype(np.float32), (2, 1))
        pp = np.arange(PD)
        out["hsel2"] = (pp[:, None] // 64 == np.arange(2)[None, :]).astype(np.float32)
        out["hexp2"] = np.ascontiguousarray(out["hsel2"].T)
        R = np.zeros((PD, PD), BF)
        for m_ in range(PD):
            if m_ % 64 < 32:
                R[m_ + 32, m_] = -1.0
            else:
                R[m_ - 32, m_] = 1.0
        out["R128"] = R
        nu = WIN // PD
        lm = np.zeros((PD, 2 * nu * WIN), BF)
        pw = np.arange(PD)[:, None]
        fw = np.arange(WIN)[None, :]
        for u in range(-(nu - 1), nu + 1):
            mi = u + nu - 1
            keep = (fw - pw + PD * u >= 0) & (fw - pw + PD * u <= WIN)
            lm[:, mi * WIN:(mi + 1) * WIN] = keep.astype(BF)
        out["lmask"] = lm
    if cfg.get("rope_l0b"):
        Tb_ = cfg["Tb"]
        nb_ = cfg.get("nb", 2)
        hs = 64
        t = np.arange(1, Tb_ + 1, dtype=np.float64)[:, None]
        i = np.arange(hs // 2, dtype=np.float64)
        theta = (1.0 / 10000.0) ** (2.0 * i / hs)
        ang = t * theta
        c32 = np.cos(ang).T.astype(np.float32)
        s32 = np.sin(ang).T.astype(np.float32)
        out["ropecos"] = np.tile(np.tile(c32, (4, 1)), (1, nb_))
        out["ropesin"] = np.tile(np.tile(s32, (4, 1)), (1, nb_))
    return out


def build_program(cfg, wspec, xcols, consts):
    nc = bass.Bass("TRN2", target_bir_lowering=False)
    E, T = cfg["E"], cfg["T"]
    H0 = cfg["l0a_H"]
    x_in = nc.dram_tensor("x_fm", [E, T], F32, kind="ExternalInput").ap()
    y_out = nc.dram_tensor("y_fm", [E, T], F32, kind="ExternalOutput").ap()
    w = {}
    for name, shape in wspec.items():
        w[name] = nc.dram_tensor(name, list(shape), F32,
                                 kind="ExternalInput").ap()
    scr = {}
    dbg0 = set(cfg.get("dbg", ()))

    def sdram(nm, shape, dt):
        kind = "ExternalOutput" if nm in dbg0 else "Internal"
        return nc.dram_tensor("s_" + nm, shape, dt, kind=kind).ap()

    scr["xn"] = sdram("xn", [E, T], BF16)
    scr["qf"] = sdram("qf", [E, T], BF16)
    scr["kf"] = sdram("kf", [E, T], BF16)
    scr["vtm"] = sdram("vtm", [T, E + H0], BF16)
    scr["ao"] = sdram("ao", [E, T], BF16)
    scr["qkro"] = sdram("qkro", [2 * E, T], F32)
    if "pdbg" in dbg0:
        scr["pdbg"] = nc.dram_tensor("s_pdbg", [2048, 1024], BF16,
                                     kind="ExternalOutput").ap()
    if any(s.startswith("l0_ffn") for s in cfg["sublayers"]):
        scr["srow"] = sdram("srow", [1, T], F32)
        scr["xr"] = sdram("xr", [E, T], BF16)
        scr["xr2"] = sdram("xr2", [E, T], BF16)
        scr["h1"] = sdram("h1", [E // 2, T], BF16)
        scr["h2"] = nc.dram_tensor("s_h2", [max(cfg["dh"], cfg.get("hid", 0)), T], BF16).ap()
        scr["resM"] = sdram("resM", [E, T], F32)
    if any(s.startswith("l1_ffn") for s in cfg["sublayers"]):
        if "resM" not in scr:
            scr["resM"] = sdram("resM", [E, T], F32)
            scr["xr"] = sdram("xr", [E, T], BF16)
            scr["xr2"] = sdram("xr2", [E, T], BF16)
            scr["h2"] = nc.dram_tensor("s_h2", [max(cfg.get("dh", 0), cfg["hid"]), T], BF16).ap()
        scr["resN"] = sdram("resN", [E, T], F32)
        scr["resO"] = sdram("resO", [E, T], F32)
        scr["resP"] = sdram("resP", [E, T], F32)
        scr["xg"] = sdram("xg", [E, T], BF16)
        scr["h3"] = sdram("h3", [2 * cfg["hid"], T], BF16)
    dbg = set(cfg.get("dbg", ()))
    if "l1_attn_b" in cfg["sublayers"]:
        for nm, shape, dt in (
                ("pj3", [E, T], BF16), ("pk3", [E, T], BF16),
                ("pv3", [E, T], BF16), ("cq", [E, T], BF16),
                ("ck", [E, T], BF16), ("cv", [E, T], BF16),
                ("cqn", [E, T], BF16), ("ckn", [E, T], BF16),
                ("ckb", [E, T], BF16), ("cvb", [E, T], BF16),
                ("t_vb", [T, E], BF16), ("t_kb", [T, E], BF16),
                ("t_kn", [T, E], BF16), ("dtm", [T, E], BF16),
                ("df", [E, T], BF16), ("fb1", [E, T], BF16),
                ("fb3", [E, T], BF16), ("fb7", [E, T], BF16),
                ("fb31", [E, T], BF16), ("gin", [E + 96, T], BF16),
                ("hg", [E, T], BF16), ("glog", [24, T], F32),
                ("wgt", [24, T], F32), ("ff", [E, T], BF16),
                ("fo", [E, T], BF16)):
            kind = "ExternalOutput" if nm in dbg else "Internal"
            scr[nm] = nc.dram_tensor("s_" + nm, shape, dt, kind=kind).ap()
    resA = sdram("resA", [E, T], F32)
    resB = sdram("resB", [E, T], F32)

    c = Ctx(nc)
    sbg = SB(nc)
    stack = ExitStack()
    P = [stack.enter_context(nc.psum_tensor(f"pb{i}", [PD, 512], F32))
         for i in range(8)]
    cst = {"_lazy": {}}
    bcst = c.tb("cst")
    cpool_vals = cfg.get("cpool", [])
    if cpool_vals:
        cp = nc.dram_tensor("cst_cpool", [PD, len(cpool_vals)], F32,
                            kind="ExternalInput").ap()
        cpt = sbg.alloc([PD, len(cpool_vals)], F32, name="c_cpool")
        c.emit("sync", lambda e: e.dma_start(cpt[:, :], cp),
               reads=[], writes=[c.tb("cst", "cpool")], dma=True, ch="CST")
        for i, v in enumerate(cpool_vals):
            nc.const_aps.register(F32, float(v), cpt[:, i:i + 1])
    for name, arr in consts.items():
        dt = BF16 if arr.dtype == BF else F32
        d = nc.dram_tensor("cst_" + name, list(arr.shape), dt,
                           kind="ExternalInput").ap()
        if name in BIG_CONSTS:
            cst["_lazy"][name] = (d, arr.shape, dt)
            continue
        tl = sbg.alloc(list(arr.shape), dt, name="c_" + name)
        c.emit("sync", lambda e, tl=tl, d=d: e.dma_start(tl[:, :], d),
               reads=[], writes=[c.tb("cst", name)], dma=True, ch="CST")
        cst[name] = tl
    for name, shape in xcols.items():
        d = nc.dram_tensor("xc_" + name, list(shape), F32,
                           kind="ExternalInput").ap()
        tl = sbg.alloc(list(shape), F32, name="c_" + name)
        c.emit("sync", lambda e, tl=tl, d=d: e.dma_start(tl[:, :], d),
               reads=[], writes=[c.tb("cst", name)], dma=True, ch="CST")
        cst[name] = tl
    c.fence("cst")
    c.global_barrier()

    seq = cfg["sublayers"]
    bufs = [resA, resB]
    cur = x_in
    for i, sl in enumerate(seq):
        res_out = y_out if i == len(seq) - 1 else bufs[i % 2]
        if sl == "l0_attn_a":
            sl_l0_attn_a(c, P, cst, cfg, w, scr, cur, res_out)
        elif sl == "l1_attn_a":
            sl_l1_attn_a(c, P, cst, cfg, w, scr, cur, res_out)
        elif sl == "l0_attn_b":
            sl_l0_attn_b(c, P, cst, cfg, w, scr, cur, res_out)
        elif sl.startswith("l0_ffn"):
            sl_l0_ffn(c, P, cst, cfg, w, scr, cur, res_out, sl[-1])
        elif sl.startswith("l1_ffn"):
            sl_l1_ffn(c, P, cst, cfg, w, scr, cur, res_out, sl[-1])
        elif sl == "l1_attn_b":
            sl_l1_attn_b(c, P, cst, cfg, w, scr, cur, res_out)
        else:
            raise KeyError(sl)
        cur = res_out
    c.finalize()
    stack.close()
    sbg.close()
    return nc


def col_layout(b):
    n = b.shape[0]
    oc = cdiv(n, PD)
    pad = np.zeros(oc * PD, np.float32)
    pad[:n] = b
    return np.ascontiguousarray(pad.reshape(oc, PD).T)


def vtm_weights(WvT, bv, H, hd):
    E = WvT.shape[0]
    wve = np.zeros((E, H * (hd + 1)), np.float32)
    bve = np.zeros((1, H * (hd + 1)), np.float32)
    for h in range(H):
        wve[:, h * (hd + 1):h * (hd + 1) + hd] = WvT[:, h * hd:(h + 1) * hd]
        bve[0, h * (hd + 1):h * (hd + 1) + hd] = bv[h * hd:(h + 1) * hd]
        bve[0, h * (hd + 1) + hd] = 1.0
    return wve, bve


def host_prep(cfg, rr):
    E = cfg["E"]
    ws, xc = {}, {}
    seq = cfg["sublayers"]
    f32 = np.float32
    if "l0_attn_a" in seq:
        H = cfg["l0a_H"]
        hd = E // H
        Wqkv = rr["L0_attn_a_Wqkv_w"]
        bqkv = rr["L0_attn_a_Wqkv_b"]
        ws["l0a_wq"] = np.ascontiguousarray(Wqkv[0:E].T, f32)
        ws["l0a_wk"] = np.ascontiguousarray(Wqkv[E:2 * E].T, f32)
        wve, bve = vtm_weights(np.ascontiguousarray(Wqkv[2 * E:3 * E].T, f32),
                               bqkv[2 * E:3 * E], H, hd)
        ws["l0a_wve"] = wve
        ws["l0a_bve"] = bve
        ws["l0a_wo"] = np.ascontiguousarray(rr["L0_attn_a_out_w"].T, f32)
        ws["l0a_dd"] = rr["L0_attn_a_distance_decay"].reshape(1, 1).astype(f32)
        xc["l0a_bq"] = col_layout(bqkv[0:E])
        xc["l0a_bk"] = col_layout(bqkv[E:2 * E])
        xc["l0a_bo"] = col_layout(rr["L0_attn_a_out_b"])
    if "l1_attn_a" in seq:
        H = E // 64
        WqT = np.ascontiguousarray(rr["L1_attn_a_Wq"].T, f32)
        WkT = np.ascontiguousarray(rr["L1_attn_a_Wk"].T, f32)
        WvT = np.ascontiguousarray(rr["L1_attn_a_Wv"].T, f32)
        kvh = WkT.shape[1] // 64
        rep = H // kvh
        dup = lambda W: np.repeat(W.reshape(E, kvh, 64), rep, axis=1).reshape(E, H * 64)
        ws["l1a_wq"] = WqT
        ws["l1a_wk"] = np.ascontiguousarray(dup(WkT))
        wve, bve = vtm_weights(np.ascontiguousarray(dup(WvT)),
                               np.zeros(H * 64, f32), H, 64)
        ws["l1a_wve"] = wve
        ws["l1a_bve"] = bve
        ws["l1a_wo"] = np.ascontiguousarray(rr["L1_attn_a_Wo"].T, f32)
    if "l0_attn_b" in seq:
        H = E // 64
        def eo(W):
            WT = np.ascontiguousarray(W.T, f32).reshape(E, H * 32, 2)
            return WT[..., 0], WT[..., 1]
        qe, qo = eo(rr["L0_attn_b_Wq"])
        ke, ko = eo(rr["L0_attn_b_Wk"])
        ws["l0b_weo"] = np.ascontiguousarray(np.concatenate([qe, qo, ke, ko], 1))
        bq = rr["L0_attn_b_bq"].reshape(H * 32, 2)
        bk = rr["L0_attn_b_bk"].reshape(H * 32, 2)
        xc["l0b_beo"] = col_layout(np.concatenate(
            [bq[:, 0], bq[:, 1], bk[:, 0], bk[:, 1]]))
        wve, bve = vtm_weights(np.ascontiguousarray(rr["L0_attn_b_Wv"].T, f32),
                               rr["L0_attn_b_bv"], H, 64)
        ws["l0b_wve"] = wve
        ws["l0b_bve"] = bve
        ws["l0b_wo"] = np.ascontiguousarray(rr["L0_attn_b_Wo"].T, f32)
        xc["l0b_bo"] = col_layout(rr["L0_attn_b_bo"])
    if any(s.startswith("l0_ffn") for s in seq):
        ws["l0f_rt"] = rr["L0_ffn_routing_token"].reshape(E, 1).astype(f32)
        ws["l0f_lw1"] = rr["L0_ffn_light_w1"].astype(f32)
        ws["l0f_lw2"] = rr["L0_ffn_light_w2"].astype(f32)
        ws["l0f_hw1"] = rr["L0_ffn_heavy_w1"].astype(f32)
        ws["l0f_hw2"] = rr["L0_ffn_heavy_w2"].astype(f32)
        xc["l0f_lb1"] = col_layout(rr["L0_ffn_light_b1"])
        xc["l0f_lb2"] = col_layout(rr["L0_ffn_light_b2"])
        xc["l0f_hb1"] = col_layout(rr["L0_ffn_heavy_b1"])
        xc["l0f_hb2"] = col_layout(rr["L0_ffn_heavy_b2"])
        xc["l0f_lgam"] = col_layout(rr["L0_ffn_light_gamma"])
        xc["l0f_hgam"] = col_layout(rr["L0_ffn_heavy_gamma"])
    if any(s.startswith("l1_ffn") for s in seq):
        I2 = E // 2
        H = I2 // 64
        wqkv = rr["L1_ffn_wqkv"].astype(f32)
        for L in range(2):
            ws[f"l1f_wqk{L}"] = np.ascontiguousarray(wqkv[L][:, 0:2 * I2])
            wv, bv = vtm_weights(np.ascontiguousarray(wqkv[L][:, 2 * I2:3 * I2]),
                                 np.zeros(I2, f32), H, 64)
            ws[f"l1f_wv{L}"] = wv
            if L == 0:
                ws["l1f_bve"] = bv
            ws[f"l1f_wo{L}"] = rr["L1_ffn_wo"][L].astype(f32)
            W1 = rr["L1_ffn_w1"][L].astype(f32)
            b1 = rr["L1_ffn_b1"][L].astype(f32)
            HIDh = W1.shape[1] // 2
            ncoh = cdiv(HIDh, PD)
            colsW, colsB = [], []
            for oc in range(ncoh):
                o0, o1 = oc * PD, min((oc + 1) * PD, HIDh)
                colsW.append(W1[:, o0:o1])
                colsW.append(W1[:, HIDh + o0:HIDh + o1])
                for b0 in (b1[o0:o1], b1[HIDh + o0:HIDh + o1]):
                    bp = np.zeros(PD, f32)
                    bp[:o1 - o0] = b0
                    colsB.append(bp)
            ws[f"l1f_w1{L}"] = np.ascontiguousarray(np.concatenate(colsW, 1))
            xc[f"l1f_b1{L}"] = np.ascontiguousarray(np.stack(colsB, 1))
            ws[f"l1f_w2{L}"] = rr["L1_ffn_w2"][L].astype(f32)
            xc[f"l1f_qs{L}"] = np.tile(rr["L1_ffn_q_scale"][L], 2)[:, None].astype(f32)
            xc[f"l1f_ks{L}"] = np.tile(rr["L1_ffn_k_scale"][L], 2)[:, None].astype(f32)
            xc[f"l1f_lg{L}"] = col_layout(rr["L1_ffn_ln_g"][L])
            xc[f"l1f_lb{L}"] = col_layout(rr["L1_ffn_ln_b"][L])
            xc[f"l1f_b2{L}"] = col_layout(rr["L1_ffn_b2"][L])
    if "l1_attn_b" in seq:
        def conv_diag(wk):
            E_, k = wk.shape
            o = np.zeros((E_, k * PD), f32)
            for ci in range(E_ // PD):
                for jj in range(k):
                    blk = o[ci * PD:(ci + 1) * PD, jj * PD:(jj + 1) * PD]
                    np.fill_diagonal(blk, wk[ci * PD:(ci + 1) * PD, jj])
            return o
        ws["l1b_qw"] = rr["L1_attn_b_q_w"].astype(f32)
        ws["l1b_kw"] = rr["L1_attn_b_k_w"].astype(f32)
        ws["l1b_vw"] = rr["L1_attn_b_v_w"].astype(f32)
        ws["l1b_bw"] = rr["L1_attn_b_b_w"].astype(f32)
        ws["l1b_qcd"] = conv_diag(rr["L1_attn_b_qc_w"][:, 0, :].astype(f32))
        ws["l1b_kcd"] = conv_diag(rr["L1_attn_b_kc_w"][:, 0, :].astype(f32))
        ws["l1b_vcd"] = conv_diag(rr["L1_attn_b_vc_w"][:, 0, :].astype(f32))
        for kk in (1, 3, 7, 31):
            ws[f"l1b_fir{kk}"] = conv_diag(
                rr[f"L1_attn_b_fir_w{kk}"][:, 0, :].astype(f32))
        ws["l1b_mw1"] = rr["L1_attn_b_mlp_w1"].astype(f32)
        ws["l1b_mw2"] = rr["L1_attn_b_mlp_w2"].astype(f32)
        ws["l1b_ow"] = rr["L1_attn_b_o_w"].astype(f32)
        ws["l1b_glt"] = rr["L1_attn_b_gate_log_temp"].reshape(1, 4).astype(f32)
        xc["l1b_b1"] = col_layout(rr["L1_attn_b_mlp_b1"])
        xc["l1b_b2"] = col_layout(rr["L1_attn_b_mlp_b2"])
        hd2_ = rr["L1_attn_b_onorm_w"].shape[0]
        rep = E // hd2_
        xc["l1b_onorm"] = np.ascontiguousarray(
            np.tile(rr["L1_attn_b_onorm_w"], rep).reshape(E // PD, PD).T
            .astype(f32))
    return ws, xc



def full_cfg():
    return dict(E=1024, T=4096, Tb=2048, nb=2, TS=512, l0a_H=16, C=20.0,
                nheavy=1024, keff=1152.0, dh=4096, hid=2730, window=512,
                hd2=256, l1b=True, l1f=True, rope_l0b=True,
                cpool=[0.0, 1e-5, 1.0],
                sublayers=["l0_attn_a", "l0_ffn_1", "l0_attn_b", "l0_ffn_2",
                           "l1_attn_a", "l1_ffn_1", "l1_attn_b", "l1_ffn_2"])


def build_full():
    cfg = full_cfg()
    consts = build_const_host(cfg)
    return cfg, consts


def kernel(**inputs):
    cfg, consts = build_full()
    ws, xc = host_prep(cfg, inputs)
    nc = build_program(cfg, {k: v.shape for k, v in ws.items()},
                       {k: v.shape for k, v in xc.items()}, consts)
    x = inputs["L0_attn_a_x"]
    base = {}
    base["cst_cpool"] = np.tile(np.array(cfg["cpool"], np.float32)[None, :],
                                (PD, 1))
    for k, v in ws.items():
        base[k] = np.ascontiguousarray(v)
    for k, v in xc.items():
        base["xc_" + k] = np.ascontiguousarray(v)
    for k, v in consts.items():
        base["cst_" + k] = np.ascontiguousarray(v)
    in_maps = []
    for ci in range(8):
        m = dict(base)
        xs = np.asarray(x[2 * ci:2 * ci + 2], np.float32)
        m["x_fm"] = np.ascontiguousarray(xs.reshape(4096, 1024).T)
        in_maps.append(m)
    res = run_bass_kernel_spmd(nc, in_maps, core_ids=list(range(8)))
    out = np.empty((16, 2048, 1024), np.float32)
    for ci in range(8):
        y = res.results[ci]["y_fm"]
        out[2 * ci:2 * ci + 2] = np.moveaxis(
            np.asarray(y).reshape(1024, 2, 2048), 0, 2)
    return out
